# version 16
# baseline (speedup 1.0000x reference)
import math
import contextlib
import numpy as np
import ml_dtypes

import concourse.bass as bass
import concourse.mybir as mybir
from concourse.bass_utils import run_bass_kernel_spmd

F32 = mybir.dt.float32
BF16 = mybir.dt.bfloat16
AF = mybir.ActivationFunctionType
ALU = mybir.AluOpType
AX = mybir.AxisListType

T = 4096
D = 1024
NT = T // 128
NF = D // 128
BF = ml_dtypes.bfloat16
TWO_PI = float(2.0 * np.pi)
MAGIC = 12582912.0
PI_C = 3.1415925
N_CORES = 8

UNITS = ("tensor", "vector", "scalar", "gpsimd", "sync", "sdma", "vdma", "gdma", "adma")
FREE_SAME = {"tensor", "sdma", "vdma", "gdma", "adma", "sync"}
DMA_UNITS = {"sdma", "vdma", "gdma", "adma"}
DMA_ISSUER = {"sdma": "sync", "vdma": "vector", "gdma": "gpsimd", "adma": "scalar"}


class Op:
    __slots__ = ("unit", "fn", "reads", "writes", "idx", "uidx", "sem_val", "waits", "tag", "_deps", "_carries", "sem_key", "issue_wait")

    def __init__(self, unit, fn, reads, writes, idx, tag):
        self.unit = unit
        self.fn = fn
        self.reads = reads
        self.writes = writes
        self.idx = idx
        self.uidx = 0
        self.sem_val = None
        self.waits = []
        self.tag = tag
        self._deps = []
        self._carries = False


class Sched:
    def __init__(self, prog):
        self.p = prog
        self.ops = []
        self.writers = {}
        self.readers = {}

    def add(self, unit, fn, reads=(), writes=(), tag=""):
        op = Op(unit, fn, tuple(reads), tuple(writes), len(self.ops), tag)
        deps = []
        for b in op.reads:
            for w in self.writers.get(b, ()):
                deps.append((w, 0))
        for b in op.writes:
            for w in self.writers.get(b, ()):
                deps.append((w, 2))
            for r in self.readers.get(b, ()):
                deps.append((r, 1))
        need = {}
        for d, kind in deps:
            if d is op:
                continue
            if d.unit == op.unit and d.unit in FREE_SAME:
                if not (kind in (0, 1) and d.unit in DMA_UNITS):
                    continue
            k = d.idx if d.unit in DMA_UNITS else d.unit
            if k not in need or need[k].idx < d.idx:
                need[k] = d
        op._deps = list(need.values())
        for b in op.reads:
            self.readers.setdefault(b, []).append(op)
        for b in op.writes:
            ws = self.writers.setdefault(b, [])
            ws[:] = [w for w in ws if w.unit in DMA_UNITS or w.unit != op.unit]
            ws.append(op)
            self.readers[b] = []
        self.ops.append(op)
        return op

    def dma(self, queue, out, in_, reads=(), writes=(), tag="", **kw):
        def fn(eng):
            return eng.dma_start(out=out, in_=in_, **kw)
        return self.add(queue, fn, reads, writes, tag)

    ROT = 12

    def finalize(self):
        for op in self.ops:
            if op.unit in DMA_UNITS:
                op._carries = True
            for d in op._deps:
                d._carries = True
        last = {}
        for op in self.ops:
            last[op.unit] = op
        for op in last.values():
            op._carries = True
        cnt = {}
        dma_n = {u: 0 for u in DMA_UNITS}
        for op in self.ops:
            if op.unit in DMA_UNITS:
                j = dma_n[op.unit]
                dma_n[op.unit] = j + 1
                key = (op.unit, j % self.ROT)
                op.sem_key = key
                op.issue_wait = (key, 16 * (j // self.ROT)) if j >= self.ROT else None
                cnt[key] = cnt.get(key, 0) + 16
                op.sem_val = cnt[key]
            elif op._carries:
                key = (op.unit, 0)
                op.sem_key = key
                op.issue_wait = None
                cnt[key] = cnt.get(key, 0) + 1
                op.sem_val = cnt[key]
            else:
                op.sem_key = None
                op.issue_wait = None
        for op in self.ops:
            wmax = {}
            for d in op._deps:
                assert d.sem_val is not None, (op.tag, d.tag)
                if wmax.get(d.sem_key, -1) < d.sem_val:
                    wmax[d.sem_key] = d.sem_val
            op.waits = sorted(wmax.items())
        return cnt

    def replay(self):
        p = self.p
        cnt = self.finalize()
        sems = {k: p.sem("q_%s_%d" % k) for k in cnt}
        per_unit = {u: [o for o in self.ops if o.unit == u] for u in UNITS}
        blk = p.block
        final = dict(cnt)

        def emit_stream(eng, allops, tail=False):
            known = {}

            def w(key, v):
                if known.get(key, 0) < v:
                    eng.wait_ge(sems[key].h, v)
                    known[key] = v

            for op in allops:
                if op.issue_wait is not None:
                    w(*op.issue_wait)
                for (key, v) in op.waits:
                    w(key, v)
                try:
                    inst = op.fn(eng)
                except Exception as e:
                    raise RuntimeError("op tag=%s unit=%s idx=%d: %s" % (op.tag, op.unit, op.idx, e)) from e
                if op.sem_val is not None:
                    d = 16 if op.unit in DMA_UNITS else 1
                    inst.then_inc(sems[op.sem_key].h, d)
            if tail:
                for key, v in final.items():
                    w(key, v)

        for u in ("tensor", "vector", "scalar", "gpsimd", "sync"):
            qu = [q for q, e in DMA_ISSUER.items() if e == u]
            allops = per_unit[u] + [o for q in qu for o in per_unit[q]]
            allops.sort(key=lambda o: o.idx)
            getattr(blk, u)(lambda eng, a=allops, t=(u == "sync"): emit_stream(eng, a, t))
        return sems, final


class Prog:
    def __init__(self, nc):
        self.nc = nc
        self.es = contextlib.ExitStack()
        self.dram = {}
        self.sems = {}
        self.block = None
        self.sched = Sched(self)
        self._sb_names = set()
        self.fconst_vals = []
        self.fconst_sb = None

    def sem(self, name):
        if name not in self.sems:
            class _C:
                pass
            c = _C()
            c.h = self.es.enter_context(self.nc.semaphore(name))
            self.sems[name] = c
        return self.sems[name]

    def dt(self, name, shape, dtype, kind, addr_space=None):
        if addr_space is not None:
            ap = self.nc.dram_tensor(name, list(shape), dtype, kind=kind,
                                     addr_space=addr_space).ap()
        else:
            ap = self.nc.dram_tensor(name, list(shape), dtype, kind=kind).ap()
        self.dram[name] = ap
        return ap

    def sb(self, name, shape, dtype):
        assert name not in self._sb_names, name
        self._sb_names.add(name)
        return self.es.enter_context(self.nc.sbuf_tensor(name, list(shape), dtype))

    def fconst(self, v):
        v = float(v)
        if v not in self.fconst_vals:
            assert len(self.fconst_vals) < 64
            i = len(self.fconst_vals)
            self.fconst_vals.append(v)
            self.nc.const_aps.register(F32, v, self.fconst_sb[:, i:i + 1])
        return v

    def ps(self, name, shape, dtype=F32):
        assert name not in self._sb_names, name
        self._sb_names.add(name)
        return self.es.enter_context(self.nc.psum_tensor(name, list(shape), dtype))

    def init_psum_pool(self):
        self.pool = [self.ps("PSP%d" % j, [128, 1024], F32) for j in range(4)]

    def bankpair(self, j):
        return self.pool[j]

    def bank(self, i):
        return self.pool[i // 2][:, 512 * (i % 2):512 * (i % 2) + 512]

    def bank_bf(self, i, cols=128):
        return self.pool[i // 2].ap().bitcast(BF16)[:, 1024 * (i % 2):1024 * (i % 2) + cols]


def bkey(i):
    return "PS%d%s" % (i // 2, "ab"[i % 2])



NW = 1
NFP = 9
NBP = 6
NSP = 6
NRP = 3


def setup_arena(p):
    p.Wsl = [p.sb("W%d" % i, [128, 8192], BF16) for i in range(NW)]
    p.B1 = p.sb("BIG1", [128, 32768], BF16)
    p.B2 = p.sb("BIG2", [128, 33280], BF16)
    p.F = [p.sb("F%d" % i, [128, 1024], F32) for i in range(NFP)]
    p.B = [p.sb("B%d" % i, [128, 1024], BF16) for i in range(NBP)]
    p.S = [p.sb("S%d" % i, [128, 128], F32) for i in range(NSP)]
    p.R = [p.sb("R%d" % i, [1, 512], F32) for i in range(NRP)]


def big1(p, q):
    return p.B1[:, q * 8192:(q + 1) * 8192], "BIG1"


def big2(p, q):
    return p.B2[:, q * 8320:(q + 1) * 8320], "BIG2"


def load_weight_bf16(p, S, dname, dst_ap, dst_key, rows, cols):
    nf = rows // 128
    src = p.dram[dname].rearrange("(f p) o -> p f o", p=128)
    S.dma("gdma", dst_ap.rearrange("p (f o) -> p f o", f=nf), src,
          reads=(dname,), writes=(dst_key,), tag="ldw_" + dname)


def mm_acc(p, S, tag, psum, psum_key, lhsT_blocks, rhs_blocks, extra_reads=()):
    nf = len(lhsT_blocks)
    for f in range(nf):
        la, lk = lhsT_blocks[f]
        ra, rk = rhs_blocks[f]

        def fn(eng, la=la, ra=ra, f=f):
            return eng.matmul(psum, la, ra, start=(f == 0), stop=(f == nf - 1))
        if nf == 1:
            w = (psum_key,)
        elif f == 0:
            w = (psum_key + "@acc", psum_key)
        elif f == nf - 1:
            w = (psum_key,)
        else:
            w = (psum_key + "@acc",)
        S.add("tensor", fn, reads=(lk, rk) + tuple(extra_reads), writes=w,
              tag=tag + "mm%d" % f)


def transposes(p, S, src_sb, src_key, dst_fn, tag, nf=NF):
    for f in range(nf):
        ps = p.bank_bf(6 + f % 2)
        pk = bkey(6 + f % 2)

        def tfn(eng, f=f, ps=ps):
            return eng.transpose(ps, src_sb[:, bass.ts(f, 128)], p.ident_sb[:, :])
        S.add("tensor", tfn, reads=(src_key, "ident"), writes=(pk,), tag=tag + "t%d" % f)
        da, dk = dst_fn(f)

        def efn(eng, da=da, ps=ps):
            return eng.tensor_copy(da, ps)
        S.add("vector", efn, reads=(pk,), writes=(dk,), tag=tag + "e%d" % f)


def transposes2(p, S, src_ap, src_key, dst_fn, tag, nf=NF):
    for f in range(nf):
        ps = p.bank_bf(6 + f % 2) if False else None
    for f in range(nf):
        ps = p.pool[0].ap().bitcast(BF16)[:, (f % 2) * 1024:(f % 2) * 1024 + 128]
        pk = bkey(f % 2)

        def tfn(eng, f=f, ps=ps):
            return eng.transpose(ps, src_ap[:, bass.ts(f, 128)], p.ident_sb[:, :])
        S.add("tensor", tfn, reads=(src_key, "ident"), writes=(pk,), tag=tag + "t%d" % f)
        da, dk = dst_fn(f)

        def efn(eng, da=da, ps=ps):
            return eng.tensor_copy(da, ps)
        S.add("vector", efn, reads=(pk,), writes=(dk,), tag=tag + "e%d" % f)


def stream_x_tiles(p, S, dname, bufs, keyfmt, n=NT, tag="ld"):
    for i in range(n):
        b = i % len(bufs)
        S.dma("sdma", bufs[b][:, :], p.dram[dname][bass.ts(i, 128), :],
              reads=(dname,), writes=(keyfmt % b,), tag=tag + "%d" % i)
        yield i, b


def emit_ln(p, src, xn_d, xnT_d, tag):
    S = p.sched
    x = p.dram[src]
    xa = p.F[0:3]
    xak = ["F0", "F1", "F2"]
    junk = [(p.B[0], "B0"), (p.B[1], "B1")]
    sums, sqs = p.S[0][:, 0:NT], p.S[1][:, 0:NT]
    mcol, varc = p.S[2][:, 0:NT], p.S[3][:, 0:NT]
    rstd, bcol = p.S[4][:, 0:NT], p.S[5][:, 0:NT]
    xnf = [(p.F[3], "F3"), (p.F[4], "F4")]
    xnb = [(p.B[2], "B2"), (p.B[3], "B3")]
    xtv = [(p.B[4], "B4"), (p.B[5], "B5")]

    for i, b in stream_x_tiles(p, S, src, xa, "F%d", tag=tag + "l1_"):
        def sq(eng, i=i, b=b, j=i % 2):
            return eng.activation(junk[j][0][:, :], xa[b][:, :], AF.Square,
                                  accum_out=sqs[:, i:i + 1])
        S.add("scalar", sq, reads=(xak[b],), writes=(junk[i % 2][1], "S1"))

        def rs(eng, i=i, b=b):
            return eng.tensor_reduce(sums[:, i:i + 1], xa[b][:, :], AX.X, ALU.add)
        S.add("vector", rs, reads=(xak[b],), writes=("S0",))

    S.add("vector", lambda eng: eng.tensor_scalar_mul(mcol, sums, 1.0 / D),
          reads=("S0",), writes=("S2",))
    S.add("vector", lambda eng: eng.tensor_tensor(varc, mcol, mcol, ALU.mult),
          reads=("S2",), writes=("S3",))
    S.add("vector", lambda eng: eng.scalar_tensor_tensor(varc, sqs, 1.0 / D, varc,
                                                         ALU.mult, ALU.subtract),
          reads=("S1", "S3"), writes=("S3",))
    p.fconst(1e-5)
    p.fconst(-0.5)
    S.add("scalar", lambda eng: eng.activation(varc, varc, AF.Ln, bias=1e-5),
          reads=("S3", "fconsts"), writes=("S3",))
    S.add("scalar", lambda eng: eng.activation(rstd, varc, AF.Exp, scale=-0.5),
          reads=("S3", "fconsts"), writes=("S4",))
    S.add("vector", lambda eng: eng.tensor_tensor(bcol, mcol, rstd, ALU.mult),
          reads=("S2", "S4"), writes=("S5",))
    S.add("vector", lambda eng: eng.tensor_scalar_mul(bcol, bcol, -1.0),
          reads=("S5",), writes=("S5",))

    for i, b in stream_x_tiles(p, S, src, xa, "F%d", tag=tag + "l2_"):
        def ap_f(eng, i=i, b=b):
            return eng.tensor_scalar(xnf[i % 2][0][:, :], xa[b][:, :],
                                     mcol[:, i:i + 1], rstd[:, i:i + 1],
                                     ALU.subtract, ALU.mult)
        S.add("vector", ap_f, reads=(xak[b], "S2", "S4"), writes=(xnf[i % 2][1],))
        S.dma("sdma", p.dram[xn_d][bass.ts(i, 128), :], xnf[i % 2][0][:, :],
              reads=(xnf[i % 2][1],), writes=(xn_d,))

        def ap_b(eng, i=i, b=b):
            return eng.activation(xnb[i % 2][0][:, :], xa[b][:, :], AF.Identity,
                                  bias=bcol[:, i:i + 1], scale=rstd[:, i:i + 1])
        S.add("scalar", ap_b, reads=(xak[b], "S5", "S4"), writes=(xnb[i % 2][1],))

        def dstf(f, i2=i % 2):
            return xtv[i2][0][:, bass.ts(f, 128)], xtv[i2][1]
        transposes(p, S, xnb[i % 2][0], xnb[i % 2][1], dstf, tag + "tr%d_" % i)
        dst = p.dram[xnT_d].rearrange("(f p) t -> p f t", p=128)[:, :, bass.ts(i, 128)]
        S.dma("sdma", dst, xtv[i % 2][0].ap().rearrange("p (f c) -> p f c", c=128),
              reads=(xtv[i % 2][1],), writes=(xnT_d,))


def emit_l0ffn(p):
    S = p.sched
    xn = p.dram["xn0"]
    wr, wrk = big1(p, 0)
    wk, wkk = big1(p, 1)
    wv, wvk = big1(p, 2)
    load_weight_bf16(p, S, "l0f_WrT", wr, wrk, D, D)
    load_weight_bf16(p, S, "l0f_WkT", wk, wkk, D, D)
    load_weight_bf16(p, S, "l0f_WvT", wv, wvk, D, D)
    mur, muk = p.F[7], p.F[3]
    S.dma("sdma", mur[:, :], p.dram["l0f_mur"][:, :], reads=("l0f_mur",), writes=("F7",))
    S.dma("sdma", muk[:, :], p.dram["l0f_muk"][:, :], reads=("l0f_muk",), writes=("F3",))

    xc = [(p.F[0], "F0"), (p.F[1], "F1")]
    xp = [(p.F[2], "F2"), (p.F[2], "F2")]
    t1 = [(p.F[4], "F4")]
    xrf = [(p.F[5], "F5")]
    xkf = [(p.F[6], "F6")]
    xrb = [(p.B[0], "B0"), (p.B[0], "B0")]
    xkb = [(p.B[1], "B1"), (p.B[1], "B1")]
    xrT = [(p.B[2], "B2")]
    xkT = [(p.B[3], "B3")]
    kab = [(p.B[4], "B4")]
    kaT = [(p.B[5], "B5")]
    psR = p.bankpair(0)
    psK = p.bankpair(1)
    psV = p.bankpair(2)

    for i in range(NT):
        b = i % 2
        S.dma("sdma", xc[b][0][:, :], xn[bass.ts(i, 128), :], reads=("xn0",),
              writes=(xc[b][1],), tag="f0lx")
        if i == 0:
            S.add("vector", lambda eng: eng.memset(xp[0][0][0:1, :], 0.0),
                  writes=(xp[0][1],))
            S.dma("sdma", xp[0][0][1:128, :], xn[0:127, :], reads=("xn0",),
                  writes=(xp[0][1],))
        else:
            S.dma("sdma", xp[b][0][:, :], xn[128 * i - 1:128 * i + 127, :],
                  reads=("xn0",), writes=(xp[b][1],))

        S.add("vector", lambda eng, b=b: eng.tensor_sub(t1[0][0][:, :], xc[b][0][:, :], xp[b][0][:, :]),
              reads=(xc[b][1], xp[b][1]), writes=(t1[0][1],))
        S.add("vector", lambda eng, b=b: eng.tensor_mul(xrf[0][0][:, :], t1[0][0][:, :], mur[:, :]),
              reads=(t1[0][1], "F7"), writes=(xrf[0][1],))
        S.add("vector", lambda eng, b=b: eng.tensor_add(xrf[0][0][:, :], xrf[0][0][:, :], xp[b][0][:, :]),
              reads=(xrf[0][1], xp[b][1]), writes=(xrf[0][1],))
        S.add("vector", lambda eng, b=b: eng.tensor_mul(xkf[0][0][:, :], t1[0][0][:, :], muk[:, :]),
              reads=(t1[0][1], "F3"), writes=(xkf[0][1],))
        S.add("vector", lambda eng, b=b: eng.tensor_add(xkf[0][0][:, :], xkf[0][0][:, :], xp[b][0][:, :]),
              reads=(xkf[0][1], xp[b][1]), writes=(xkf[0][1],))
        S.add("scalar", lambda eng, b=b: eng.copy(xrb[b][0][:, :], xrf[0][0][:, :]),
              reads=(xrf[0][1],), writes=(xrb[b][1],))
        S.add("scalar", lambda eng, b=b: eng.copy(xkb[b][0][:, :], xkf[0][0][:, :]),
              reads=(xkf[0][1],), writes=(xkb[b][1],))
        transposes(p, S, xrb[b][0], xrb[b][1],
                   lambda f: (xrT[0][0][:, bass.ts(f, 128)], xrT[0][1]), "f0tA")
        transposes(p, S, xkb[b][0], xkb[b][1],
                   lambda f: (xkT[0][0][:, bass.ts(f, 128)], xkT[0][1]), "f0tB")
        for oc in range(2):
            mm_acc(p, S, "f0R%d" % oc, psR[:, bass.ts(oc, 512)], bkey(0 + oc),
                   [(xrT[0][0][:, bass.ts(f, 128)], xrT[0][1]) for f in range(NF)],
                   [(wr[:, f * D + oc * 512: f * D + oc * 512 + 512], wrk) for f in range(NF)])
            mm_acc(p, S, "f0K%d" % oc, psK[:, bass.ts(oc, 512)], bkey(2 + oc),
                   [(xkT[0][0][:, bass.ts(f, 128)], xkT[0][1]) for f in range(NF)],
                   [(wk[:, f * D + oc * 512: f * D + oc * 512 + 512], wkk) for f in range(NF)])
        rsgb = xrf[0]
        S.add("scalar", lambda eng: eng.activation(rsgb[0][:, :], psR[:, :], AF.Sigmoid),
              reads=(bkey(0), bkey(1)), writes=(rsgb[1],), tag="f0sig")
        relb = xkf[0]
        S.add("vector", lambda eng: eng.tensor_relu(relb[0][:, :], psK[:, :]),
              reads=(bkey(2), bkey(3)), writes=(relb[1],), tag="f0relu")
        S.add("scalar", lambda eng, b=b: eng.activation(kab[0][0][:, :], relb[0][:, :], AF.Square),
              reads=(relb[1],), writes=(kab[0][1],), tag="f0sq")
        transposes(p, S, kab[0][0], kab[0][1],
                   lambda f: (kaT[0][0][:, bass.ts(f, 128)], kaT[0][1]), "f0tC")
        for oc in range(2):
            mm_acc(p, S, "f0V%d" % oc, psV[:, bass.ts(oc, 512)], bkey(4 + oc),
                   [(kaT[0][0][:, bass.ts(f, 128)], kaT[0][1]) for f in range(NF)],
                   [(wv[:, f * D + oc * 512: f * D + oc * 512 + 512], wvk) for f in range(NF)])
        gbu = t1[0]
        S.add("vector", lambda eng: eng.tensor_mul(gbu[0][:, :], rsgb[0][:, :], psV[:, :]),
              reads=(bkey(4), bkey(5), rsgb[1]), writes=(gbu[1],), tag="f0rv")
        S.add("scalar", lambda eng: eng.activation(gbu[0][:, :], gbu[0][:, :], AF.Silu),
              reads=(gbu[1],), writes=(gbu[1],), tag="f0silu")
        s0t = xp[b]
        x0t = xc[b]
        S.dma("sdma", s0t[0][:, :], p.dram["s0"][bass.ts(i, 128), :],
              reads=("s0",), writes=(s0t[1],), tag="f0lds")
        S.dma("sdma", x0t[0][:, :], p.dram["x0"][bass.ts(i, 128), :],
              reads=("x0",), writes=(x0t[1],), tag="f0ldx0")
        x1f = rsgb
        S.add("vector", lambda eng, x1f=x1f, s0t=s0t: eng.tensor_mul(
            x1f[0][:, :], s0t[0][:, :], gbu[0][:, :]),
              reads=(s0t[1], gbu[1]), writes=(x1f[1],), tag="f0sg")
        S.add("vector", lambda eng, x1f=x1f, x0t=x0t: eng.tensor_add(
            x1f[0][:, :], x1f[0][:, :], x0t[0][:, :]),
              reads=(x1f[1], x0t[1]), writes=(x1f[1],), tag="f0res")
        S.dma("sdma", p.dram["x1"][bass.ts(i, 128), :], x1f[0][:, :],
              reads=(x1f[1],), writes=("x1",), tag="f0stx1")


def _tile128(v):
    v = np.asarray(v, np.float32)
    return np.ascontiguousarray(np.broadcast_to(v[None, :], (128, v.shape[0])))


def build_consts():
    c = {}
    c["ident"] = np.eye(128, dtype=BF)
    c["tri128"] = np.tril(np.ones((128, 128), np.float32)).T.astype(BF)
    c["jrow16"] = (-np.arange(16, dtype=np.float32) / 16.0)[None, :]
    c["poscols"] = np.arange(T, dtype=np.float32).reshape(NT, 128).T.copy()
    c["ones1"] = np.ones((1, 128), dtype=BF)
    pos = np.arange(T, dtype=np.float32)
    hi = np.floor(pos / 16.0) * 16.0
    c["posrows2"] = np.concatenate([hi, pos - hi])[None, :].astype(BF)
    inv32 = 1.0 / 10000.0 ** (np.arange(0, 64, 2, dtype=np.float32) / 64.0)
    fr = np.outer(pos, inv32)
    c["cs32"] = np.concatenate([np.cos(fr), np.sin(fr)], 1).astype(np.float32)
    mk = np.zeros((128, 5 * 512), np.float32)
    for mi in range(5):
        off = 128 * (1 - mi)
        kl = np.arange(128)[:, None]
        ql = np.arange(512)[None, :]
        dd = off + ql - kl
        mk[:, mi * 512:(mi + 1) * 512] = ((dd >= 0) & (dd <= 64)).astype(np.float32)
    c["l1_masks"] = mk.astype(BF)
    c["triexc"] = np.tril(np.ones((32, 32), np.float32), -1).T.astype(BF)
    ind = np.zeros((128, 1024), np.float32)
    for i2 in range(NT):
        ind[:, 32 * i2 + i2] = 1.0
    c["indcols"] = ind.astype(BF)
    m2g, k2g = np.meshgrid(np.arange(64), np.arange(128), indexing="ij")
    ang = -2 * np.pi * m2g * k2g / 128.0
    c["fF2r"] = np.cos(ang).astype(BF)
    c["fF2i"] = np.sin(ang).astype(BF)
    f1 = np.zeros((128, 66), np.complex128)
    m1g, k1g = np.meshgrid(np.arange(64), np.arange(33), indexing="ij")
    f1blk = np.exp(-2j * np.pi * m1g * k1g / 64.0)
    f1[0:64, 0:33] = f1blk
    f1[64:128, 33:66] = f1blk
    c["fF1r"] = f1.real.astype(BF)
    c["fF1i"] = f1.imag.astype(BF)
    c["fF1n"] = (-f1.imag).astype(BF)
    e1 = np.zeros((66, 128), np.complex128)
    e1blk = np.exp(2j * np.pi * k1g.T * m1g.T / 64.0)
    e1[0:33, 0:64] = e1blk
    e1[33:66, 64:128] = e1blk
    c["fE1r"] = e1.real.astype(BF)
    c["fE1i"] = e1.imag.astype(BF)
    c["fE1n2"] = (-e1.imag).astype(BF)
    k2g2, m2g2 = np.meshgrid(np.arange(128), np.arange(64), indexing="ij")
    ange = 2 * np.pi * m2g2 * k2g2 / 128.0
    c["fE2r"] = (np.cos(ange) * (2.0 / 8192.0)).astype(BF)
    c["fE2n"] = (-np.sin(ange) * (2.0 / 8192.0)).astype(BF)
    m1r = np.tile(np.arange(64), 2)[:, None]
    k2r = np.arange(128)[None, :]
    angw = -2 * np.pi * m1r * k2r / 8192.0
    c["ftWr"] = np.ascontiguousarray(np.tile(np.cos(angw), (1, 4))).astype(np.float32)
    c["ftWi"] = np.ascontiguousarray(np.tile(np.sin(angw), (1, 4))).astype(np.float32)
    k2c = np.arange(128)[:, None]
    m1c = np.tile(np.arange(64), 2)[None, :]
    angv = 2 * np.pi * m1c * k2c / 8192.0
    c["ftVr"] = np.ascontiguousarray(np.tile(np.cos(angv), (1, 4))).astype(np.float32)
    c["ftVi"] = np.ascontiguousarray(np.tile(np.sin(angv), (1, 4))).astype(np.float32)
    km = np.zeros((128, 256), np.float32)
    km[:, 0] = 0.5
    km[:, 128] = 0.5
    c["l2_k32m"] = km
    c["l2_bdz"] = np.zeros((128, 8), np.float32)
    return c


def build_nc_phase(phases, dram_specs):
    nc = bass.Bass("TRN2", target_bir_lowering=False)
    p = Prog(nc)
    with p.es:
        for spec in dram_specs:
            p.dt(*spec)
        p.ident_sb = p.sb("ident_sb", [128, 128], BF16)
        p.fconst_sb = p.sb("fconst_sb", [128, 64], F32)
        p.init_psum_pool()
        setup_arena(p)
        p.block = p.es.enter_context(nc.Block())
        S = p.sched
        S.dma("sdma", p.ident_sb[:, :], p.dram["ident"][:, :],
              reads=("ident_d",), writes=("ident",))
        S.dma("sdma", p.fconst_sb[:, :], p.dram["fconsts"][:, :],
              reads=("fconsts_d",), writes=("fconsts",))
        p.ones1_sb = p.sb("ones1_sb", [1, 128], BF16)
        S.dma("sdma", p.ones1_sb[:, :], p.dram["ones1"][:, :],
              reads=("ones1_d",), writes=("ones1",))
        p.tri_sb = p.sb("tri_sb", [128, 128], BF16)
        S.dma("sdma", p.tri_sb[:, :], p.dram["tri128"][:, :],
              reads=("tri128_d",), writes=("tri",))
        for ph in phases:
            ph(p)
        S.replay()
    nc._prog = p
    return nc


def kernel(**inputs):
    raise NotImplementedError("assembled at the end")


def _ropev(buf):
    return buf.ap().rearrange("p (h d) -> p h d", h=16)


def emit_rope(p, S, unit, buf, bkeyn, scr, scrk, i):
    ct = p.F[7][:, 0:512]
    st = p.F[7][:, 512:1024]
    cs = ct[:, bass.ts(i, 16)].unsqueeze(1).to_broadcast([128, 16, 16])
    sn = st[:, bass.ts(i, 16)].unsqueeze(1).to_broadcast([128, 16, 16])
    x1 = _ropev(buf)[:, :, 32:48]
    x2 = _ropev(buf)[:, :, 48:64]
    t1 = _ropev(scr)[:, :, 0:16]
    t2 = _ropev(scr)[:, :, 16:32]
    t3 = _ropev(scr)[:, :, 32:48]
    t4 = _ropev(scr)[:, :, 48:64]
    A = lambda fn, r, w: S.add(unit, fn, reads=r, writes=w, tag="rope")
    A(lambda eng, x1=x1, cs=cs, t1=t1: eng.tensor_tensor(t1, x1, cs, ALU.mult),
      (bkeyn, "F7"), (scrk,))
    A(lambda eng, x1=x1, sn=sn, t2=t2: eng.tensor_tensor(t2, x1, sn, ALU.mult),
      (bkeyn, "F7"), (scrk,))
    A(lambda eng, x2=x2, sn=sn, t3=t3: eng.tensor_tensor(t3, x2, sn, ALU.mult),
      (bkeyn, "F7"), (scrk,))
    A(lambda eng, x2=x2, cs=cs, t4=t4: eng.tensor_tensor(t4, x2, cs, ALU.mult),
      (bkeyn, "F7"), (scrk,))
    A(lambda eng, x1=x1, t1=t1, t3=t3: eng.tensor_tensor(x1, t1, t3, ALU.subtract),
      (scrk,), (bkeyn,))
    A(lambda eng, x2=x2, t2=t2, t4=t4: eng.tensor_tensor(x2, t2, t4, ALU.add),
      (scrk,), (bkeyn,))


def emit_l0attn(p):
    S = p.sched
    nc = p.nc
    xnT = p.dram["xn0T"]
    HP = 8

    nr = p.R[0][0:1, 0:1]
    nrb = p.R[1][0:1, 0:16]
    invr = p.S[0][:, 0:16]
    jrow = p.R[2][0:1, 0:16]
    postc = p.S[1][:, 0:NT]
    scr16 = p.S[2][:, 0:16]
    scr16b = p.S[3][:, 0:16]
    S.dma("sdma", nr, p.dram["l0_nrope"][:, :], reads=("l0_nrope",), writes=("R0",))
    S.dma("sdma", jrow, p.dram["jrow16"][:, :], reads=("jrow16",), writes=("R2",))
    S.dma("sdma", postc, p.dram["poscols"][:, :], reads=("poscols",), writes=("S1",))
    MIN_L = 1.0 - 17 / 100.0
    SDT_L = (10000.0 - MIN_L * 10000.0) * 2.0
    S.add("scalar", lambda eng: eng.activation(nr, nr, AF.Sigmoid),
          reads=("R0", "fconsts"), writes=("R0",))
    S.add("vector", lambda eng: eng.tensor_scalar(nr, nr, float(SDT_L),
                                                  float(MIN_L * 10000.0), ALU.mult, ALU.add),
          reads=("R0",), writes=("R0",))
    S.add("scalar", lambda eng: eng.activation(nr, nr, AF.Ln),
          reads=("R0", "fconsts"), writes=("R0",))
    S.add("vector", lambda eng: eng.tensor_scalar_mul(nrb, jrow, nr),
          reads=("R0", "R2"), writes=("R1",))
    S.add("scalar", lambda eng: eng.activation(nrb, nrb, AF.Exp),
          reads=("R1", "fconsts"), writes=("R1",))
    SC = float(2.0 * np.pi / (T - 1))
    invs = p.R[1][0:1, 16:32]
    ivh = p.R[1].ap().bitcast(BF16)[0:1, 64:80]
    ivl = p.R[1].ap().bitcast(BF16)[0:1, 80:96]
    ivhf = p.R[1][0:1, 48:64]
    S.add("vector", lambda eng: eng.tensor_scalar_mul(invs, nrb, SC),
          reads=("R1",), writes=("R1",))
    S.add("vector", lambda eng: eng.tensor_copy(ivh, invs), reads=("R1",), writes=("R1",))
    S.add("vector", lambda eng: eng.tensor_copy(ivhf, ivh), reads=("R1",), writes=("R1",))
    S.add("vector", lambda eng: eng.tensor_sub(ivhf, invs, ivhf), reads=("R1",), writes=("R1",))
    S.add("vector", lambda eng: eng.tensor_copy(ivl, ivhf), reads=("R1",), writes=("R1",))
    p.fconst(float(np.pi / 2))
    for i in range(NT):
        pr2 = p.R[2].ap().bitcast(BF16)
        phl = p.dram["posrows2"].rearrange("o (a t) -> o a t", a=2)[0:1, :, bass.ts(i, 128)]
        S.dma("sdma", pr2[0:1, 256:512].rearrange("o (a c) -> o a c", a=2), phl,
              reads=("posrows2",), writes=("R2",))
        k = 0
        for (pa, pb) in ((pr2[0:1, 256:384], ivh), (pr2[0:1, 256:384], ivl),
                         (pr2[0:1, 384:512], ivh), (pr2[0:1, 384:512], ivl)):
            def ao(eng, pa=pa, pb=pb, k=k):
                return eng.matmul(p.bank(6)[:, 0:16], pa, pb,
                                  start=(k == 0), stop=(k == 3))
            S.add("tensor", ao, reads=("R2", "R1"),
                  writes=(bkey(6),) if k in (0, 3) else (bkey(6) + "@acc",), tag="ao")
            k += 1
        S.add("vector", lambda eng: eng.tensor_copy(scr16, p.bank(6)[:, 0:16]),
              reads=(bkey(6),), writes=("S2",))
        S.add("vector", lambda eng: eng.tensor_scalar(
            scr16b, scr16, 1.0 / TWO_PI, MAGIC, ALU.mult, ALU.add),
            reads=("S2",), writes=("S3",))
        S.add("vector", lambda eng: eng.tensor_scalar(
            scr16b, scr16b, MAGIC, None, ALU.subtract, ALU.bypass),
            reads=("S3",), writes=("S3",))
        S.add("vector", lambda eng: eng.scalar_tensor_tensor(
            scr16, scr16b, -TWO_PI, scr16, ALU.mult, ALU.add),
            reads=("S2", "S3"), writes=("S2",))
        S.add("vector", lambda eng: eng.tensor_scalar(
            scr16, scr16, PI_C, -PI_C, ALU.min, ALU.max),
            reads=("S2",), writes=("S2",))
        S.add("scalar", lambda eng, i=i: eng.activation(
            p.F[7][:, 512 + i * 16:512 + i * 16 + 16], scr16, AF.Sin),
            reads=("S2", "fconsts"), writes=("F7",))
        S.add("vector", lambda eng: eng.tensor_scalar_mul(scr16b, scr16, -1.0),
              reads=("S2",), writes=("S3",))
        S.add("vector", lambda eng: eng.tensor_max(scr16b, scr16b, scr16),
              reads=("S2", "S3"), writes=("S3",))
        S.add("scalar", lambda eng, i=i: eng.activation(
            p.F[7][:, i * 16:i * 16 + 16], scr16b, AF.Sin,
            bias=float(np.pi / 2), scale=-1.0),
            reads=("S3", "fconsts"), writes=("F7",))

    w0, w0k = p.Wsl[0], "W0"
    xt = [(p.B[2], "B2"), (p.B[3], "B3")]
    kn = p.F[6][:, 0:256]
    qn = p.F[6][:, 256:512]

    def load_xnT_tile(i, b):
        src = xnT.rearrange("(f p) t -> p f t", p=128)[:, :, bass.ts(i, 128)]
        S.dma("sdma", xt[b][0].ap().rearrange("p (f c) -> p f c", f=NF), src,
              reads=("xn0T",), writes=(xt[b][1],), tag="lxT")

    def proj_psum(tag, i, b, wsl, wslk, pair):
        for oc in range(2):
            mm_acc(p, S, tag + "%d" % oc, p.bank(2 * pair + oc), bkey(2 * pair + oc),
                   [(xt[b][0][:, bass.ts(f, 128)], xt[b][1]) for f in range(NF)],
                   [(wsl[:, f * D + oc * 512: f * D + oc * 512 + 512], wslk)
                    for f in range(NF)])

    load_weight_bf16(p, S, "l0_Wk", w0[:, :], w0k, D, D)
    bres = p.F[5]
    S.dma("sdma", bres[:, :], p.dram["l0_bk"][:, :], reads=("l0_bk",), writes=("F5",))
    kf = [(p.F[0], "F0"), (p.F[1], "F1")]
    kb = [(p.B[0], "B0"), (p.B[1], "B1")]
    for i in range(NT):
        b = i % 2
        load_xnT_tile(i, b)
        proj_psum("kP", i, b, w0, w0k, b)
        S.add("vector", lambda eng, b=b: eng.tensor_tensor(
            kf[b][0][:, :], p.bankpair(b)[:, :], bres[:, :], ALU.add),
            reads=(bkey(2 * b), bkey(2 * b + 1), "F5"), writes=(kf[b][1],), tag="kev")
        emit_rope(p, S, "gpsimd", kf[b][0], kf[b][1], p.F[2], "F2", i)
        S.add("vector", lambda eng, i=i, b=b: eng.tensor_reduce(
            kn[:, i * 8:i * 8 + 8].unsqueeze(2),
            kf[b][0].ap().rearrange("p (g c) -> p g c", g=8), AX.X, ALU.max,
            apply_absolute_value=True),
            reads=(kf[b][1],), writes=("F6",), tag="knm")
        S.add("scalar", lambda eng, b=b: eng.copy(kb[b][0][:, :], kf[b][0][:, :]),
              reads=(kf[b][1],), writes=(kb[b][1],))
        transposes(p, S, kb[b][0], kb[b][1],
                   lambda f, i=i: (p.B1[:, f * T + i * 128: f * T + i * 128 + 128],
                                   "BIG1"), "katr")

    load_weight_bf16(p, S, "l0_Wv", w0[:, :], w0k, D, D)
    S.dma("sdma", bres[:, :], p.dram["l0_bv"][:, :], reads=("l0_bv",), writes=("F5",))
    for i in range(NT):
        b = i % 2
        load_xnT_tile(i, b)
        proj_psum("vP", i, b, w0, w0k, b)
        vbase = i * 16 * 65
        q4 = i // 8

        def vev(eng, vbase=vbase, b=b):
            dst = p.B2.ap()[:, vbase:vbase + 16 * 65].rearrange(
                "p (h e) -> p h e", h=16)[:, :, 0:64]
            return eng.tensor_tensor(
                dst, p.bankpair(b).ap().rearrange("p (h d) -> p h d", h=16),
                bres.ap().rearrange("p (h d) -> p h d", h=16), ALU.add)
        S.add("vector", vev, reads=(bkey(2 * b), bkey(2 * b + 1), "F5"),
              writes=("BIG2",), tag="vev")

        def vones(eng, vbase=vbase):
            return eng.memset(
                p.B2.ap()[:, vbase: vbase + 16 * 65].rearrange(
                    "p (h e) -> p h e", e=65)[:, :, 64:65], 1.0)
        S.add("vector", vones, writes=("BIG2",), tag="vones")

    load_weight_bf16(p, S, "l0_Wq", w0[:, :], w0k, D, D)
    S.dma("sdma", bres[:, :], p.dram["l0_bq"][:, :], reads=("l0_bq",), writes=("F5",))
    qT = p.dram["l0_qT"]
    qtv = [(p.B[4], "B4"), (p.B[5], "B5")]
    for i in range(NT):
        b = i % 2
        load_xnT_tile(i, b)
        proj_psum("qP", i, b, w0, w0k, b)
        S.add("vector", lambda eng, b=b: eng.tensor_tensor(
            kf[b][0][:, :], p.bankpair(b)[:, :], bres[:, :], ALU.add),
            reads=(bkey(2 * b), bkey(2 * b + 1), "F5"), writes=(kf[b][1],), tag="qev")
        emit_rope(p, S, "gpsimd", kf[b][0], kf[b][1], p.F[2], "F2", i)
        S.add("vector", lambda eng, i=i, b=b: eng.tensor_reduce(
            qn[:, i * 8:i * 8 + 8].unsqueeze(2),
            kf[b][0].ap().rearrange("p (g c) -> p g c", g=8), AX.X, ALU.add,
            apply_absolute_value=True),
            reads=(kf[b][1],), writes=("F6",), tag="qnm")
        S.add("scalar", lambda eng, b=b: eng.copy(kb[b][0][:, :], kf[b][0][:, :]),
              reads=(kf[b][1],), writes=(kb[b][1],))
        transposes(p, S, kb[b][0], kb[b][1],
                   lambda f, i2=i % 2: (qtv[i2][0][:, bass.ts(f, 128)], qtv[i2][1]),
                   "qtr")
        dst = qT.rearrange("(f p) t -> p f t", p=128)[:, :, bass.ts(i, 128)]
        S.dma("sdma", dst, qtv[i % 2][0].ap().rearrange("p (f c) -> p f c", c=128),
              reads=(qtv[i % 2][1],), writes=("l0_qT",))

    redk = p.R[0][0:1, 0:256]
    redq = p.R[1][0:1, 0:256]
    S.add("gpsimd", lambda eng: eng.reduce_max(redk, kn, AX.C),
          reads=("F6",), writes=("R0",), tag="knr")
    S.add("gpsimd", lambda eng: eng.reduce_max(redq, qn, AX.C),
          reads=("F6",), writes=("R1",), tag="qnr")
    mk8 = p.R[2][0:1, 0:8]
    mq8 = p.R[2][0:1, 8:16]
    S.add("vector", lambda eng: eng.tensor_reduce(
        mk8.unsqueeze(2), redk.rearrange("o (i g) -> o g i", g=8), AX.X, ALU.max),
        reads=("R0",), writes=("R2",))
    S.add("vector", lambda eng: eng.tensor_reduce(
        mq8.unsqueeze(2), redq.rearrange("o (i g) -> o g i", g=8), AX.X, ALU.max),
        reads=("R1",), writes=("R2",))
    negc = p.S[4][:, 0:8]
    S.add("vector", lambda eng: eng.tensor_mul(mk8, mk8, mq8),
          reads=("R2",), writes=("R2",))
    S.add("vector", lambda eng: eng.tensor_scalar_mul(mk8, mk8, -1.0 / 8.0),
          reads=("R2",), writes=("R2",))
    mk8b = p.R[2].ap().bitcast(BF16)[0:1, 32:40]
    S.add("vector", lambda eng: eng.tensor_copy(mk8b, mk8), reads=("R2",), writes=("R2",))
    S.add("tensor", lambda eng: eng.matmul(p.bank(6)[:, 0:8], p.ones1_sb[0:1, :],
                                           mk8b, start=True, stop=True),
          reads=("R2", "ones1"), writes=(bkey(6),), tag="negb")
    S.add("vector", lambda eng: eng.tensor_copy(negc, p.bank(6)[:, 0:8]),
          reads=(bkey(6),), writes=("S4",))

    load_weight_bf16(p, S, "l0_Wo", w0[:, :], w0k, D, D)
    S.dma("sdma", bres[:, :], p.dram["l0_bo"][:, :], reads=("l0_bo",), writes=("F5",))
    qts = [(p.B[0], "B0"), (p.B[1], "B1")]
    Pt = [(p.B[2], "B2"), (p.B[3], "B3")]
    osb = [p.F[0].ap().bitcast(BF16), p.F[1].ap().bitcast(BF16)]
    osbk = ["F0", "F1"]
    rc_sb = p.S[2][:, 0:8]
    otT = [(p.B[4], "B4"), (p.B[5], "B5")]
    sev = [(p.F[2], "F2")]
    p.fconst(0.125)
    for qc in range(8):
        for g in range(HP):
            bb = (qc * HP + g) % 2
            S.dma("sdma", qts[bb][0][:, 0:512],
                  qT[bass.ts(g, 128), bass.ts(qc, 512)],
                  reads=("l0_qT",), writes=(qts[bb][1],), tag="lq")
            def emit_av(kt, g=g):
                for hh in range(2):
                    for qs in range(4):
                        def av(eng, g=g, kt=kt, hh=hh, qs=qs):
                            lhsT = Pt[kt % 2][0][:, 512 * hh + 128 * qs:
                                                 512 * hh + 128 * qs + 128]
                            rhs = p.B2[:, kt * 1040 + (2 * g + hh) * 65:
                                       kt * 1040 + (2 * g + hh) * 65 + 65]
                            return eng.matmul(
                                p.bank(4 + hh)[:, 65 * qs:65 * qs + 65],
                                lhsT, rhs, start=(kt == 0 and qs == 0),
                                stop=(kt == NT - 1 and qs == 3))
                        wk2 = bkey(4 + hh)
                        if kt == 0:
                            wkey = (wk2 + "@acc", wk2)
                        elif kt == NT - 1:
                            wkey = (wk2,)
                        else:
                            wkey = (wk2 + "@acc",)
                        S.add("tensor", av, reads=("BIG2", Pt[kt % 2][1]),
                              writes=wkey, tag="av")

            for kt in range(NT):
                pp = kt % 2
                for hh in range(2):
                    def qk(eng, g=g, kt=kt, hh=hh, pp=pp, bb=bb):
                        lhsT = p.B1[64 * hh:64 * hh + 64,
                                    g * T + kt * 128: g * T + kt * 128 + 128]
                        rhs = qts[bb][0][64 * hh:64 * hh + 64, 0:512]
                        return eng.matmul(p.bankpair(pp)[:, 512 * hh:512 * hh + 512],
                                          lhsT, rhs, start=True, stop=True,
                                          tile_position=(64 * hh, 0))
                    S.add("tensor", qk, reads=("BIG1", qts[bb][1]),
                          writes=(bkey(2 * pp + hh),), tag="qk")

                def ex(eng, g=g, pp=pp, kt=kt):
                    return eng.activation(Pt[kt % 2][0][:, :], p.bankpair(pp)[:, :],
                                          AF.Exp, bias=negc[:, g:g + 1], scale=0.125)
                S.add("scalar", ex, reads=(bkey(2 * pp), bkey(2 * pp + 1), "S4", "fconsts"),
                      writes=(Pt[kt % 2][1],), tag="exp")
                if kt > 0:
                    emit_av(kt - 1)
            emit_av(NT - 1)
            for hh in range(2):
                for qs in range(4):
                    ci = hh * 4 + qs

                    def rc(eng, hh=hh, qs=qs, ci=ci):
                        return eng.reciprocal(rc_sb[:, ci:ci + 1],
                                              p.bank(4 + hh)[:, 65 * qs + 64:65 * qs + 65])
                    S.add("vector", rc, reads=(bkey(4), bkey(5)), writes=("S2",), tag="rc")

                    def nm(eng, g=g, hh=hh, qs=qs, ci=ci):
                        dst = osb[qs // 2][:, (qs % 2) * 1024 + g * 128 + hh * 64:
                                           (qs % 2) * 1024 + g * 128 + hh * 64 + 64]
                        return eng.tensor_scalar(dst, p.bank(4 + hh)[:, 65 * qs:65 * qs + 64],
                                                 rc_sb[:, ci:ci + 1], None, ALU.mult, ALU.bypass)
                    S.add("vector", nm, reads=(bkey(4), bkey(5), "S2"),
                          writes=(osbk[qs // 2],), tag="nm")
        for qs in range(4):
            ob = qs % 2
            osrc = osb[qs // 2][:, (qs % 2) * 1024:(qs % 2) * 1024 + 1024]

            def dstf(f, ob=ob):
                return otT[ob][0][:, bass.ts(f, 128)], otT[ob][1]
            transposes2(p, S, osrc, osbk[qs // 2], dstf, "otr")
            for oc in range(2):
                mm_acc(p, S, "woP%d" % oc, p.bank(6 + oc), bkey(6 + oc),
                       [(otT[ob][0][:, bass.ts(f, 128)], otT[ob][1]) for f in range(NF)],
                       [(w0[:, f * D + oc * 512: f * D + oc * 512 + 512], w0k)
                        for f in range(NF)])
            S.add("vector", lambda eng, ob=ob: eng.tensor_tensor(
                sev[0][0][:, :], p.bankpair(3)[:, :], bres[:, :], ALU.add),
                reads=(bkey(6), bkey(7), "F5"), writes=(sev[0][1],), tag="sev")
            S.dma("sdma", p.dram["s0"][qc * 512 + qs * 128: qc * 512 + qs * 128 + 128, :],
                  sev[0][0][:, :], reads=(sev[0][1],), writes=("s0",), tag="sts")


def emit_l1attn(p):
    S = p.sched
    xnT = p.dram["xn1T"]
    HP = 8
    wq_, wqk = big1(p, 0)
    wk_, wkk = big1(p, 1)
    wv_, wvk = big1(p, 2)
    for qi, (wsl, wslk) in enumerate(((wq_, wqk), (wk_, wkk), (wv_, wvk))):
        src = p.dram["l1_wattnT"].rearrange("(f p) o -> p f o", p=128)[:, :, qi * D:(qi + 1) * D]
        S.dma("gdma", wsl.rearrange("p (f o) -> p f o", f=NF), src,
              reads=("l1_wattnT",), writes=(wslk,), tag="lw1")

    xt = [(p.B[2], "B2"), (p.B[3], "B3")]
    kn = p.F[6][:, 0:256]
    qn = p.F[6][:, 256:512]
    qf = [(p.F[0], "F0"), (p.F[1], "F1")]
    qb = [(p.B[0], "B0"), (p.B[1], "B1")]
    rtab = p.F[7]
    csl = [(p.S[2][:, 0:64], "S2"), (p.S[3][:, 0:64], "S3")]

    def l1rope2(unit, buf, bk2, scr, scrk, i, b):
        cs = csl[b][0][:, 0:32].unsqueeze(1).to_broadcast([128, 16, 32])
        sn = csl[b][0][:, 32:64].unsqueeze(1).to_broadcast([128, 16, 32])
        ck = csl[b][1]
        x1 = _ropev(buf)[:, :, 0:32]
        x2 = _ropev(buf)[:, :, 32:64]
        t1 = _ropev(scr)[:, :, 0:32]
        t2 = _ropev(scr)[:, :, 32:64]
        t3 = _ropev(scr)[:, :, 48:64] if False else None
        A = lambda fn, r, w: S.add(unit, fn, reads=r, writes=w, tag="r1")
        A(lambda eng, t1=t1, x1=x1, cs=cs: eng.tensor_tensor(t1, x1, cs, ALU.mult),
          (bk2, ck), (scrk,))
        A(lambda eng, t2=t2, x1=x1, sn=sn: eng.tensor_tensor(t2, x1, sn, ALU.mult),
          (bk2, ck), (scrk,))
        A(lambda eng, x1=x1, x2=x2, sn=sn, t1=t1: eng.tensor_tensor(x1, x2, sn, ALU.mult),
          (bk2, ck), (bk2,))
        A(lambda eng, x2=x2, cs=cs: eng.tensor_tensor(x2, x2, cs, ALU.mult),
          (bk2, ck), (bk2,))
        A(lambda eng, x1=x1, t1=t1: eng.tensor_tensor(x1, t1, x1, ALU.add),
          (scrk, bk2), (bk2,))
        A(lambda eng, x2=x2, t2=t2: eng.tensor_tensor(x2, x2, t2, ALU.subtract),
          (scrk, bk2), (bk2,))

    for i in range(NT):
        b = i % 2
        src = xnT.rearrange("(f p) t -> p f t", p=128)[:, :, bass.ts(i, 128)]
        S.dma("sdma", xt[b][0].ap().rearrange("p (f c) -> p f c", f=NF), src,
              reads=("xn1T",), writes=(xt[b][1],), tag="lxT1")
        S.dma("sdma", csl[b][0], p.dram["cs32"][bass.ts(i, 128), :],
              reads=("cs32",), writes=(csl[b][1],), tag="lcs")
        for oc in range(2):
            mm_acc(p, S, "1q%d" % oc, p.bank(oc), bkey(oc),
                   [(xt[b][0][:, bass.ts(f, 128)], xt[b][1]) for f in range(NF)],
                   [(wq_[:, f * D + oc * 512: f * D + oc * 512 + 512], wqk) for f in range(NF)])
        S.add("vector", lambda eng, b=b: eng.tensor_copy(qf[b][0][:, :], p.bankpair(0)[:, :]),
              reads=(bkey(0), bkey(1)), writes=(qf[b][1],), tag="qe1")
        l1rope2("gpsimd", qf[b][0], qf[b][1], p.F[2], "F2", i, b)
        S.add("vector", lambda eng, i=i, b=b: eng.tensor_reduce(
            qn[:, i * 8:i * 8 + 8].unsqueeze(2),
            qf[b][0].ap().rearrange("p (g c) -> p g c", g=8), AX.X, ALU.add,
            apply_absolute_value=True),
            reads=(qf[b][1],), writes=("F6",), tag="qn1")
        S.add("scalar", lambda eng, b=b: eng.copy(qb[b][0][:, :], qf[b][0][:, :]),
              reads=(qf[b][1],), writes=(qb[b][1],))
        transposes(p, S, qb[b][0], qb[b][1],
                   lambda f, i2=b: (p.B[4 + i2][:, bass.ts(f, 128)], "B%d" % (4 + i2)),
                   "q1tr")
        dst = p.dram["l1_qT"].rearrange("(f p) t -> p f t", p=128)[:, :, bass.ts(i, 128)]
        S.dma("sdma", dst, p.B[4 + b].ap().rearrange("p (f c) -> p f c", c=128),
              reads=("B%d" % (4 + b),), writes=("l1_qT",))
        for oc in range(2):
            mm_acc(p, S, "1k%d" % oc, p.bank(2 + oc), bkey(2 + oc),
                   [(xt[b][0][:, bass.ts(f, 128)], xt[b][1]) for f in range(NF)],
                   [(wk_[:, f * D + oc * 512: f * D + oc * 512 + 512], wkk) for f in range(NF)])
        S.add("vector", lambda eng, b=b: eng.tensor_copy(qf[b][0][:, :], p.bankpair(1)[:, :]),
              reads=(bkey(2), bkey(3)), writes=(qf[b][1],), tag="ke1")
        l1rope2("gpsimd", qf[b][0], qf[b][1], p.F[2], "F2", i, b)
        S.add("vector", lambda eng, i=i, b=b: eng.tensor_reduce(
            kn[:, i * 8:i * 8 + 8].unsqueeze(2),
            qf[b][0].ap().rearrange("p (g c) -> p g c", g=8), AX.X, ALU.max,
            apply_absolute_value=True),
            reads=(qf[b][1],), writes=("F6",), tag="kn1")
        S.add("scalar", lambda eng, b=b: eng.copy(qb[b][0][:, :], qf[b][0][:, :]),
              reads=(qf[b][1],), writes=(qb[b][1],))
        transposes(p, S, qb[b][0], qb[b][1],
                   lambda f, i2=b: (p.B[4 + i2][:, bass.ts(f, 128)], "B%d" % (4 + i2)),
                   "k1tr")
        dstk = p.dram["l1_kT"].rearrange("(f p) t -> p f t", p=128)[:, :, bass.ts(i, 128)]
        S.dma("sdma", dstk, p.B[4 + b].ap().rearrange("p (f c) -> p f c", c=128),
              reads=("B%d" % (4 + b),), writes=("l1_kT",))
        for oc in range(2):
            mm_acc(p, S, "1v%d" % oc, p.bank(4 + oc), bkey(4 + oc),
                   [(xt[b][0][:, bass.ts(f, 128)], xt[b][1]) for f in range(NF)],
                   [(wv_[:, f * D + oc * 512: f * D + oc * 512 + 512], wvk) for f in range(NF)])
        vbase = i * 16 * 65

        def vev(eng, vbase=vbase):
            dst = p.B2.ap()[:, vbase:vbase + 16 * 65].rearrange(
                "p (h e) -> p h e", h=16)[:, :, 0:64]
            return eng.tensor_copy(dst, p.bankpair(2).ap().rearrange("p (h d) -> p h d", h=16))
        S.add("vector", vev, reads=(bkey(4), bkey(5)), writes=("BIG2",), tag="vev1")

        def vones(eng, vbase=vbase):
            return eng.memset(
                p.B2.ap()[:, vbase: vbase + 16 * 65].rearrange(
                    "p (h e) -> p h e", e=65)[:, :, 64:65], 1.0)
        S.add("vector", vones, writes=("BIG2",), tag="von1")

    redk = p.R[0][0:1, 0:256]
    redq = p.R[1][0:1, 0:256]
    S.add("gpsimd", lambda eng: eng.reduce_max(redk, kn, AX.C), reads=("F6",), writes=("R0",))
    S.add("gpsimd", lambda eng: eng.reduce_max(redq, qn, AX.C), reads=("F6",), writes=("R1",))
    mk8 = p.R[2][0:1, 0:8]
    mq8 = p.R[2][0:1, 8:16]
    S.add("vector", lambda eng: eng.tensor_reduce(
        mk8.unsqueeze(2), redk.rearrange("o (i g) -> o g i", g=8), AX.X, ALU.max),
        reads=("R0",), writes=("R2",))
    S.add("vector", lambda eng: eng.tensor_reduce(
        mq8.unsqueeze(2), redq.rearrange("o (i g) -> o g i", g=8), AX.X, ALU.max),
        reads=("R1",), writes=("R2",))
    negc = p.S[4][:, 0:8]
    S.add("vector", lambda eng: eng.tensor_mul(mk8, mk8, mq8), reads=("R2",), writes=("R2",))
    S.add("vector", lambda eng: eng.tensor_scalar_mul(mk8, mk8, -1.0 / 8.0),
          reads=("R2",), writes=("R2",))
    mk8b = p.R[2].ap().bitcast(BF16)[0:1, 32:40]
    S.add("vector", lambda eng: eng.tensor_copy(mk8b, mk8), reads=("R2",), writes=("R2",))
    S.add("tensor", lambda eng: eng.matmul(p.bank(6)[:, 0:8], p.ones1_sb[0:1, :],
                                           mk8b, start=True, stop=True),
          reads=("R2", "ones1"), writes=(bkey(6),), tag="negb1")
    S.add("vector", lambda eng: eng.tensor_copy(negc, p.bank(6)[:, 0:8]),
          reads=(bkey(6),), writes=("S4",))

    load_weight_bf16(p, S, "l1_wprojT", p.Wsl[0][:, :], "W0", D, D)
    qts = [(p.B[0], "B0"), (p.B[1], "B1")]
    kts = [(p.B[2], "B2"), (p.B[3], "B3")]
    Pt = [(p.B[4], "B4"), (p.B[5], "B5")]
    osb = [p.F[0].ap().bitcast(BF16), p.F[1].ap().bitcast(BF16)]
    osbk = ["F0", "F1"]
    rc_sb = p.S[2][:, 0:8]
    otT = [(p.B[4], "B4")]
    sev = [(p.F[2], "F2")]
    masks = p.B1[:, 24576:24576 + 5 * 512]
    S.dma("sdma", masks, p.dram["l1_masks"][:, :], reads=("l1_masks",),
          writes=("BIG1",))
    for qc in range(globals().get('_QCMAX', 8)):
        kts_list = [kt for kt in range(qc * 4 - 1, qc * 4 + 4) if kt >= 0]
        for g in range(globals().get('_GMAX', HP)):
            bb = (qc * HP + g) % 2
            S.dma("sdma", qts[bb][0][:, 0:512],
                  p.dram["l1_qT"][bass.ts(g, 128), bass.ts(qc, 512)],
                  reads=("l1_qT",), writes=(qts[bb][1],), tag="lq1")
            def emit_av1(ki, kt, g=g, nki=len(kts_list)):
                for hh in range(2):
                    for qs in range(4):
                        def av(eng, g=g, kt=kt, hh=hh, qs=qs, ki=ki,
                               last=(ki == nki - 1)):
                            lhsT = Pt[ki % 2][0][:, 512 * hh + 128 * qs:
                                                 512 * hh + 128 * qs + 128]
                            rhs = p.B2[:, kt * 1040 + (2 * g + hh) * 65:
                                       kt * 1040 + (2 * g + hh) * 65 + 65]
                            return eng.matmul(
                                p.bank(4 + hh)[:, 65 * qs:65 * qs + 65],
                                lhsT, rhs, start=(ki == 0 and qs == 0),
                                stop=(ki == nki - 1 and qs == 3))
                        wk2 = bkey(4 + hh)
                        if ki == 0 and qs == 0:
                            wkey = (wk2 + "@acc", wk2)
                        elif ki == nki - 1 and qs == 3:
                            wkey = (wk2,)
                        else:
                            wkey = (wk2 + "@acc",)
                        S.add("tensor", av, reads=("BIG2", Pt[ki % 2][1]),
                              writes=wkey, tag="av1")

            for ki, kt in enumerate(kts_list):
                pp = ki % 2
                S.dma("sdma", kts[pp][0][:, 0:128],
                      p.dram["l1_kT"][bass.ts(g, 128), bass.ts(kt, 128)],
                      reads=("l1_kT",), writes=(kts[pp][1],), tag="lk1")
                for hh in range(2):
                    def qk(eng, kt=kt, hh=hh, pp=pp, bb=bb):
                        lhsT = kts[pp][0][64 * hh:64 * hh + 64, 0:128]
                        rhs = qts[bb][0][64 * hh:64 * hh + 64, 0:512]
                        return eng.matmul(p.bankpair(pp)[:, 512 * hh:512 * hh + 512],
                                          lhsT, rhs, start=True, stop=True,
                                          tile_position=(64 * hh, 0))
                    S.add("tensor", qk, reads=(kts[pp][1], qts[bb][1]),
                          writes=(bkey(2 * pp + hh),), tag="qk1")

                def ex(eng, g=g, pp=pp, kt=kt, ki=ki):
                    return eng.activation(Pt[ki % 2][0][:, :], p.bankpair(pp)[:, :],
                                          AF.Exp, bias=negc[:, g:g + 1], scale=0.125)
                S.add("scalar", ex, reads=(bkey(2 * pp), bkey(2 * pp + 1), "S4", "fconsts"),
                      writes=(Pt[ki % 2][1],), tag="ex1")
                mi = kt - qc * 4 + 1

                def msk(eng, ki=ki, mi=mi):
                    mm = masks[:, mi * 512:(mi + 1) * 512].unsqueeze(1).to_broadcast(
                        [128, 2, 512])
                    src = Pt[ki % 2][0].ap().rearrange("p (h q) -> p h q", h=2)
                    return eng.tensor_tensor(src, src, mm, ALU.mult)
                S.add("vector", msk, reads=(Pt[ki % 2][1], "BIG1"),
                      writes=(Pt[ki % 2][1],), tag="msk")
                if ki > 0:
                    emit_av1(ki - 1, kts_list[ki - 1])
            emit_av1(len(kts_list) - 1, kts_list[-1])
            for hh in range(2):
                for qs in range(4):
                    ci = hh * 4 + qs

                    def rc(eng, hh=hh, qs=qs, ci=ci):
                        return eng.reciprocal(rc_sb[:, ci:ci + 1],
                                              p.bank(4 + hh)[:, 65 * qs + 64:65 * qs + 65])
                    S.add("vector", rc, reads=(bkey(4), bkey(5)), writes=("S2",), tag="rc1")

                    def nm(eng, g=g, hh=hh, qs=qs, ci=ci):
                        dst = osb[qs // 2][:, (qs % 2) * 1024 + g * 128 + hh * 64:
                                           (qs % 2) * 1024 + g * 128 + hh * 64 + 64]
                        return eng.tensor_scalar(dst, p.bank(4 + hh)[:, 65 * qs:65 * qs + 64],
                                                 rc_sb[:, ci:ci + 1], None, ALU.mult, ALU.bypass)
                    S.add("vector", nm, reads=(bkey(4), bkey(5), "S2"),
                          writes=(osbk[qs // 2],), tag="nm1")
        for qs in range(4):
            osrc = osb[qs // 2][:, (qs % 2) * 1024:(qs % 2) * 1024 + 1024]

            def dstf(f):
                return otT[0][0][:, bass.ts(f, 128)], otT[0][1]
            transposes2(p, S, osrc, osbk[qs // 2], dstf, "otr1")
            for oc in range(2):
                mm_acc(p, S, "wp%d" % oc, p.bank(6 + oc), bkey(6 + oc),
                       [(otT[0][0][:, bass.ts(f, 128)], otT[0][1]) for f in range(NF)],
                       [(p.Wsl[0][:, f * D + oc * 512: f * D + oc * 512 + 512], "W0")
                        for f in range(NF)])
            S.add("vector", lambda eng: eng.tensor_copy(sev[0][0][:, :], p.bankpair(3)[:, :]),
                  reads=(bkey(6), bkey(7)), writes=(sev[0][1],), tag="sev1")
            S.dma("sdma", p.dram["s1"][qc * 512 + qs * 128: qc * 512 + qs * 128 + 128, :],
                  sev[0][0][:, :], reads=(sev[0][1],), writes=("s1",), tag="sts1")


def emit_l1ffn(p):
    S = p.sched
    w, wk2 = big1(p, 0)
    load_weight_bf16(p, S, "l1f_wT", w, wk2, D, D)
    scl = p.F[7]
    bia = p.F[6]
    S.dma("sdma", scl[:, :], p.dram["l1f_scale"][:, :], reads=("l1f_scale",), writes=("F7",))
    S.dma("sdma", bia[:, :], p.dram["l1f_bias"][:, :], reads=("l1f_bias",), writes=("F6",))
    for f in range(NF):
        blk = w[:, f * D:(f + 1) * D]
        S.add("vector", lambda eng, blk=blk: eng.tensor_single_scalar(
            blk, blk, 0.0, ALU.is_ge), reads=(wk2,), writes=(wk2,), tag="sg1")
        S.add("vector", lambda eng, blk=blk: eng.tensor_scalar(
            blk, blk, 2.0, -1.0, ALU.mult, ALU.add), reads=(wk2,), writes=(wk2,))
        S.add("vector", lambda eng, blk=blk: eng.tensor_tensor(
            blk, blk, scl[:, :], ALU.mult), reads=(wk2, "F7"), writes=(wk2,))
    xt = [(p.B[2], "B2"), (p.B[3], "B3")]
    gt = [(p.F[0], "F0"), (p.F[1], "F1")]
    st_ = [(p.F[2], "F2")]
    x1t = [(p.F[4], "F4")]
    for i in range(NT):
        b = i % 2
        src = p.dram["xn1T"].rearrange("(f p) t -> p f t", p=128)[:, :, bass.ts(i, 128)]
        S.dma("sdma", xt[b][0].ap().rearrange("p (f c) -> p f c", f=NF), src,
              reads=("xn1T",), writes=(xt[b][1],), tag="lx1f")
        for oc in range(2):
            mm_acc(p, S, "1f%d" % oc, p.bank(oc), bkey(oc),
                   [(xt[b][0][:, bass.ts(f, 128)], xt[b][1]) for f in range(NF)],
                   [(w[:, f * D + oc * 512: f * D + oc * 512 + 512], wk2) for f in range(NF)])
        S.add("vector", lambda eng, b=b: eng.tensor_tensor(
            gt[b][0][:, :], p.bankpair(0)[:, :], bia[:, :], ALU.add),
            reads=(bkey(0), bkey(1), "F6"), writes=(gt[b][1],), tag="gev1")
        S.add("vector", lambda eng, b=b: eng.tensor_scalar(
            gt[b][0][:, :], gt[b][0][:, :], 100.0, -100.0, ALU.min, ALU.max),
            reads=(gt[b][1],), writes=(gt[b][1],))
        S.add("scalar", lambda eng, b=b: eng.activation(gt[b][0][:, :], gt[b][0][:, :], AF.Silu),
              reads=(gt[b][1],), writes=(gt[b][1],))
        S.dma("sdma", st_[0][0][:, :], p.dram["s1"][bass.ts(i, 128), :],
              reads=("s1",), writes=(st_[0][1],), tag="ls1")
        S.dma("sdma", x1t[0][0][:, :], p.dram["x1"][bass.ts(i, 128), :],
              reads=("x1",), writes=(x1t[0][1],), tag="lx1")
        S.add("vector", lambda eng, b=b: eng.tensor_mul(gt[b][0][:, :], gt[b][0][:, :], st_[0][0][:, :]),
              reads=(gt[b][1], st_[0][1]), writes=(gt[b][1],), tag="sg1e")
        S.add("vector", lambda eng, b=b: eng.tensor_add(gt[b][0][:, :], gt[b][0][:, :], x1t[0][0][:, :]),
              reads=(gt[b][1], x1t[0][1]), writes=(gt[b][1],), tag="res1")
        S.dma("sdma", p.dram["x2"][bass.ts(i, 128), :], gt[b][0][:, :],
              reads=(gt[b][1],), writes=("x2",), tag="stx2")


def emit_l2ffn_a(p):
    S = p.sched
    wq_, wqk = big1(p, 0)
    load_weight_bf16(p, S, "l2f_Wq", wq_, wqk, D, D)
    wo_, wok = big1(p, 1)
    load_weight_bf16(p, S, "l2f_Wo", wo_, wok, D, D)
    wdn = p.B1[:, 24576:25088]
    S.dma("gdma", wdn.rearrange("p (f o) -> p f o", f=NF),
          p.dram["l2f_Wdown"].rearrange("(f p) o -> p f o", p=128),
          reads=("l2f_Wdown",), writes=("BIG1",))
    wup = p.B1[0:64, 25088:27136]
    S.dma("gdma", wup, p.dram["l2f_Wup"][:, :], reads=("l2f_Wup",), writes=("BIG1",))
    wg = p.B1[:, 27136:27200]
    S.dma("gdma", wg.rearrange("p (f o) -> p f o", f=NF),
          p.dram["l2f_Wg"].rearrange("(f p) o -> p f o", p=128),
          reads=("l2f_Wg",), writes=("BIG1",))
    g1r = p.F[6]
    g2r = p.F[7]
    S.dma("sdma", g1r[:, :], p.dram["l2f_g1"][:, :], reads=("l2f_g1",), writes=("F6",))
    S.dma("sdma", g2r[:, :], p.dram["l2f_g2"][:, :], reads=("l2f_g2",), writes=("F7",))
    bqr = p.F[5]
    S.dma("sdma", bqr[:, :], p.dram["l2f_bq"][:, :], reads=("l2f_bq",), writes=("F5",))
    bor = p.F[4]
    S.dma("sdma", bor[:, :], p.dram["l2f_bo"][:, :], reads=("l2f_bo",), writes=("F4",))
    bups = p.B1.ap().bitcast(F32)[:, 8192:10240]
    S.dma("sdma", bups, p.dram["l2f_bup"][:, :], reads=("l2f_bup",), writes=("BIG1",))
    bdns = p.S[5][:, 0:64]
    S.dma("sdma", bdns, p.dram["l2f_bdn"][:, :], reads=("l2f_bdn",), writes=("S5",))
    bgr = p.S[4][:, 0:8]
    S.dma("sdma", bgr, p.dram["l2f_bg"][:, :], reads=("l2f_bg",), writes=("S4",))

    xtf = [(p.F[0], "F0"), (p.F[1], "F1")]
    hb = [(p.B[0], "B0")]
    hT = [(p.B[1], "B1")]
    qsb = [(p.B[2], "B2")]
    ksb = [(p.B[3], "B3")]
    vsb = [(p.B[4], "B4")]
    osb = [(p.B[5], "B5")]
    hf = [(p.F[2], "F2")]
    att = [(p.F[3], "F3")]
    prod = p.B2[:, 0:16384]
    junk = p.S[2]
    stats = p.S[3]

    for i in range(NT):
        b = 0
        un = ("vector", "gpsimd")[i % 2]
        S.dma("sdma", xtf[i % 2][0][:, :], p.dram["xn2"][bass.ts(i, 128), :],
              reads=("xn2",), writes=(xtf[i % 2][1],), tag="lx2")
        xk = xtf[i % 2]
        S.add("scalar", lambda eng, xk=xk: eng.activation(
            junk.ap().bitcast(BF16)[:, 0:128], xk[0][:, 0:128], AF.Square,
            accum_out=stats[:, 0:1]) if False else eng.activation(
            p.B[0][:, :], xk[0][:, :], AF.Square, accum_out=stats[:, 0:1]),
            reads=(xk[1],), writes=("B0", "S3"), tag="l2sq")
        p.fconst(1e-6)
        p.fconst(-0.5)
        S.add("scalar", lambda eng: eng.activation(stats[:, 1:2], stats[:, 0:1],
                                                   AF.Ln, bias=1e-6, scale=1.0 / D),
              reads=("S3", "fconsts"), writes=("S3",))
        S.add("scalar", lambda eng: eng.activation(stats[:, 1:2], stats[:, 1:2],
                                                   AF.Exp, scale=-0.5),
              reads=("S3", "fconsts"), writes=("S3",))
        S.add("vector", lambda eng, xk=xk: eng.tensor_scalar(
            hf[0][0][:, :], xk[0][:, :], stats[:, 1:2], None, ALU.mult, ALU.bypass),
            reads=(xk[1], "S3"), writes=(hf[0][1],), tag="l2h1")
        S.add("vector", lambda eng: eng.tensor_tensor(hf[0][0][:, :], hf[0][0][:, :],
                                                      g1r[:, :], ALU.mult),
            reads=(hf[0][1], "F6"), writes=(hf[0][1],))
        S.add("scalar", lambda eng: eng.copy(hb[0][0][:, :], hf[0][0][:, :]),
              reads=(hf[0][1],), writes=(hb[0][1],))
        transposes(p, S, hb[0][0], hb[0][1],
                   lambda f: (hT[0][0][:, bass.ts(f, 128)], hT[0][1]), "l2ht")
        for oc in range(2):
            mm_acc(p, S, "2q%d" % oc, p.bank(oc), bkey(oc),
                   [(hT[0][0][:, bass.ts(f, 128)], hT[0][1]) for f in range(NF)],
                   [(wq_[:, f * D + oc * 512: f * D + oc * 512 + 512], wqk) for f in range(NF)])
        S.add("vector", lambda eng: eng.tensor_tensor(
            qsb[0][0][:, :], p.bankpair(0)[:, :], bqr[:, :], ALU.add),
            reads=(bkey(0), bkey(1), "F5"), writes=(qsb[0][1],), tag="qev2")
        mm_acc(p, S, "2dn", p.bank(4)[:, 0:64], bkey(4),
               [(hT[0][0][:, bass.ts(f, 128)], hT[0][1]) for f in range(NF)],
               [(wdn[:, f * 64:(f + 1) * 64], "BIG1") for f in range(NF)])
        S.add("vector", lambda eng: eng.tensor_tensor(
            junk.ap().bitcast(BF16)[:, 0:64], p.bank(4)[:, 0:64], bdns, ALU.add),
            reads=(bkey(4), "S5"), writes=("S2",), tag="cev")
        S.add("tensor", lambda eng: eng.transpose(
            p.bank_bf(5, 128)[0:64, :], junk.ap().bitcast(BF16)[:, 0:64],
            p.ident_sb[:, :]),
            reads=("S2", "ident"), writes=(bkey(5),), tag="ctr")
        S.add("vector", lambda eng: eng.tensor_copy(
            junk.ap().bitcast(BF16)[0:64, 128:256], p.bank_bf(5, 128)[0:64, :]),
            reads=(bkey(5),), writes=("S2",), tag="cte")
        for oc in range(4):
            def kvmm(eng, oc=oc):
                return eng.matmul(p.bank(oc)[:, :] if False else p.bankpair(oc // 2)[:, 512 * (oc % 2):512 * (oc % 2) + 512],
                                  junk.ap().bitcast(BF16)[0:64, 128:256],
                                  wup[0:64, oc * 512:(oc + 1) * 512],
                                  start=True, stop=True)
            S.add("tensor", kvmm, reads=("S2", "BIG1"), writes=(bkey(oc),), tag="kv2")
        kv0 = p.pool[0].ap().rearrange("p (h x) -> p h x", h=8)
        kv1 = p.pool[1].ap().rearrange("p (h x) -> p h x", h=8)
        bup3 = bups.rearrange("p (h x) -> p h x", h=16)

        def kev(eng):
            dst = ksb[0][0].ap().rearrange("p (h d) -> p h d", h=16)
            eng.tensor_tensor(dst[:, 0:8, :], kv0[:, :, 0:64],
                              bup3[:, 0:8, 0:64], ALU.add)
            return eng.tensor_tensor(dst[:, 8:16, :], kv1[:, :, 0:64],
                                     bup3[:, 8:16, 0:64], ALU.add)
        S.add("vector", kev, reads=(bkey(0), bkey(1), bkey(2), bkey(3), "BIG1"),
              writes=(ksb[0][1],), tag="kev2")

        def vev2(eng):
            dst = vsb[0][0].ap().rearrange("p (h d) -> p h d", h=16)
            eng.tensor_tensor(dst[:, 0:8, :], kv0[:, :, 64:128],
                              bup3[:, 0:8, 64:128], ALU.add)
            return eng.tensor_tensor(dst[:, 8:16, :], kv1[:, :, 64:128],
                                     bup3[:, 8:16, 64:128], ALU.add)
        S.add("vector", vev2, reads=(bkey(0), bkey(1), bkey(2), bkey(3), "BIG1"),
              writes=(vsb[0][1],), tag="vev2")
        qB = qsb[0][0].ap().rearrange("p (h d) -> p h d", h=16).unsqueeze(2) \
            .to_broadcast([128, 16, 16, 64])
        kB = ksb[0][0].ap().rearrange("p (g d) -> p g d", g=16).unsqueeze(1) \
            .to_broadcast([128, 16, 16, 64])
        prod4 = prod.rearrange("p (h g d) -> p h g d", h=16, g=16)
        S.add(un, lambda eng: eng.tensor_tensor(prod4, qB, kB, ALU.mult),
              reads=(qsb[0][1], ksb[0][1]), writes=("BIG2",), tag="attm")
        S.add("vector", lambda eng: eng.tensor_reduce(
            att[0][0].ap()[:, 0:256].rearrange("p (h g) -> p h g", h=16).unsqueeze(3),
            prod4, AX.X, ALU.add),
            reads=("BIG2",), writes=(att[0][1],), tag="attr")
        S.add("vector", lambda eng: eng.tensor_reduce(
            stats[:, 2:18].rearrange("p (h o) -> p h o", h=16),
            att[0][0].ap()[:, 0:256].rearrange("p (h g) -> p h g", h=16),
            AX.X, ALU.max),
            reads=(att[0][1],), writes=("S3",), tag="amax")
        S.add("vector", lambda eng: eng.tensor_tensor(
            att[0][0].ap()[:, 0:256].rearrange("p (h g) -> p h g", h=16),
            att[0][0].ap()[:, 0:256].rearrange("p (h g) -> p h g", h=16),
            stats[:, 2:18].rearrange("p (h o) -> p h o", h=16).to_broadcast([128, 16, 16]),
            ALU.subtract),
            reads=(att[0][1], "S3"), writes=(att[0][1],), tag="asub")
        p.fconst(0.125)
        S.add("scalar", lambda eng: eng.activation(
            att[0][0][:, 0:256], att[0][0][:, 0:256], AF.Exp, scale=0.125),
            reads=(att[0][1], "fconsts"), writes=(att[0][1],), tag="aexp")
        S.add("vector", lambda eng: eng.tensor_reduce(
            stats[:, 18:34].rearrange("p (h o) -> p h o", h=16),
            att[0][0].ap()[:, 0:256].rearrange("p (h g) -> p h g", h=16),
            AX.X, ALU.add),
            reads=(att[0][1],), writes=("S3",), tag="asum")
        S.add("vector", lambda eng: eng.reciprocal(stats[:, 18:34], stats[:, 18:34]),
              reads=("S3",), writes=("S3",), tag="arec")
        S.add("scalar", lambda eng: eng.copy(junk.ap().bitcast(BF16)[:, 0:256],
                                             att[0][0][:, 0:256]),
              reads=(att[0][1],), writes=("S2",), tag="acast")
        PB = junk.ap().bitcast(BF16)[:, 0:256].rearrange("p (h g) -> p h g", h=16) \
            .unsqueeze(2).to_broadcast([128, 16, 64, 16])
        vB = vsb[0][0].ap().rearrange("p (g d) -> p g d", g=16).transpose([0, 2, 1]) \
            .unsqueeze(1).to_broadcast([128, 16, 64, 16])
        prod2 = prod.rearrange("p (h d g) -> p h d g", h=16, d=64)
        S.add(un, lambda eng: eng.tensor_tensor(prod2, PB, vB, ALU.mult),
              reads=("S2", vsb[0][1]), writes=("BIG2",), tag="avm")
        S.add("vector", lambda eng: eng.tensor_reduce(
            hf[0][0].ap().rearrange("p (h d) -> p h d", h=16).unsqueeze(3),
            prod2, AX.X, ALU.add),
            reads=("BIG2",), writes=(hf[0][1],), tag="avr")
        S.add("vector", lambda eng: eng.tensor_tensor(
            hf[0][0].ap().rearrange("p (h d) -> p h d", h=16),
            hf[0][0].ap().rearrange("p (h d) -> p h d", h=16),
            stats[:, 18:34].rearrange("p (h o) -> p h o", h=16).to_broadcast([128, 16, 64]),
            ALU.mult),
            reads=(hf[0][1], "S3"), writes=(hf[0][1],), tag="anorm")
        S.add("scalar", lambda eng: eng.copy(osb[0][0][:, :], hf[0][0][:, :]),
              reads=(hf[0][1],), writes=(osb[0][1],))
        transposes(p, S, osb[0][0], osb[0][1],
                   lambda f: (hT[0][0][:, bass.ts(f, 128)], hT[0][1]), "l2ot")
        for oc in range(2):
            mm_acc(p, S, "2o%d" % oc, p.bank(oc), bkey(oc),
                   [(hT[0][0][:, bass.ts(f, 128)], hT[0][1]) for f in range(NF)],
                   [(wo_[:, f * D + oc * 512: f * D + oc * 512 + 512], wok) for f in range(NF)])
        S.add("vector", lambda eng: eng.tensor_tensor(
            hf[0][0][:, :], p.bankpair(0)[:, :], bor[:, :], ALU.add),
            reads=(bkey(0), bkey(1), "F4"), writes=(hf[0][1],), tag="xmev")
        S.add("vector", lambda eng, xk=xk: eng.tensor_add(
            hf[0][0][:, :], hf[0][0][:, :], xk[0][:, :]),
            reads=(hf[0][1], xk[1]), writes=(hf[0][1],))
        S.dma("sdma", p.dram["l2_xmid"][bass.ts(i, 128), :], hf[0][0][:, :],
              reads=(hf[0][1],), writes=("l2_xmid",), tag="stxm")
        S.add("scalar", lambda eng: eng.activation(
            p.B[0][:, :], hf[0][0][:, :], AF.Square, accum_out=stats[:, 0:1]),
            reads=(hf[0][1],), writes=("B0", "S3"), tag="h2sq")
        S.add("scalar", lambda eng: eng.activation(stats[:, 1:2], stats[:, 0:1],
                                                   AF.Ln, bias=1e-6, scale=1.0 / D),
              reads=("S3", "fconsts"), writes=("S3",))
        S.add("scalar", lambda eng: eng.activation(stats[:, 1:2], stats[:, 1:2],
                                                   AF.Exp, scale=-0.5),
              reads=("S3", "fconsts"), writes=("S3",))
        S.add("vector", lambda eng: eng.tensor_scalar(
            hf[0][0][:, :], hf[0][0][:, :], stats[:, 1:2], None, ALU.mult, ALU.bypass),
            reads=(hf[0][1], "S3"), writes=(hf[0][1],))
        S.add("vector", lambda eng: eng.tensor_tensor(hf[0][0][:, :], hf[0][0][:, :],
                                                      g2r[:, :], ALU.mult),
            reads=(hf[0][1], "F7"), writes=(hf[0][1],))
        S.add("scalar", lambda eng: eng.copy(hb[0][0][:, :], hf[0][0][:, :]),
              reads=(hf[0][1],), writes=(hb[0][1],))
        transposes(p, S, hb[0][0], hb[0][1],
                   lambda f: (hT[0][0][:, bass.ts(f, 128)], hT[0][1]), "l2h2t")
        dst = p.dram["l2_h2T"].rearrange("(f p) t -> p f t", p=128)[:, :, bass.ts(i, 128)]
        S.dma("sdma", dst, hT[0][0].ap().rearrange("p (f c) -> p f c", c=128),
              reads=(hT[0][1],), writes=("l2_h2T",), tag="sth2")
        for f in range(NF):
            def gmm(eng, f=f):
                return eng.matmul(p.bank(6)[:, 0:8], hT[0][0][:, bass.ts(f, 128)],
                                  wg[:, f * 8:(f + 1) * 8], start=(f == 0),
                                  stop=(f == NF - 1))
            wkk2 = (bkey(6) + "@acc", bkey(6)) if f == 0 else \
                ((bkey(6),) if f == NF - 1 else (bkey(6) + "@acc",))
            S.add("tensor", gmm, reads=(hT[0][1], "BIG1"), writes=wkk2, tag="gmm")
        gatec = att[0]
        S.add("vector", lambda eng: eng.tensor_tensor(
            gatec[0][:, 0:8], p.bank(6)[:, 0:8], bgr, ALU.add),
            reads=(bkey(6), "S4"), writes=(gatec[1],), tag="gev")
        S.add("vector", lambda eng: eng.tensor_reduce(
            stats[:, 0:1], gatec[0][:, 0:8], AX.X, ALU.max, negate=True),
            reads=(gatec[1],), writes=("S3",))
        S.add("scalar", lambda eng: eng.activation(
            gatec[0][:, 0:8], gatec[0][:, 0:8], AF.Exp, bias=stats[:, 0:1]),
            reads=(gatec[1], "S3", "fconsts"), writes=(gatec[1],))
        S.add("vector", lambda eng: eng.tensor_reduce(
            stats[:, 0:1], gatec[0][:, 0:8], AX.X, ALU.add),
            reads=(gatec[1],), writes=("S3",))
        S.add("vector", lambda eng: eng.reciprocal(stats[:, 1:2], stats[:, 0:1]),
              reads=("S3",), writes=("S3",))
        S.add("vector", lambda eng: eng.tensor_scalar(
            gatec[0][:, 0:8], gatec[0][:, 0:8], stats[:, 1:2], None, ALU.mult, ALU.bypass),
            reads=(gatec[1], "S3"), writes=(gatec[1],))
        S.dma("sdma", p.dram["l2_gate"][bass.ts(i, 128), :], gatec[0][:, 0:8],
              reads=(gatec[1],), writes=("l2_gate",), tag="stg")


def emit_l2ffn_b(p):
    S = p.sched
    combv = p.B1.ap().bitcast(F32)
    h2T = p.dram["l2_h2T"]
    ht = [(p.B[2], "B2"), (p.B[3], "B3")]
    gt = [(p.F[3], "F3")]
    wsl = [(p.B2[:, 0:8192], "BIG2"), (p.B2[:, 8192:16384], "BIG2")]
    for half in range(2):
        S.add("vector", lambda eng: eng.memset(combv[:, :], 0.0),
              writes=("BIG1",), tag="cz")
        for e in range(8):
            ws = wsl[e % 2]
            src = p.dram["l2f_We"].rearrange("e (f p) o -> e p f o", p=128)[e]
            S.dma("gdma", ws[0].rearrange("p (f o) -> p f o", f=NF), src,
                  reads=("l2f_We",), writes=(ws[1],), tag="lwe")
            for ti in range(16):
                i = half * 16 + ti
                b = ti % 2
                src2 = h2T.rearrange("(f p) t -> p f t", p=128)[:, :, bass.ts(i, 128)]
                S.dma("sdma", ht[b][0].ap().rearrange("p (f c) -> p f c", f=NF), src2,
                      reads=("l2_h2T",), writes=(ht[b][1],), tag="lh2")
                S.dma("sdma", gt[0][0][:, 0:8], p.dram["l2_gate"][bass.ts(i, 128), :],
                      reads=("l2_gate",), writes=(gt[0][1],), tag="lg")
                for oc in range(2):
                    mm_acc(p, S, "we%d" % oc, p.bank(oc), bkey(oc),
                           [(ht[b][0][:, bass.ts(f, 128)], ht[b][1]) for f in range(NF)],
                           [(ws[0][:, f * D + oc * 512: f * D + oc * 512 + 512], ws[1])
                            for f in range(NF)])
                def acc(eng, ti=ti, e=e):
                    dst = combv[:, ti * 1024:(ti + 1) * 1024]
                    return eng.scalar_tensor_tensor(
                        dst, p.bankpair(0)[:, :], gt[0][0][:, e:e + 1], dst,
                        ALU.mult, ALU.add)
                S.add("vector", acc, reads=(bkey(0), bkey(1), gt[0][1], "BIG1"),
                      writes=("BIG1",), tag="cacc")
        for ti in range(16):
            i = half * 16 + ti
            xm = [(p.F[0], "F0"), (p.F[1], "F1")][ti % 2]
            S.dma("sdma", xm[0][:, :], p.dram["l2_xmid"][bass.ts(i, 128), :],
                  reads=("l2_xmid",), writes=(xm[1],), tag="lxm")
            def fin(eng, ti=ti, xm=xm):
                return eng.tensor_tensor(xm[0][:, :], xm[0][:, :],
                                         combv[:, ti * 1024:(ti + 1) * 1024], ALU.add)
            S.add("vector", fin, reads=(xm[1], "BIG1"), writes=(xm[1],),
                  tag="fin")
            S.add("scalar", lambda eng, xm=xm: eng.activation(xm[0][:, :], xm[0][:, :], AF.Silu),
                  reads=(xm[1],), writes=(xm[1],), tag="fsil")
            s2t = (p.F[2], "F2")
            x2t = (p.F[4], "F4")
            S.dma("sdma", s2t[0][:, :], p.dram["s2"][bass.ts(i, 128), :],
                  reads=("s2",), writes=(s2t[1],), tag="ls2")
            S.dma("sdma", x2t[0][:, :], p.dram["x2"][bass.ts(i, 128), :],
                  reads=("x2",), writes=(x2t[1],), tag="lx2e")
            S.add("vector", lambda eng, xm=xm, s2t=s2t: eng.tensor_mul(
                xm[0][:, :], xm[0][:, :], s2t[0][:, :]),
                reads=(xm[1], s2t[1]), writes=(xm[1],))
            S.add("vector", lambda eng, xm=xm, x2t=x2t: eng.tensor_add(
                xm[0][:, :], xm[0][:, :], x2t[0][:, :]),
                reads=(xm[1], x2t[1]), writes=(xm[1],))
            S.dma("sdma", p.dram["x3"][bass.ts(i, 128), :], xm[0][:, :],
                  reads=(xm[1],), writes=("x3",), tag="stx3")


def tr_banks(p, S, src_ap, src_key, dst_fn, tag, banks=(6, 7), nf=NF):
    for f in range(nf):
        bi = banks[f % 2]
        ps = p.bank_bf(bi)
        pk = bkey(bi)

        def tfn(eng, f=f, ps=ps):
            return eng.transpose(ps, src_ap[:, bass.ts(f, 128)], p.ident_sb[:, :])
        S.add("tensor", tfn, reads=(src_key, "ident"), writes=(pk,), tag=tag + "t%d" % f)
        da, dk = dst_fn(f)

        def efn(eng, da=da, ps=ps):
            return eng.tensor_copy(da, ps)
        S.add("vector", efn, reads=(pk,), writes=(dk,), tag=tag + "e%d" % f)


def emit_l3a(p):
    S = p.sched
    wm, wmk = big1(p, 0)
    load_weight_bf16(p, S, "l3_WmT", wm, wmk, D, D)
    wq_, wqk = big1(p, 1)
    load_weight_bf16(p, S, "l3_WqT", wq_, wqk, D, D)
    bmr, bqr = p.F[6], p.F[7]
    S.dma("sdma", bmr[:, :], p.dram["l3_bm"][:, :], reads=("l3_bm",), writes=("F6",))
    S.dma("sdma", bqr[:, :], p.dram["l3_bq"][:, :], reads=("l3_bq",), writes=("F7",))
    phr = p.F[5]
    S.dma("sdma", phr[:, :], p.dram["l3_phases"][:, :], reads=("l3_phases",), writes=("F5",))
    postc = p.S[1][:, 0:NT]
    S.dma("sdma", postc, p.dram["poscols"][:, :], reads=("poscols",), writes=("S1",))
    inds = p.B1[:, 16384:17408]
    S.dma("sdma", inds, p.dram["indcols"][:, :], reads=("indcols",), writes=("BIG1",))

    xt = [(p.B[4], "B4"), (p.B[5], "B5")]
    mg = [(p.F[0], "F0")]
    qo = [(p.F[1], "F1")]
    phi = [(p.F[2], "F2")]
    t4 = [(p.F[3], "F3")]
    t5 = [(p.F[4], "F4")]
    cb = [(p.B[0], "B0"), (p.B[1], "B1"), (p.B[2], "B2")]
    p.fconst(float(np.pi / 2))
    for i in range(NT):
        b = i % 2
        src = p.dram["xn3T"].rearrange("(f p) t -> p f t", p=128)[:, :, bass.ts(i, 128)]
        S.dma("sdma", xt[b][0].ap().rearrange("p (f c) -> p f c", f=NF), src,
              reads=("xn3T",), writes=(xt[b][1],), tag="lx3T")
        xf = [(p.F[8], "F8")]
        S.dma("sdma", xf[0][0][:, :], p.dram["xn3"][bass.ts(i, 128), :],
              reads=("xn3",), writes=(xf[0][1],), tag="lx3")
        for oc in range(2):
            mm_acc(p, S, "3m%d" % oc, p.bank(oc), bkey(oc),
                   [(xt[b][0][:, bass.ts(f, 128)], xt[b][1]) for f in range(NF)],
                   [(wm[:, f * D + oc * 512: f * D + oc * 512 + 512], wmk) for f in range(NF)])
        S.add("vector", lambda eng: eng.tensor_tensor(
            mg[0][0][:, :], p.bankpair(0)[:, :], bmr[:, :], ALU.add),
            reads=(bkey(0), bkey(1), "F6"), writes=(mg[0][1],), tag="mgev")
        S.add("scalar", lambda eng: eng.activation(mg[0][0][:, :], mg[0][0][:, :], AF.Sigmoid),
              reads=(mg[0][1], "fconsts"), writes=(mg[0][1],))
        S.add("vector", lambda eng: eng.tensor_scalar_mul(mg[0][0][:, :], mg[0][0][:, :], 5.0),
              reads=(mg[0][1],), writes=(mg[0][1],))
        for oc in range(2):
            mm_acc(p, S, "3q%d" % oc, p.bank(oc), bkey(oc),
                   [(xt[b][0][:, bass.ts(f, 128)], xt[b][1]) for f in range(NF)],
                   [(wq_[:, f * D + oc * 512: f * D + oc * 512 + 512], wqk) for f in range(NF)])
        S.add("vector", lambda eng: eng.tensor_tensor(
            qo[0][0][:, :], p.bankpair(0)[:, :], bqr[:, :], ALU.add),
            reads=(bkey(0), bkey(1), "F7"), writes=(qo[0][1],), tag="qoev")
        S.dma("sdma", p.dram["l3_qoff"][bass.ts(i, 128), :], qo[0][0][:, :],
              reads=(qo[0][1],), writes=("l3_qoff",), tag="stqo")
        S.add("vector", lambda eng, i=i: eng.tensor_scalar(
            phi[0][0][:, :], phr[:, :], postc[:, i:i + 1], 0.01, ALU.mult, ALU.mult),
            reads=("F5", "S1"), writes=(phi[0][1],), tag="phi")
        S.add("vector", lambda eng: eng.tensor_scalar(
            t4[0][0][:, :], phi[0][0][:, :], 1.0 / TWO_PI, MAGIC, ALU.mult, ALU.add),
            reads=(phi[0][1],), writes=(t4[0][1],))
        S.add("vector", lambda eng: eng.tensor_scalar(
            t4[0][0][:, :], t4[0][0][:, :], MAGIC, None, ALU.subtract, ALU.bypass),
            reads=(t4[0][1],), writes=(t4[0][1],))
        S.add("vector", lambda eng: eng.scalar_tensor_tensor(
            phi[0][0][:, :], t4[0][0][:, :], -TWO_PI, phi[0][0][:, :], ALU.mult, ALU.add),
            reads=(t4[0][1], phi[0][1]), writes=(phi[0][1],))
        S.add("vector", lambda eng: eng.tensor_scalar(
            phi[0][0][:, :], phi[0][0][:, :], PI_C, -PI_C, ALU.min, ALU.max),
            reads=(phi[0][1],), writes=(phi[0][1],))
        S.add("scalar", lambda eng: eng.activation(t4[0][0][:, :], phi[0][0][:, :], AF.Sin),
              reads=(phi[0][1], "fconsts"), writes=(t4[0][1],), tag="sinp")
        S.add("vector", lambda eng: eng.tensor_scalar_mul(t5[0][0][:, :], phi[0][0][:, :], -1.0),
              reads=(phi[0][1],), writes=(t5[0][1],))
        S.add("vector", lambda eng: eng.tensor_max(t5[0][0][:, :], t5[0][0][:, :], phi[0][0][:, :]),
              reads=(t5[0][1], phi[0][1]), writes=(t5[0][1],))
        S.add("scalar", lambda eng: eng.activation(t5[0][0][:, :], t5[0][0][:, :], AF.Sin,
                                                   bias=float(np.pi / 2), scale=-1.0),
              reads=(t5[0][1], "fconsts"), writes=(t5[0][1],), tag="cosp")
        S.add("vector", lambda eng, xf=xf: eng.tensor_mul(xf[0][0][:, :], xf[0][0][:, :], mg[0][0][:, :]),
              reads=(xf[0][1], mg[0][1]), writes=(xf[0][1],), tag="wgt")
        S.add("vector", lambda eng, xf=xf: eng.tensor_tensor(
            cb[0][0][:, :], xf[0][0][:, :], t5[0][0][:, :], ALU.mult),
            reads=(xf[0][1], t5[0][1]), writes=(cb[0][1],), tag="wc")
        S.add("vector", lambda eng, xf=xf: eng.tensor_tensor(
            cb[1][0][:, :], xf[0][0][:, :], t4[0][0][:, :], ALU.mult),
            reads=(xf[0][1], t4[0][1]), writes=(cb[1][1],), tag="ws")
        S.add("scalar", lambda eng: eng.copy(cb[2][0][:, :], mg[0][0][:, :]),
              reads=(mg[0][1],), writes=(cb[2][1],))
        for ci, nm in ((0, "l3_wc"), (1, "l3_ws"), (2, "l3_mg")):
            S.dma("sdma", p.dram[nm][bass.ts(i, 128), :], cb[ci][0][:, :],
                  reads=(cb[ci][1],), writes=(nm,), tag="st" + nm)
        for ci in range(3):
            for oc in range(2):
                def bs(eng, ci=ci, oc=oc, i=i):
                    return eng.matmul(p.bank(2 + 2 * ci + oc)[0:32, :],
                                      inds[:, i * 32:i * 32 + 32],
                                      cb[ci][0][:, bass.ts(oc, 512)],
                                      start=(i == 0), stop=(i == NT - 1))
                kk = bkey(2 + 2 * ci + oc)
                wkey = (kk + "@acc", kk) if i == 0 else ((kk,) if i == NT - 1 else (kk + "@acc",))
                S.add("tensor", bs, reads=("BIG1", cb[ci][1]), writes=wkey, tag="bs")
    trix = p.B1[0:32, 17408:17440]
    S.dma("sdma", trix, p.dram["triexc"][:, :], reads=("triexc",), writes=("BIG1",))
    bsb = [(p.B[0], "B0"), (p.B[1], "B1"), (p.B[2], "B2")]
    for ci in range(3):
        def ev(eng, ci=ci):
            return eng.tensor_copy(bsb[ci][0][0:32, :], p.bankpair(1 + ci)[0:32, :])
        S.add("vector", ev, reads=(bkey(2 + 2 * ci), bkey(3 + 2 * ci)),
              writes=(bsb[ci][1],), tag="bse")
    for ci in range(3):
        for oc in range(2):
            def pf(eng, ci=ci, oc=oc):
                return eng.matmul(p.bank(oc)[0:32, :], trix[:, :],
                                  bsb[ci][0][0:32, bass.ts(oc, 512)],
                                  start=True, stop=True)
            S.add("tensor", pf, reads=("BIG1", bsb[ci][1]), writes=(bkey(oc),), tag="pf")
        def ev2(eng, ci=ci):
            return eng.tensor_copy(p.B[3][0:32, :], p.bankpair(0)[0:32, :])
        S.add("vector", ev2, reads=(bkey(0), bkey(1)), writes=("B3",), tag="pfe")
        S.dma("sdma", p.dram["l3_carr"][:, ci * 1024:(ci + 1) * 1024], p.B[3][0:32, :],
              reads=("B3",), writes=("l3_carr",), tag="stcar")


def emit_l3b(p):
    S = p.sched
    def w1blk(f):
        if f < 16:
            return p.B1[:, f * 2048:(f + 1) * 2048], "BIG1"
        return p.B2[:, (f - 16) * 2048:(f - 16) * 2048 + 2048], "BIG2"
    for f in range(NT):
        dst, dk = w1blk(f)
        src = p.dram["l3_W1T"].rearrange("(f p) o -> f p o", p=128)[f]
        S.dma("gdma", dst, src, reads=("l3_W1T",), writes=(dk,), tag="lw1")
    gcol = p.S[1][:, 0:NT]
    S.dma("sdma", gcol, p.dram["l3_lng"][:, :], reads=("l3_lng",), writes=("S1",), tag="lgc")
    lbcol = p.S[2][:, 0:NT]
    S.dma("sdma", lbcol, p.dram["l3_lnb"][:, :], reads=("l3_lnb",), writes=("S2",), tag="lbc")
    ginv = p.S[3][:, 0:NT]
    S.add("vector", lambda eng: eng.reciprocal(ginv, gcol), reads=("S1",), writes=("S3",), tag="lbg0")
    S.add("vector", lambda eng: eng.tensor_tensor(lbcol, lbcol, ginv, ALU.mult),
          reads=("S2", "S3"), writes=("S2",), tag="lbg")
    lbb = p.S[2].ap().bitcast(BF16)[:, 64:96]
    S.add("vector", lambda eng: eng.tensor_copy(lbb, lbcol), reads=("S2",), writes=("S2",))
    for f in range(NT):
        dst, dk = w1blk(f)
        S.add("vector", lambda eng, dst=dst, f=f: eng.tensor_scalar(
            dst, dst, gcol[:, f:f + 1], None, ALU.mult, ALU.bypass),
            reads=(dk, "S1"), writes=(dk,), tag="gfold")
    for f in range(NT):
        wv, wk2 = w1blk(f)
        for oc in range(4):
            def mv(eng, f=f, oc=oc, wv=wv):
                return eng.matmul(p.bank(4 + oc)[0:1, :], lbb[:, f:f + 1],
                                  wv[:, oc * 512:(oc + 1) * 512],
                                  start=(f == 0), stop=(f == NT - 1))
            kk = bkey(4 + oc)
            wkey = (kk + "@acc", kk) if f == 0 else ((kk,) if f == NT - 1 else (kk + "@acc",))
            S.add("tensor", mv, reads=("S2", wk2), writes=wkey, tag="b1mv")
    b1r = [p.R[0].ap().bitcast(BF16)[0:1, :], p.R[1].ap().bitcast(BF16)[0:1, :]]
    b1l = p.R[2][0:1, 0:512]
    for oc in range(4):
        S.dma("sdma", b1l, p.dram["l3_b1"][0:1, oc * 512:(oc + 1) * 512],
              reads=("l3_b1",), writes=("R2",), tag="lb1")
        def bev(eng, oc=oc):
            return eng.tensor_tensor(b1l, b1l, p.bank(4 + oc)[0:1, :], ALU.add)
        S.add("vector", bev, reads=("R2", bkey(4 + oc)), writes=("R2",), tag="bev")
        def bcast(eng, oc=oc):
            return eng.tensor_copy(b1r[oc // 2][0:1, (oc % 2) * 512:(oc % 2) * 512 + 512], b1l)
        S.add("vector", bcast, reads=("R2",), writes=("R%d" % (oc // 2),), tag="bcv")

    cstg = p.R[2].ap().bitcast(BF16)[0:1, :]
    wcb = [(p.B[0], "B0"), (p.B[1], "B1"), (p.B[2], "B2")]
    trio = ("l3_wc", "l3_ws", "l3_mg")
    invs = p.F[2]
    cq, sq = p.F[5], p.F[6]
    sphi, cphi = p.F[3], p.F[4]
    phi = p.F[2]
    ta, tb = p.F[7], p.F[8]
    xf = p.F[0]
    qf2 = p.F[1]
    postc = p.S[1][:, 32:64]
    S.dma("sdma", postc, p.dram["poscols"][:, :], reads=("poscols",), writes=("S1",), tag="lpc")
    l3ph = p.sb("l3ph", [128, 1024], F32)
    phrv = l3ph[:, 0:512]
    phrv2 = l3ph[:, 512:1024]
    S.dma("sdma", l3ph[:, :], p.dram["l3_phases"][:, :], reads=("l3_phases",),
          writes=("l3ph",), tag="lph")
    p.fconst(float(np.pi / 2))
    p.fconst(1e-8)
    p.fconst(-0.5)

    for i in range(NT):
        for ci in range(3):
            S.dma("sdma", wcb[ci][0][:, :], p.dram[trio[ci]][bass.ts(i, 128), :],
                  reads=(trio[ci],), writes=(wcb[ci][1],), tag="lwc")
        S.dma("sdma", xf[:, :], p.dram["xn3"][bass.ts(i, 128), :],
              reads=("xn3",), writes=("F0",), tag="lx3b")
        S.dma("sdma", qf2[:, :], p.dram["l3_qoff"][bass.ts(i, 128), :],
              reads=("l3_qoff",), writes=("F1",), tag="lqo")
        for ci in range(3):
            S.dma("sdma", cstg, p.dram["l3_carr"][i:i + 1, ci * 1024:(ci + 1) * 1024],
                  reads=("l3_carr",), writes=("R2",), tag="lcar")
            for oc in range(2):
                def cs1(eng, ci=ci, oc=oc):
                    return eng.matmul(p.bank(2 * ci + oc), p.tri_sb[:, :],
                                      wcb[ci][0][:, bass.ts(oc, 512)],
                                      start=True, stop=False)
                S.add("tensor", cs1, reads=("tri", wcb[ci][1]),
                      writes=(bkey(2 * ci + oc) + "@acc", bkey(2 * ci + oc)), tag="cs1")
                def cs2(eng, ci=ci, oc=oc):
                    return eng.matmul(p.bank(2 * ci + oc), p.ones1_sb[0:1, :],
                                      cstg[0:1, oc * 512:oc * 512 + 512],
                                      start=False, stop=True)
                S.add("tensor", cs2, reads=("ones1", "R2"),
                      writes=(bkey(2 * ci + oc),), tag="cs2")
        S.add("scalar", lambda eng: eng.activation(invs[:, :], p.bankpair(2)[:, :],
                                                   AF.Ln, bias=1e-8),
              reads=(bkey(4), bkey(5), "fconsts"), writes=("F2",), tag="ln3")
        S.add("scalar", lambda eng: eng.activation(invs[:, :], invs[:, :],
                                                   AF.Exp, scale=-0.5),
              reads=("F2", "fconsts"), writes=("F2",), tag="ex3")
        def mkphi(dst, dstk, addq):
            def f1(eng, dst=dst, i=i):
                eng.tensor_scalar(dst[:, 0:512], phrv, postc[:, i:i + 1], 0.01,
                                  ALU.mult, ALU.mult)
                return eng.tensor_scalar(dst[:, 512:1024], phrv2, postc[:, i:i + 1],
                                         0.01, ALU.mult, ALU.mult)
            S.add("vector", f1, reads=("l3ph", "S1"), writes=(dstk,), tag="mph")
            if addq:
                S.add("vector", lambda eng, dst=dst: eng.tensor_add(dst[:, :], dst[:, :], qf2[:, :]),
                      reads=(dstk, "F1"), writes=(dstk,))
            S.add("vector", lambda eng, dst=dst: eng.tensor_scalar(
                ta[:, :], dst[:, :], 1.0 / TWO_PI, MAGIC, ALU.mult, ALU.add),
                reads=(dstk,), writes=("F7",))
            S.add("vector", lambda eng: eng.tensor_scalar(
                ta[:, :], ta[:, :], MAGIC, None, ALU.subtract, ALU.bypass),
                reads=("F7",), writes=("F7",))
            S.add("vector", lambda eng, dst=dst: eng.scalar_tensor_tensor(
                dst[:, :], ta[:, :], -TWO_PI, dst[:, :], ALU.mult, ALU.add),
                reads=("F7", dstk), writes=(dstk,))
            S.add("vector", lambda eng, dst=dst: eng.tensor_scalar(
                dst[:, :], dst[:, :], PI_C, -PI_C, ALU.min, ALU.max),
                reads=(dstk,), writes=(dstk,))

        mkphi(cq, "F5", True)
        S.add("scalar", lambda eng: eng.activation(sq[:, :], cq[:, :], AF.Sin),
              reads=("F5", "fconsts"), writes=("F6",), tag="sq3")
        S.add("vector", lambda eng: eng.tensor_scalar_mul(tb[:, :], cq[:, :], -1.0),
              reads=("F5",), writes=("F8",))
        S.add("vector", lambda eng: eng.tensor_max(tb[:, :], tb[:, :], cq[:, :]),
              reads=("F8", "F5"), writes=("F8",))
        S.add("scalar", lambda eng: eng.activation(cq[:, :], tb[:, :], AF.Sin,
                                                   bias=float(np.pi / 2), scale=-1.0),
              reads=("F8", "fconsts"), writes=("F5",), tag="cq3")
        mkphi(sphi, "F3", False)
        S.add("vector", lambda eng: eng.tensor_scalar_mul(tb[:, :], sphi[:, :], -1.0),
              reads=("F3",), writes=("F8",))
        S.add("vector", lambda eng: eng.tensor_max(tb[:, :], tb[:, :], sphi[:, :]),
              reads=("F8", "F3"), writes=("F8",))
        S.add("scalar", lambda eng: eng.activation(cphi[:, :], tb[:, :], AF.Sin,
                                                   bias=float(np.pi / 2), scale=-1.0),
              reads=("F8", "fconsts"), writes=("F4",), tag="cph3")
        S.add("scalar", lambda eng: eng.activation(sphi[:, :], sphi[:, :], AF.Sin),
              reads=("F3", "fconsts"), writes=("F3",), tag="sph3")
        S.add("vector", lambda eng: eng.tensor_tensor(ta[:, :], p.bankpair(0)[:, :],
                                                      cq[:, :], ALU.mult),
              reads=(bkey(0), bkey(1), "F5"), writes=("F7",), tag="rr1")
        S.add("vector", lambda eng: eng.tensor_tensor(tb[:, :], p.bankpair(1)[:, :],
                                                      sq[:, :], ALU.mult),
              reads=(bkey(2), bkey(3), "F6"), writes=("F8",), tag="rr2")
        S.add("vector", lambda eng: eng.tensor_add(ta[:, :], ta[:, :], tb[:, :]),
              reads=("F7", "F8"), writes=("F7",))
        S.add("vector", lambda eng: eng.tensor_mul(ta[:, :], ta[:, :], invs[:, :]),
              reads=("F7", "F2"), writes=("F7",))
        S.add("vector", lambda eng: eng.tensor_tensor(tb[:, :], p.bankpair(1)[:, :],
                                                      cq[:, :], ALU.mult),
              reads=(bkey(2), bkey(3), "F5"), writes=("F8",), tag="ri1")
        S.add("vector", lambda eng: eng.tensor_tensor(cq[:, :], p.bankpair(0)[:, :],
                                                      sq[:, :], ALU.mult),
              reads=(bkey(0), bkey(1), "F6"), writes=("F5",), tag="ri2")
        S.add("vector", lambda eng: eng.tensor_sub(tb[:, :], tb[:, :], cq[:, :]),
              reads=("F8", "F5"), writes=("F8",))
        S.add("vector", lambda eng: eng.tensor_mul(tb[:, :], tb[:, :], invs[:, :]),
              reads=("F8", "F2"), writes=("F8",))
        st4 = p.S[4]
        st5 = p.S[5]
        pieces = []
        c1 = cq
        S.add("vector", lambda eng: eng.tensor_tensor(c1[:, :], xf[:, :], cphi[:, :], ALU.mult),
              reads=("F0", "F4"), writes=("F5",), tag="c1")
        c2 = sq
        S.add("vector", lambda eng: eng.tensor_tensor(c2[:, :], xf[:, :], sphi[:, :], ALU.mult),
              reads=("F0", "F3"), writes=("F6",), tag="c2")
        pieces = [(c1, "F5"), (c2, "F6"), (ta, "F7"), (tb, "F8")]
        for pi, (pc, pk) in enumerate(pieces):
            S.add("vector", lambda eng, pc=pc, pi=pi: eng.tensor_reduce(
                st4[:, pi:pi + 1], pc[:, :], AX.X, ALU.add),
                reads=(pk,), writes=("S4",), tag="csum")
            S.add("scalar", lambda eng, pc=pc, pi=pi: eng.activation(
                p.B[3][:, :], pc[:, :], AF.Square, accum_out=st5[:, pi:pi + 1]),
                reads=(pk,), writes=("B3", "S5"), tag="csq")
        S.add("vector", lambda eng: eng.tensor_reduce(
            st4[:, 4:5], st4[:, 0:4], AX.X, ALU.add),
            reads=("S4",), writes=("S4",))
        S.add("vector", lambda eng: eng.tensor_reduce(
            st5[:, 4:5], st5[:, 0:4], AX.X, ALU.add),
            reads=("S5",), writes=("S5",))
        S.add("vector", lambda eng: eng.tensor_scalar_mul(st4[:, 5:6], st4[:, 4:5], 1.0 / (4 * D)),
              reads=("S4",), writes=("S4",))
        S.add("vector", lambda eng: eng.tensor_tensor(st4[:, 6:7], st4[:, 5:6], st4[:, 5:6], ALU.mult),
              reads=("S4",), writes=("S4",))
        S.add("vector", lambda eng: eng.scalar_tensor_tensor(
            st5[:, 5:6], st5[:, 4:5], 1.0 / (4 * D), st4[:, 6:7], ALU.mult, ALU.subtract),
            reads=("S5", "S4"), writes=("S5",))
        p.fconst(1e-5)
        S.add("scalar", lambda eng: eng.activation(st5[:, 6:7], st5[:, 5:6], AF.Ln, bias=1e-5),
              reads=("S5", "fconsts"), writes=("S5",))
        S.add("scalar", lambda eng: eng.activation(st5[:, 6:7], st5[:, 6:7], AF.Exp, scale=-0.5),
              reads=("S5", "fconsts"), writes=("S5",))
        S.add("vector", lambda eng: eng.tensor_tensor(st4[:, 7:8], st4[:, 5:6], st5[:, 6:7], ALU.mult),
              reads=("S4", "S5"), writes=("S4",))
        S.add("vector", lambda eng: eng.tensor_scalar_mul(st4[:, 7:8], st4[:, 7:8], -1.0),
              reads=("S4",), writes=("S4",))
        ctx = [(p.B[0], "B0"), (p.B[1], "B1"), (p.B[2], "B2"), (p.B[3], "B3")]
        for pi, (pc, pk) in enumerate(pieces):
            S.add("scalar", lambda eng, pc=pc, pi=pi: eng.activation(
                ctx[pi][0][:, :], pc[:, :], AF.Identity,
                bias=st4[:, 7:8], scale=st5[:, 6:7]),
                reads=(pk, "S4", "S5"), writes=(ctx[pi][1],), tag="lnap")
        for fb in range(NT):
            pc, pk = ctx[fb // 8]
            sl = fb % 8
            stgb = p.B[4].ap()[:, (fb % 8) * 128:(fb % 8) * 128 + 128]
            stgk = "B4s%d" % (fb % 8)
            bi = fb % 2
            def trf(eng, pc=pc, sl=sl, bi=bi):
                return eng.transpose(p.bank_bf(bi), pc[:, sl * 128:sl * 128 + 128],
                                     p.ident_sb[:, :])
            S.add("tensor", trf, reads=(pk, "ident"), writes=(bkey(bi),), tag="w1t")
            def tre(eng, stgb=stgb, bi=bi):
                return eng.tensor_copy(stgb, p.bank_bf(bi))
            S.add("vector", tre, reads=(bkey(bi),), writes=("B4", stgk), tag="w1e")
            wv, wk2 = w1blk(fb)
            for oc in range(4):
                def w1m(eng, stgb=stgb, wv=wv, oc=oc, fb=fb):
                    return eng.matmul(
                        p.bank(4 + oc), stgb,
                        wv[:, oc * 512: oc * 512 + 512],
                        start=(fb == 0), stop=False)
                kk = bkey(4 + oc)
                wkey = (kk + "@acc", kk) if fb == 0 else (kk + "@acc",)
                S.add("tensor", w1m, reads=(stgk, wk2), writes=wkey, tag="w1m")
        for oc in range(4):
            def b1m(eng, oc=oc):
                return eng.matmul(p.bank(4 + oc), p.ones1_sb[0:1, :],
                                  b1r[oc // 2][0:1, (oc % 2) * 512:(oc % 2) * 512 + 512],
                                  start=False, stop=True)
            S.add("tensor", b1m, reads=("ones1", "R%d" % (oc // 2)),
                  writes=(bkey(4 + oc),), tag="b1m")
        for och in range(2):
            def gel(eng, och=och):
                return eng.activation(p.B[5][:, :], p.bankpair(2 + och)[:, :], AF.Gelu)
            S.add("scalar", gel, reads=(bkey(4 + 2 * och), bkey(5 + 2 * och), "fconsts"),
                  writes=("B5",), tag="gelu")
            for f in range(NF):
                bi = f % 2
                def htf(eng, f=f, bi=bi):
                    return eng.transpose(p.bank_bf(bi), p.B[5][:, bass.ts(f, 128)],
                                         p.ident_sb[:, :])
                S.add("tensor", htf, reads=("B5", "ident"), writes=(bkey(bi),), tag="hT")
                def hte(eng, f=f, bi=bi):
                    return eng.tensor_copy(p.B[4][:, f * 128:f * 128 + 128], p.bank_bf(bi))
                S.add("vector", hte, reads=(bkey(bi),), writes=("B4", "B4s%d" % f), tag="hTe")
            dst = p.dram["l3_hT"].rearrange("(f p) t -> p f t", p=128)[
                :, och * 8:och * 8 + 8, bass.ts(i, 128)]
            S.dma("sdma", dst, p.B[4].ap().rearrange("p (f c) -> p f c", c=128),
                  reads=tuple("B4s%d" % f for f in range(NF)) + ("B4",),
                  writes=("l3_hT",), tag="sthT")


def emit_l3c(p):
    S = p.sched
    w2, w2k = big1(p, 0)
    nf2 = 16
    src = p.dram["l3_W2T"].rearrange("(f p) o -> p f o", p=128)
    S.dma("gdma", p.B1[:, 0:16384].rearrange("p (f o) -> p f o", f=nf2), src,
          reads=("l3_W2T",), writes=("BIG1",), tag="lw2")
    ws_, wsk = big1(p, 2)
    load_weight_bf16(p, S, "l3f_wT", ws_, wsk, D, D)
    for f in range(NF):
        blk = ws_[:, f * D:(f + 1) * D]
        S.add("scalar", lambda eng, blk=blk: eng.activation(blk, blk, AF.Sign),
              reads=(wsk,), writes=(wsk,), tag="sgn3")
    b2r = p.F[6]
    S.dma("sdma", b2r[:, :], p.dram["l3_b2"][:, :], reads=("l3_b2",), writes=("F6",))
    ht = [(p.B[2], "B2"), (p.B[3], "B3")]
    xt = [(p.B[4], "B4"), (p.B[5], "B5")]
    for i in range(NT):
        b = i % 2
        src2 = p.dram["l3_hT"].rearrange("(f p) t -> p f t", p=128)[:, :, bass.ts(i, 128)]
        S.dma("sdma", ht[b][0].ap().rearrange("p (f c) -> p f c", f=NF), src2[:, 0:8, :],
              reads=("l3_hT",), writes=(ht[b][1],), tag="lhT")
        S.dma("sdma", xt[b][0].ap().rearrange("p (f c) -> p f c", f=NF), src2[:, 8:16, :],
              reads=("l3_hT",), writes=(xt[b][1],), tag="lhT2")
        for oc in range(2):
            blocks = [(ht[b][0][:, bass.ts(f, 128)], ht[b][1]) for f in range(NF)] + \
                     [(xt[b][0][:, bass.ts(f, 128)], xt[b][1]) for f in range(NF)]
            mm_acc(p, S, "3w2%d" % oc, p.bank(oc), bkey(oc),
                   blocks,
                   [(p.B1[:, f * 1024 + oc * 512: f * 1024 + oc * 512 + 512], "BIG1")
                    for f in range(nf2)])
        sf = p.F[0]
        S.add("vector", lambda eng: eng.tensor_tensor(sf[:, :], p.bankpair(0)[:, :],
                                                      b2r[:, :], ALU.add),
              reads=(bkey(0), bkey(1), "F6"), writes=("F0",), tag="s3ev")
        xnf = p.F[1]
        S.dma("sdma", xnf[:, :], p.dram["xn3"][bass.ts(i, 128), :],
              reads=("xn3",), writes=("F1",), tag="lx3c")
        S.add("vector", lambda eng: eng.tensor_add(sf[:, :], sf[:, :], xnf[:, :]),
              reads=("F0", "F1"), writes=("F0",))
        xT = [(p.B[0], "B0"), (p.B[1], "B1")][b]
        src3 = p.dram["xn3T"].rearrange("(f p) t -> p f t", p=128)[:, :, bass.ts(i, 128)]
        S.dma("sdma", xT[0].ap().rearrange("p (f c) -> p f c", f=NF), src3,
              reads=("xn3T",), writes=(xT[1],), tag="lx3T")
        for oc in range(2):
            mm_acc(p, S, "3sg%d" % oc, p.bank(2 + oc), bkey(2 + oc),
                   [(xT[0][:, bass.ts(f, 128)], xT[1]) for f in range(NF)],
                   [(ws_[:, f * D + oc * 512: f * D + oc * 512 + 512], wsk)
                    for f in range(NF)])
        gf = p.F[2]
        S.add("scalar", lambda eng: eng.activation(gf[:, :], p.bankpair(1)[:, :], AF.Silu),
              reads=(bkey(2), bkey(3), "fconsts"), writes=("F2",), tag="g3sil")
        x3t = p.F[3]
        S.dma("sdma", x3t[:, :], p.dram["x3"][bass.ts(i, 128), :],
              reads=("x3",), writes=("F3",), tag="lx3e")
        S.add("vector", lambda eng: eng.tensor_mul(sf[:, :], sf[:, :], gf[:, :]),
              reads=("F0", "F2"), writes=("F0",))
        S.add("vector", lambda eng: eng.tensor_add(sf[:, :], sf[:, :], x3t[:, :]),
              reads=("F0", "F3"), writes=("F0",))
        S.dma("sdma", p.dram["y"][bass.ts(i, 128), :], sf[:, :],
              reads=("F0",), writes=("y",), tag="sty")


def emit_l2khat(p, dbg_nocc=False):
    S = p.sched
    p.fconst(float(np.pi / 2))
    ldt = p.S[0][:, 0:32]
    lar = p.S[1][:, 0:32]
    aim = p.S[2][:, 0:32]
    Cr = p.S[3][:, 0:32]
    Ci = p.S[4][:, 0:32]
    for nm, dst, dk in (("l2_ldt", ldt, "S0"), ("l2_lar", lar, "S1"),
                        ("l2_aim", aim, "S2"), ("l2_Cr", Cr, "S3"), ("l2_Ci", Ci, "S4")):
        S.dma("sdma", dst, p.dram[nm][:, :], reads=(nm,), writes=(dk,), tag="lmod")
    t1 = p.S[5][:, 0:32]
    t2 = p.S[5][:, 32:64]
    t3 = p.S[5][:, 64:96]
    lr = p.F[7][:, 0:32]
    li = p.F[7][:, 32:64]
    cfr = p.F[7][:, 64:96]
    cfi = p.F[7][:, 96:128]
    S.add("scalar", lambda eng: eng.activation(ldt, ldt, AF.Exp), reads=("S0", "fconsts"), writes=("S0",))
    S.add("scalar", lambda eng: eng.activation(lar, lar, AF.Exp), reads=("S1", "fconsts"), writes=("S1",))
    S.add("vector", lambda eng: eng.tensor_scalar_mul(lar, lar, -1.0), reads=("S1",), writes=("S1",))
    S.add("vector", lambda eng: eng.tensor_tensor(t1, lar, ldt, ALU.mult), reads=("S0", "S1"), writes=("S5",))
    S.add("vector", lambda eng: eng.tensor_tensor(t2, aim, ldt, ALU.mult), reads=("S0", "S2"), writes=("S5",))
    S.add("scalar", lambda eng: eng.activation(t1, t1, AF.Exp), reads=("S5", "fconsts"), writes=("S5",))
    S.add("vector", lambda eng: eng.tensor_scalar(t3, t2, 1.0 / TWO_PI, MAGIC, ALU.mult, ALU.add),
          reads=("S5",), writes=("S5",))
    S.add("vector", lambda eng: eng.tensor_scalar(t3, t3, MAGIC, None, ALU.subtract, ALU.bypass),
          reads=("S5",), writes=("S5",))
    S.add("vector", lambda eng: eng.scalar_tensor_tensor(t2, t3, -TWO_PI, t2, ALU.mult, ALU.add),
          reads=("S5",), writes=("S5",))
    S.add("vector", lambda eng: eng.tensor_scalar(t2, t2, PI_C, -PI_C, ALU.min, ALU.max),
          reads=("S5",), writes=("S5",))
    S.add("scalar", lambda eng: eng.activation(li, t2, AF.Sin), reads=("S5", "fconsts"), writes=("F7",))
    S.add("vector", lambda eng: eng.tensor_scalar_mul(t3, t2, -1.0), reads=("S5",), writes=("S5",))
    S.add("vector", lambda eng: eng.tensor_max(t3, t3, t2), reads=("S5",), writes=("S5",))
    S.add("scalar", lambda eng: eng.activation(lr, t3, AF.Sin, bias=float(np.pi / 2), scale=-1.0),
          reads=("S5", "fconsts"), writes=("F7",))
    S.add("vector", lambda eng: eng.tensor_tensor(lr, lr, t1, ALU.mult), reads=("F7", "S5"), writes=("F7",))
    S.add("vector", lambda eng: eng.tensor_tensor(li, li, t1, ALU.mult), reads=("F7", "S5"), writes=("F7",))
    aa = p.F[6][:, 0:32]
    nr = p.F[6][:, 32:64]
    ni = p.F[6][:, 64:96]
    S.add("vector", lambda eng: eng.tensor_tensor(aa, lar, lar, ALU.mult), reads=("S1",), writes=("F6",))
    S.add("vector", lambda eng: eng.scalar_tensor_tensor(t1, aim, 1.0, aim, ALU.mult, ALU.mult),
          reads=("S2",), writes=("S5",))
    S.add("vector", lambda eng: eng.tensor_add(aa, aa, t1), reads=("F6", "S5"), writes=("F6",))
    S.add("vector", lambda eng: eng.tensor_scalar(t2, lr, 1.0, None, ALU.subtract, ALU.bypass),
          reads=("F7",), writes=("S5",))
    S.add("vector", lambda eng: eng.tensor_tensor(nr, t2, lar, ALU.mult), reads=("S5", "S1"), writes=("F6",))
    S.add("vector", lambda eng: eng.scalar_tensor_tensor(t3, li, 1.0, aim, ALU.mult, ALU.mult),
          reads=("F7", "S2"), writes=("S5",))
    S.add("vector", lambda eng: eng.tensor_add(nr, nr, t3), reads=("F6", "S5"), writes=("F6",))
    S.add("vector", lambda eng: eng.tensor_tensor(ni, li, lar, ALU.mult), reads=("F7", "S1"), writes=("F6",))
    S.add("vector", lambda eng: eng.tensor_tensor(t3, t2, aim, ALU.mult), reads=("S5", "S2"), writes=("S5",))
    S.add("vector", lambda eng: eng.tensor_sub(ni, ni, t3), reads=("F6", "S5"), writes=("F6",))
    S.add("vector", lambda eng: eng.reciprocal(aa, aa), reads=("F6",), writes=("F6",))
    S.add("vector", lambda eng: eng.tensor_tensor(nr, nr, aa, ALU.mult), reads=("F6",), writes=("F6",))
    S.add("vector", lambda eng: eng.tensor_tensor(ni, ni, aa, ALU.mult), reads=("F6",), writes=("F6",))
    S.add("vector", lambda eng: eng.tensor_tensor(cfr, Cr, nr, ALU.mult), reads=("S3", "F6"), writes=("F7",))
    S.add("vector", lambda eng: eng.tensor_tensor(t3, Ci, ni, ALU.mult), reads=("S4", "F6"), writes=("S5",))
    S.add("vector", lambda eng: eng.tensor_sub(cfr, cfr, t3), reads=("F7", "S5"), writes=("F7",))
    S.add("vector", lambda eng: eng.tensor_scalar_mul(cfr, cfr, 2.0), reads=("F7",), writes=("F7",))
    S.add("vector", lambda eng: eng.tensor_tensor(cfi, Cr, ni, ALU.mult), reads=("S3", "F6"), writes=("F7",))
    S.add("vector", lambda eng: eng.tensor_tensor(t3, Ci, nr, ALU.mult), reads=("S4", "F6"), writes=("S5",))
    S.add("vector", lambda eng: eng.tensor_add(cfi, cfi, t3), reads=("F7", "S5"), writes=("F7",))
    S.add("vector", lambda eng: eng.tensor_scalar_mul(cfi, cfi, -2.0), reads=("F7",), writes=("F7",))
    Br = p.F[0]
    Bi = p.F[1]
    Ar = p.F[2]
    Ai = p.F[3]
    sr = p.S[5][:, 96:97]
    si = p.S[5][:, 97:98]
    tmpa = p.S[5][:, 98:99]
    tmpb = p.S[5][:, 99:100]
    kst = p.B[0].ap().bitcast(F32)[0:4, 0:512]
    for t in range(NT):
        def powtab(dstR, dstRk, dstI, dstIk, base_r, base_i, rdk):
            S.add("vector", lambda eng: eng.memset(dstR[:, t * 32:t * 32 + 1] if False else dstR[:, 0:1], 1.0),
                  writes=(dstRk,))
            S.add("vector", lambda eng: eng.memset(dstI[:, 0:1], 0.0), writes=(dstIk,))
            S.add("vector", lambda eng: eng.tensor_copy(dstR[:, 1:2], base_r), reads=rdk, writes=(dstRk,))
            S.add("vector", lambda eng: eng.tensor_copy(dstI[:, 1:2], base_i), reads=rdk, writes=(dstIk,))
            S.add("vector", lambda eng: eng.tensor_copy(sr, base_r), reads=rdk, writes=("S5",))
            S.add("vector", lambda eng: eng.tensor_copy(si, base_i), reads=rdk, writes=("S5",))
            n = 2
            while n < 64:
                S.add("vector", lambda eng: eng.tensor_tensor(tmpa, sr, sr, ALU.mult), reads=("S5",), writes=("S5",))
                S.add("vector", lambda eng: eng.tensor_tensor(tmpb, si, si, ALU.mult), reads=("S5",), writes=("S5",))
                S.add("vector", lambda eng: eng.tensor_tensor(si, sr, si, ALU.mult), reads=("S5",), writes=("S5",))
                S.add("vector", lambda eng: eng.tensor_scalar_mul(si, si, 2.0), reads=("S5",), writes=("S5",))
                S.add("vector", lambda eng: eng.tensor_sub(sr, tmpa, tmpb), reads=("S5",), writes=("S5",))
                nn = n
                S.add("vector", lambda eng, nn=nn: eng.tensor_scalar(
                    dstR[:, nn:2 * nn], dstR[:, 0:nn], sr, None, ALU.mult, ALU.bypass),
                    reads=(dstRk, "S5"), writes=(dstRk,))
                S.add("vector", lambda eng, nn=nn: eng.tensor_scalar(
                    tmpa if False else p.F[4][:, 0:nn], dstI[:, 0:nn], si, None, ALU.mult, ALU.bypass),
                    reads=(dstIk, "S5"), writes=("F4",))
                S.add("vector", lambda eng, nn=nn: eng.tensor_sub(
                    dstR[:, nn:2 * nn], dstR[:, nn:2 * nn], p.F[4][:, 0:nn]),
                    reads=(dstRk, "F4"), writes=(dstRk,))
                S.add("vector", lambda eng, nn=nn: eng.tensor_scalar(
                    dstI[:, nn:2 * nn], dstI[:, 0:nn], sr, None, ALU.mult, ALU.bypass),
                    reads=(dstIk, "S5"), writes=(dstIk,))
                S.add("vector", lambda eng, nn=nn: eng.tensor_scalar(
                    p.F[4][:, 0:nn], dstR[:, 0:nn], si, None, ALU.mult, ALU.bypass),
                    reads=(dstRk, "S5"), writes=("F4",))
                S.add("vector", lambda eng, nn=nn: eng.tensor_add(
                    dstI[:, nn:2 * nn], dstI[:, nn:2 * nn], p.F[4][:, 0:nn]),
                    reads=(dstIk, "F4"), writes=(dstIk,))
                n *= 2
        powtab(Br, "F0", Bi, "F1", lr[:, t:t + 1], li[:, t:t + 1], ("F7",))
        S.add("vector", lambda eng: eng.tensor_tensor(tmpa, sr, sr, ALU.mult), reads=("S5",), writes=("S5",))
        S.add("vector", lambda eng: eng.tensor_tensor(tmpb, si, si, ALU.mult), reads=("S5",), writes=("S5",))
        S.add("vector", lambda eng: eng.tensor_tensor(si, sr, si, ALU.mult), reads=("S5",), writes=("S5",))
        S.add("vector", lambda eng: eng.tensor_scalar_mul(si, si, 2.0), reads=("S5",), writes=("S5",))
        S.add("vector", lambda eng: eng.tensor_sub(sr, tmpa, tmpb), reads=("S5",), writes=("S5",))
        l64r = p.S[5][:, 100:101]
        l64i = p.S[5][:, 101:102]
        S.add("vector", lambda eng: eng.tensor_copy(l64r, sr), reads=("S5",), writes=("S5",))
        S.add("vector", lambda eng: eng.tensor_copy(l64i, si), reads=("S5",), writes=("S5",))
        powtab(Ar, "F2", Ai, "F3", l64r, l64i, ("S5",))
        for pl, col in ((0, cfr), (1, cfi)):
            dstap = p.dram["l2_bd"].rearrange("(q p) c -> p q c", p=32)
            dst2 = p.dram["l2_bd"][:, :].rearrange("(q r) c -> q r c", q=4)
            def sc(eng, pl=pl, col=col, t=t, dst2=dst2):
                return eng.dma_start(out=dst2[:, :, 4 * pl:4 * pl + 4].rearrange(
                    "q r c -> q r c")[:, :, 0:1] if False else
                    p.dram["l2_bd"].rearrange("(q r) c -> q r c", q=4)
                    .transpose([0, 2, 1])[:, 4 * pl:4 * pl + 1, :].squeeze(1)
                    .rearrange("q r -> (q r)").unsqueeze(0) if False else None, in_=None)
            for q in range(4):
                def sc2(eng, pl=pl, col=col, t=t, q=q):
                    return eng.dma_start(
                        out=p.dram["l2_bd"][32 * q:32 * q + 32, 4 * pl + q:4 * pl + q + 1],
                        in_=col[32 * q:32 * q + 32, t:t + 1],
                        allow_slow_non_contiguous=True)
                S.add("sdma", sc2, reads=("F7",), writes=("l2_bd",), tag="bdsc")
        S.dma("sdma", p.B[1].ap().bitcast(F32)[:, 0:8], p.dram["l2_bd"][:, :],
              reads=("l2_bd",), writes=("B1",), tag="bdld")
        bdb = p.B[2].ap()[:, 0:8]
        S.add("vector", lambda eng: eng.tensor_copy(bdb, p.B[1].ap().bitcast(F32)[:, 0:8]),
              reads=("B1",), writes=("B2",), tag="bdc")
        for lc in range(8):
            aof = lc * 8
            Ab = Ar.ap()[:, aof:aof + 8].unsqueeze(2).to_broadcast([128, 8, 64])
            Aib = Ai.ap()[:, aof:aof + 8].unsqueeze(2).to_broadcast([128, 8, 64])
            Bb = Br.ap()[:, 0:64].unsqueeze(1).to_broadcast([128, 8, 64])
            Bib = Bi.ap()[:, 0:64].unsqueeze(1).to_broadcast([128, 8, 64])
            L1v = p.F[4].ap().rearrange("p (a b) -> p a b", a=8)[:, :, 0:64]
            L2v = p.F[5].ap().rearrange("p (a b) -> p a b", a=8)[:, :, 0:64]
            lrb = p.B[3].ap()[:, 0:512].rearrange("p (a b) -> p a b", a=8)
            lib = p.B[4].ap()[:, 0:512].rearrange("p (a b) -> p a b", a=8)
            S.add("vector", lambda eng, Ab=Ab, Bb=Bb, L1v=L1v: eng.tensor_tensor(L1v, Ab, Bb, ALU.mult),
                  reads=("F2", "F0"), writes=("F4",), tag="lx1")
            S.add("vector", lambda eng, Aib=Aib, Bib=Bib, L2v=L2v: eng.tensor_tensor(L2v, Aib, Bib, ALU.mult),
                  reads=("F3", "F1"), writes=("F5",), tag="lx2")
            S.add("vector", lambda eng, L1v=L1v, L2v=L2v, lrb=lrb: eng.tensor_tensor(lrb, L1v, L2v, ALU.subtract),
                  reads=("F4", "F5"), writes=("B3",), tag="lx3")
            S.add("vector", lambda eng, Ab=Ab, Bib=Bib, L1v=L1v: eng.tensor_tensor(L1v, Ab, Bib, ALU.mult),
                  reads=("F2", "F1"), writes=("F4",), tag="lx4")
            S.add("vector", lambda eng, Aib=Aib, Bb=Bb, L2v=L2v: eng.tensor_tensor(L2v, Aib, Bb, ALU.mult),
                  reads=("F3", "F0"), writes=("F5",), tag="lx5")
            S.add("vector", lambda eng, L1v=L1v, L2v=L2v, lib=lib: eng.tensor_tensor(lib, L1v, L2v, ALU.add),
                  reads=("F4", "F5"), writes=("B4",), tag="lx6")
            def km1(eng, lc=lc):
                return eng.matmul(p.bank(0)[0:4, :], bdb[:, 0:4], p.B[3][:, 0:512],
                                  start=True, stop=False)
            S.add("tensor", km1, reads=("B2", "B3"), writes=(bkey(0) + "@acc", bkey(0)), tag="km1")
            def km2(eng, lc=lc):
                return eng.matmul(p.bank(0)[0:4, :], bdb[:, 4:8], p.B[4][:, 0:512],
                                  start=False, stop=True)
            S.add("tensor", km2, reads=("B2", "B4"), writes=(bkey(0),), tag="km2")
            def kev(eng, lc=lc):
                return eng.tensor_copy(p.B[5].ap().bitcast(F32)[0:4, lc * 512:(lc + 1) * 512]
                                       if False else p.B1.ap().bitcast(F32)[0:4, lc * 512:(lc + 1) * 512],
                                       p.bank(0)[0:4, :])
            S.add("vector", kev, reads=(bkey(0),), writes=("BIG1",), tag="kev")
        def kst2(eng, t=t):
            return eng.dma_start(out=p.dram["l2_kt"][4 * t:4 * t + 4, :],
                                 in_=p.B1.ap().bitcast(F32)[0:4, 0:4096])
        S.add("sdma", kst2, reads=("BIG1",), writes=("l2_kt",), tag="kstore")


def emit_l2inproj(p):
    S = p.sched
    wi, wik = big1(p, 0)
    load_weight_bf16(p, S, "l2_inWT", wi, wik, D, D)
    inb = p.S[0][:, 0:8]
    S.dma("sdma", inb, p.dram["l2_inb"][:, :], reads=("l2_inb",), writes=("S0",))
    xt = [(p.B[2], "B2"), (p.B[3], "B3")]
    ev = [(p.F[0], "F0"), (p.F[1], "F1")]
    for i in range(NT):
        b = i % 2
        src = p.dram["xn2T"].rearrange("(f p) t -> p f t", p=128)[:, :, bass.ts(i, 128)]
        S.dma("sdma", xt[b][0].ap().rearrange("p (f c) -> p f c", f=NF), src,
              reads=("xn2T",), writes=(xt[b][1],), tag="l2lx")
        for ot in range(NF):
            bk = ot // 4
            reg = p.bank(bk)[:, (ot % 4) * 128:(ot % 4) * 128 + 128]
            for f in range(NF):
                def ipm(eng, reg=reg, f=f, ot=ot, b=b):
                    return eng.matmul(reg,
                                      wi[:, f * D + ot * 128: f * D + ot * 128 + 128],
                                      xt[b][0][:, bass.ts(f, 128)],
                                      start=(ot % 4 == 0 and f == 0), stop=(ot % 4 == 3 and f == NF - 1))
                kk = bkey(bk)
                wkey = (kk + "@acc", kk) if (ot % 4 == 0 and f == 0) else                     ((kk,) if (ot % 4 == 3 and f == NF - 1) else (kk + "@acc",))
                S.add("tensor", ipm, reads=(wik, xt[b][1]), writes=wkey, tag="ipm")
        for bk in range(2):
            def uev(eng, bk=bk, b=b):
                srcv = p.bank(bk).rearrange("p (o c) -> p o c", o=4)
                dstv = ev[b][0].ap()[:, bk * 512:bk * 512 + 512].rearrange("p (o c) -> p o c", o=4)
                bb2 = inb[:, bk * 4:bk * 4 + 4].unsqueeze(2).to_broadcast([128, 4, 128])
                return eng.tensor_tensor(dstv, srcv, bb2, ALU.add)
            S.add("vector", uev, reads=(bkey(bk), "S0"), writes=(ev[b][1],), tag="uev")
        dst = p.dram["u32"].rearrange("(o p) t -> p o t", p=128)[:, :, bass.ts(i, 128)]
        S.dma("sdma", dst, ev[b][0].ap().rearrange("p (o c) -> p o c", o=NF),
              reads=(ev[b][1],), writes=("u32",), tag="ust")


def emit_l2fft(p, dbg_nocc=False, part="all"):
    S = p.sched
    nc = p.nc
    kdst = "l2_kout" if part in ("fused", "fusedk") else "l2_kin"
    S.add("vector", lambda eng: eng.engine_nop(), reads=("BIG1",), writes=("BIG1",),
          tag="fcbar")
    cF2r = p.B1[0:64, 0:128]
    cF2i = p.B1[0:64, 128:256]
    cF1r = p.B1[:, 256:322]
    cF1i = p.B1[:, 322:388]
    cF1n = p.B1[:, 388:454]
    cE1r = p.B1[0:66, 454:582]
    cE1i = p.B1[0:66, 582:710]
    cE1n2 = p.B1[0:66, 838:966]
    cE2r = p.B1[:, 710:774]
    cE2n = p.B1[:, 774:838]
    for nm, dst in (("fF2r", cF2r), ("fF2i", cF2i), ("fF1r", cF1r), ("fF1i", cF1i),
                    ("fF1n", cF1n), ("fE1r", cE1r), ("fE1i", cE1i), ("fE2r", cE2r),
                    ("fE2n", cE2n), ("fE1n2", cE1n2)):
        S.dma("sdma", dst, p.dram[nm][:, :], reads=(nm, "BIG1"), writes=("FC",), tag="lfc")
    tWr = p.B1.ap().bitcast(F32)[:, 512:1024]
    tWi = p.B1.ap().bitcast(F32)[:, 1024:1536]
    tVr = p.B1.ap().bitcast(F32)[:, 1536:2048]
    tVi = p.B1.ap().bitcast(F32)[:, 2048:2560]
    for nm, dst in (("ftWr", tWr), ("ftWi", tWi), ("ftVr", tVr), ("ftVi", tVi)):
        S.dma("sdma", dst, p.dram[nm][:, :], reads=(nm, "BIG1"), writes=("FC",), tag="lft")

    GP = 4
    Gp = [(p.B[2], "B2"), (p.B[3], "B3")]
    Hp = [(p.B[4], "B4")]
    Ap_ = [(p.B[5], "B5")]
    kh = p.B1.ap().bitcast(F32)[0:66, 4096:4608]
    ys = p.B1.ap().bitcast(F32)[0:64, 4608:5120]

    def fwd_batch(src_name, ch0, nch, out_khat):
        for gi in range(GP):
            c0 = ch0 + 2 * gi
            srcap = p.dram[src_name].rearrange("c (m2 m1) -> c m2 m1", m1=64)
            sl = srcap[c0:c0 + 2]
            sl2 = sl.transpose([1, 0, 2])
            def ld(eng, sl2=sl2, gi=gi):
                return eng.dma_start(out=p.B[0].ap()[0:64, gi * 128:(gi + 1) * 128]
                                     .rearrange("p (c m) -> p c m", c=2), in_=sl2)
            S.add("gdma", ld, reads=(src_name,), writes=("B0",), tag="fld")
        for gi in range(GP):
            bk = gi // 2
            first = (gi % 2 == 0)
            lastb = (gi % 2 == 1)
            def s1r(eng, gi=gi):
                return eng.matmul(p.pool[0][:, gi * 256:gi * 256 + 128],
                                  p.B[0][0:64, gi * 128:(gi + 1) * 128], cF2r,
                                  start=(gi % 2 == 0), stop=False)
            S.add("tensor", s1r, reads=("B0", "BIG1"),
                  writes=(bkey(bk) + "@acc", bkey(bk)) if first else (bkey(bk) + "@acc",),
                  tag="s1r")
            def s1i(eng, gi=gi):
                return eng.matmul(p.pool[0][:, gi * 256 + 128:gi * 256 + 256],
                                  p.B[0][0:64, gi * 128:(gi + 1) * 128], cF2i,
                                  start=False, stop=(gi % 2 == 1))
            S.add("tensor", s1i, reads=("B0", "BIG1"),
                  writes=(bkey(bk),) if lastb else (bkey(bk) + "@acc",), tag="s1i")
        ps4 = p.pool[0].ap().rearrange("p (g x) -> p g x", g=GP)
        Gr4 = ps4[:, :, 0:128]
        Gi4 = ps4[:, :, 128:256]
        W4r = tWr.rearrange("p (g x) -> p g x", g=GP)
        W4i = tWi.rearrange("p (g x) -> p g x", g=GP)
        g4r = Gp[0][0].ap().rearrange("p (g x) -> p g x", g=GP)[:, :, 0:128]
        g4i = Gp[0][0].ap().rearrange("p (g x) -> p g x", g=GP)[:, :, 128:256]
        scr1 = p.F[4].ap().rearrange("p (g x) -> p g x", g=GP)[:, :, 0:128]
        scr2 = p.F[5].ap().rearrange("p (g x) -> p g x", g=GP)[:, :, 0:128]
        S.add("vector", lambda eng: eng.tensor_tensor(scr1, Gr4, W4r, ALU.mult),
              reads=(bkey(0), bkey(1), "BIG1"), writes=("F4",), tag="tw1")
        S.add("vector", lambda eng: eng.tensor_tensor(scr2, Gi4, W4i, ALU.mult),
              reads=(bkey(0), bkey(1), "BIG1"), writes=("F5",), tag="tw2")
        S.add("vector", lambda eng: eng.tensor_tensor(g4r, scr1, scr2, ALU.subtract),
              reads=("F4", "F5"), writes=(Gp[0][1],), tag="tw3")
        S.add("vector", lambda eng: eng.tensor_tensor(scr1, Gr4, W4i, ALU.mult),
              reads=(bkey(0), bkey(1), "BIG1"), writes=("F4",), tag="tw4")
        S.add("vector", lambda eng: eng.tensor_tensor(scr2, Gi4, W4r, ALU.mult),
              reads=(bkey(0), bkey(1), "BIG1"), writes=("F5",), tag="tw5")
        S.add("vector", lambda eng: eng.tensor_tensor(g4i, scr1, scr2, ALU.add),
              reads=("F4", "F5"), writes=(Gp[0][1],), tag="tw6")
        for gi in range(GP):
            gr = Gp[0][0][:, gi * 256:gi * 256 + 128]
            gb = Gp[0][0][:, gi * 256 + 128:gi * 256 + 256]
            def x1(eng, gr=gr, gi2=gi):
                return eng.matmul(p.bank(2)[0:66, gi2 * 128:gi2 * 128 + 128], cF1r, gr,
                                  start=(gi2 == 0), stop=False)
            S.add("tensor", x1, reads=(Gp[0][1], "BIG1"),
                  writes=(bkey(2) + "@acc", bkey(2)) if gi == 0 else (bkey(2) + "@acc",), tag="x1")
            def x2(eng, gb=gb, gi2=gi):
                return eng.matmul(p.bank(2)[0:66, gi2 * 128:gi2 * 128 + 128], cF1n, gb,
                                  start=False, stop=(gi2 == GP - 1))
            S.add("tensor", x2, reads=(Gp[0][1], "BIG1"),
                  writes=(bkey(2),) if gi == GP - 1 else (bkey(2) + "@acc",), tag="x2")
            def x3(eng, gr=gr, gi2=gi):
                return eng.matmul(p.bank(3)[0:66, gi2 * 128:gi2 * 128 + 128], cF1i, gr,
                                  start=(gi2 == 0), stop=False)
            S.add("tensor", x3, reads=(Gp[0][1], "BIG1"),
                  writes=(bkey(3) + "@acc", bkey(3)) if gi == 0 else (bkey(3) + "@acc",), tag="x3")
            def x4(eng, gb=gb, gi2=gi):
                return eng.matmul(p.bank(3)[0:66, gi2 * 128:gi2 * 128 + 128], cF1r, gb,
                                  start=False, stop=(gi2 == GP - 1))
            S.add("tensor", x4, reads=(Gp[0][1], "BIG1"),
                  writes=(bkey(3),) if gi == GP - 1 else (bkey(3) + "@acc",), tag="x4")
        if out_khat:
            for gi in range(nch // 2):
                c0 = ch0 + 2 * gi
                def khe(eng, gi=gi):
                    eng.tensor_copy(kh[0:66, 0:128], p.bank(2)[0:66, gi * 128:gi * 128 + 128])
                    return eng.tensor_copy(kh[0:66, 128:256], p.bank(3)[0:66, gi * 128:gi * 128 + 128])
                S.add("vector", khe, reads=(bkey(2), bkey(3)), writes=("BIG1",), tag="khe")
                for ci in range(2):
                    dstc = p.dram[kdst][c0 + ci:c0 + ci + 1].rearrange(
                        "c (pl k1 k2) -> (c k1) pl k2", pl=2, k1=33)
                    def khs(eng, ci=ci, dstc=dstc):
                        srcv = kh[33 * ci:33 * ci + 33, 0:256].rearrange(
                            "p (pl k2) -> p pl k2", pl=2)
                        return eng.dma_start(out=dstc, in_=srcv)
                    S.add("sdma", khs, reads=("BIG1",), writes=(kdst,), tag="khs")
            return
        for gi in range(GP):
            c0 = ch0 + 2 * gi
            for ci in range(2):
                srckc = p.dram["l2_kout"][c0 + ci:c0 + ci + 1].rearrange(
                    "c (pl k1 k2) -> (c k1) pl k2", pl=2, k1=33)
                def kl(eng, ci=ci, srckc=srckc):
                    dstv = kh[33 * ci:33 * ci + 33, 0:256].rearrange(
                        "p (pl k2) -> p pl k2", pl=2)
                    return eng.dma_start(out=dstv, in_=srckc)
                S.add("adma", kl, reads=("l2_kout",), writes=("BIG1",), tag="kl")
            Xr = p.bank(2)[0:66, gi * 128:gi * 128 + 128]
            Xi = p.bank(3)[0:66, gi * 128:gi * 128 + 128]
            khr = kh[0:66, 0:128]
            khi = kh[0:66, 128:256]
            hr = Hp[0][0][0:66, gi * 256:gi * 256 + 128]
            hi = Hp[0][0][0:66, gi * 256 + 128:gi * 256 + 256]
            s1v = p.F[4][0:66, 0:128]
            s2v = p.F[5][0:66, 0:128]
            S.add("vector", lambda eng, Xr=Xr, khr=khr, s1v=s1v: eng.tensor_tensor(s1v, Xr, khr, ALU.mult),
                  reads=(bkey(2), "BIG1"), writes=("F4",), tag="pw1")
            S.add("vector", lambda eng, Xi=Xi, khi=khi, s2v=s2v: eng.tensor_tensor(s2v, Xi, khi, ALU.mult),
                  reads=(bkey(3), "BIG1"), writes=("F5",), tag="pw2")
            S.add("vector", lambda eng, hr=hr, s1v=s1v, s2v=s2v: eng.tensor_tensor(hr, s1v, s2v, ALU.subtract),
                  reads=("F4", "F5"), writes=(Hp[0][1],), tag="pw3")
            S.add("vector", lambda eng, Xr=Xr, khi=khi, s1v=s1v: eng.tensor_tensor(s1v, Xr, khi, ALU.mult),
                  reads=(bkey(2), "BIG1"), writes=("F4",), tag="pw4")
            S.add("vector", lambda eng, Xi=Xi, khr=khr, s2v=s2v: eng.tensor_tensor(s2v, Xi, khr, ALU.mult),
                  reads=(bkey(3), "BIG1"), writes=("F5",), tag="pw5")
            S.add("vector", lambda eng, hi=hi, s1v=s1v, s2v=s2v: eng.tensor_tensor(hi, s1v, s2v, ALU.add),
                  reads=("F4", "F5"), writes=(Hp[0][1],), tag="pw6")
        for gi in range(GP):
            hr = Hp[0][0][0:66, gi * 256:gi * 256 + 128]
            hi = Hp[0][0][0:66, gi * 256 + 128:gi * 256 + 256]
            def a1(eng, hr=hr, gi2=gi):
                return eng.matmul(p.bank(4)[:, gi2 * 128:gi2 * 128 + 128], hr, cE1r,
                                  start=(gi2 == 0), stop=False)
            S.add("tensor", a1, reads=(Hp[0][1], "BIG1"),
                  writes=(bkey(4) + "@acc", bkey(4)) if gi == 0 else (bkey(4) + "@acc",), tag="a1")
            def a2(eng, hi=hi, gi2=gi):
                return eng.matmul(p.bank(4)[:, gi2 * 128:gi2 * 128 + 128], hi, cE1n2,
                                  start=False, stop=(gi2 == GP - 1))
            S.add("tensor", a2, reads=(Hp[0][1], "BIG1"),
                  writes=(bkey(4),) if gi == GP - 1 else (bkey(4) + "@acc",), tag="a2")
            def a3(eng, hr=hr, gi2=gi):
                return eng.matmul(p.bank(5)[:, gi2 * 128:gi2 * 128 + 128], hr, cE1i,
                                  start=(gi2 == 0), stop=False)
            S.add("tensor", a3, reads=(Hp[0][1], "BIG1"),
                  writes=(bkey(5) + "@acc", bkey(5)) if gi == 0 else (bkey(5) + "@acc",), tag="a3")
            def a4(eng, hi=hi, gi2=gi):
                return eng.matmul(p.bank(5)[:, gi2 * 128:gi2 * 128 + 128], hi, cE1r,
                                  start=False, stop=(gi2 == GP - 1))
            S.add("tensor", a4, reads=(Hp[0][1], "BIG1"),
                  writes=(bkey(5),) if gi == GP - 1 else (bkey(5) + "@acc",), tag="a4")
        A4r = p.pool[2].ap().rearrange("p (g x) -> p g x", g=GP)[:, :, 0:128] if False else \
            p.bank(4).rearrange("p (g x) -> p g x", g=GP)
        A4i = p.bank(5).rearrange("p (g x) -> p g x", g=GP)
        V4r = tVr.rearrange("p (g x) -> p g x", g=GP)
        V4i = tVi.rearrange("p (g x) -> p g x", g=GP)
        ap4r = Ap_[0][0].ap().rearrange("p (g x) -> p g x", g=GP)[:, :, 0:128]
        ap4i = Ap_[0][0].ap().rearrange("p (g x) -> p g x", g=GP)[:, :, 128:256]
        scr1f = p.F[4].ap().rearrange("p (g x) -> p g x", g=GP)[:, :, 0:128]
        scr2f = p.F[5].ap().rearrange("p (g x) -> p g x", g=GP)[:, :, 0:128]
        S.add("vector", lambda eng: eng.tensor_tensor(scr1f, A4r, V4r, ALU.mult),
              reads=(bkey(4), "BIG1"), writes=("F4",), tag="iv1")
        S.add("vector", lambda eng: eng.tensor_tensor(scr2f, A4i, V4i, ALU.mult),
              reads=(bkey(5), "BIG1"), writes=("F5",), tag="iv2")
        S.add("vector", lambda eng: eng.tensor_tensor(ap4r, scr1f, scr2f, ALU.subtract),
              reads=("F4", "F5"), writes=(Ap_[0][1],), tag="iv3")
        S.add("vector", lambda eng: eng.tensor_tensor(scr1f, A4r, V4i, ALU.mult),
              reads=(bkey(4), "BIG1"), writes=("F4",), tag="iv4")
        S.add("vector", lambda eng: eng.tensor_tensor(scr2f, A4i, V4r, ALU.mult),
              reads=(bkey(5), "BIG1"), writes=("F5",), tag="iv5")
        S.add("vector", lambda eng: eng.tensor_tensor(ap4i, scr1f, scr2f, ALU.add),
              reads=("F4", "F5"), writes=(Ap_[0][1],), tag="iv6")
        for gi in range(GP):
            apr = Ap_[0][0][:, gi * 256:gi * 256 + 128]
            api = Ap_[0][0][:, gi * 256 + 128:gi * 256 + 256]
            def y1(eng, apr=apr, gi2=gi):
                return eng.matmul(p.bank(6)[0:64, gi2 * 128:gi2 * 128 + 128], cE2r, apr,
                                  start=(gi2 == 0), stop=False)
            S.add("tensor", y1, reads=(Ap_[0][1], "BIG1"),
                  writes=(bkey(6) + "@acc", bkey(6)) if gi == 0 else (bkey(6) + "@acc",), tag="y1")
            def y2(eng, api=api, gi2=gi):
                return eng.matmul(p.bank(6)[0:64, gi2 * 128:gi2 * 128 + 128], cE2n, api,
                                  start=False, stop=(gi2 == GP - 1))
            S.add("tensor", y2, reads=(Ap_[0][1], "BIG1"),
                  writes=(bkey(6),) if gi == GP - 1 else (bkey(6) + "@acc",), tag="y2")
        S.add("scalar", lambda eng: eng.copy(ys[0:64, :], p.bank(6)[0:64, :]),
              reads=(bkey(6),), writes=("BIG1",), tag="yev")
        for gi in range(GP):
            c0 = ch0 + 2 * gi
            dst = p.dram["yconv"][c0:c0 + 2].rearrange("c (m2 m1) -> m2 c m1", m1=64)
            def yst(eng, dst=dst, gi=gi):
                return eng.dma_start(out=dst, in_=ys[0:64, gi * 128:gi * 128 + 128]
                                     .rearrange("p (c m) -> p c m", c=2))
            S.add("sdma", yst, reads=("BIG1",), writes=("yconv",), tag="yst")

    if part == "u":
        S.add("vector", lambda eng: eng.engine_nop(), reads=(), writes=("l2_kout",))
        for bb in range(128):
            fwd_batch("u32", bb * 8, 8, False)
        return
    b1f = p.B1.ap().bitcast(F32)
    Bin = [(p.B[0], "B0"), (p.B[1], "B1")]
    Gpar = [(p.B[2].ap(), "B2"), (p.B[3].ap(), "B3")]
    Hpar = [(p.B[4].ap(), "B4"), (p.B2.ap()[:, 0:1024], "BIG2")]
    Apar = [(p.B[5].ap(), "B5"), (p.B2.ap()[:, 1024:2048], "BIG2")]
    kh4 = [b1f[0:66, 4096:5120], b1f[0:66, 5120:6144]]
    ys2 = [b1f[0:64, 6144:6656], b1f[0:64, 6656:7168]]

    def fld_st(i, src_name):
        par = i % 2
        ch0 = i * 8
        for gi in range(GP):
            c0 = ch0 + 2 * gi
            srcap = p.dram[src_name].rearrange("c (m2 m1) -> c m2 m1", m1=64)
            sl2 = srcap[c0:c0 + 2].transpose([1, 0, 2])
            def ld(eng, sl2=sl2, gi=gi, par=par):
                return eng.dma_start(out=Bin[par][0].ap()[0:64, gi * 128:(gi + 1) * 128]
                                     .rearrange("p (c m) -> p c m", c=2), in_=sl2)
            S.add("gdma", ld, reads=(src_name,), writes=(Bin[par][1],), tag="fld")

    def s1_st(i):
        par = i % 2
        for gi in range(GP):
            bk = gi // 2
            def s1r(eng, gi=gi, par=par):
                return eng.matmul(p.pool[0][:, gi * 256:gi * 256 + 128],
                                  Bin[par][0][0:64, gi * 128:(gi + 1) * 128], cF2r,
                                  start=(gi % 2 == 0), stop=False)
            S.add("tensor", s1r, reads=(Bin[par][1], "FC"),
                  writes=(bkey(bk) + "@acc", bkey(bk)) if gi % 2 == 0 else (bkey(bk) + "@acc",),
                  tag="s1r")
            def s1i(eng, gi=gi, par=par):
                return eng.matmul(p.pool[0][:, gi * 256 + 128:gi * 256 + 256],
                                  Bin[par][0][0:64, gi * 128:(gi + 1) * 128], cF2i,
                                  start=False, stop=(gi % 2 == 1))
            S.add("tensor", s1i, reads=(Bin[par][1], "FC"),
                  writes=(bkey(bk),) if gi % 2 == 1 else (bkey(bk) + "@acc",), tag="s1i")

    def tw_st(i):
        par = i % 2
        Ga, Gk = Gpar[par]
        ps4 = p.pool[0].ap().rearrange("p (g x) -> p g x", g=GP)
        Gr4 = ps4[:, :, 0:128]
        Gi4 = ps4[:, :, 128:256]
        W4r = tWr.rearrange("p (g x) -> p g x", g=GP)
        W4i = tWi.rearrange("p (g x) -> p g x", g=GP)
        g4r = Ga.rearrange("p (g x) -> p g x", g=GP)[:, :, 0:128]
        g4i = Ga.rearrange("p (g x) -> p g x", g=GP)[:, :, 128:256]
        scr1 = p.F[4].ap().rearrange("p (g x) -> p g x", g=GP)[:, :, 0:128]
        scr2 = p.F[5].ap().rearrange("p (g x) -> p g x", g=GP)[:, :, 0:128]
        S.add("vector", lambda eng: eng.tensor_tensor(scr1, Gr4, W4r, ALU.mult),
              reads=(bkey(0), bkey(1), "FC"), writes=("F4",), tag="tw1")
        S.add("vector", lambda eng: eng.tensor_tensor(scr2, Gi4, W4i, ALU.mult),
              reads=(bkey(0), bkey(1), "FC"), writes=("F5",), tag="tw2")
        S.add("vector", lambda eng, g4r=g4r: eng.tensor_tensor(g4r, scr1, scr2, ALU.subtract),
              reads=("F4", "F5"), writes=(Gk,), tag="tw3")
        S.add("vector", lambda eng: eng.tensor_tensor(scr1, Gr4, W4i, ALU.mult),
              reads=(bkey(0), bkey(1), "FC"), writes=("F4",), tag="tw4")
        S.add("vector", lambda eng: eng.tensor_tensor(scr2, Gi4, W4r, ALU.mult),
              reads=(bkey(0), bkey(1), "FC"), writes=("F5",), tag="tw5")
        S.add("vector", lambda eng, g4i=g4i: eng.tensor_tensor(g4i, scr1, scr2, ALU.add),
              reads=("F4", "F5"), writes=(Gk,), tag="tw6")

    def x_st(i):
        par = i % 2
        Ga, Gk = Gpar[par]
        for gi in range(GP):
            gr = Ga[:, gi * 256:gi * 256 + 128]
            gb = Ga[:, gi * 256 + 128:gi * 256 + 256]
            def x1(eng, gr=gr, gi2=gi):
                return eng.matmul(p.bank(2)[0:66, gi2 * 128:gi2 * 128 + 128], cF1r, gr,
                                  start=(gi2 == 0), stop=False)
            S.add("tensor", x1, reads=(Gk, "FC"),
                  writes=(bkey(2) + "@acc", bkey(2)) if gi == 0 else (bkey(2) + "@acc",), tag="x1")
            def x2(eng, gb=gb, gi2=gi):
                return eng.matmul(p.bank(2)[0:66, gi2 * 128:gi2 * 128 + 128], cF1n, gb,
                                  start=False, stop=(gi2 == GP - 1))
            S.add("tensor", x2, reads=(Gk, "FC"),
                  writes=(bkey(2),) if gi == GP - 1 else (bkey(2) + "@acc",), tag="x2")
            def x3(eng, gr=gr, gi2=gi):
                return eng.matmul(p.bank(3)[0:66, gi2 * 128:gi2 * 128 + 128], cF1i, gr,
                                  start=(gi2 == 0), stop=False)
            S.add("tensor", x3, reads=(Gk, "FC"),
                  writes=(bkey(3) + "@acc", bkey(3)) if gi == 0 else (bkey(3) + "@acc",), tag="x3")
            def x4(eng, gb=gb, gi2=gi):
                return eng.matmul(p.bank(3)[0:66, gi2 * 128:gi2 * 128 + 128], cF1r, gb,
                                  start=False, stop=(gi2 == GP - 1))
            S.add("tensor", x4, reads=(Gk, "FC"),
                  writes=(bkey(3),) if gi == GP - 1 else (bkey(3) + "@acc",), tag="x4")

    def khe_st(i):
        par = i % 2
        k4 = kh4[par]
        def khe(eng, k4=k4):
            eng.copy(k4.rearrange("p (g x) -> p g x", g=GP)[:, :, 0:128],
                     p.bank(2)[0:66, :].rearrange("p (g x) -> p g x", g=GP))
            return eng.copy(k4.rearrange("p (g x) -> p g x", g=GP)[:, :, 128:256],
                            p.bank(3)[0:66, :].rearrange("p (g x) -> p g x", g=GP))
        S.add("scalar", khe, reads=(bkey(2), bkey(3)), writes=("KH%d" % par,), tag="khe")
        ch0 = i * 8
        for gi in range(GP):
            c0 = ch0 + 2 * gi
            for ci in range(2):
                dstc = p.dram[kdst][c0 + ci:c0 + ci + 1].rearrange(
                    "c (pl k1 k2) -> (c k1) pl k2", pl=2, k1=33)
                def khs(eng, ci=ci, dstc=dstc, gi=gi, k4=k4):
                    srcv = k4[33 * ci:33 * ci + 33, 256 * gi:256 * gi + 256].rearrange(
                        "p (pl k2) -> p pl k2", pl=2)
                    return eng.dma_start(out=dstc, in_=srcv)
                S.add("sdma", khs, reads=("KH%d" % par,), writes=(kdst,), tag="khs")

    def kl_st(i):
        par = i % 2
        k4 = kh4[par]
        ch0 = i * 8
        for gi in range(GP):
            c0 = ch0 + 2 * gi
            for ci in range(2):
                srckc = p.dram["l2_kout"][c0 + ci:c0 + ci + 1].rearrange(
                    "c (pl k1 k2) -> (c k1) pl k2", pl=2, k1=33)
                def kl(eng, ci=ci, srckc=srckc, gi=gi, k4=k4):
                    dstv = k4[33 * ci:33 * ci + 33, 256 * gi:256 * gi + 256].rearrange(
                        "p (pl k2) -> p pl k2", pl=2)
                    return eng.dma_start(out=dstv, in_=srckc)
                S.add("adma", kl, reads=("l2_kout",), writes=("KH%d" % par,), tag="kl")

    def pw_st(i):
        par = i % 2
        k4 = kh4[par]
        Ha, Hk = Hpar[par]
        X4r = p.bank(2)[0:66, :].rearrange("p (g x) -> p g x", g=GP)
        X4i = p.bank(3)[0:66, :].rearrange("p (g x) -> p g x", g=GP)
        kv = k4.rearrange("p (g x) -> p g x", g=GP)
        khr4 = kv[:, :, 0:128]
        khi4 = kv[:, :, 128:256]
        Hv = Ha[0:66, 0:1024].rearrange("p (g x) -> p g x", g=GP)
        hr4 = Hv[:, :, 0:128]
        hi4 = Hv[:, :, 128:256]
        s1v = p.F[4].ap()[0:66, 0:512].rearrange("p (g x) -> p g x", g=GP)
        s2v = p.F[5].ap()[0:66, 0:512].rearrange("p (g x) -> p g x", g=GP)
        S.add("vector", lambda eng: eng.tensor_tensor(s1v, X4r, khr4, ALU.mult),
              reads=(bkey(2), "KH%d" % par), writes=("F4",), tag="pw1")
        S.add("vector", lambda eng: eng.tensor_tensor(s2v, X4i, khi4, ALU.mult),
              reads=(bkey(3), "KH%d" % par), writes=("F5",), tag="pw2")
        S.add("vector", lambda eng: eng.tensor_tensor(hr4, s1v, s2v, ALU.subtract),
              reads=("F4", "F5"), writes=(Hk,), tag="pw3")
        S.add("vector", lambda eng: eng.tensor_tensor(s1v, X4r, khi4, ALU.mult),
              reads=(bkey(2), "KH%d" % par), writes=("F4",), tag="pw4")
        S.add("vector", lambda eng: eng.tensor_tensor(s2v, X4i, khr4, ALU.mult),
              reads=(bkey(3), "KH%d" % par), writes=("F5",), tag="pw5")
        S.add("vector", lambda eng: eng.tensor_tensor(hi4, s1v, s2v, ALU.add),
              reads=("F4", "F5"), writes=(Hk,), tag="pw6")

    def a_st(i):
        par = i % 2
        Ha, Hk = Hpar[par]
        for gi in range(GP):
            hr = Ha[0:66, gi * 256:gi * 256 + 128]
            hi = Ha[0:66, gi * 256 + 128:gi * 256 + 256]
            def a1(eng, hr=hr, gi2=gi):
                return eng.matmul(p.bank(4)[:, gi2 * 128:gi2 * 128 + 128], hr, cE1r,
                                  start=(gi2 == 0), stop=False)
            S.add("tensor", a1, reads=(Hk, "FC"),
                  writes=(bkey(4) + "@acc", bkey(4)) if gi == 0 else (bkey(4) + "@acc",), tag="a1")
            def a2(eng, hi=hi, gi2=gi):
                return eng.matmul(p.bank(4)[:, gi2 * 128:gi2 * 128 + 128], hi, cE1n2,
                                  start=False, stop=(gi2 == GP - 1))
            S.add("tensor", a2, reads=(Hk, "FC"),
                  writes=(bkey(4),) if gi == GP - 1 else (bkey(4) + "@acc",), tag="a2")
            def a3(eng, hr=hr, gi2=gi):
                return eng.matmul(p.bank(5)[:, gi2 * 128:gi2 * 128 + 128], hr, cE1i,
                                  start=(gi2 == 0), stop=False)
            S.add("tensor", a3, reads=(Hk, "FC"),
                  writes=(bkey(5) + "@acc", bkey(5)) if gi == 0 else (bkey(5) + "@acc",), tag="a3")
            def a4(eng, hi=hi, gi2=gi):
                return eng.matmul(p.bank(5)[:, gi2 * 128:gi2 * 128 + 128], hi, cE1r,
                                  start=False, stop=(gi2 == GP - 1))
            S.add("tensor", a4, reads=(Hk, "FC"),
                  writes=(bkey(5),) if gi == GP - 1 else (bkey(5) + "@acc",), tag="a4")

    def iv_st(i):
        par = i % 2
        Aa, Ak = Apar[par]
        A4r = p.bank(4).rearrange("p (g x) -> p g x", g=GP)
        A4i = p.bank(5).rearrange("p (g x) -> p g x", g=GP)
        V4r = tVr.rearrange("p (g x) -> p g x", g=GP)
        V4i = tVi.rearrange("p (g x) -> p g x", g=GP)
        ap4r = Aa.rearrange("p (g x) -> p g x", g=GP)[:, :, 0:128]
        ap4i = Aa.rearrange("p (g x) -> p g x", g=GP)[:, :, 128:256]
        scr1f = p.F[4].ap().rearrange("p (g x) -> p g x", g=GP)[:, :, 0:128]
        scr2f = p.F[5].ap().rearrange("p (g x) -> p g x", g=GP)[:, :, 0:128]
        S.add("vector", lambda eng: eng.tensor_tensor(scr1f, A4r, V4r, ALU.mult),
              reads=(bkey(4), "FC"), writes=("F4",), tag="iv1")
        S.add("vector", lambda eng: eng.tensor_tensor(scr2f, A4i, V4i, ALU.mult),
              reads=(bkey(5), "FC"), writes=("F5",), tag="iv2")
        S.add("vector", lambda eng, ap4r=ap4r: eng.tensor_tensor(ap4r, scr1f, scr2f, ALU.subtract),
              reads=("F4", "F5"), writes=(Ak,), tag="iv3")
        S.add("vector", lambda eng: eng.tensor_tensor(scr1f, A4r, V4i, ALU.mult),
              reads=(bkey(4), "FC"), writes=("F4",), tag="iv4")
        S.add("vector", lambda eng: eng.tensor_tensor(scr2f, A4i, V4r, ALU.mult),
              reads=(bkey(5), "FC"), writes=("F5",), tag="iv5")
        S.add("vector", lambda eng, ap4i=ap4i: eng.tensor_tensor(ap4i, scr1f, scr2f, ALU.add),
              reads=("F4", "F5"), writes=(Ak,), tag="iv6")

    def y_st(i):
        par = i % 2
        Aa, Ak = Apar[par]
        for gi in range(GP):
            apr = Aa[:, gi * 256:gi * 256 + 128]
            api = Aa[:, gi * 256 + 128:gi * 256 + 256]
            def y1(eng, apr=apr, gi2=gi):
                return eng.matmul(p.bank(6)[0:64, gi2 * 128:gi2 * 128 + 128], cE2r, apr,
                                  start=(gi2 == 0), stop=False)
            S.add("tensor", y1, reads=(Ak, "FC"),
                  writes=(bkey(6) + "@acc", bkey(6)) if gi == 0 else (bkey(6) + "@acc",), tag="y1")
            def y2(eng, api=api, gi2=gi):
                return eng.matmul(p.bank(6)[0:64, gi2 * 128:gi2 * 128 + 128], cE2n, api,
                                  start=False, stop=(gi2 == GP - 1))
            S.add("tensor", y2, reads=(Ak, "FC"),
                  writes=(bkey(6),) if gi == GP - 1 else (bkey(6) + "@acc",), tag="y2")

    def yev_st(i):
        par = i % 2
        ysb = ys2[par]
        S.add("scalar", lambda eng, ysb=ysb: eng.copy(ysb[0:64, :], p.bank(6)[0:64, :]),
              reads=(bkey(6),), writes=("YS%d" % par,), tag="yev")
        ch0 = i * 8
        for gi in range(GP):
            c0 = ch0 + 2 * gi
            dst = p.dram["yconv"][c0:c0 + 2].rearrange("c (m2 m1) -> m2 c m1", m1=64)
            def yst(eng, dst=dst, gi=gi, ysb=ysb):
                return eng.dma_start(out=dst, in_=ysb[0:64, gi * 128:gi * 128 + 128]
                                     .rearrange("p (c m) -> p c m", c=2))
            S.add("sdma", yst, reads=("YS%d" % par,), writes=("yconv",), tag="yst")

    if part in ("fused", "fusedk"):
        nbk = 128
        for j in range(nbk + 6):
            if 0 <= j - 5 < nbk:
                khe_st(j - 5)
            if 0 <= j - 4 < nbk:
                x_st(j - 4)
            if 0 <= j - 3 < nbk:
                tw_st(j - 3)
            if 0 <= j - 2 < nbk:
                s1_st(j - 2)
            if 0 <= j < nbk:
                fld_st(j, "l2_kt")
        m32f = p.B1.ap().bitcast(F32)[:, 7168:7424]
        S.dma("sdma", m32f, p.dram["l2_k32m"][:, :], reads=("l2_k32m", "BIG1"), writes=("M32",))
        stg32 = p.F[4][:, 0:256]
        stg0 = p.F[5][:, 0:2]
        for tt in range(8):
            src32 = p.dram["l2_kout"].rearrange(
                "c (pl k1 k2) -> c pl k1 k2", pl=2, k1=33)[128 * tt:128 * tt + 128, :, 32, :]
            S.dma("sdma", stg32.rearrange("p (pl k2) -> p pl k2", pl=2), src32,
                  reads=("l2_kout",), writes=("F4",), tag="l32")
            S.add("vector", lambda eng: eng.tensor_tensor(stg32, stg32, m32f, ALU.mult),
                  reads=("F4", "M32"), writes=("F4",), tag="m32")
            S.dma("sdma", src32, stg32.rearrange("p (pl k2) -> p pl k2", pl=2),
                  reads=("F4",), writes=("l2_kout",), tag="s32")
            src0 = p.dram["l2_kout"].rearrange(
                "c (pl x) -> c pl x", pl=2)[128 * tt:128 * tt + 128, :, 0]
            S.dma("sdma", stg0, src0, reads=("l2_kout",), writes=("F5",), tag="l0b",
                  allow_slow_non_contiguous=True)
            S.add("vector", lambda eng: eng.tensor_scalar_mul(stg0, stg0, 0.5),
                  reads=("F5",), writes=("F5",), tag="h0b")
            S.dma("sdma", src0, stg0, reads=("F5",), writes=("l2_kout",), tag="s0b",
                  allow_slow_non_contiguous=True)
        if part == "fusedk":
            return
        nbu = 128
        for j in range(nbu + 10):
            if 0 <= j - 9 < nbu:
                yev_st(j - 9)
            if 0 <= j - 8 < nbu:
                y_st(j - 8)
            if 0 <= j - 7 < nbu:
                iv_st(j - 7)
            if 0 <= j - 6 < nbu:
                a_st(j - 6)
            if 0 <= j - 5 < nbu:
                pw_st(j - 5)
            if 0 <= j - 4 < nbu:
                x_st(j - 4)
            if 0 <= j - 3 < nbu:
                tw_st(j - 3)
                kl_st(j - 3)
            if 0 <= j - 2 < nbu:
                s1_st(j - 2)
            if 0 <= j < nbu:
                fld_st(j, "u32")
        S.add("vector", lambda eng: eng.engine_nop(),
              reads=("FC", "KH0", "KH1", "YS0", "YS1", "M32"), writes=("BIG1",),
              tag="fcbar2")
        return
    for bb in (range(16) if dbg_nocc != "input" else ()):
        fwd_batch("l2_kt", bb * 8, 8, True)
    if dbg_nocc == "input":
        S.add("vector", lambda eng: eng.engine_nop(), reads=(), writes=("l2_kout",))
        for bb in range(128):
            fwd_batch("u32", bb * 8, 8, False)
        return
    m32 = p.B1.ap().bitcast(F32)[:, 5120:5376]
    S.dma("sdma", m32, p.dram["l2_k32m"][:, :], reads=("l2_k32m",), writes=("BIG1",))
    stg32 = p.F[4][:, 0:256]
    src32 = p.dram["l2_kin"].rearrange("c (pl k1 k2) -> c pl k1 k2", pl=2, k1=33)[:, :, 32, :]
    S.dma("sdma", stg32.rearrange("p (pl k2) -> p pl k2", pl=2), src32,
          reads=("l2_kin",), writes=("F4",), tag="l32")
    S.add("vector", lambda eng: eng.tensor_tensor(stg32, stg32, m32, ALU.mult),
          reads=("F4", "BIG1"), writes=("F4",), tag="m32")
    S.dma("sdma", src32, stg32.rearrange("p (pl k2) -> p pl k2", pl=2),
          reads=("F4",), writes=("l2_kin",), tag="s32")
    stg0 = p.F[4][:, 256:258]
    src0 = p.dram["l2_kin"].rearrange("c (pl x) -> c pl x", pl=2)[:, :, 0]
    S.dma("sdma", stg0, src0, reads=("l2_kin",), writes=("F4",), tag="l0b",
          allow_slow_non_contiguous=True)
    S.add("vector", lambda eng: eng.tensor_scalar_mul(stg0, stg0, 0.5),
          reads=("F4",), writes=("F4",))
    S.dma("sdma", src0, stg0, reads=("F4",), writes=("l2_kin",), tag="s0b",
          allow_slow_non_contiguous=True)
    if part == "k":
        return
    if not dbg_nocc:
        def cc(eng):
            return eng.collective_compute(
                "AllGather", ALU.bypass,
                replica_groups=[list(range(N_CORES))],
                ins=[p.dram["l2_kin"][:, :]],
                outs=[p.dram["l2_kout"][:, :]])
        S.add("gdma", cc, reads=("l2_kin",), writes=("l2_kout",), tag="ccag")
    elif dbg_nocc == "copy":
        S.dma("sdma", p.dram["l2_kout"][0:128, :], p.dram["l2_kin"][:, :],
              reads=("l2_kin",), writes=("l2_kout",), tag="ccdbg")
    for bb in range(128):
        fwd_batch("u32", bb * 8, 8, False)





def emit_l2glu(p):
    S = p.sched
    src = p.dram["l2_gluWT"].rearrange("(f p) o -> p f o", p=128)
    S.dma("gdma", p.B1[:, 0:16384].rearrange("p (f o) -> p f o", f=NF), src,
          reads=("l2_gluWT",), writes=("BIG1",), tag="lglu")
    dsk = p.S[0][:, 0:8]
    S.dma("sdma", dsk, p.dram["l2_dsk"][:, :], reads=("l2_dsk",), writes=("S0",))
    glb = p.S[1][:, 0:16]
    S.dma("sdma", glb, p.dram["l2_glub"][:, :], reads=("l2_glub",), writes=("S1",))
    yb = [(p.F[0], "F0"), (p.F[1], "F1")]
    ub = [(p.F[2], "F2")]
    for ot in range(NF):
        for tc in range(8):
            b = tc % 2
            S.dma("sdma", yb[b][0][:, 0:512], p.dram["yconv"][bass.ts(ot, 128), bass.ts(tc, 512)],
                  reads=("yconv",), writes=(yb[b][1],), tag="ly")
            S.dma("sdma", ub[0][0][:, 0:512], p.dram["u32"][bass.ts(ot, 128), bass.ts(tc, 512)],
                  reads=("u32",), writes=(ub[0][1],), tag="lu")
            def yds(eng, b=b, ot=ot):
                return eng.tensor_scalar(ub[0][0][:, 0:512], ub[0][0][:, 0:512],
                                         dsk[:, ot:ot + 1], None, ALU.mult, ALU.bypass)
            S.add("vector", yds, reads=(ub[0][1], "S0"), writes=(ub[0][1],), tag="uds")
            S.add("vector", lambda eng, b=b: eng.tensor_add(yb[b][0][:, 0:512],
                                                            yb[b][0][:, 0:512],
                                                            ub[0][0][:, 0:512]),
                  reads=(yb[b][1], ub[0][1]), writes=(yb[b][1],))
            def gl(eng, b=b, ot=ot, tc=tc):
                return eng.activation(p.B2[:, ot * 4096 + tc * 512: ot * 4096 + tc * 512 + 512],
                                      yb[b][0][:, 0:512], AF.Gelu)
            S.add("scalar", gl, reads=(yb[b][1], "fconsts"), writes=("BIG2",), tag="gel2")
    gsb = [(p.F[3], "F3")]
    for tc in range(8):
        for ot in range(NF):
            mm_acc(p, S, "gla%d_%d" % (tc, ot), p.bank(0), bkey(0),
                   [(p.B1[:, f * 2048 + ot * 128: f * 2048 + ot * 128 + 128], "BIG1")
                    for f in range(NF)],
                   [(p.B2.ap()[:, f * 4096 + tc * 512: f * 4096 + tc * 512 + 512], "BIG2")
                    for f in range(NF)])
            mm_acc(p, S, "glg%d_%d" % (tc, ot), p.bank(1), bkey(1),
                   [(p.B1[:, f * 2048 + (ot + 8) * 128: f * 2048 + (ot + 8) * 128 + 128],
                     "BIG1") for f in range(NF)],
                   [(p.B2.ap()[:, f * 4096 + tc * 512: f * 4096 + tc * 512 + 512], "BIG2")
                    for f in range(NF)])
            def sg(eng, ot=ot):
                return eng.activation(gsb[0][0][:, 0:512], p.bank(1), AF.Sigmoid,
                                      bias=glb[:, 8 + ot:9 + ot])
            S.add("scalar", sg, reads=(bkey(1), "S1", "fconsts"), writes=("F3",), tag="sg2")
            def y3a(eng, ot=ot):
                return eng.scalar_tensor_tensor(p.F[4][:, 0:512], p.bank(0),
                                                glb[:, ot:ot + 1], gsb[0][0][:, 0:512],
                                                ALU.add, ALU.mult)
            S.add("vector", y3a, reads=(bkey(0), "S1", "F3"), writes=("F4",), tag="y3a")
            def y3c(eng, ot=ot):
                return eng.copy(p.B[0][:, 0:512], p.F[4][:, 0:512])
            S.add("scalar", y3c, reads=("F4",), writes=("B0",), tag="y3c")
            S.dma("sdma", p.dram["l2_y3"][bass.ts(ot, 128), bass.ts(tc, 512)],
                  p.B[0][:, 0:512], reads=("B0",), writes=("l2_y3",), tag="y3s")
    load_weight_bf16(p, S, "l2_outWT", p.Wsl[0][:, :], "W0", D, D)
    obr = p.F[6]
    S.dma("sdma", obr[:, :], p.dram["l2_ob"][:, :], reads=("l2_ob",), writes=("F6",))
    y3t = [(p.B[2], "B2"), (p.B[3], "B3")]
    xnt = [(p.F[7], "F7")]
    sev = [(p.F[5], "F5")]
    for i in range(NT):
        b = i % 2
        src2 = p.dram["l2_y3"].rearrange("(f p) t -> p f t", p=128)[:, :, bass.ts(i, 128)]
        S.dma("sdma", y3t[b][0].ap().rearrange("p (f c) -> p f c", f=NF), src2,
              reads=("l2_y3",), writes=(y3t[b][1],), tag="ly3")
        for oc in range(2):
            mm_acc(p, S, "op%d_%d" % (i, oc), p.bank(4 + oc), bkey(4 + oc),
                   [(y3t[b][0][:, bass.ts(f, 128)], y3t[b][1]) for f in range(NF)],
                   [(p.Wsl[0][:, f * D + oc * 512: f * D + oc * 512 + 512], "W0")
                    for f in range(NF)])
        S.add("vector", lambda eng, b=b: eng.tensor_tensor(
            sev[0][0][:, :], p.bankpair(2)[:, :], obr[:, :], ALU.add),
            reads=(bkey(4), bkey(5), "F6"), writes=(sev[0][1],), tag="s2ev")
        S.dma("sdma", xnt[0][0][:, :], p.dram["xn2"][bass.ts(i, 128), :],
              reads=("xn2",), writes=(xnt[0][1],), tag="lxn2g")
        S.add("vector", lambda eng: eng.tensor_add(sev[0][0][:, :], sev[0][0][:, :],
                                                   xnt[0][0][:, :]),
              reads=(sev[0][1], xnt[0][1]), writes=(sev[0][1],))
        S.dma("sdma", p.dram["s2"][bass.ts(i, 128), :], sev[0][0][:, :],
              reads=(sev[0][1],), writes=("s2",), tag="sts2")


def full_phase_list():
    return [
        emit_l2khat,
        lambda p: emit_ln(p, "x0", "xn0", "xn0T", "ln0"),
        emit_l0attn,
        emit_l0ffn,
        lambda p: emit_ln(p, "x1", "xn1", "xn1T", "ln1"),
        emit_l1attn,
        emit_l1ffn,
        lambda p: emit_ln(p, "x2", "xn2", "xn2T", "ln2"),
        emit_l2inproj,
        emit_l2fft,
        emit_l2glu,
        emit_l2ffn_a,
        emit_l2ffn_b,
        lambda p: emit_ln(p, "x3", "xn3", "xn3T", "ln3"),
        emit_l3a,
        emit_l3b,
        emit_l3c,
    ]


def full_dram_specs():
    EI = "ExternalInput"
    IN = "Internal"
    sp = [
        ("ident", [128, 128], BF16, EI), ("fconsts", [128, 64], F32, EI),
        ("ones1", [1, 128], BF16, EI), ("tri128", [128, 128], BF16, EI),
        ("triexc", [32, 32], BF16, EI), ("indcols", [128, 1024], BF16, EI),
        ("jrow16", [1, 16], F32, EI), ("poscols", [128, NT], F32, EI),
        ("posrows2", [1, 2 * T], BF16, EI), ("cs32", [T, 64], F32, EI),
        ("l1_masks", [128, 5 * 512], BF16, EI),
        ("fF2r", [64, 128], BF16, EI), ("fF2i", [64, 128], BF16, EI),
        ("fF1r", [128, 66], BF16, EI), ("fF1i", [128, 66], BF16, EI),
        ("fF1n", [128, 66], BF16, EI),
        ("fE1r", [66, 128], BF16, EI), ("fE1i", [66, 128], BF16, EI),
        ("fE1n2", [66, 128], BF16, EI),
        ("fE2r", [128, 64], BF16, EI), ("fE2n", [128, 64], BF16, EI),
        ("ftWr", [128, 512], F32, EI), ("ftWi", [128, 512], F32, EI),
        ("ftVr", [128, 512], F32, EI), ("ftVi", [128, 512], F32, EI),
        ("l2_k32m", [128, 256], F32, EI),
        ("x0", [T, D], F32, EI),
        ("l0_Wq", [D, D], F32, EI), ("l0_Wk", [D, D], F32, EI),
        ("l0_Wv", [D, D], F32, EI), ("l0_Wo", [D, D], F32, EI),
        ("l0_bq", [128, D], F32, EI), ("l0_bk", [128, D], F32, EI),
        ("l0_bv", [128, D], F32, EI), ("l0_bo", [128, D], F32, EI),
        ("l0_nrope", [1, 1], F32, EI),
        ("l0f_WrT", [D, D], F32, EI), ("l0f_WkT", [D, D], F32, EI),
        ("l0f_WvT", [D, D], F32, EI),
        ("l0f_mur", [128, D], F32, EI), ("l0f_muk", [128, D], F32, EI),
        ("l1_wattnT", [D, 3 * D], F32, EI), ("l1_wprojT", [D, D], F32, EI),
        ("l1f_wT", [D, D], F32, EI),
        ("l1f_scale", [128, D], F32, EI), ("l1f_bias", [128, D], F32, EI),
        ("l2_inWT", [D, D], F32, EI), ("l2_inb", [128, 8], F32, EI),
        ("l2_ldt", [128, 32], F32, EI), ("l2_lar", [128, 32], F32, EI),
        ("l2_aim", [128, 32], F32, EI), ("l2_Cr", [128, 32], F32, EI),
        ("l2_Ci", [128, 32], F32, EI), ("l2_bd", [128, 8], F32, EI),
        ("l2_gluWT", [D, 2048], F32, EI), ("l2_glub", [128, 16], F32, EI),
        ("l2_dsk", [128, 8], F32, EI),
        ("l2_outWT", [D, D], F32, EI), ("l2_ob", [128, D], F32, EI),
        ("l2f_Wq", [D, D], F32, EI), ("l2f_Wo", [D, D], F32, EI),
        ("l2f_Wdown", [D, 64], F32, EI), ("l2f_Wup", [64, 2048], F32, EI),
        ("l2f_Wg", [D, 8], F32, EI), ("l2f_We", [8, D, D], F32, EI),
        ("l2f_g1", [128, D], F32, EI), ("l2f_g2", [128, D], F32, EI),
        ("l2f_bq", [128, D], F32, EI), ("l2f_bo", [128, D], F32, EI),
        ("l2f_bup", [128, 2048], F32, EI), ("l2f_bdn", [128, 64], F32, EI),
        ("l2f_bg", [128, 8], F32, EI),
        ("l3_WmT", [D, D], F32, EI), ("l3_WqT", [D, D], F32, EI),
        ("l3_bm", [128, D], F32, EI), ("l3_bq", [128, D], F32, EI),
        ("l3_phases", [128, D], F32, EI),
        ("l3_W1T", [4 * D, 2048], F32, EI), ("l3_b1", [1, 2048], F32, EI),
        ("l3_lng", [128, 32], F32, EI), ("l3_lnb", [128, 32], F32, EI),
        ("l3_W2T", [2048, D], F32, EI), ("l3_b2", [128, D], F32, EI),
        ("l3f_wT", [D, D], F32, EI),
        ("xn0", [T, D], F32, IN), ("xn0T", [D, T], BF16, IN),
        ("l0_qT", [D, T], BF16, IN), ("s0", [T, D], F32, IN),
        ("x1", [T, D], F32, IN), ("xn1", [T, D], F32, IN), ("xn1T", [D, T], BF16, IN),
        ("l1_qT", [D, T], BF16, IN), ("l1_kT", [D, T], BF16, IN),
        ("s1", [T, D], F32, IN),
        ("x2", [T, D], F32, IN), ("xn2", [T, D], F32, IN), ("xn2T", [D, T], BF16, IN),
        ("u32", [D, T], F32, IN), ("l2_kt", [128, T], F32, IN),
        ("l2_kin", [128, 8448], F32, IN), ("l2_kout", [1024, 8448], F32, IN, "Shared"),
        ("yconv", [D, T], F32, IN), ("l2_y3", [D, T], BF16, IN),
        ("s2", [T, D], F32, IN),
        ("l2_xmid", [T, D], F32, IN), ("l2_h2T", [D, T], BF16, IN),
        ("l2_gate", [T, 8], F32, IN),
        ("x3", [T, D], F32, IN), ("xn3", [T, D], F32, IN), ("xn3T", [D, T], BF16, IN),
        ("l3_qoff", [T, D], F32, IN),
        ("l3_wc", [T, D], BF16, IN), ("l3_ws", [T, D], BF16, IN),
        ("l3_mg", [T, D], BF16, IN),
        ("l3_carr", [32, 3072], BF16, IN), ("l3_hT", [2048, T], BF16, IN),
        ("y", [T, D], F32, "ExternalOutput"),
    ]
    return sp


def build_inmaps(inputs, fconst_vals, fused=False):
    z = inputs
    c = build_consts()
    fc = np.zeros((128, 64), np.float32)
    fc[:, :len(fconst_vals)] = np.asarray(fconst_vals, np.float32)[None, :]
    shared = dict(c)
    shared["fconsts"] = fc
    shared["l2_bd"] = c["l2_bdz"]
    del shared["l2_bdz"]
    shared.update({
        "l0_Wq": z["L0_attn_Wq"], "l0_Wk": z["L0_attn_Wk"],
        "l0_Wv": z["L0_attn_Wv"], "l0_Wo": z["L0_attn_Wo"],
        "l0_bq": _tile128(z["L0_attn_bq"]), "l0_bk": _tile128(z["L0_attn_bk"]),
        "l0_bv": _tile128(z["L0_attn_bv"]), "l0_bo": _tile128(z["L0_attn_bo"]),
        "l0_nrope": np.asarray(z["L0_attn_nrope_freq"], np.float32).reshape(1, 1),
        "l0f_WrT": np.ascontiguousarray(np.asarray(z["L0_ffn_W_r"]).T),
        "l0f_WkT": np.ascontiguousarray(np.asarray(z["L0_ffn_W_k"]).T),
        "l0f_WvT": np.ascontiguousarray(np.asarray(z["L0_ffn_W_v"]).T),
        "l0f_mur": _tile128(z["L0_ffn_mu_r"]), "l0f_muk": _tile128(z["L0_ffn_mu_k"]),
        "l1_wattnT": np.ascontiguousarray(np.asarray(z["L1_attn_w_attn"]).T),
        "l1_wprojT": np.ascontiguousarray(np.asarray(z["L1_attn_w_proj"]).T),
        "l1f_wT": np.ascontiguousarray(np.asarray(z["L1_ffn_weights"]).T),
        "l1f_scale": _tile128(z["L1_ffn_scale"]), "l1f_bias": _tile128(z["L1_ffn_bias"]),
        "l2_inWT": np.ascontiguousarray(np.asarray(z["L2_attn_in_W"]).T),
        "l2_inb": np.ascontiguousarray(np.asarray(z["L2_attn_in_b"], np.float32).reshape(8, 128).T),
        "l2_gluWT": np.ascontiguousarray(np.asarray(z["L2_attn_glu_W"]).T),
        "l2_glub": np.ascontiguousarray(np.asarray(z["L2_attn_glu_b"], np.float32).reshape(16, 128).T),
        "l2_dsk": np.ascontiguousarray(np.asarray(z["L2_attn_D_skip"], np.float32).reshape(8, 128).T),
        "l2_outWT": np.ascontiguousarray(np.asarray(z["L2_attn_out_W"]).T),
        "l2_ob": _tile128(z["L2_attn_out_b"]),
        "l2f_Wq": np.asarray(z["L2_ffn_Wq"]), "l2f_Wo": np.asarray(z["L2_ffn_Wo"]),
        "l2f_Wdown": np.asarray(z["L2_ffn_Wdown"]), "l2f_Wup": np.asarray(z["L2_ffn_Wup"]),
        "l2f_Wg": np.asarray(z["L2_ffn_Wg"]), "l2f_We": np.asarray(z["L2_ffn_We"]),
        "l2f_g1": _tile128(z["L2_ffn_gamma1"]), "l2f_g2": _tile128(z["L2_ffn_gamma2"]),
        "l2f_bq": _tile128(z["L2_ffn_bq"]), "l2f_bo": _tile128(z["L2_ffn_bo"]),
        "l2f_bup": _tile128(z["L2_ffn_bup"]),
        "l2f_bdn": np.ascontiguousarray(np.broadcast_to(
            np.asarray(z["L2_ffn_bdown"], np.float32)[None, :], (128, 64))),
        "l2f_bg": np.ascontiguousarray(np.broadcast_to(
            np.asarray(z["L2_ffn_bg"], np.float32)[None, :], (128, 8))),
        "l3_WmT": np.ascontiguousarray(np.asarray(z["L3_attn_W_mag"]).T),
        "l3_WqT": np.ascontiguousarray(np.asarray(z["L3_attn_W_q"]).T),
        "l3_bm": _tile128(z["L3_attn_b_mag"]), "l3_bq": _tile128(z["L3_attn_b_q"]),
        "l3_phases": _tile128(z["L3_attn_random_phases"]),
        "l3_W1T": np.ascontiguousarray(np.asarray(z["L3_attn_W1"]).T),
        "l3_b1": np.asarray(z["L3_attn_b1"], np.float32).reshape(1, 2048),
        "l3_lng": np.ascontiguousarray(np.asarray(z["L3_attn_ln_g"], np.float32).reshape(32, 128).T),
        "l3_lnb": np.ascontiguousarray(np.asarray(z["L3_attn_ln_b"], np.float32).reshape(32, 128).T),
        "l3_W2T": np.ascontiguousarray(np.asarray(z["L3_attn_W2"]).T),
        "l3_b2": _tile128(z["L3_attn_b2"]),
        "l3f_wT": np.ascontiguousarray(np.asarray(z["L3_ffn_weight"]).T),
    })

    def shard_arrange(v2):
        v2 = np.asarray(v2, np.float32).reshape(128, 32)
        out = v2.reshape(32, 4, 32).transpose(1, 2, 0).reshape(128, 32)
        return np.ascontiguousarray(out)

    X = np.asarray(z["L0_attn_x"], np.float32)
    ldt_full = np.broadcast_to(np.asarray(z["L2_attn_log_dt"], np.float32)[:, None], (1024, 32))
    if fused:
        def chrow(v2):
            v2 = np.asarray(v2, np.float32).reshape(1024, 32)
            return np.ascontiguousarray(
                v2.reshape(8, 128, 32).transpose(1, 0, 2).reshape(128, 256))
        shared["l2_ldt"] = chrow(ldt_full)
        shared["l2_lar"] = chrow(np.asarray(z["L2_attn_log_A_real"]))
        shared["l2_aim"] = chrow(np.asarray(z["L2_attn_A_imag"]))
        shared["l2_Cr"] = chrow(np.asarray(z["L2_attn_C"])[:, :, 0])
        shared["l2_Ci"] = chrow(np.asarray(z["L2_attn_C"])[:, :, 1])
        return [dict(shared, x0=np.ascontiguousarray(X[core]))
                for core in range(N_CORES)]
    maps = []
    for core in range(N_CORES):
        m = dict(shared)
        m["x0"] = np.ascontiguousarray(X[core])
        c0 = 128 * core
        m["l2_ldt"] = shard_arrange(ldt_full[c0:c0 + 128])
        m["l2_lar"] = shard_arrange(np.asarray(z["L2_attn_log_A_real"])[c0:c0 + 128])
        m["l2_aim"] = shard_arrange(np.asarray(z["L2_attn_A_imag"])[c0:c0 + 128])
        m["l2_Cr"] = shard_arrange(np.asarray(z["L2_attn_C"])[c0:c0 + 128, :, 0])
        m["l2_Ci"] = shard_arrange(np.asarray(z["L2_attn_C"])[c0:c0 + 128, :, 1])
        maps.append(m)
    return maps


_NC_CACHE = {}


def phases_launch1():
    return [
        emit_l2khat,
        lambda p: emit_ln(p, "x0", "xn0", "xn0T", "ln0"),
        emit_l0attn,
        emit_l0ffn,
        lambda p: emit_ln(p, "x1", "xn1", "xn1T", "ln1"),
        emit_l1attn,
        emit_l1ffn,
        lambda p: emit_ln(p, "x2", "xn2", "xn2T", "ln2"),
        emit_l2inproj,
        lambda p: emit_l2fft(p, part="k"),
    ]


def phases_launch2():
    return [
        lambda p: emit_l2fft(p, part="u"),
        emit_l2glu,
        emit_l2ffn_a,
        emit_l2ffn_b,
        lambda p: emit_ln(p, "x3", "xn3", "xn3T", "ln3"),
        emit_l3a,
        emit_l3b,
        emit_l3c,
    ]


_L1_OUT = ("l2_kin", "x2", "xn2", "xn2T", "u32")
_L2_EXTRA_IN = ("l2_kout", "x2", "xn2", "xn2T", "u32")


def specs_launch1():
    used = ("ident fconsts ones1 tri128 jrow16 poscols posrows2 cs32 l1_masks "
            "fF2r fF2i fF1r fF1i fF1n fE1r fE1i fE1n2 fE2r fE2n ftWr ftWi ftVr ftVi "
            "l2_k32m x0 l0_Wq l0_Wk l0_Wv l0_Wo l0_bq l0_bk l0_bv l0_bo l0_nrope "
            "l0f_WrT l0f_WkT l0f_WvT l0f_mur l0f_muk l1_wattnT l1_wprojT l1f_wT "
            "l1f_scale l1f_bias l2_inWT l2_inb l2_ldt l2_lar l2_aim l2_Cr l2_Ci "
            "l2_bd").split()
    scr = ("xn0 xn0T l0_qT s0 x1 xn1 xn1T l1_qT l1_kT s1 l2_kt").split()
    sp = [s for s in full_dram_specs() if s[0] in used]
    sp += [(n, sh, dt, "Internal") for (n, sh, dt, k, *r) in full_dram_specs() if n in scr]
    for (n, sh, dt, k, *r) in full_dram_specs():
        if n in _L1_OUT:
            sp.append((n, sh, dt, "ExternalOutput"))
    return sp


def specs_launch2():
    used = ("ident fconsts ones1 tri128 triexc indcols poscols "
            "fF2r fF2i fF1r fF1i fF1n fE1r fE1i fE1n2 fE2r fE2n ftWr ftWi ftVr ftVi "
            "l2_gluWT l2_glub l2_dsk l2_outWT l2_ob "
            "l2f_Wq l2f_Wo l2f_Wdown l2f_Wup l2f_Wg l2f_We l2f_g1 l2f_g2 l2f_bq "
            "l2f_bo l2f_bup l2f_bdn l2f_bg l3_WmT l3_WqT l3_bm l3_bq l3_phases "
            "l3_W1T l3_b1 l3_lng l3_lnb l3_W2T l3_b2 l3f_wT").split()
    scr = ("yconv l2_y3 s2 l2_xmid l2_h2T l2_gate x3 xn3 xn3T l3_qoff l3_wc "
           "l3_ws l3_mg l3_carr l3_hT").split()
    sp = [s for s in full_dram_specs() if s[0] in used]
    sp += [(n, sh, dt, "Internal") for (n, sh, dt, k, *r) in full_dram_specs() if n in scr]
    for (n, sh, dt, k, *r) in full_dram_specs():
        if n in _L2_EXTRA_IN:
            sp.append((n, sh, dt, "ExternalInput"))
    sp.append(("y", [T, D], F32, "ExternalOutput"))
    return sp


def kernel_2launch(**inputs):
    if "nc1" not in _NC_CACHE:
        _NC_CACHE["nc1"] = build_nc_phase(phases_launch1(), specs_launch1())
        _NC_CACHE["nc2"] = build_nc_phase(phases_launch2(), specs_launch2())
    nc1, nc2 = _NC_CACHE["nc1"], _NC_CACHE["nc2"]
    fvals = sorted(set(nc1._prog.fconst_vals) | set(nc2._prog.fconst_vals))
    m1_all = build_inmaps(inputs, nc1._prog.fconst_vals)
    names1 = set(s[0] for s in specs_launch1() if s[3] == "ExternalInput")
    in1 = [{k2: v for k2, v in m.items() if k2 in names1} for m in m1_all]
    r1 = run_bass_kernel_spmd(nc1, in1, core_ids=list(range(N_CORES)))
    kout = np.concatenate([np.asarray(r1.results[c]["l2_kin"]) for c in range(N_CORES)], 0)
    m2_all = build_inmaps(inputs, nc2._prog.fconst_vals)
    names2 = set(s[0] for s in specs_launch2() if s[3] == "ExternalInput")
    in2 = []
    for c in range(N_CORES):
        m = {k2: v for k2, v in m2_all[c].items() if k2 in names2}
        m["l2_kout"] = kout
        for nm in ("x2", "xn2", "xn2T", "u32"):
            m[nm] = np.asarray(r1.results[c][nm])
        in2.append(m)
    r2 = run_bass_kernel_spmd(nc2, in2, core_ids=list(range(N_CORES)))
    return np.stack([np.asarray(r2.results[c]["y"]) for c in range(N_CORES)])



def emit_l2khat_full(p):
    S = p.sched
    p.fconst(float(np.pi / 2))
    W = 256
    ldt = p.F[4][:, 0:W]
    lar = p.F[4][:, 256:512]
    aim = p.F[4][:, 512:768]
    Cr = p.F[5][:, 0:W]
    Ci = p.F[5][:, 256:512]
    t1 = p.F[6][:, 0:W]
    t2 = p.F[6][:, 256:512]
    t3 = p.F[6][:, 512:768]
    lr = p.F[8][:, 0:256]
    li = p.F[8][:, 256:512]
    cfr = p.F[8][:, 512:768]
    cfi = p.F[8][:, 768:1024]
    for nm, dst, dk in (("l2_ldt", ldt, "F4"), ("l2_lar", lar, "F4"),
                        ("l2_aim", aim, "F4"), ("l2_Cr", Cr, "F5"),
                        ("l2_Ci", Ci, "F5")):
        S.dma("sdma", dst, p.dram[nm][:, :], reads=(nm,), writes=(dk,), tag="lmod")
    A_ = lambda u, fn, r, w, tg="kh": S.add(u, fn, reads=r, writes=w, tag=tg)
    A_("scalar", lambda eng: eng.activation(ldt, ldt, AF.Exp), ("F4", "fconsts"), ("F4",))
    A_("scalar", lambda eng: eng.activation(lar, lar, AF.Exp), ("F4", "fconsts"), ("F4",))
    A_("vector", lambda eng: eng.tensor_scalar_mul(lar, lar, -1.0), ("F4",), ("F4",))
    A_("vector", lambda eng: eng.tensor_tensor(t1, lar, ldt, ALU.mult), ("F4",), ("F6",))
    A_("vector", lambda eng: eng.tensor_tensor(t2, aim, ldt, ALU.mult), ("F4",), ("F6",))
    A_("scalar", lambda eng: eng.activation(t1, t1, AF.Exp), ("F6", "fconsts"), ("F6",))
    A_("vector", lambda eng: eng.tensor_scalar(t3, t2, 1.0 / TWO_PI, MAGIC, ALU.mult, ALU.add),
       ("F6",), ("F6",))
    A_("vector", lambda eng: eng.tensor_scalar(t3, t3, MAGIC, None, ALU.subtract, ALU.bypass),
       ("F6",), ("F6",))
    A_("vector", lambda eng: eng.scalar_tensor_tensor(t2, t3, -TWO_PI, t2, ALU.mult, ALU.add),
       ("F6",), ("F6",))
    A_("vector", lambda eng: eng.tensor_scalar(t2, t2, PI_C, -PI_C, ALU.min, ALU.max),
       ("F6",), ("F6",))
    A_("scalar", lambda eng: eng.activation(li, t2, AF.Sin), ("F6", "fconsts"), ("F8",))
    A_("vector", lambda eng: eng.tensor_scalar_mul(t3, t2, -1.0), ("F6",), ("F6",))
    A_("vector", lambda eng: eng.tensor_max(t3, t3, t2), ("F6",), ("F6",))
    A_("scalar", lambda eng: eng.activation(lr, t3, AF.Sin, bias=float(np.pi / 2), scale=-1.0),
       ("F6", "fconsts"), ("F8",))
    A_("vector", lambda eng: eng.tensor_tensor(lr, lr, t1, ALU.mult), ("F8", "F6"), ("F8",))
    A_("vector", lambda eng: eng.tensor_tensor(li, li, t1, ALU.mult), ("F8", "F6"), ("F8",))
    aa = p.F[7][:, 0:W]
    nr = p.F[7][:, 256:512]
    ni = p.F[7][:, 512:768]
    A_("vector", lambda eng: eng.tensor_tensor(aa, lar, lar, ALU.mult), ("F4",), ("F7",))
    A_("vector", lambda eng: eng.tensor_tensor(t1, aim, aim, ALU.mult), ("F4",), ("F6",))
    A_("vector", lambda eng: eng.tensor_add(aa, aa, t1), ("F7", "F6"), ("F7",))
    A_("vector", lambda eng: eng.reciprocal(aa, aa), ("F7",), ("F7",))
    A_("vector", lambda eng: eng.tensor_scalar(t2, lr, 1.0, None, ALU.subtract, ALU.bypass),
       ("F8",), ("F6",))
    A_("vector", lambda eng: eng.tensor_tensor(nr, t2, lar, ALU.mult), ("F6", "F4"), ("F7",))
    A_("vector", lambda eng: eng.tensor_tensor(t3, li, aim, ALU.mult), ("F8", "F4"), ("F6",))
    A_("vector", lambda eng: eng.tensor_add(nr, nr, t3), ("F7", "F6"), ("F7",))
    A_("vector", lambda eng: eng.tensor_tensor(ni, li, lar, ALU.mult), ("F8", "F4"), ("F7",))
    A_("vector", lambda eng: eng.tensor_tensor(t3, t2, aim, ALU.mult), ("F6", "F4"), ("F6",))
    A_("vector", lambda eng: eng.tensor_sub(ni, ni, t3), ("F7", "F6"), ("F7",))
    A_("vector", lambda eng: eng.tensor_tensor(nr, nr, aa, ALU.mult), ("F7",), ("F7",))
    A_("vector", lambda eng: eng.tensor_tensor(ni, ni, aa, ALU.mult), ("F7",), ("F7",))
    A_("vector", lambda eng: eng.tensor_tensor(cfr, Cr, nr, ALU.mult), ("F5", "F7"), ("F8",))
    A_("vector", lambda eng: eng.tensor_tensor(t3, Ci, ni, ALU.mult), ("F5", "F7"), ("F6",))
    A_("vector", lambda eng: eng.tensor_sub(cfr, cfr, t3), ("F8", "F6"), ("F8",))
    A_("vector", lambda eng: eng.tensor_scalar_mul(cfr, cfr, 2.0), ("F8",), ("F8",))
    A_("vector", lambda eng: eng.tensor_tensor(cfi, Cr, ni, ALU.mult), ("F5", "F7"), ("F8",))
    A_("vector", lambda eng: eng.tensor_tensor(t3, Ci, nr, ALU.mult), ("F5", "F7"), ("F6",))
    A_("vector", lambda eng: eng.tensor_add(cfi, cfi, t3), ("F8", "F6"), ("F8",))
    A_("vector", lambda eng: eng.tensor_scalar_mul(cfi, cfi, -2.0), ("F8",), ("F8",))

    b1f = p.B2.ap().bitcast(F32)
    vBr = b1f[:, 0:2048].rearrange("p (n b) -> p n b", n=32)
    vBi = b1f[:, 2048:4096].rearrange("p (n b) -> p n b", n=32)
    vAr = b1f[:, 4096:6144].rearrange("p (n b) -> p n b", n=32)
    vAi = b1f[:, 6144:8192].rearrange("p (n b) -> p n b", n=32)
    vCr = b1f[:, 10240:12288]
    vCn = b1f[:, 12288:14336]
    bstg_r = p.B2[:, 28672:30720]
    bstg_i = p.B2[:, 30720:32768]
    pr = p.S[4][:, 0:32]
    pi_ = p.S[4][:, 32:64]
    sa = p.S[4][:, 64:96]
    sb_ = p.S[4][:, 96:128]
    l64r = p.S[5][:, 0:32]
    l64i = p.S[5][:, 32:64]

    def sq_pow():
        A_("vector", lambda eng: eng.tensor_tensor(sa, pr, pr, ALU.mult), ("S4",), ("S4",))
        A_("vector", lambda eng: eng.tensor_tensor(sb_, pi_, pi_, ALU.mult), ("S4",), ("S4",))
        A_("vector", lambda eng: eng.tensor_tensor(pi_, pr, pi_, ALU.mult), ("S4",), ("S4",))
        A_("vector", lambda eng: eng.tensor_scalar_mul(pi_, pi_, 2.0), ("S4",), ("S4",))
        A_("vector", lambda eng: eng.tensor_sub(pr, sa, sb_), ("S4",), ("S4",))

    def powtab(vR, vI, base_r, base_i, rdk):
        A_("vector", lambda eng, vR=vR: eng.memset(vR[:, :, 0:1], 1.0), rdk, ("BIG2",))
        A_("vector", lambda eng, vI=vI: eng.memset(vI[:, :, 0:1], 0.0), rdk, ("BIG2",))
        A_("vector", lambda eng, vR=vR, base_r=base_r: eng.tensor_copy(
            vR[:, :, 1:2], base_r.unsqueeze(2)), rdk, ("BIG2",))
        A_("vector", lambda eng, vI=vI, base_i=base_i: eng.tensor_copy(
            vI[:, :, 1:2], base_i.unsqueeze(2)), rdk, ("BIG2",))
        A_("vector", lambda eng, base_r=base_r: eng.tensor_copy(pr, base_r), rdk, ("S4",))
        A_("vector", lambda eng, base_i=base_i: eng.tensor_copy(pi_, base_i), rdk, ("S4",))
        m = 2
        while m < 64:
            sq_pow()
            mm = m
            prb = pr.unsqueeze(2).to_broadcast([128, 32, mm])
            pib = pi_.unsqueeze(2).to_broadcast([128, 32, mm])
            s1v = b1f[:, 8192:8192 + 32 * mm].rearrange("p (n b) -> p n b", n=32)
            s2v = b1f[:, 9216:9216 + 32 * mm].rearrange("p (n b) -> p n b", n=32)
            A_("vector", lambda eng, vR=vR, prb=prb, s1v=s1v, mm=mm: eng.tensor_tensor(
                s1v, vR[:, :, 0:mm], prb, ALU.mult), ("BIG2", "S4"), ("BIG2",))
            A_("vector", lambda eng, vI=vI, pib=pib, s2v=s2v, mm=mm: eng.tensor_tensor(
                s2v, vI[:, :, 0:mm], pib, ALU.mult), ("BIG2", "S4"), ("BIG2",))
            A_("vector", lambda eng, vR=vR, s1v=s1v, s2v=s2v, mm=mm: eng.tensor_tensor(
                vR[:, :, mm:2 * mm], s1v, s2v, ALU.subtract), ("BIG2",), ("BIG2",))
            A_("vector", lambda eng, vR=vR, pib=pib, s1v=s1v, mm=mm: eng.tensor_tensor(
                s1v, vR[:, :, 0:mm], pib, ALU.mult), ("BIG2", "S4"), ("BIG2",))
            A_("vector", lambda eng, vI=vI, prb=prb, s2v=s2v, mm=mm: eng.tensor_tensor(
                s2v, vI[:, :, 0:mm], prb, ALU.mult), ("BIG2", "S4"), ("BIG2",))
            A_("vector", lambda eng, vI=vI, s1v=s1v, s2v=s2v, mm=mm: eng.tensor_tensor(
                vI[:, :, mm:2 * mm], s1v, s2v, ALU.add), ("BIG2",), ("BIG2",))
            m *= 2

    for t in range(NF):
        powtab(vBr, vBi, lr[:, t * 32:(t + 1) * 32], li[:, t * 32:(t + 1) * 32], ("F8",))
        A_("vector", lambda eng: eng.tensor_copy(bstg_r, b1f[:, 0:2048]), ("BIG2",), ("BIG2",))
        A_("vector", lambda eng: eng.tensor_copy(bstg_i, b1f[:, 2048:4096]), ("BIG2",), ("BIG2",))
        for nm, stg in (("l2_Btr", bstg_r), ("l2_Bti", bstg_i)):
            def bst(eng, nm=nm, stg=stg, t=t):
                return eng.dma_start(
                    out=p.dram[nm][128 * t:128 * t + 128, :], in_=stg)
            S.add("sdma", bst, reads=("BIG2",), writes=(nm,), tag="bst")
        sq_pow()
        A_("vector", lambda eng: eng.tensor_copy(l64r, pr), ("S4",), ("S5",))
        A_("vector", lambda eng: eng.tensor_copy(l64i, pi_), ("S4",), ("S5",))
        powtab(vAr, vAi, l64r, l64i, ("S5",))
        cfrb = cfr[:, t * 32:(t + 1) * 32].unsqueeze(2).to_broadcast([128, 32, 64])
        cfib = cfi[:, t * 32:(t + 1) * 32].unsqueeze(2).to_broadcast([128, 32, 64])
        tX = b1f[:, 8192:10240].rearrange("p (n b) -> p n b", n=32)
        tY = b1f[:, 0:2048].rearrange("p (n b) -> p n b", n=32)
        v3Cr = vCr.rearrange("p (n b) -> p n b", n=32)
        v3Cn = vCn.rearrange("p (n b) -> p n b", n=32)
        A_("vector", lambda eng, cfrb=cfrb: eng.tensor_tensor(tX, vAr, cfrb, ALU.mult),
           ("BIG2", "F8"), ("BIG2",))
        A_("vector", lambda eng, cfib=cfib: eng.tensor_tensor(tY, vAi, cfib, ALU.mult),
           ("BIG2", "F8"), ("BIG2",))
        A_("vector", lambda eng, v3Cr=v3Cr: eng.tensor_tensor(v3Cr, tX, tY, ALU.add),
           ("BIG2",), ("BIG2",))
        A_("vector", lambda eng, cfib=cfib: eng.tensor_tensor(tX, vAr, cfib, ALU.mult),
           ("BIG2", "F8"), ("BIG2",))
        A_("vector", lambda eng, cfrb=cfrb: eng.tensor_tensor(tY, vAi, cfrb, ALU.mult),
           ("BIG2", "F8"), ("BIG2",))
        A_("vector", lambda eng, v3Cn=v3Cn: eng.tensor_tensor(v3Cn, tX, tY, ALU.subtract),
           ("BIG2",), ("BIG2",))
        A_("vector", lambda eng: eng.tensor_copy(bstg_r, vCr), ("BIG2",), ("BIG2",))
        A_("vector", lambda eng: eng.tensor_copy(bstg_i, vCn), ("BIG2",), ("BIG2",))
        for cof, stg in ((0, bstg_r), (2048, bstg_i)):
            def cst(eng, cof=cof, stg=stg, t=t):
                return eng.dma_start(
                    out=p.dram["l2_CAc"][128 * t:128 * t + 128, cof:cof + 2048], in_=stg)
            S.add("sdma", cst, reads=("BIG2",), writes=("l2_CAc",), tag="cst")

    cabd = p.B2.ap()[:, 0:512]
    A_("vector", lambda eng: eng.memset(cabd, 0.0), (), ("BIG2",))
    for tq in range(256):
        ch0 = 4 * tq
        brs = p.B2.ap()[:, 512:576]
        bis = p.B2.ap()[:, 576:640]
        for nm, dst in (("l2_Btr", brs), ("l2_Bti", bis)):
            def lb(eng, nm=nm, dst=dst, ch0=ch0):
                return eng.dma_start(
                    out=dst,
                    in_=p.dram[nm][ch0:ch0 + 4, :].rearrange("c (n b) -> (c n) b", n=32))
            S.add("adma", lb, reads=(nm,), writes=("BIG2",), tag="lb")
        for q in range(4):
            def sc(eng, q=q, ch0=ch0):
                dstv = p.B2.ap()[32 * q:32 * q + 32, 0:512].rearrange(
                    "p (pl x) -> p pl x", pl=2)[:, :, 64 * q:64 * q + 64]
                return eng.dma_start(
                    out=dstv,
                    in_=p.dram["l2_CAc"][ch0 + q:ch0 + q + 1, :].rearrange(
                        "c (pl n a) -> n (c pl) a", pl=2, n=32))
            S.add("adma" if q % 2 == 0 else "gdma", sc, reads=("l2_CAc",),
                  writes=("BIG2",), tag="casc")
        kb = tq % 2
        ps = p.bank(kb)
        kk = bkey(kb)
        mms = [(cabd[:, 0:128], brs, 0), (cabd[:, 256:384], bis, 0),
               (cabd[:, 128:256], brs, 64), (cabd[:, 384:512], bis, 64)]
        for j, (lh, rh, of) in enumerate(mms):
            def km(eng, lh=lh, rh=rh, of=of, j=j, ps=ps):
                return eng.matmul(ps[:, of:of + 64], lh, rh,
                                  start=(j == 0), stop=(j == 3))
            wk = (kk + "@acc", kk) if j == 0 else ((kk,) if j == 3 else (kk + "@acc",))
            S.add("tensor", km, reads=("BIG2",), writes=wk, tag="km")
        ev = p.F[8][:, 0:128] if tq % 2 == 0 else p.F[8][:, 128:256]
        evk = "F8"
        A_("vector", lambda eng, ev=ev, ps=ps, kk=kk: eng.tensor_copy(ev, ps[:, 0:128]),
           (kk,), (evk,))
        for hp in range(2):
            def kst(eng, hp=hp, ev=ev, ch0=ch0):
                dstv = p.dram["l2_kt"][ch0:ch0 + 4, :].rearrange(
                    "(c u) m -> u c m", u=2)[hp].rearrange("c (a b) -> a c b", b=64)
                return eng.dma_start(
                    out=dstv,
                    in_=ev[64 * hp:64 * hp + 64, :].rearrange("p (h b) -> p h b", h=2))
            S.add("gdma", kst, reads=(evk,), writes=("l2_kt",), tag="kst")


def phases_fused():
    return [
        lambda p: emit_ln(p, "x0", "xn0", "xn0T", "ln0"),
        emit_l0attn,
        emit_l0ffn,
        lambda p: emit_ln(p, "x1", "xn1", "xn1T", "ln1"),
        emit_l2khat_full,
        emit_l1attn,
        emit_l1ffn,
        lambda p: emit_ln(p, "x2", "xn2", "xn2T", "ln2"),
        emit_l2inproj,
        lambda p: emit_l2fft(p, part="fused"),
        emit_l2glu,
        emit_l2ffn_a,
        emit_l2ffn_b,
        lambda p: emit_ln(p, "x3", "xn3", "xn3T", "ln3"),
        emit_l3a,
        emit_l3b,
        emit_l3c,
    ]


def specs_fused():
    drop = {"l2_kin", "l2_bd"}
    out = []
    for s in full_dram_specs():
        n = s[0]
        if n in drop:
            continue
        if n in ("l2_ldt", "l2_lar", "l2_aim", "l2_Cr", "l2_Ci"):
            out.append((n, [128, 256], F32, "ExternalInput"))
        elif n == "l2_kt":
            out.append((n, [1024, T], F32, "Internal"))
        elif n == "l2_kout":
            out.append((n, [1024, 8448], F32, "Internal"))
        else:
            out.append(s)
    out += [("l2_Btr", [1024, 2048], BF16, "Internal"),
            ("l2_Bti", [1024, 2048], BF16, "Internal"),
            ("l2_CAc", [1024, 4096], BF16, "Internal")]
    return out


def kernel(**inputs):
    if "ncf" not in _NC_CACHE:
        _NC_CACHE["ncf"] = build_nc_phase(phases_fused(), specs_fused())
    ncf = _NC_CACHE["ncf"]
    maps = build_inmaps(inputs, ncf._prog.fconst_vals, fused=True)
    names = set(s[0] for s in specs_fused() if s[3] == "ExternalInput")
    ins = [{k2: v for k2, v in m.items() if k2 in names} for m in maps]
    r = run_bass_kernel_spmd(ncf, ins, core_ids=list(range(N_CORES)))
    return np.stack([np.asarray(r.results[c]["y"]) for c in range(N_CORES)])


# version 17 (final)
# speedup vs baseline: 1.0449x; 1.0449x over previous
import math
import contextlib
import numpy as np
import ml_dtypes

import concourse.bass as bass
import concourse.mybir as mybir
from concourse.bass_utils import run_bass_kernel_spmd

F32 = mybir.dt.float32
BF16 = mybir.dt.bfloat16
AF = mybir.ActivationFunctionType
ALU = mybir.AluOpType
AX = mybir.AxisListType

T = 4096
D = 1024
NT = T // 128
NF = D // 128
BF = ml_dtypes.bfloat16
TWO_PI = float(2.0 * np.pi)
MAGIC = 12582912.0
PI_C = 3.1415925
N_CORES = 8

UNITS = ("tensor", "vector", "scalar", "gpsimd", "sync", "sdma", "vdma", "gdma", "adma")
FREE_SAME = {"tensor", "sdma", "vdma", "gdma", "adma", "sync"}
DMA_UNITS = {"sdma", "vdma", "gdma", "adma"}
DMA_ISSUER = {"sdma": "sync", "vdma": "vector", "gdma": "gpsimd", "adma": "scalar"}


class Op:
    __slots__ = ("unit", "fn", "reads", "writes", "idx", "uidx", "sem_val", "waits", "tag", "_deps", "_carries", "sem_key", "issue_wait")

    def __init__(self, unit, fn, reads, writes, idx, tag):
        self.unit = unit
        self.fn = fn
        self.reads = reads
        self.writes = writes
        self.idx = idx
        self.uidx = 0
        self.sem_val = None
        self.waits = []
        self.tag = tag
        self._deps = []
        self._carries = False


class Sched:
    def __init__(self, prog):
        self.p = prog
        self.ops = []
        self.writers = {}
        self.readers = {}

    def add(self, unit, fn, reads=(), writes=(), tag=""):
        op = Op(unit, fn, tuple(reads), tuple(writes), len(self.ops), tag)
        deps = []
        for b in op.reads:
            for w in self.writers.get(b, ()):
                deps.append((w, 0))
        for b in op.writes:
            for w in self.writers.get(b, ()):
                deps.append((w, 2))
            for r in self.readers.get(b, ()):
                deps.append((r, 1))
        need = {}
        for d, kind in deps:
            if d is op:
                continue
            if d.unit == op.unit and d.unit in FREE_SAME:
                if not (kind in (0, 1) and d.unit in DMA_UNITS):
                    continue
            k = d.idx if d.unit in DMA_UNITS else d.unit
            if k not in need or need[k].idx < d.idx:
                need[k] = d
        op._deps = list(need.values())
        for b in op.reads:
            self.readers.setdefault(b, []).append(op)
        for b in op.writes:
            ws = self.writers.setdefault(b, [])
            ws[:] = [w for w in ws if w.unit in DMA_UNITS or w.unit != op.unit]
            ws.append(op)
            self.readers[b] = []
        self.ops.append(op)
        return op

    def dma(self, queue, out, in_, reads=(), writes=(), tag="", **kw):
        def fn(eng):
            return eng.dma_start(out=out, in_=in_, **kw)
        return self.add(queue, fn, reads, writes, tag)

    ROT = 12

    def finalize(self):
        for op in self.ops:
            if op.unit in DMA_UNITS:
                op._carries = True
            for d in op._deps:
                d._carries = True
        last = {}
        for op in self.ops:
            last[op.unit] = op
        for op in last.values():
            op._carries = True
        cnt = {}
        dma_n = {u: 0 for u in DMA_UNITS}
        for op in self.ops:
            if op.unit in DMA_UNITS:
                j = dma_n[op.unit]
                dma_n[op.unit] = j + 1
                key = (op.unit, j % self.ROT)
                op.sem_key = key
                op.issue_wait = (key, 16 * (j // self.ROT)) if j >= self.ROT else None
                cnt[key] = cnt.get(key, 0) + 16
                op.sem_val = cnt[key]
            elif op._carries:
                key = (op.unit, 0)
                op.sem_key = key
                op.issue_wait = None
                cnt[key] = cnt.get(key, 0) + 1
                op.sem_val = cnt[key]
            else:
                op.sem_key = None
                op.issue_wait = None
        for op in self.ops:
            wmax = {}
            for d in op._deps:
                assert d.sem_val is not None, (op.tag, d.tag)
                if wmax.get(d.sem_key, -1) < d.sem_val:
                    wmax[d.sem_key] = d.sem_val
            op.waits = sorted(wmax.items())
        return cnt

    def replay(self):
        p = self.p
        cnt = self.finalize()
        sems = {k: p.sem("q_%s_%d" % k) for k in cnt}
        per_unit = {u: [o for o in self.ops if o.unit == u] for u in UNITS}
        blk = p.block
        final = dict(cnt)

        def emit_stream(eng, allops, tail=False):
            known = {}

            def w(key, v):
                if known.get(key, 0) < v:
                    eng.wait_ge(sems[key].h, v)
                    known[key] = v

            for op in allops:
                if op.issue_wait is not None:
                    w(*op.issue_wait)
                for (key, v) in op.waits:
                    w(key, v)
                try:
                    inst = op.fn(eng)
                except Exception as e:
                    raise RuntimeError("op tag=%s unit=%s idx=%d: %s" % (op.tag, op.unit, op.idx, e)) from e
                if op.sem_val is not None:
                    d = 16 if op.unit in DMA_UNITS else 1
                    inst.then_inc(sems[op.sem_key].h, d)
            if tail:
                for key, v in final.items():
                    w(key, v)

        for u in ("tensor", "vector", "scalar", "gpsimd", "sync"):
            qu = [q for q, e in DMA_ISSUER.items() if e == u]
            allops = per_unit[u] + [o for q in qu for o in per_unit[q]]
            allops.sort(key=lambda o: o.idx)
            getattr(blk, u)(lambda eng, a=allops, t=(u == "sync"): emit_stream(eng, a, t))
        return sems, final


class Prog:
    def __init__(self, nc):
        self.nc = nc
        self.es = contextlib.ExitStack()
        self.dram = {}
        self.sems = {}
        self.block = None
        self.sched = Sched(self)
        self._sb_names = set()
        self.fconst_vals = []
        self.fconst_sb = None

    def sem(self, name):
        if name not in self.sems:
            class _C:
                pass
            c = _C()
            c.h = self.es.enter_context(self.nc.semaphore(name))
            self.sems[name] = c
        return self.sems[name]

    def dt(self, name, shape, dtype, kind, addr_space=None):
        if addr_space is not None:
            ap = self.nc.dram_tensor(name, list(shape), dtype, kind=kind,
                                     addr_space=addr_space).ap()
        else:
            ap = self.nc.dram_tensor(name, list(shape), dtype, kind=kind).ap()
        self.dram[name] = ap
        return ap

    def sb(self, name, shape, dtype):
        assert name not in self._sb_names, name
        self._sb_names.add(name)
        return self.es.enter_context(self.nc.sbuf_tensor(name, list(shape), dtype))

    def fconst(self, v):
        v = float(v)
        if v not in self.fconst_vals:
            assert len(self.fconst_vals) < 64
            i = len(self.fconst_vals)
            self.fconst_vals.append(v)
            self.nc.const_aps.register(F32, v, self.fconst_sb[:, i:i + 1])
        return v

    def ps(self, name, shape, dtype=F32):
        assert name not in self._sb_names, name
        self._sb_names.add(name)
        return self.es.enter_context(self.nc.psum_tensor(name, list(shape), dtype))

    def init_psum_pool(self):
        self.pool = [self.ps("PSP%d" % j, [128, 1024], F32) for j in range(4)]

    def bankpair(self, j):
        return self.pool[j]

    def bank(self, i):
        return self.pool[i // 2][:, 512 * (i % 2):512 * (i % 2) + 512]

    def bank_bf(self, i, cols=128):
        return self.pool[i // 2].ap().bitcast(BF16)[:, 1024 * (i % 2):1024 * (i % 2) + cols]


def bkey(i):
    return "PS%d%s" % (i // 2, "ab"[i % 2])



NW = 1
NFP = 9
NBP = 6
NSP = 6
NRP = 3


def setup_arena(p):
    p.Wsl = [p.sb("W%d" % i, [128, 8192], BF16) for i in range(NW)]
    p.B1 = p.sb("BIG1", [128, 32768], BF16)
    p.B2 = p.sb("BIG2", [128, 33280], BF16)
    p.F = [p.sb("F%d" % i, [128, 1024], F32) for i in range(NFP)]
    p.B = [p.sb("B%d" % i, [128, 1024], BF16) for i in range(NBP)]
    p.S = [p.sb("S%d" % i, [128, 128], F32) for i in range(NSP)]
    p.R = [p.sb("R%d" % i, [1, 512], F32) for i in range(NRP)]


def big1(p, q):
    return p.B1[:, q * 8192:(q + 1) * 8192], "BIG1"


def big2(p, q):
    return p.B2[:, q * 8320:(q + 1) * 8320], "BIG2"


def load_weight_bf16(p, S, dname, dst_ap, dst_key, rows, cols):
    nf = rows // 128
    src = p.dram[dname].rearrange("(f p) o -> p f o", p=128)
    S.dma("gdma", dst_ap.rearrange("p (f o) -> p f o", f=nf), src,
          reads=(dname,), writes=(dst_key,), tag="ldw_" + dname)


def mm_acc(p, S, tag, psum, psum_key, lhsT_blocks, rhs_blocks, extra_reads=()):
    nf = len(lhsT_blocks)
    for f in range(nf):
        la, lk = lhsT_blocks[f]
        ra, rk = rhs_blocks[f]

        def fn(eng, la=la, ra=ra, f=f):
            return eng.matmul(psum, la, ra, start=(f == 0), stop=(f == nf - 1))
        if nf == 1:
            w = (psum_key,)
        elif f == 0:
            w = (psum_key + "@acc", psum_key)
        elif f == nf - 1:
            w = (psum_key,)
        else:
            w = (psum_key + "@acc",)
        S.add("tensor", fn, reads=(lk, rk) + tuple(extra_reads), writes=w,
              tag=tag + "mm%d" % f)


def transposes(p, S, src_sb, src_key, dst_fn, tag, nf=NF):
    for f in range(nf):
        ps = p.bank_bf(6 + f % 2)
        pk = bkey(6 + f % 2)

        def tfn(eng, f=f, ps=ps):
            return eng.transpose(ps, src_sb[:, bass.ts(f, 128)], p.ident_sb[:, :])
        S.add("tensor", tfn, reads=(src_key, "ident"), writes=(pk,), tag=tag + "t%d" % f)
        da, dk = dst_fn(f)

        def efn(eng, da=da, ps=ps):
            return eng.tensor_copy(da, ps)
        S.add("vector", efn, reads=(pk,), writes=(dk,), tag=tag + "e%d" % f)


def transposes2(p, S, src_ap, src_key, dst_fn, tag, nf=NF):
    for f in range(nf):
        ps = p.bank_bf(6 + f % 2) if False else None
    for f in range(nf):
        ps = p.pool[0].ap().bitcast(BF16)[:, (f % 2) * 1024:(f % 2) * 1024 + 128]
        pk = bkey(f % 2)

        def tfn(eng, f=f, ps=ps):
            return eng.transpose(ps, src_ap[:, bass.ts(f, 128)], p.ident_sb[:, :])
        S.add("tensor", tfn, reads=(src_key, "ident"), writes=(pk,), tag=tag + "t%d" % f)
        da, dk = dst_fn(f)

        def efn(eng, da=da, ps=ps):
            return eng.tensor_copy(da, ps)
        S.add("vector", efn, reads=(pk,), writes=(dk,), tag=tag + "e%d" % f)


def stream_x_tiles(p, S, dname, bufs, keyfmt, n=NT, tag="ld"):
    for i in range(n):
        b = i % len(bufs)
        S.dma("sdma", bufs[b][:, :], p.dram[dname][bass.ts(i, 128), :],
              reads=(dname,), writes=(keyfmt % b,), tag=tag + "%d" % i)
        yield i, b


def emit_ln(p, src, xn_d, xnT_d, tag):
    S = p.sched
    x = p.dram[src]
    xa = p.F[0:3]
    xak = ["F0", "F1", "F2"]
    junk = [(p.B[0], "B0"), (p.B[1], "B1")]
    sums, sqs = p.S[0][:, 0:NT], p.S[1][:, 0:NT]
    mcol, varc = p.S[2][:, 0:NT], p.S[3][:, 0:NT]
    rstd, bcol = p.S[4][:, 0:NT], p.S[5][:, 0:NT]
    xnf = [(p.F[3], "F3"), (p.F[4], "F4")]
    xnb = [(p.B[2], "B2"), (p.B[3], "B3")]
    xtv = [(p.B[4], "B4"), (p.B[5], "B5")]

    for i, b in stream_x_tiles(p, S, src, xa, "F%d", tag=tag + "l1_"):
        def sq(eng, i=i, b=b, j=i % 2):
            return eng.activation(junk[j][0][:, :], xa[b][:, :], AF.Square,
                                  accum_out=sqs[:, i:i + 1])
        S.add("scalar", sq, reads=(xak[b],), writes=(junk[i % 2][1], "S1"))

        def rs(eng, i=i, b=b):
            return eng.tensor_reduce(sums[:, i:i + 1], xa[b][:, :], AX.X, ALU.add)
        S.add("vector", rs, reads=(xak[b],), writes=("S0",))

    S.add("vector", lambda eng: eng.tensor_scalar_mul(mcol, sums, 1.0 / D),
          reads=("S0",), writes=("S2",))
    S.add("vector", lambda eng: eng.tensor_tensor(varc, mcol, mcol, ALU.mult),
          reads=("S2",), writes=("S3",))
    S.add("vector", lambda eng: eng.scalar_tensor_tensor(varc, sqs, 1.0 / D, varc,
                                                         ALU.mult, ALU.subtract),
          reads=("S1", "S3"), writes=("S3",))
    p.fconst(1e-5)
    p.fconst(-0.5)
    S.add("scalar", lambda eng: eng.activation(varc, varc, AF.Ln, bias=1e-5),
          reads=("S3", "fconsts"), writes=("S3",))
    S.add("scalar", lambda eng: eng.activation(rstd, varc, AF.Exp, scale=-0.5),
          reads=("S3", "fconsts"), writes=("S4",))
    S.add("vector", lambda eng: eng.tensor_tensor(bcol, mcol, rstd, ALU.mult),
          reads=("S2", "S4"), writes=("S5",))
    S.add("vector", lambda eng: eng.tensor_scalar_mul(bcol, bcol, -1.0),
          reads=("S5",), writes=("S5",))

    for i, b in stream_x_tiles(p, S, src, xa, "F%d", tag=tag + "l2_"):
        def ap_f(eng, i=i, b=b):
            return eng.tensor_scalar(xnf[i % 2][0][:, :], xa[b][:, :],
                                     mcol[:, i:i + 1], rstd[:, i:i + 1],
                                     ALU.subtract, ALU.mult)
        S.add("vector", ap_f, reads=(xak[b], "S2", "S4"), writes=(xnf[i % 2][1],))
        S.dma("sdma", p.dram[xn_d][bass.ts(i, 128), :], xnf[i % 2][0][:, :],
              reads=(xnf[i % 2][1],), writes=(xn_d,))

        def ap_b(eng, i=i, b=b):
            return eng.activation(xnb[i % 2][0][:, :], xa[b][:, :], AF.Identity,
                                  bias=bcol[:, i:i + 1], scale=rstd[:, i:i + 1])
        S.add("scalar", ap_b, reads=(xak[b], "S5", "S4"), writes=(xnb[i % 2][1],))

        def dstf(f, i2=i % 2):
            return xtv[i2][0][:, bass.ts(f, 128)], xtv[i2][1]
        transposes(p, S, xnb[i % 2][0], xnb[i % 2][1], dstf, tag + "tr%d_" % i)
        dst = p.dram[xnT_d].rearrange("(f p) t -> p f t", p=128)[:, :, bass.ts(i, 128)]
        S.dma("sdma", dst, xtv[i % 2][0].ap().rearrange("p (f c) -> p f c", c=128),
              reads=(xtv[i % 2][1],), writes=(xnT_d,))


def emit_l0ffn(p):
    S = p.sched
    xn = p.dram["xn0"]
    wr, wrk = big1(p, 0)
    wk, wkk = big1(p, 1)
    wv, wvk = big1(p, 2)
    load_weight_bf16(p, S, "l0f_WrT", wr, wrk, D, D)
    load_weight_bf16(p, S, "l0f_WkT", wk, wkk, D, D)
    load_weight_bf16(p, S, "l0f_WvT", wv, wvk, D, D)
    mur, muk = p.F[7], p.F[3]
    S.dma("sdma", mur[:, :], p.dram["l0f_mur"][:, :], reads=("l0f_mur",), writes=("F7",))
    S.dma("sdma", muk[:, :], p.dram["l0f_muk"][:, :], reads=("l0f_muk",), writes=("F3",))

    xc = [(p.F[0], "F0"), (p.F[1], "F1")]
    xp = [(p.F[2], "F2"), (p.F[2], "F2")]
    t1 = [(p.F[4], "F4")]
    xrf = [(p.F[5], "F5")]
    xkf = [(p.F[6], "F6")]
    xrb = [(p.B[0], "B0"), (p.B[0], "B0")]
    xkb = [(p.B[1], "B1"), (p.B[1], "B1")]
    xrT = [(p.B[2], "B2")]
    xkT = [(p.B[3], "B3")]
    kab = [(p.B[4], "B4")]
    kaT = [(p.B[5], "B5")]
    psR = p.bankpair(0)
    psK = p.bankpair(1)
    psV = p.bankpair(2)

    for i in range(NT):
        b = i % 2
        S.dma("sdma", xc[b][0][:, :], xn[bass.ts(i, 128), :], reads=("xn0",),
              writes=(xc[b][1],), tag="f0lx")
        if i == 0:
            S.add("vector", lambda eng: eng.memset(xp[0][0][0:1, :], 0.0),
                  writes=(xp[0][1],))
            S.dma("sdma", xp[0][0][1:128, :], xn[0:127, :], reads=("xn0",),
                  writes=(xp[0][1],))
        else:
            S.dma("sdma", xp[b][0][:, :], xn[128 * i - 1:128 * i + 127, :],
                  reads=("xn0",), writes=(xp[b][1],))

        S.add("vector", lambda eng, b=b: eng.tensor_sub(t1[0][0][:, :], xc[b][0][:, :], xp[b][0][:, :]),
              reads=(xc[b][1], xp[b][1]), writes=(t1[0][1],))
        S.add("vector", lambda eng, b=b: eng.tensor_mul(xrf[0][0][:, :], t1[0][0][:, :], mur[:, :]),
              reads=(t1[0][1], "F7"), writes=(xrf[0][1],))
        S.add("vector", lambda eng, b=b: eng.tensor_add(xrf[0][0][:, :], xrf[0][0][:, :], xp[b][0][:, :]),
              reads=(xrf[0][1], xp[b][1]), writes=(xrf[0][1],))
        S.add("vector", lambda eng, b=b: eng.tensor_mul(xkf[0][0][:, :], t1[0][0][:, :], muk[:, :]),
              reads=(t1[0][1], "F3"), writes=(xkf[0][1],))
        S.add("vector", lambda eng, b=b: eng.tensor_add(xkf[0][0][:, :], xkf[0][0][:, :], xp[b][0][:, :]),
              reads=(xkf[0][1], xp[b][1]), writes=(xkf[0][1],))
        S.add("scalar", lambda eng, b=b: eng.copy(xrb[b][0][:, :], xrf[0][0][:, :]),
              reads=(xrf[0][1],), writes=(xrb[b][1],))
        S.add("scalar", lambda eng, b=b: eng.copy(xkb[b][0][:, :], xkf[0][0][:, :]),
              reads=(xkf[0][1],), writes=(xkb[b][1],))
        transposes(p, S, xrb[b][0], xrb[b][1],
                   lambda f: (xrT[0][0][:, bass.ts(f, 128)], xrT[0][1]), "f0tA")
        transposes(p, S, xkb[b][0], xkb[b][1],
                   lambda f: (xkT[0][0][:, bass.ts(f, 128)], xkT[0][1]), "f0tB")
        for oc in range(2):
            mm_acc(p, S, "f0R%d" % oc, psR[:, bass.ts(oc, 512)], bkey(0 + oc),
                   [(xrT[0][0][:, bass.ts(f, 128)], xrT[0][1]) for f in range(NF)],
                   [(wr[:, f * D + oc * 512: f * D + oc * 512 + 512], wrk) for f in range(NF)])
            mm_acc(p, S, "f0K%d" % oc, psK[:, bass.ts(oc, 512)], bkey(2 + oc),
                   [(xkT[0][0][:, bass.ts(f, 128)], xkT[0][1]) for f in range(NF)],
                   [(wk[:, f * D + oc * 512: f * D + oc * 512 + 512], wkk) for f in range(NF)])
        rsgb = xrf[0]
        S.add("scalar", lambda eng: eng.activation(rsgb[0][:, :], psR[:, :], AF.Sigmoid),
              reads=(bkey(0), bkey(1)), writes=(rsgb[1],), tag="f0sig")
        relb = xkf[0]
        S.add("vector", lambda eng: eng.tensor_relu(relb[0][:, :], psK[:, :]),
              reads=(bkey(2), bkey(3)), writes=(relb[1],), tag="f0relu")
        S.add("scalar", lambda eng, b=b: eng.activation(kab[0][0][:, :], relb[0][:, :], AF.Square),
              reads=(relb[1],), writes=(kab[0][1],), tag="f0sq")
        transposes(p, S, kab[0][0], kab[0][1],
                   lambda f: (kaT[0][0][:, bass.ts(f, 128)], kaT[0][1]), "f0tC")
        for oc in range(2):
            mm_acc(p, S, "f0V%d" % oc, psV[:, bass.ts(oc, 512)], bkey(4 + oc),
                   [(kaT[0][0][:, bass.ts(f, 128)], kaT[0][1]) for f in range(NF)],
                   [(wv[:, f * D + oc * 512: f * D + oc * 512 + 512], wvk) for f in range(NF)])
        gbu = t1[0]
        S.add("vector", lambda eng: eng.tensor_mul(gbu[0][:, :], rsgb[0][:, :], psV[:, :]),
              reads=(bkey(4), bkey(5), rsgb[1]), writes=(gbu[1],), tag="f0rv")
        S.add("scalar", lambda eng: eng.activation(gbu[0][:, :], gbu[0][:, :], AF.Silu),
              reads=(gbu[1],), writes=(gbu[1],), tag="f0silu")
        s0t = xp[b]
        x0t = xc[b]
        S.dma("sdma", s0t[0][:, :], p.dram["s0"][bass.ts(i, 128), :],
              reads=("s0",), writes=(s0t[1],), tag="f0lds")
        S.dma("sdma", x0t[0][:, :], p.dram["x0"][bass.ts(i, 128), :],
              reads=("x0",), writes=(x0t[1],), tag="f0ldx0")
        x1f = rsgb
        S.add("vector", lambda eng, x1f=x1f, s0t=s0t: eng.tensor_mul(
            x1f[0][:, :], s0t[0][:, :], gbu[0][:, :]),
              reads=(s0t[1], gbu[1]), writes=(x1f[1],), tag="f0sg")
        S.add("vector", lambda eng, x1f=x1f, x0t=x0t: eng.tensor_add(
            x1f[0][:, :], x1f[0][:, :], x0t[0][:, :]),
              reads=(x1f[1], x0t[1]), writes=(x1f[1],), tag="f0res")
        S.dma("sdma", p.dram["x1"][bass.ts(i, 128), :], x1f[0][:, :],
              reads=(x1f[1],), writes=("x1",), tag="f0stx1")


def _tile128(v):
    v = np.asarray(v, np.float32)
    return np.ascontiguousarray(np.broadcast_to(v[None, :], (128, v.shape[0])))


def build_consts():
    c = {}
    c["ident"] = np.eye(128, dtype=BF)
    c["tri128"] = np.tril(np.ones((128, 128), np.float32)).T.astype(BF)
    c["jrow16"] = (-np.arange(16, dtype=np.float32) / 16.0)[None, :]
    c["poscols"] = np.arange(T, dtype=np.float32).reshape(NT, 128).T.copy()
    c["ones1"] = np.ones((1, 128), dtype=BF)
    pos = np.arange(T, dtype=np.float32)
    hi = np.floor(pos / 16.0) * 16.0
    c["posrows2"] = np.concatenate([hi, pos - hi])[None, :].astype(BF)
    inv32 = 1.0 / 10000.0 ** (np.arange(0, 64, 2, dtype=np.float32) / 64.0)
    fr = np.outer(pos, inv32)
    c["cs32"] = np.concatenate([np.cos(fr), np.sin(fr)], 1).astype(np.float32)
    mk = np.zeros((128, 5 * 512), np.float32)
    for mi in range(5):
        off = 128 * (1 - mi)
        kl = np.arange(128)[:, None]
        ql = np.arange(512)[None, :]
        dd = off + ql - kl
        mk[:, mi * 512:(mi + 1) * 512] = ((dd >= 0) & (dd <= 64)).astype(np.float32)
    c["l1_masks"] = mk.astype(BF)
    c["triexc"] = np.tril(np.ones((32, 32), np.float32), -1).T.astype(BF)
    ind = np.zeros((128, 1024), np.float32)
    for i2 in range(NT):
        ind[:, 32 * i2 + i2] = 1.0
    c["indcols"] = ind.astype(BF)
    m2g, k2g = np.meshgrid(np.arange(64), np.arange(128), indexing="ij")
    ang = -2 * np.pi * m2g * k2g / 128.0
    c["fF2r"] = np.cos(ang).astype(BF)
    c["fF2i"] = np.sin(ang).astype(BF)
    f1 = np.zeros((128, 66), np.complex128)
    m1g, k1g = np.meshgrid(np.arange(64), np.arange(33), indexing="ij")
    f1blk = np.exp(-2j * np.pi * m1g * k1g / 64.0)
    f1[0:64, 0:33] = f1blk
    f1[64:128, 33:66] = f1blk
    c["fF1r"] = f1.real.astype(BF)
    c["fF1i"] = f1.imag.astype(BF)
    c["fF1n"] = (-f1.imag).astype(BF)
    e1 = np.zeros((66, 128), np.complex128)
    e1blk = np.exp(2j * np.pi * k1g.T * m1g.T / 64.0)
    e1[0:33, 0:64] = e1blk
    e1[33:66, 64:128] = e1blk
    c["fE1r"] = e1.real.astype(BF)
    c["fE1i"] = e1.imag.astype(BF)
    c["fE1n2"] = (-e1.imag).astype(BF)
    k2g2, m2g2 = np.meshgrid(np.arange(128), np.arange(64), indexing="ij")
    ange = 2 * np.pi * m2g2 * k2g2 / 128.0
    c["fE2r"] = (np.cos(ange) * (2.0 / 8192.0)).astype(BF)
    c["fE2n"] = (-np.sin(ange) * (2.0 / 8192.0)).astype(BF)
    m1r = np.tile(np.arange(64), 2)[:, None]
    k2r = np.arange(128)[None, :]
    angw = -2 * np.pi * m1r * k2r / 8192.0
    c["ftWr"] = np.ascontiguousarray(np.tile(np.cos(angw), (1, 4))).astype(np.float32)
    c["ftWi"] = np.ascontiguousarray(np.tile(np.sin(angw), (1, 4))).astype(np.float32)
    k2c = np.arange(128)[:, None]
    m1c = np.tile(np.arange(64), 2)[None, :]
    angv = 2 * np.pi * m1c * k2c / 8192.0
    c["ftVr"] = np.ascontiguousarray(np.tile(np.cos(angv), (1, 4))).astype(np.float32)
    c["ftVi"] = np.ascontiguousarray(np.tile(np.sin(angv), (1, 4))).astype(np.float32)
    km = np.zeros((128, 256), np.float32)
    km[:, 0] = 0.5
    km[:, 128] = 0.5
    c["l2_k32m"] = km
    c["l2_bdz"] = np.zeros((128, 8), np.float32)
    return c


def build_nc_phase(phases, dram_specs):
    nc = bass.Bass("TRN2", target_bir_lowering=False)
    p = Prog(nc)
    with p.es:
        for spec in dram_specs:
            p.dt(*spec)
        p.ident_sb = p.sb("ident_sb", [128, 128], BF16)
        p.fconst_sb = p.sb("fconst_sb", [128, 64], F32)
        p.init_psum_pool()
        setup_arena(p)
        p.block = p.es.enter_context(nc.Block())
        S = p.sched
        S.dma("sdma", p.ident_sb[:, :], p.dram["ident"][:, :],
              reads=("ident_d",), writes=("ident",))
        S.dma("sdma", p.fconst_sb[:, :], p.dram["fconsts"][:, :],
              reads=("fconsts_d",), writes=("fconsts",))
        p.ones1_sb = p.sb("ones1_sb", [1, 128], BF16)
        S.dma("sdma", p.ones1_sb[:, :], p.dram["ones1"][:, :],
              reads=("ones1_d",), writes=("ones1",))
        p.tri_sb = p.sb("tri_sb", [128, 128], BF16)
        S.dma("sdma", p.tri_sb[:, :], p.dram["tri128"][:, :],
              reads=("tri128_d",), writes=("tri",))
        for ph in phases:
            ph(p)
        S.replay()
    nc._prog = p
    return nc


def kernel(**inputs):
    raise NotImplementedError("assembled at the end")


def _ropev(buf):
    return buf.ap().rearrange("p (h d) -> p h d", h=16)


def emit_rope(p, S, unit, buf, bkeyn, scr, scrk, i):
    ct = p.F[7][:, 0:512]
    st = p.F[7][:, 512:1024]
    cs = ct[:, bass.ts(i, 16)].unsqueeze(1).to_broadcast([128, 16, 16])
    sn = st[:, bass.ts(i, 16)].unsqueeze(1).to_broadcast([128, 16, 16])
    x1 = _ropev(buf)[:, :, 32:48]
    x2 = _ropev(buf)[:, :, 48:64]
    t1 = _ropev(scr)[:, :, 0:16]
    t2 = _ropev(scr)[:, :, 16:32]
    t3 = _ropev(scr)[:, :, 32:48]
    t4 = _ropev(scr)[:, :, 48:64]
    A = lambda fn, r, w: S.add(unit, fn, reads=r, writes=w, tag="rope")
    A(lambda eng, x1=x1, cs=cs, t1=t1: eng.tensor_tensor(t1, x1, cs, ALU.mult),
      (bkeyn, "F7"), (scrk,))
    A(lambda eng, x1=x1, sn=sn, t2=t2: eng.tensor_tensor(t2, x1, sn, ALU.mult),
      (bkeyn, "F7"), (scrk,))
    A(lambda eng, x2=x2, sn=sn, t3=t3: eng.tensor_tensor(t3, x2, sn, ALU.mult),
      (bkeyn, "F7"), (scrk,))
    A(lambda eng, x2=x2, cs=cs, t4=t4: eng.tensor_tensor(t4, x2, cs, ALU.mult),
      (bkeyn, "F7"), (scrk,))
    A(lambda eng, x1=x1, t1=t1, t3=t3: eng.tensor_tensor(x1, t1, t3, ALU.subtract),
      (scrk,), (bkeyn,))
    A(lambda eng, x2=x2, t2=t2, t4=t4: eng.tensor_tensor(x2, t2, t4, ALU.add),
      (scrk,), (bkeyn,))


def emit_l0attn(p):
    S = p.sched
    nc = p.nc
    xnT = p.dram["xn0T"]
    HP = 8

    nr = p.R[0][0:1, 0:1]
    nrb = p.R[1][0:1, 0:16]
    invr = p.S[0][:, 0:16]
    jrow = p.R[2][0:1, 0:16]
    postc = p.S[1][:, 0:NT]
    scr16 = p.S[2][:, 0:16]
    scr16b = p.S[3][:, 0:16]
    S.dma("sdma", nr, p.dram["l0_nrope"][:, :], reads=("l0_nrope",), writes=("R0",))
    S.dma("sdma", jrow, p.dram["jrow16"][:, :], reads=("jrow16",), writes=("R2",))
    S.dma("sdma", postc, p.dram["poscols"][:, :], reads=("poscols",), writes=("S1",))
    MIN_L = 1.0 - 17 / 100.0
    SDT_L = (10000.0 - MIN_L * 10000.0) * 2.0
    S.add("scalar", lambda eng: eng.activation(nr, nr, AF.Sigmoid),
          reads=("R0", "fconsts"), writes=("R0",))
    S.add("vector", lambda eng: eng.tensor_scalar(nr, nr, float(SDT_L),
                                                  float(MIN_L * 10000.0), ALU.mult, ALU.add),
          reads=("R0",), writes=("R0",))
    S.add("scalar", lambda eng: eng.activation(nr, nr, AF.Ln),
          reads=("R0", "fconsts"), writes=("R0",))
    S.add("vector", lambda eng: eng.tensor_scalar_mul(nrb, jrow, nr),
          reads=("R0", "R2"), writes=("R1",))
    S.add("scalar", lambda eng: eng.activation(nrb, nrb, AF.Exp),
          reads=("R1", "fconsts"), writes=("R1",))
    SC = float(2.0 * np.pi / (T - 1))
    invs = p.R[1][0:1, 16:32]
    ivh = p.R[1].ap().bitcast(BF16)[0:1, 64:80]
    ivl = p.R[1].ap().bitcast(BF16)[0:1, 80:96]
    ivhf = p.R[1][0:1, 48:64]
    S.add("vector", lambda eng: eng.tensor_scalar_mul(invs, nrb, SC),
          reads=("R1",), writes=("R1",))
    S.add("vector", lambda eng: eng.tensor_copy(ivh, invs), reads=("R1",), writes=("R1",))
    S.add("vector", lambda eng: eng.tensor_copy(ivhf, ivh), reads=("R1",), writes=("R1",))
    S.add("vector", lambda eng: eng.tensor_sub(ivhf, invs, ivhf), reads=("R1",), writes=("R1",))
    S.add("vector", lambda eng: eng.tensor_copy(ivl, ivhf), reads=("R1",), writes=("R1",))
    p.fconst(float(np.pi / 2))
    for i in range(NT):
        pr2 = p.R[2].ap().bitcast(BF16)
        phl = p.dram["posrows2"].rearrange("o (a t) -> o a t", a=2)[0:1, :, bass.ts(i, 128)]
        S.dma("sdma", pr2[0:1, 256:512].rearrange("o (a c) -> o a c", a=2), phl,
              reads=("posrows2",), writes=("R2",))
        k = 0
        for (pa, pb) in ((pr2[0:1, 256:384], ivh), (pr2[0:1, 256:384], ivl),
                         (pr2[0:1, 384:512], ivh), (pr2[0:1, 384:512], ivl)):
            def ao(eng, pa=pa, pb=pb, k=k):
                return eng.matmul(p.bank(6)[:, 0:16], pa, pb,
                                  start=(k == 0), stop=(k == 3))
            S.add("tensor", ao, reads=("R2", "R1"),
                  writes=(bkey(6),) if k in (0, 3) else (bkey(6) + "@acc",), tag="ao")
            k += 1
        S.add("vector", lambda eng: eng.tensor_copy(scr16, p.bank(6)[:, 0:16]),
              reads=(bkey(6),), writes=("S2",))
        S.add("vector", lambda eng: eng.tensor_scalar(
            scr16b, scr16, 1.0 / TWO_PI, MAGIC, ALU.mult, ALU.add),
            reads=("S2",), writes=("S3",))
        S.add("vector", lambda eng: eng.tensor_scalar(
            scr16b, scr16b, MAGIC, None, ALU.subtract, ALU.bypass),
            reads=("S3",), writes=("S3",))
        S.add("vector", lambda eng: eng.scalar_tensor_tensor(
            scr16, scr16b, -TWO_PI, scr16, ALU.mult, ALU.add),
            reads=("S2", "S3"), writes=("S2",))
        S.add("vector", lambda eng: eng.tensor_scalar(
            scr16, scr16, PI_C, -PI_C, ALU.min, ALU.max),
            reads=("S2",), writes=("S2",))
        S.add("scalar", lambda eng, i=i: eng.activation(
            p.F[7][:, 512 + i * 16:512 + i * 16 + 16], scr16, AF.Sin),
            reads=("S2", "fconsts"), writes=("F7",))
        S.add("vector", lambda eng: eng.tensor_scalar_mul(scr16b, scr16, -1.0),
              reads=("S2",), writes=("S3",))
        S.add("vector", lambda eng: eng.tensor_max(scr16b, scr16b, scr16),
              reads=("S2", "S3"), writes=("S3",))
        S.add("scalar", lambda eng, i=i: eng.activation(
            p.F[7][:, i * 16:i * 16 + 16], scr16b, AF.Sin,
            bias=float(np.pi / 2), scale=-1.0),
            reads=("S3", "fconsts"), writes=("F7",))

    w0, w0k = p.Wsl[0], "W0"
    xt = [(p.B[2], "B2"), (p.B[3], "B3")]
    kn = p.F[6][:, 0:256]
    qn = p.F[6][:, 256:512]

    def load_xnT_tile(i, b):
        src = xnT.rearrange("(f p) t -> p f t", p=128)[:, :, bass.ts(i, 128)]
        S.dma("sdma", xt[b][0].ap().rearrange("p (f c) -> p f c", f=NF), src,
              reads=("xn0T",), writes=(xt[b][1],), tag="lxT")

    def proj_psum(tag, i, b, wsl, wslk, pair):
        for oc in range(2):
            mm_acc(p, S, tag + "%d" % oc, p.bank(2 * pair + oc), bkey(2 * pair + oc),
                   [(xt[b][0][:, bass.ts(f, 128)], xt[b][1]) for f in range(NF)],
                   [(wsl[:, f * D + oc * 512: f * D + oc * 512 + 512], wslk)
                    for f in range(NF)])

    load_weight_bf16(p, S, "l0_Wk", w0[:, :], w0k, D, D)
    bres = p.F[5]
    S.dma("sdma", bres[:, :], p.dram["l0_bk"][:, :], reads=("l0_bk",), writes=("F5",))
    kf = [(p.F[0], "F0"), (p.F[1], "F1")]
    kb = [(p.B[0], "B0"), (p.B[1], "B1")]
    for i in range(NT):
        b = i % 2
        load_xnT_tile(i, b)
        proj_psum("kP", i, b, w0, w0k, b)
        S.add("vector", lambda eng, b=b: eng.tensor_tensor(
            kf[b][0][:, :], p.bankpair(b)[:, :], bres[:, :], ALU.add),
            reads=(bkey(2 * b), bkey(2 * b + 1), "F5"), writes=(kf[b][1],), tag="kev")
        emit_rope(p, S, "gpsimd", kf[b][0], kf[b][1], p.F[2], "F2", i)
        S.add("vector", lambda eng, i=i, b=b: eng.tensor_reduce(
            kn[:, i * 8:i * 8 + 8].unsqueeze(2),
            kf[b][0].ap().rearrange("p (g c) -> p g c", g=8), AX.X, ALU.max,
            apply_absolute_value=True),
            reads=(kf[b][1],), writes=("F6",), tag="knm")
        S.add("scalar", lambda eng, b=b: eng.copy(kb[b][0][:, :], kf[b][0][:, :]),
              reads=(kf[b][1],), writes=(kb[b][1],))
        transposes(p, S, kb[b][0], kb[b][1],
                   lambda f, i=i: (p.B1[:, f * T + i * 128: f * T + i * 128 + 128],
                                   "BIG1"), "katr")

    load_weight_bf16(p, S, "l0_Wv", w0[:, :], w0k, D, D)
    S.dma("sdma", bres[:, :], p.dram["l0_bv"][:, :], reads=("l0_bv",), writes=("F5",))
    for i in range(NT):
        b = i % 2
        load_xnT_tile(i, b)
        proj_psum("vP", i, b, w0, w0k, b)
        vbase = i * 16 * 65
        q4 = i // 8

        def vev(eng, vbase=vbase, b=b):
            dst = p.B2.ap()[:, vbase:vbase + 16 * 65].rearrange(
                "p (h e) -> p h e", h=16)[:, :, 0:64]
            return eng.tensor_tensor(
                dst, p.bankpair(b).ap().rearrange("p (h d) -> p h d", h=16),
                bres.ap().rearrange("p (h d) -> p h d", h=16), ALU.add)
        S.add("vector", vev, reads=(bkey(2 * b), bkey(2 * b + 1), "F5"),
              writes=("BIG2",), tag="vev")

        def vones(eng, vbase=vbase):
            return eng.memset(
                p.B2.ap()[:, vbase: vbase + 16 * 65].rearrange(
                    "p (h e) -> p h e", e=65)[:, :, 64:65], 1.0)
        S.add("vector", vones, writes=("BIG2",), tag="vones")

    load_weight_bf16(p, S, "l0_Wq", w0[:, :], w0k, D, D)
    S.dma("sdma", bres[:, :], p.dram["l0_bq"][:, :], reads=("l0_bq",), writes=("F5",))
    qT = p.dram["l0_qT"]
    qtv = [(p.B[4], "B4"), (p.B[5], "B5")]
    for i in range(NT):
        b = i % 2
        load_xnT_tile(i, b)
        proj_psum("qP", i, b, w0, w0k, b)
        S.add("vector", lambda eng, b=b: eng.tensor_tensor(
            kf[b][0][:, :], p.bankpair(b)[:, :], bres[:, :], ALU.add),
            reads=(bkey(2 * b), bkey(2 * b + 1), "F5"), writes=(kf[b][1],), tag="qev")
        emit_rope(p, S, "gpsimd", kf[b][0], kf[b][1], p.F[2], "F2", i)
        S.add("vector", lambda eng, i=i, b=b: eng.tensor_reduce(
            qn[:, i * 8:i * 8 + 8].unsqueeze(2),
            kf[b][0].ap().rearrange("p (g c) -> p g c", g=8), AX.X, ALU.add,
            apply_absolute_value=True),
            reads=(kf[b][1],), writes=("F6",), tag="qnm")
        S.add("scalar", lambda eng, b=b: eng.copy(kb[b][0][:, :], kf[b][0][:, :]),
              reads=(kf[b][1],), writes=(kb[b][1],))
        transposes(p, S, kb[b][0], kb[b][1],
                   lambda f, i2=i % 2: (qtv[i2][0][:, bass.ts(f, 128)], qtv[i2][1]),
                   "qtr")
        dst = qT.rearrange("(f p) t -> p f t", p=128)[:, :, bass.ts(i, 128)]
        S.dma("sdma", dst, qtv[i % 2][0].ap().rearrange("p (f c) -> p f c", c=128),
              reads=(qtv[i % 2][1],), writes=("l0_qT",))

    redk = p.R[0][0:1, 0:256]
    redq = p.R[1][0:1, 0:256]
    S.add("gpsimd", lambda eng: eng.reduce_max(redk, kn, AX.C),
          reads=("F6",), writes=("R0",), tag="knr")
    S.add("gpsimd", lambda eng: eng.reduce_max(redq, qn, AX.C),
          reads=("F6",), writes=("R1",), tag="qnr")
    mk8 = p.R[2][0:1, 0:8]
    mq8 = p.R[2][0:1, 8:16]
    S.add("vector", lambda eng: eng.tensor_reduce(
        mk8.unsqueeze(2), redk.rearrange("o (i g) -> o g i", g=8), AX.X, ALU.max),
        reads=("R0",), writes=("R2",))
    S.add("vector", lambda eng: eng.tensor_reduce(
        mq8.unsqueeze(2), redq.rearrange("o (i g) -> o g i", g=8), AX.X, ALU.max),
        reads=("R1",), writes=("R2",))
    negc = p.S[4][:, 0:8]
    S.add("vector", lambda eng: eng.tensor_mul(mk8, mk8, mq8),
          reads=("R2",), writes=("R2",))
    S.add("vector", lambda eng: eng.tensor_scalar_mul(mk8, mk8, -1.0 / 8.0),
          reads=("R2",), writes=("R2",))
    mk8b = p.R[2].ap().bitcast(BF16)[0:1, 32:40]
    S.add("vector", lambda eng: eng.tensor_copy(mk8b, mk8), reads=("R2",), writes=("R2",))
    S.add("tensor", lambda eng: eng.matmul(p.bank(6)[:, 0:8], p.ones1_sb[0:1, :],
                                           mk8b, start=True, stop=True),
          reads=("R2", "ones1"), writes=(bkey(6),), tag="negb")
    S.add("vector", lambda eng: eng.tensor_copy(negc, p.bank(6)[:, 0:8]),
          reads=(bkey(6),), writes=("S4",))

    load_weight_bf16(p, S, "l0_Wo", w0[:, :], w0k, D, D)
    S.dma("sdma", bres[:, :], p.dram["l0_bo"][:, :], reads=("l0_bo",), writes=("F5",))
    qts = [(p.B[0], "B0"), (p.B[1], "B1")]
    Pt = [(p.B[2], "B2"), (p.B[3], "B3")]
    osb = [p.F[0].ap().bitcast(BF16), p.F[1].ap().bitcast(BF16)]
    osbk = ["F0", "F1"]
    rc_sb = p.S[2][:, 0:8]
    otT = [(p.B[4], "B4"), (p.B[5], "B5")]
    sev = [(p.F[2], "F2")]
    p.fconst(0.125)
    for qc in range(8):
        for g in range(HP):
            bb = (qc * HP + g) % 2
            S.dma("sdma", qts[bb][0][:, 0:512],
                  qT[bass.ts(g, 128), bass.ts(qc, 512)],
                  reads=("l0_qT",), writes=(qts[bb][1],), tag="lq")
            def emit_av(kt, g=g):
                for hh in range(2):
                    for qs in range(4):
                        def av(eng, g=g, kt=kt, hh=hh, qs=qs):
                            lhsT = Pt[kt % 2][0][:, 512 * hh + 128 * qs:
                                                 512 * hh + 128 * qs + 128]
                            rhs = p.B2[:, kt * 1040 + (2 * g + hh) * 65:
                                       kt * 1040 + (2 * g + hh) * 65 + 65]
                            return eng.matmul(
                                p.bank(4 + hh)[:, 65 * qs:65 * qs + 65],
                                lhsT, rhs, start=(kt == 0 and qs == 0),
                                stop=(kt == NT - 1 and qs == 3))
                        wk2 = bkey(4 + hh)
                        if kt == 0:
                            wkey = (wk2 + "@acc", wk2)
                        elif kt == NT - 1:
                            wkey = (wk2,)
                        else:
                            wkey = (wk2 + "@acc",)
                        S.add("tensor", av, reads=("BIG2", Pt[kt % 2][1]),
                              writes=wkey, tag="av")

            for kt in range(NT):
                pp = kt % 2
                for hh in range(2):
                    def qk(eng, g=g, kt=kt, hh=hh, pp=pp, bb=bb):
                        lhsT = p.B1[64 * hh:64 * hh + 64,
                                    g * T + kt * 128: g * T + kt * 128 + 128]
                        rhs = qts[bb][0][64 * hh:64 * hh + 64, 0:512]
                        return eng.matmul(p.bankpair(pp)[:, 512 * hh:512 * hh + 512],
                                          lhsT, rhs, start=True, stop=True,
                                          tile_position=(64 * hh, 0))
                    S.add("tensor", qk, reads=("BIG1", qts[bb][1]),
                          writes=(bkey(2 * pp + hh),), tag="qk")

                def ex(eng, g=g, pp=pp, kt=kt):
                    return eng.activation(Pt[kt % 2][0][:, :], p.bankpair(pp)[:, :],
                                          AF.Exp, bias=negc[:, g:g + 1], scale=0.125)
                S.add("scalar", ex, reads=(bkey(2 * pp), bkey(2 * pp + 1), "S4", "fconsts"),
                      writes=(Pt[kt % 2][1],), tag="exp")
                if kt > 0:
                    emit_av(kt - 1)
            emit_av(NT - 1)
            for hh in range(2):
                def rc(eng, hh=hh):
                    return eng.reciprocal(
                        rc_sb[:, 4 * hh:4 * hh + 4].unsqueeze(2),
                        p.bank(4 + hh)[:, 64:324].rearrange(
                            "p (q e) -> p q e", e=65)[:, :, 0:1])
                S.add("vector", rc, reads=(bkey(4), bkey(5)), writes=("S2",), tag="rc")
            for hh in range(2):
                for qs in range(4):
                    ci = hh * 4 + qs

                    def nm(eng, g=g, hh=hh, qs=qs, ci=ci):
                        dst = osb[qs // 2][:, (qs % 2) * 1024 + g * 128 + hh * 64:
                                           (qs % 2) * 1024 + g * 128 + hh * 64 + 64]
                        return eng.tensor_scalar(dst, p.bank(4 + hh)[:, 65 * qs:65 * qs + 64],
                                                 rc_sb[:, ci:ci + 1], None, ALU.mult, ALU.bypass)
                    S.add("vector", nm, reads=(bkey(4), bkey(5), "S2"),
                          writes=(osbk[qs // 2],), tag="nm")
        for qs in range(4):
            ob = qs % 2
            osrc = osb[qs // 2][:, (qs % 2) * 1024:(qs % 2) * 1024 + 1024]

            def dstf(f, ob=ob):
                return otT[ob][0][:, bass.ts(f, 128)], otT[ob][1]
            transposes2(p, S, osrc, osbk[qs // 2], dstf, "otr")
            for oc in range(2):
                mm_acc(p, S, "woP%d" % oc, p.bank(6 + oc), bkey(6 + oc),
                       [(otT[ob][0][:, bass.ts(f, 128)], otT[ob][1]) for f in range(NF)],
                       [(w0[:, f * D + oc * 512: f * D + oc * 512 + 512], w0k)
                        for f in range(NF)])
            S.add("vector", lambda eng, ob=ob: eng.tensor_tensor(
                sev[0][0][:, :], p.bankpair(3)[:, :], bres[:, :], ALU.add),
                reads=(bkey(6), bkey(7), "F5"), writes=(sev[0][1],), tag="sev")
            S.dma("sdma", p.dram["s0"][qc * 512 + qs * 128: qc * 512 + qs * 128 + 128, :],
                  sev[0][0][:, :], reads=(sev[0][1],), writes=("s0",), tag="sts")


def emit_l1attn(p):
    S = p.sched
    xnT = p.dram["xn1T"]
    HP = 8
    wq_, wqk = big1(p, 0)
    wk_, wkk = big1(p, 1)
    wv_, wvk = big1(p, 2)
    for qi, (wsl, wslk) in enumerate(((wq_, wqk), (wk_, wkk), (wv_, wvk))):
        src = p.dram["l1_wattnT"].rearrange("(f p) o -> p f o", p=128)[:, :, qi * D:(qi + 1) * D]
        S.dma("gdma", wsl.rearrange("p (f o) -> p f o", f=NF), src,
              reads=("l1_wattnT",), writes=(wslk,), tag="lw1")

    xt = [(p.B[2], "B2"), (p.B[3], "B3")]
    kn = p.F[6][:, 0:256]
    qn = p.F[6][:, 256:512]
    qf = [(p.F[0], "F0"), (p.F[1], "F1")]
    qb = [(p.B[0], "B0"), (p.B[1], "B1")]
    rtab = p.F[7]
    csl = [(p.S[2][:, 0:64], "S2"), (p.S[3][:, 0:64], "S3")]

    def l1rope2(unit, buf, bk2, scr, scrk, i, b):
        cs = csl[b][0][:, 0:32].unsqueeze(1).to_broadcast([128, 16, 32])
        sn = csl[b][0][:, 32:64].unsqueeze(1).to_broadcast([128, 16, 32])
        ck = csl[b][1]
        x1 = _ropev(buf)[:, :, 0:32]
        x2 = _ropev(buf)[:, :, 32:64]
        t1 = _ropev(scr)[:, :, 0:32]
        t2 = _ropev(scr)[:, :, 32:64]
        t3 = _ropev(scr)[:, :, 48:64] if False else None
        A = lambda fn, r, w: S.add(unit, fn, reads=r, writes=w, tag="r1")
        A(lambda eng, t1=t1, x1=x1, cs=cs: eng.tensor_tensor(t1, x1, cs, ALU.mult),
          (bk2, ck), (scrk,))
        A(lambda eng, t2=t2, x1=x1, sn=sn: eng.tensor_tensor(t2, x1, sn, ALU.mult),
          (bk2, ck), (scrk,))
        A(lambda eng, x1=x1, x2=x2, sn=sn, t1=t1: eng.tensor_tensor(x1, x2, sn, ALU.mult),
          (bk2, ck), (bk2,))
        A(lambda eng, x2=x2, cs=cs: eng.tensor_tensor(x2, x2, cs, ALU.mult),
          (bk2, ck), (bk2,))
        A(lambda eng, x1=x1, t1=t1: eng.tensor_tensor(x1, t1, x1, ALU.add),
          (scrk, bk2), (bk2,))
        A(lambda eng, x2=x2, t2=t2: eng.tensor_tensor(x2, x2, t2, ALU.subtract),
          (scrk, bk2), (bk2,))

    for i in range(NT):
        b = i % 2
        src = xnT.rearrange("(f p) t -> p f t", p=128)[:, :, bass.ts(i, 128)]
        S.dma("sdma", xt[b][0].ap().rearrange("p (f c) -> p f c", f=NF), src,
              reads=("xn1T",), writes=(xt[b][1],), tag="lxT1")
        S.dma("sdma", csl[b][0], p.dram["cs32"][bass.ts(i, 128), :],
              reads=("cs32",), writes=(csl[b][1],), tag="lcs")
        for oc in range(2):
            mm_acc(p, S, "1q%d" % oc, p.bank(oc), bkey(oc),
                   [(xt[b][0][:, bass.ts(f, 128)], xt[b][1]) for f in range(NF)],
                   [(wq_[:, f * D + oc * 512: f * D + oc * 512 + 512], wqk) for f in range(NF)])
        S.add("vector", lambda eng, b=b: eng.tensor_copy(qf[b][0][:, :], p.bankpair(0)[:, :]),
              reads=(bkey(0), bkey(1)), writes=(qf[b][1],), tag="qe1")
        l1rope2("gpsimd", qf[b][0], qf[b][1], p.F[2], "F2", i, b)
        S.add("vector", lambda eng, i=i, b=b: eng.tensor_reduce(
            qn[:, i * 8:i * 8 + 8].unsqueeze(2),
            qf[b][0].ap().rearrange("p (g c) -> p g c", g=8), AX.X, ALU.add,
            apply_absolute_value=True),
            reads=(qf[b][1],), writes=("F6",), tag="qn1")
        S.add("scalar", lambda eng, b=b: eng.copy(qb[b][0][:, :], qf[b][0][:, :]),
              reads=(qf[b][1],), writes=(qb[b][1],))
        transposes(p, S, qb[b][0], qb[b][1],
                   lambda f, i2=b: (p.B[4 + i2][:, bass.ts(f, 128)], "B%d" % (4 + i2)),
                   "q1tr")
        dst = p.dram["l1_qT"].rearrange("(f p) t -> p f t", p=128)[:, :, bass.ts(i, 128)]
        S.dma("sdma", dst, p.B[4 + b].ap().rearrange("p (f c) -> p f c", c=128),
              reads=("B%d" % (4 + b),), writes=("l1_qT",))
        for oc in range(2):
            mm_acc(p, S, "1k%d" % oc, p.bank(2 + oc), bkey(2 + oc),
                   [(xt[b][0][:, bass.ts(f, 128)], xt[b][1]) for f in range(NF)],
                   [(wk_[:, f * D + oc * 512: f * D + oc * 512 + 512], wkk) for f in range(NF)])
        S.add("vector", lambda eng, b=b: eng.tensor_copy(qf[b][0][:, :], p.bankpair(1)[:, :]),
              reads=(bkey(2), bkey(3)), writes=(qf[b][1],), tag="ke1")
        l1rope2("gpsimd", qf[b][0], qf[b][1], p.F[2], "F2", i, b)
        S.add("vector", lambda eng, i=i, b=b: eng.tensor_reduce(
            kn[:, i * 8:i * 8 + 8].unsqueeze(2),
            qf[b][0].ap().rearrange("p (g c) -> p g c", g=8), AX.X, ALU.max,
            apply_absolute_value=True),
            reads=(qf[b][1],), writes=("F6",), tag="kn1")
        S.add("scalar", lambda eng, b=b: eng.copy(qb[b][0][:, :], qf[b][0][:, :]),
              reads=(qf[b][1],), writes=(qb[b][1],))
        transposes(p, S, qb[b][0], qb[b][1],
                   lambda f, i2=b: (p.B[4 + i2][:, bass.ts(f, 128)], "B%d" % (4 + i2)),
                   "k1tr")
        dstk = p.dram["l1_kT"].rearrange("(f p) t -> p f t", p=128)[:, :, bass.ts(i, 128)]
        S.dma("sdma", dstk, p.B[4 + b].ap().rearrange("p (f c) -> p f c", c=128),
              reads=("B%d" % (4 + b),), writes=("l1_kT",))
        for oc in range(2):
            mm_acc(p, S, "1v%d" % oc, p.bank(4 + oc), bkey(4 + oc),
                   [(xt[b][0][:, bass.ts(f, 128)], xt[b][1]) for f in range(NF)],
                   [(wv_[:, f * D + oc * 512: f * D + oc * 512 + 512], wvk) for f in range(NF)])
        vbase = i * 16 * 65

        def vev(eng, vbase=vbase):
            dst = p.B2.ap()[:, vbase:vbase + 16 * 65].rearrange(
                "p (h e) -> p h e", h=16)[:, :, 0:64]
            return eng.tensor_copy(dst, p.bankpair(2).ap().rearrange("p (h d) -> p h d", h=16))
        S.add("vector", vev, reads=(bkey(4), bkey(5)), writes=("BIG2",), tag="vev1")

        def vones(eng, vbase=vbase):
            return eng.memset(
                p.B2.ap()[:, vbase: vbase + 16 * 65].rearrange(
                    "p (h e) -> p h e", e=65)[:, :, 64:65], 1.0)
        S.add("vector", vones, writes=("BIG2",), tag="von1")

    redk = p.R[0][0:1, 0:256]
    redq = p.R[1][0:1, 0:256]
    S.add("gpsimd", lambda eng: eng.reduce_max(redk, kn, AX.C), reads=("F6",), writes=("R0",))
    S.add("gpsimd", lambda eng: eng.reduce_max(redq, qn, AX.C), reads=("F6",), writes=("R1",))
    mk8 = p.R[2][0:1, 0:8]
    mq8 = p.R[2][0:1, 8:16]
    S.add("vector", lambda eng: eng.tensor_reduce(
        mk8.unsqueeze(2), redk.rearrange("o (i g) -> o g i", g=8), AX.X, ALU.max),
        reads=("R0",), writes=("R2",))
    S.add("vector", lambda eng: eng.tensor_reduce(
        mq8.unsqueeze(2), redq.rearrange("o (i g) -> o g i", g=8), AX.X, ALU.max),
        reads=("R1",), writes=("R2",))
    negc = p.S[4][:, 0:8]
    S.add("vector", lambda eng: eng.tensor_mul(mk8, mk8, mq8), reads=("R2",), writes=("R2",))
    S.add("vector", lambda eng: eng.tensor_scalar_mul(mk8, mk8, -1.0 / 8.0),
          reads=("R2",), writes=("R2",))
    mk8b = p.R[2].ap().bitcast(BF16)[0:1, 32:40]
    S.add("vector", lambda eng: eng.tensor_copy(mk8b, mk8), reads=("R2",), writes=("R2",))
    S.add("tensor", lambda eng: eng.matmul(p.bank(6)[:, 0:8], p.ones1_sb[0:1, :],
                                           mk8b, start=True, stop=True),
          reads=("R2", "ones1"), writes=(bkey(6),), tag="negb1")
    S.add("vector", lambda eng: eng.tensor_copy(negc, p.bank(6)[:, 0:8]),
          reads=(bkey(6),), writes=("S4",))

    load_weight_bf16(p, S, "l1_wprojT", p.Wsl[0][:, :], "W0", D, D)
    qts = [(p.B[0], "B0"), (p.B[1], "B1")]
    kts = [(p.B[2], "B2"), (p.B[3], "B3")]
    Pt = [(p.B[4], "B4"), (p.B[5], "B5")]
    osb = [p.F[0].ap().bitcast(BF16), p.F[1].ap().bitcast(BF16)]
    osbk = ["F0", "F1"]
    rc_sb = p.S[2][:, 0:8]
    otT = [(p.B[4], "B4")]
    sev = [(p.F[2], "F2")]
    masks = p.B1[:, 24576:24576 + 5 * 512]
    S.dma("sdma", masks, p.dram["l1_masks"][:, :], reads=("l1_masks",),
          writes=("BIG1",))
    for qc in range(globals().get('_QCMAX', 8)):
        kts_list = [kt for kt in range(qc * 4 - 1, qc * 4 + 4) if kt >= 0]
        for g in range(globals().get('_GMAX', HP)):
            bb = (qc * HP + g) % 2
            S.dma("sdma", qts[bb][0][:, 0:512],
                  p.dram["l1_qT"][bass.ts(g, 128), bass.ts(qc, 512)],
                  reads=("l1_qT",), writes=(qts[bb][1],), tag="lq1")
            def emit_av1(ki, kt, g=g, nki=len(kts_list)):
                for hh in range(2):
                    for qs in range(4):
                        def av(eng, g=g, kt=kt, hh=hh, qs=qs, ki=ki,
                               last=(ki == nki - 1)):
                            lhsT = Pt[ki % 2][0][:, 512 * hh + 128 * qs:
                                                 512 * hh + 128 * qs + 128]
                            rhs = p.B2[:, kt * 1040 + (2 * g + hh) * 65:
                                       kt * 1040 + (2 * g + hh) * 65 + 65]
                            return eng.matmul(
                                p.bank(4 + hh)[:, 65 * qs:65 * qs + 65],
                                lhsT, rhs, start=(ki == 0 and qs == 0),
                                stop=(ki == nki - 1 and qs == 3))
                        wk2 = bkey(4 + hh)
                        if ki == 0 and qs == 0:
                            wkey = (wk2 + "@acc", wk2)
                        elif ki == nki - 1 and qs == 3:
                            wkey = (wk2,)
                        else:
                            wkey = (wk2 + "@acc",)
                        S.add("tensor", av, reads=("BIG2", Pt[ki % 2][1]),
                              writes=wkey, tag="av1")

            for ki, kt in enumerate(kts_list):
                pp = ki % 2
                S.dma("sdma", kts[pp][0][:, 0:128],
                      p.dram["l1_kT"][bass.ts(g, 128), bass.ts(kt, 128)],
                      reads=("l1_kT",), writes=(kts[pp][1],), tag="lk1")
                for hh in range(2):
                    def qk(eng, kt=kt, hh=hh, pp=pp, bb=bb):
                        lhsT = kts[pp][0][64 * hh:64 * hh + 64, 0:128]
                        rhs = qts[bb][0][64 * hh:64 * hh + 64, 0:512]
                        return eng.matmul(p.bankpair(pp)[:, 512 * hh:512 * hh + 512],
                                          lhsT, rhs, start=True, stop=True,
                                          tile_position=(64 * hh, 0))
                    S.add("tensor", qk, reads=(kts[pp][1], qts[bb][1]),
                          writes=(bkey(2 * pp + hh),), tag="qk1")

                def ex(eng, g=g, pp=pp, kt=kt, ki=ki):
                    return eng.activation(Pt[ki % 2][0][:, :], p.bankpair(pp)[:, :],
                                          AF.Exp, bias=negc[:, g:g + 1], scale=0.125)
                S.add("scalar", ex, reads=(bkey(2 * pp), bkey(2 * pp + 1), "S4", "fconsts"),
                      writes=(Pt[ki % 2][1],), tag="ex1")
                mi = kt - qc * 4 + 1

                def msk(eng, ki=ki, mi=mi):
                    mm = masks[:, mi * 512:(mi + 1) * 512].unsqueeze(1).to_broadcast(
                        [128, 2, 512])
                    src = Pt[ki % 2][0].ap().rearrange("p (h q) -> p h q", h=2)
                    return eng.tensor_tensor(src, src, mm, ALU.mult)
                S.add("vector", msk, reads=(Pt[ki % 2][1], "BIG1"),
                      writes=(Pt[ki % 2][1],), tag="msk")
                if ki > 0:
                    emit_av1(ki - 1, kts_list[ki - 1])
            emit_av1(len(kts_list) - 1, kts_list[-1])
            for hh in range(2):
                def rc(eng, hh=hh):
                    return eng.reciprocal(
                        rc_sb[:, 4 * hh:4 * hh + 4].unsqueeze(2),
                        p.bank(4 + hh)[:, 64:324].rearrange(
                            "p (q e) -> p q e", e=65)[:, :, 0:1])
                S.add("vector", rc, reads=(bkey(4), bkey(5)), writes=("S2",), tag="rc1")
            for hh in range(2):
                for qs in range(4):
                    ci = hh * 4 + qs

                    def nm(eng, g=g, hh=hh, qs=qs, ci=ci):
                        dst = osb[qs // 2][:, (qs % 2) * 1024 + g * 128 + hh * 64:
                                           (qs % 2) * 1024 + g * 128 + hh * 64 + 64]
                        return eng.tensor_scalar(dst, p.bank(4 + hh)[:, 65 * qs:65 * qs + 64],
                                                 rc_sb[:, ci:ci + 1], None, ALU.mult, ALU.bypass)
                    S.add("vector", nm, reads=(bkey(4), bkey(5), "S2"),
                          writes=(osbk[qs // 2],), tag="nm1")
        for qs in range(4):
            osrc = osb[qs // 2][:, (qs % 2) * 1024:(qs % 2) * 1024 + 1024]

            def dstf(f):
                return otT[0][0][:, bass.ts(f, 128)], otT[0][1]
            transposes2(p, S, osrc, osbk[qs // 2], dstf, "otr1")
            for oc in range(2):
                mm_acc(p, S, "wp%d" % oc, p.bank(6 + oc), bkey(6 + oc),
                       [(otT[0][0][:, bass.ts(f, 128)], otT[0][1]) for f in range(NF)],
                       [(p.Wsl[0][:, f * D + oc * 512: f * D + oc * 512 + 512], "W0")
                        for f in range(NF)])
            S.add("vector", lambda eng: eng.tensor_copy(sev[0][0][:, :], p.bankpair(3)[:, :]),
                  reads=(bkey(6), bkey(7)), writes=(sev[0][1],), tag="sev1")
            S.dma("sdma", p.dram["s1"][qc * 512 + qs * 128: qc * 512 + qs * 128 + 128, :],
                  sev[0][0][:, :], reads=(sev[0][1],), writes=("s1",), tag="sts1")


def emit_l1ffn(p):
    S = p.sched
    w, wk2 = big1(p, 0)
    load_weight_bf16(p, S, "l1f_wT", w, wk2, D, D)
    scl = p.F[7]
    bia = p.F[6]
    S.dma("sdma", scl[:, :], p.dram["l1f_scale"][:, :], reads=("l1f_scale",), writes=("F7",))
    S.dma("sdma", bia[:, :], p.dram["l1f_bias"][:, :], reads=("l1f_bias",), writes=("F6",))
    for f in range(NF):
        blk = w[:, f * D:(f + 1) * D]
        S.add("vector", lambda eng, blk=blk: eng.tensor_single_scalar(
            blk, blk, 0.0, ALU.is_ge), reads=(wk2,), writes=(wk2,), tag="sg1")
        S.add("vector", lambda eng, blk=blk: eng.tensor_scalar(
            blk, blk, 2.0, -1.0, ALU.mult, ALU.add), reads=(wk2,), writes=(wk2,))
        S.add("vector", lambda eng, blk=blk: eng.tensor_tensor(
            blk, blk, scl[:, :], ALU.mult), reads=(wk2, "F7"), writes=(wk2,))
    xt = [(p.B[2], "B2"), (p.B[3], "B3")]
    gt = [(p.F[0], "F0"), (p.F[1], "F1")]
    st_ = [(p.F[2], "F2")]
    x1t = [(p.F[4], "F4")]
    for i in range(NT):
        b = i % 2
        src = p.dram["xn1T"].rearrange("(f p) t -> p f t", p=128)[:, :, bass.ts(i, 128)]
        S.dma("sdma", xt[b][0].ap().rearrange("p (f c) -> p f c", f=NF), src,
              reads=("xn1T",), writes=(xt[b][1],), tag="lx1f")
        for oc in range(2):
            mm_acc(p, S, "1f%d" % oc, p.bank(oc), bkey(oc),
                   [(xt[b][0][:, bass.ts(f, 128)], xt[b][1]) for f in range(NF)],
                   [(w[:, f * D + oc * 512: f * D + oc * 512 + 512], wk2) for f in range(NF)])
        S.add("vector", lambda eng, b=b: eng.tensor_tensor(
            gt[b][0][:, :], p.bankpair(0)[:, :], bia[:, :], ALU.add),
            reads=(bkey(0), bkey(1), "F6"), writes=(gt[b][1],), tag="gev1")
        S.add("vector", lambda eng, b=b: eng.tensor_scalar(
            gt[b][0][:, :], gt[b][0][:, :], 100.0, -100.0, ALU.min, ALU.max),
            reads=(gt[b][1],), writes=(gt[b][1],))
        S.add("scalar", lambda eng, b=b: eng.activation(gt[b][0][:, :], gt[b][0][:, :], AF.Silu),
              reads=(gt[b][1],), writes=(gt[b][1],))
        S.dma("sdma", st_[0][0][:, :], p.dram["s1"][bass.ts(i, 128), :],
              reads=("s1",), writes=(st_[0][1],), tag="ls1")
        S.dma("sdma", x1t[0][0][:, :], p.dram["x1"][bass.ts(i, 128), :],
              reads=("x1",), writes=(x1t[0][1],), tag="lx1")
        S.add("vector", lambda eng, b=b: eng.tensor_mul(gt[b][0][:, :], gt[b][0][:, :], st_[0][0][:, :]),
              reads=(gt[b][1], st_[0][1]), writes=(gt[b][1],), tag="sg1e")
        S.add("vector", lambda eng, b=b: eng.tensor_add(gt[b][0][:, :], gt[b][0][:, :], x1t[0][0][:, :]),
              reads=(gt[b][1], x1t[0][1]), writes=(gt[b][1],), tag="res1")
        S.dma("sdma", p.dram["x2"][bass.ts(i, 128), :], gt[b][0][:, :],
              reads=(gt[b][1],), writes=("x2",), tag="stx2")


def emit_l2ffn_a(p):
    S = p.sched
    wq_, wqk = big1(p, 0)
    load_weight_bf16(p, S, "l2f_Wq", wq_, wqk, D, D)
    wo_, wok = big1(p, 1)
    load_weight_bf16(p, S, "l2f_Wo", wo_, wok, D, D)
    wdn = p.B1[:, 24576:25088]
    S.dma("gdma", wdn.rearrange("p (f o) -> p f o", f=NF),
          p.dram["l2f_Wdown"].rearrange("(f p) o -> p f o", p=128),
          reads=("l2f_Wdown",), writes=("BIG1",))
    wup = p.B1[0:64, 25088:27136]
    S.dma("gdma", wup, p.dram["l2f_Wup"][:, :], reads=("l2f_Wup",), writes=("BIG1",))
    wg = p.B1[:, 27136:27200]
    S.dma("gdma", wg.rearrange("p (f o) -> p f o", f=NF),
          p.dram["l2f_Wg"].rearrange("(f p) o -> p f o", p=128),
          reads=("l2f_Wg",), writes=("BIG1",))
    g1r = p.F[6]
    g2r = p.F[7]
    S.dma("sdma", g1r[:, :], p.dram["l2f_g1"][:, :], reads=("l2f_g1",), writes=("F6",))
    S.dma("sdma", g2r[:, :], p.dram["l2f_g2"][:, :], reads=("l2f_g2",), writes=("F7",))
    bqr = p.F[5]
    S.dma("sdma", bqr[:, :], p.dram["l2f_bq"][:, :], reads=("l2f_bq",), writes=("F5",))
    bor = p.F[4]
    S.dma("sdma", bor[:, :], p.dram["l2f_bo"][:, :], reads=("l2f_bo",), writes=("F4",))
    bups = p.B1.ap().bitcast(F32)[:, 8192:10240]
    S.dma("sdma", bups, p.dram["l2f_bup"][:, :], reads=("l2f_bup",), writes=("BIG1",))
    bdns = p.S[5][:, 0:64]
    S.dma("sdma", bdns, p.dram["l2f_bdn"][:, :], reads=("l2f_bdn",), writes=("S5",))
    bgr = p.S[4][:, 0:8]
    S.dma("sdma", bgr, p.dram["l2f_bg"][:, :], reads=("l2f_bg",), writes=("S4",))

    xtf = [(p.F[0], "F0"), (p.F[1], "F1")]
    hb = [(p.B[0], "B0")]
    hT = [(p.B[1], "B1")]
    qsb = [(p.B[2], "B2")]
    ksb = [(p.B[3], "B3")]
    vsb = [(p.B[4], "B4")]
    osb = [(p.B[5], "B5")]
    hf = [(p.F[2], "F2")]
    att = [(p.F[3], "F3")]
    prod = p.B2[:, 0:16384]
    junk = p.S[2]
    stats = p.S[3]

    for i in range(NT):
        b = 0
        un = ("vector", "gpsimd")[i % 2]
        S.dma("sdma", xtf[i % 2][0][:, :], p.dram["xn2"][bass.ts(i, 128), :],
              reads=("xn2",), writes=(xtf[i % 2][1],), tag="lx2")
        xk = xtf[i % 2]
        S.add("scalar", lambda eng, xk=xk: eng.activation(
            junk.ap().bitcast(BF16)[:, 0:128], xk[0][:, 0:128], AF.Square,
            accum_out=stats[:, 0:1]) if False else eng.activation(
            p.B[0][:, :], xk[0][:, :], AF.Square, accum_out=stats[:, 0:1]),
            reads=(xk[1],), writes=("B0", "S3"), tag="l2sq")
        p.fconst(1e-6)
        p.fconst(-0.5)
        S.add("scalar", lambda eng: eng.activation(stats[:, 1:2], stats[:, 0:1],
                                                   AF.Ln, bias=1e-6, scale=1.0 / D),
              reads=("S3", "fconsts"), writes=("S3",))
        S.add("scalar", lambda eng: eng.activation(stats[:, 1:2], stats[:, 1:2],
                                                   AF.Exp, scale=-0.5),
              reads=("S3", "fconsts"), writes=("S3",))
        S.add("vector", lambda eng, xk=xk: eng.tensor_scalar(
            hf[0][0][:, :], xk[0][:, :], stats[:, 1:2], None, ALU.mult, ALU.bypass),
            reads=(xk[1], "S3"), writes=(hf[0][1],), tag="l2h1")
        S.add("vector", lambda eng: eng.tensor_tensor(hf[0][0][:, :], hf[0][0][:, :],
                                                      g1r[:, :], ALU.mult),
            reads=(hf[0][1], "F6"), writes=(hf[0][1],))
        S.add("scalar", lambda eng: eng.copy(hb[0][0][:, :], hf[0][0][:, :]),
              reads=(hf[0][1],), writes=(hb[0][1],))
        transposes(p, S, hb[0][0], hb[0][1],
                   lambda f: (hT[0][0][:, bass.ts(f, 128)], hT[0][1]), "l2ht")
        for oc in range(2):
            mm_acc(p, S, "2q%d" % oc, p.bank(oc), bkey(oc),
                   [(hT[0][0][:, bass.ts(f, 128)], hT[0][1]) for f in range(NF)],
                   [(wq_[:, f * D + oc * 512: f * D + oc * 512 + 512], wqk) for f in range(NF)])
        S.add("vector", lambda eng: eng.tensor_tensor(
            qsb[0][0][:, :], p.bankpair(0)[:, :], bqr[:, :], ALU.add),
            reads=(bkey(0), bkey(1), "F5"), writes=(qsb[0][1],), tag="qev2")
        mm_acc(p, S, "2dn", p.bank(4)[:, 0:64], bkey(4),
               [(hT[0][0][:, bass.ts(f, 128)], hT[0][1]) for f in range(NF)],
               [(wdn[:, f * 64:(f + 1) * 64], "BIG1") for f in range(NF)])
        S.add("vector", lambda eng: eng.tensor_tensor(
            junk.ap().bitcast(BF16)[:, 0:64], p.bank(4)[:, 0:64], bdns, ALU.add),
            reads=(bkey(4), "S5"), writes=("S2",), tag="cev")
        S.add("tensor", lambda eng: eng.transpose(
            p.bank_bf(5, 128)[0:64, :], junk.ap().bitcast(BF16)[:, 0:64],
            p.ident_sb[:, :]),
            reads=("S2", "ident"), writes=(bkey(5),), tag="ctr")
        S.add("vector", lambda eng: eng.tensor_copy(
            junk.ap().bitcast(BF16)[0:64, 128:256], p.bank_bf(5, 128)[0:64, :]),
            reads=(bkey(5),), writes=("S2",), tag="cte")
        for oc in range(4):
            def kvmm(eng, oc=oc):
                return eng.matmul(p.bank(oc)[:, :] if False else p.bankpair(oc // 2)[:, 512 * (oc % 2):512 * (oc % 2) + 512],
                                  junk.ap().bitcast(BF16)[0:64, 128:256],
                                  wup[0:64, oc * 512:(oc + 1) * 512],
                                  start=True, stop=True)
            S.add("tensor", kvmm, reads=("S2", "BIG1"), writes=(bkey(oc),), tag="kv2")
        kv0 = p.pool[0].ap().rearrange("p (h x) -> p h x", h=8)
        kv1 = p.pool[1].ap().rearrange("p (h x) -> p h x", h=8)
        bup3 = bups.rearrange("p (h x) -> p h x", h=16)

        def kev(eng):
            dst = ksb[0][0].ap().rearrange("p (h d) -> p h d", h=16)
            eng.tensor_tensor(dst[:, 0:8, :], kv0[:, :, 0:64],
                              bup3[:, 0:8, 0:64], ALU.add)
            return eng.tensor_tensor(dst[:, 8:16, :], kv1[:, :, 0:64],
                                     bup3[:, 8:16, 0:64], ALU.add)
        S.add("vector", kev, reads=(bkey(0), bkey(1), bkey(2), bkey(3), "BIG1"),
              writes=(ksb[0][1],), tag="kev2")

        def vev2(eng):
            dst = vsb[0][0].ap().rearrange("p (h d) -> p h d", h=16)
            eng.tensor_tensor(dst[:, 0:8, :], kv0[:, :, 64:128],
                              bup3[:, 0:8, 64:128], ALU.add)
            return eng.tensor_tensor(dst[:, 8:16, :], kv1[:, :, 64:128],
                                     bup3[:, 8:16, 64:128], ALU.add)
        S.add("vector", vev2, reads=(bkey(0), bkey(1), bkey(2), bkey(3), "BIG1"),
              writes=(vsb[0][1],), tag="vev2")
        qB = qsb[0][0].ap().rearrange("p (h d) -> p h d", h=16).unsqueeze(2) \
            .to_broadcast([128, 16, 16, 64])
        kB = ksb[0][0].ap().rearrange("p (g d) -> p g d", g=16).unsqueeze(1) \
            .to_broadcast([128, 16, 16, 64])
        prod4 = prod.rearrange("p (h g d) -> p h g d", h=16, g=16)
        S.add(un, lambda eng: eng.tensor_tensor(prod4, qB, kB, ALU.mult),
              reads=(qsb[0][1], ksb[0][1]), writes=("BIG2",), tag="attm")
        S.add("vector", lambda eng: eng.tensor_reduce(
            att[0][0].ap()[:, 0:256].rearrange("p (h g) -> p h g", h=16).unsqueeze(3),
            prod4, AX.X, ALU.add),
            reads=("BIG2",), writes=(att[0][1],), tag="attr")
        S.add("vector", lambda eng: eng.tensor_reduce(
            stats[:, 2:18].rearrange("p (h o) -> p h o", h=16),
            att[0][0].ap()[:, 0:256].rearrange("p (h g) -> p h g", h=16),
            AX.X, ALU.max),
            reads=(att[0][1],), writes=("S3",), tag="amax")
        S.add("vector", lambda eng: eng.tensor_tensor(
            att[0][0].ap()[:, 0:256].rearrange("p (h g) -> p h g", h=16),
            att[0][0].ap()[:, 0:256].rearrange("p (h g) -> p h g", h=16),
            stats[:, 2:18].rearrange("p (h o) -> p h o", h=16).to_broadcast([128, 16, 16]),
            ALU.subtract),
            reads=(att[0][1], "S3"), writes=(att[0][1],), tag="asub")
        p.fconst(0.125)
        S.add("scalar", lambda eng: eng.activation(
            att[0][0][:, 0:256], att[0][0][:, 0:256], AF.Exp, scale=0.125),
            reads=(att[0][1], "fconsts"), writes=(att[0][1],), tag="aexp")
        S.add("vector", lambda eng: eng.tensor_reduce(
            stats[:, 18:34].rearrange("p (h o) -> p h o", h=16),
            att[0][0].ap()[:, 0:256].rearrange("p (h g) -> p h g", h=16),
            AX.X, ALU.add),
            reads=(att[0][1],), writes=("S3",), tag="asum")
        S.add("vector", lambda eng: eng.reciprocal(stats[:, 18:34], stats[:, 18:34]),
              reads=("S3",), writes=("S3",), tag="arec")
        S.add("scalar", lambda eng: eng.copy(junk.ap().bitcast(BF16)[:, 0:256],
                                             att[0][0][:, 0:256]),
              reads=(att[0][1],), writes=("S2",), tag="acast")
        PB = junk.ap().bitcast(BF16)[:, 0:256].rearrange("p (h g) -> p h g", h=16) \
            .unsqueeze(2).to_broadcast([128, 16, 64, 16])
        vB = vsb[0][0].ap().rearrange("p (g d) -> p g d", g=16).transpose([0, 2, 1]) \
            .unsqueeze(1).to_broadcast([128, 16, 64, 16])
        prod2 = prod.rearrange("p (h d g) -> p h d g", h=16, d=64)
        S.add(un, lambda eng: eng.tensor_tensor(prod2, PB, vB, ALU.mult),
              reads=("S2", vsb[0][1]), writes=("BIG2",), tag="avm")
        S.add("vector", lambda eng: eng.tensor_reduce(
            hf[0][0].ap().rearrange("p (h d) -> p h d", h=16).unsqueeze(3),
            prod2, AX.X, ALU.add),
            reads=("BIG2",), writes=(hf[0][1],), tag="avr")
        S.add("vector", lambda eng: eng.tensor_tensor(
            hf[0][0].ap().rearrange("p (h d) -> p h d", h=16),
            hf[0][0].ap().rearrange("p (h d) -> p h d", h=16),
            stats[:, 18:34].rearrange("p (h o) -> p h o", h=16).to_broadcast([128, 16, 64]),
            ALU.mult),
            reads=(hf[0][1], "S3"), writes=(hf[0][1],), tag="anorm")
        S.add("scalar", lambda eng: eng.copy(osb[0][0][:, :], hf[0][0][:, :]),
              reads=(hf[0][1],), writes=(osb[0][1],))
        transposes(p, S, osb[0][0], osb[0][1],
                   lambda f: (hT[0][0][:, bass.ts(f, 128)], hT[0][1]), "l2ot")
        for oc in range(2):
            mm_acc(p, S, "2o%d" % oc, p.bank(oc), bkey(oc),
                   [(hT[0][0][:, bass.ts(f, 128)], hT[0][1]) for f in range(NF)],
                   [(wo_[:, f * D + oc * 512: f * D + oc * 512 + 512], wok) for f in range(NF)])
        S.add("vector", lambda eng: eng.tensor_tensor(
            hf[0][0][:, :], p.bankpair(0)[:, :], bor[:, :], ALU.add),
            reads=(bkey(0), bkey(1), "F4"), writes=(hf[0][1],), tag="xmev")
        S.add("vector", lambda eng, xk=xk: eng.tensor_add(
            hf[0][0][:, :], hf[0][0][:, :], xk[0][:, :]),
            reads=(hf[0][1], xk[1]), writes=(hf[0][1],))
        S.dma("sdma", p.dram["l2_xmid"][bass.ts(i, 128), :], hf[0][0][:, :],
              reads=(hf[0][1],), writes=("l2_xmid",), tag="stxm")
        S.add("scalar", lambda eng: eng.activation(
            p.B[0][:, :], hf[0][0][:, :], AF.Square, accum_out=stats[:, 0:1]),
            reads=(hf[0][1],), writes=("B0", "S3"), tag="h2sq")
        S.add("scalar", lambda eng: eng.activation(stats[:, 1:2], stats[:, 0:1],
                                                   AF.Ln, bias=1e-6, scale=1.0 / D),
              reads=("S3", "fconsts"), writes=("S3",))
        S.add("scalar", lambda eng: eng.activation(stats[:, 1:2], stats[:, 1:2],
                                                   AF.Exp, scale=-0.5),
              reads=("S3", "fconsts"), writes=("S3",))
        S.add("vector", lambda eng: eng.tensor_scalar(
            hf[0][0][:, :], hf[0][0][:, :], stats[:, 1:2], None, ALU.mult, ALU.bypass),
            reads=(hf[0][1], "S3"), writes=(hf[0][1],))
        S.add("vector", lambda eng: eng.tensor_tensor(hf[0][0][:, :], hf[0][0][:, :],
                                                      g2r[:, :], ALU.mult),
            reads=(hf[0][1], "F7"), writes=(hf[0][1],))
        S.add("scalar", lambda eng: eng.copy(hb[0][0][:, :], hf[0][0][:, :]),
              reads=(hf[0][1],), writes=(hb[0][1],))
        transposes(p, S, hb[0][0], hb[0][1],
                   lambda f: (hT[0][0][:, bass.ts(f, 128)], hT[0][1]), "l2h2t")
        dst = p.dram["l2_h2T"].rearrange("(f p) t -> p f t", p=128)[:, :, bass.ts(i, 128)]
        S.dma("sdma", dst, hT[0][0].ap().rearrange("p (f c) -> p f c", c=128),
              reads=(hT[0][1],), writes=("l2_h2T",), tag="sth2")
        for f in range(NF):
            def gmm(eng, f=f):
                return eng.matmul(p.bank(6)[:, 0:8], hT[0][0][:, bass.ts(f, 128)],
                                  wg[:, f * 8:(f + 1) * 8], start=(f == 0),
                                  stop=(f == NF - 1))
            wkk2 = (bkey(6) + "@acc", bkey(6)) if f == 0 else \
                ((bkey(6),) if f == NF - 1 else (bkey(6) + "@acc",))
            S.add("tensor", gmm, reads=(hT[0][1], "BIG1"), writes=wkk2, tag="gmm")
        gatec = att[0]
        S.add("vector", lambda eng: eng.tensor_tensor(
            gatec[0][:, 0:8], p.bank(6)[:, 0:8], bgr, ALU.add),
            reads=(bkey(6), "S4"), writes=(gatec[1],), tag="gev")
        S.add("vector", lambda eng: eng.tensor_reduce(
            stats[:, 0:1], gatec[0][:, 0:8], AX.X, ALU.max, negate=True),
            reads=(gatec[1],), writes=("S3",))
        S.add("scalar", lambda eng: eng.activation(
            gatec[0][:, 0:8], gatec[0][:, 0:8], AF.Exp, bias=stats[:, 0:1]),
            reads=(gatec[1], "S3", "fconsts"), writes=(gatec[1],))
        S.add("vector", lambda eng: eng.tensor_reduce(
            stats[:, 0:1], gatec[0][:, 0:8], AX.X, ALU.add),
            reads=(gatec[1],), writes=("S3",))
        S.add("vector", lambda eng: eng.reciprocal(stats[:, 1:2], stats[:, 0:1]),
              reads=("S3",), writes=("S3",))
        S.add("vector", lambda eng: eng.tensor_scalar(
            gatec[0][:, 0:8], gatec[0][:, 0:8], stats[:, 1:2], None, ALU.mult, ALU.bypass),
            reads=(gatec[1], "S3"), writes=(gatec[1],))
        S.dma("sdma", p.dram["l2_gate"][bass.ts(i, 128), :], gatec[0][:, 0:8],
              reads=(gatec[1],), writes=("l2_gate",), tag="stg")


def emit_l2ffn_b(p):
    S = p.sched
    combv = p.B1.ap().bitcast(F32)
    h2T = p.dram["l2_h2T"]
    ht = [(p.B[2], "B2"), (p.B[3], "B3")]
    gt = [(p.F[3], "F3")]
    wsl = [(p.B2[:, 0:8192], "BIG2"), (p.B2[:, 8192:16384], "BIG2")]
    for half in range(2):
        S.add("vector", lambda eng: eng.memset(combv[:, :], 0.0),
              writes=("BIG1",), tag="cz")
        for e in range(8):
            ws = wsl[e % 2]
            src = p.dram["l2f_We"].rearrange("e (f p) o -> e p f o", p=128)[e]
            S.dma("gdma", ws[0].rearrange("p (f o) -> p f o", f=NF), src,
                  reads=("l2f_We",), writes=(ws[1],), tag="lwe")
            for ti in range(16):
                i = half * 16 + ti
                b = ti % 2
                src2 = h2T.rearrange("(f p) t -> p f t", p=128)[:, :, bass.ts(i, 128)]
                S.dma("sdma", ht[b][0].ap().rearrange("p (f c) -> p f c", f=NF), src2,
                      reads=("l2_h2T",), writes=(ht[b][1],), tag="lh2")
                S.dma("sdma", gt[0][0][:, 0:8], p.dram["l2_gate"][bass.ts(i, 128), :],
                      reads=("l2_gate",), writes=(gt[0][1],), tag="lg")
                for oc in range(2):
                    mm_acc(p, S, "we%d" % oc, p.bank(oc), bkey(oc),
                           [(ht[b][0][:, bass.ts(f, 128)], ht[b][1]) for f in range(NF)],
                           [(ws[0][:, f * D + oc * 512: f * D + oc * 512 + 512], ws[1])
                            for f in range(NF)])
                def acc(eng, ti=ti, e=e):
                    dst = combv[:, ti * 1024:(ti + 1) * 1024]
                    return eng.scalar_tensor_tensor(
                        dst, p.bankpair(0)[:, :], gt[0][0][:, e:e + 1], dst,
                        ALU.mult, ALU.add)
                S.add("vector", acc, reads=(bkey(0), bkey(1), gt[0][1], "BIG1"),
                      writes=("BIG1",), tag="cacc")
        for ti in range(16):
            i = half * 16 + ti
            xm = [(p.F[0], "F0"), (p.F[1], "F1")][ti % 2]
            S.dma("sdma", xm[0][:, :], p.dram["l2_xmid"][bass.ts(i, 128), :],
                  reads=("l2_xmid",), writes=(xm[1],), tag="lxm")
            def fin(eng, ti=ti, xm=xm):
                return eng.tensor_tensor(xm[0][:, :], xm[0][:, :],
                                         combv[:, ti * 1024:(ti + 1) * 1024], ALU.add)
            S.add("vector", fin, reads=(xm[1], "BIG1"), writes=(xm[1],),
                  tag="fin")
            S.add("scalar", lambda eng, xm=xm: eng.activation(xm[0][:, :], xm[0][:, :], AF.Silu),
                  reads=(xm[1],), writes=(xm[1],), tag="fsil")
            s2t = (p.F[2], "F2")
            x2t = (p.F[4], "F4")
            S.dma("sdma", s2t[0][:, :], p.dram["s2"][bass.ts(i, 128), :],
                  reads=("s2",), writes=(s2t[1],), tag="ls2")
            S.dma("sdma", x2t[0][:, :], p.dram["x2"][bass.ts(i, 128), :],
                  reads=("x2",), writes=(x2t[1],), tag="lx2e")
            S.add("vector", lambda eng, xm=xm, s2t=s2t: eng.tensor_mul(
                xm[0][:, :], xm[0][:, :], s2t[0][:, :]),
                reads=(xm[1], s2t[1]), writes=(xm[1],))
            S.add("vector", lambda eng, xm=xm, x2t=x2t: eng.tensor_add(
                xm[0][:, :], xm[0][:, :], x2t[0][:, :]),
                reads=(xm[1], x2t[1]), writes=(xm[1],))
            S.dma("sdma", p.dram["x3"][bass.ts(i, 128), :], xm[0][:, :],
                  reads=(xm[1],), writes=("x3",), tag="stx3")


def tr_banks(p, S, src_ap, src_key, dst_fn, tag, banks=(6, 7), nf=NF):
    for f in range(nf):
        bi = banks[f % 2]
        ps = p.bank_bf(bi)
        pk = bkey(bi)

        def tfn(eng, f=f, ps=ps):
            return eng.transpose(ps, src_ap[:, bass.ts(f, 128)], p.ident_sb[:, :])
        S.add("tensor", tfn, reads=(src_key, "ident"), writes=(pk,), tag=tag + "t%d" % f)
        da, dk = dst_fn(f)

        def efn(eng, da=da, ps=ps):
            return eng.tensor_copy(da, ps)
        S.add("vector", efn, reads=(pk,), writes=(dk,), tag=tag + "e%d" % f)


def emit_l3a(p):
    S = p.sched
    wm, wmk = big1(p, 0)
    load_weight_bf16(p, S, "l3_WmT", wm, wmk, D, D)
    wq_, wqk = big1(p, 1)
    load_weight_bf16(p, S, "l3_WqT", wq_, wqk, D, D)
    bmr, bqr = p.F[6], p.F[7]
    S.dma("sdma", bmr[:, :], p.dram["l3_bm"][:, :], reads=("l3_bm",), writes=("F6",))
    S.dma("sdma", bqr[:, :], p.dram["l3_bq"][:, :], reads=("l3_bq",), writes=("F7",))
    phr = p.F[5]
    S.dma("sdma", phr[:, :], p.dram["l3_phases"][:, :], reads=("l3_phases",), writes=("F5",))
    postc = p.S[1][:, 0:NT]
    S.dma("sdma", postc, p.dram["poscols"][:, :], reads=("poscols",), writes=("S1",))
    inds = p.B1[:, 16384:17408]
    S.dma("sdma", inds, p.dram["indcols"][:, :], reads=("indcols",), writes=("BIG1",))

    xt = [(p.B[4], "B4"), (p.B[5], "B5")]
    mg = [(p.F[0], "F0")]
    qo = [(p.F[1], "F1")]
    phi = [(p.F[2], "F2")]
    t4 = [(p.F[3], "F3")]
    t5 = [(p.F[4], "F4")]
    cb = [(p.B[0], "B0"), (p.B[1], "B1"), (p.B[2], "B2")]
    p.fconst(float(np.pi / 2))
    for i in range(NT):
        b = i % 2
        src = p.dram["xn3T"].rearrange("(f p) t -> p f t", p=128)[:, :, bass.ts(i, 128)]
        S.dma("sdma", xt[b][0].ap().rearrange("p (f c) -> p f c", f=NF), src,
              reads=("xn3T",), writes=(xt[b][1],), tag="lx3T")
        xf = [(p.F[8], "F8")]
        S.dma("sdma", xf[0][0][:, :], p.dram["xn3"][bass.ts(i, 128), :],
              reads=("xn3",), writes=(xf[0][1],), tag="lx3")
        for oc in range(2):
            mm_acc(p, S, "3m%d" % oc, p.bank(oc), bkey(oc),
                   [(xt[b][0][:, bass.ts(f, 128)], xt[b][1]) for f in range(NF)],
                   [(wm[:, f * D + oc * 512: f * D + oc * 512 + 512], wmk) for f in range(NF)])
        S.add("vector", lambda eng: eng.tensor_tensor(
            mg[0][0][:, :], p.bankpair(0)[:, :], bmr[:, :], ALU.add),
            reads=(bkey(0), bkey(1), "F6"), writes=(mg[0][1],), tag="mgev")
        S.add("scalar", lambda eng: eng.activation(mg[0][0][:, :], mg[0][0][:, :], AF.Sigmoid),
              reads=(mg[0][1], "fconsts"), writes=(mg[0][1],))
        S.add("vector", lambda eng: eng.tensor_scalar_mul(mg[0][0][:, :], mg[0][0][:, :], 5.0),
              reads=(mg[0][1],), writes=(mg[0][1],))
        for oc in range(2):
            mm_acc(p, S, "3q%d" % oc, p.bank(oc), bkey(oc),
                   [(xt[b][0][:, bass.ts(f, 128)], xt[b][1]) for f in range(NF)],
                   [(wq_[:, f * D + oc * 512: f * D + oc * 512 + 512], wqk) for f in range(NF)])
        S.add("vector", lambda eng: eng.tensor_tensor(
            qo[0][0][:, :], p.bankpair(0)[:, :], bqr[:, :], ALU.add),
            reads=(bkey(0), bkey(1), "F7"), writes=(qo[0][1],), tag="qoev")
        S.dma("sdma", p.dram["l3_qoff"][bass.ts(i, 128), :], qo[0][0][:, :],
              reads=(qo[0][1],), writes=("l3_qoff",), tag="stqo")
        S.add("vector", lambda eng, i=i: eng.tensor_scalar(
            phi[0][0][:, :], phr[:, :], postc[:, i:i + 1], 0.01, ALU.mult, ALU.mult),
            reads=("F5", "S1"), writes=(phi[0][1],), tag="phi")
        S.add("vector", lambda eng: eng.tensor_scalar(
            t4[0][0][:, :], phi[0][0][:, :], 1.0 / TWO_PI, MAGIC, ALU.mult, ALU.add),
            reads=(phi[0][1],), writes=(t4[0][1],))
        S.add("vector", lambda eng: eng.tensor_scalar(
            t4[0][0][:, :], t4[0][0][:, :], MAGIC, None, ALU.subtract, ALU.bypass),
            reads=(t4[0][1],), writes=(t4[0][1],))
        S.add("vector", lambda eng: eng.scalar_tensor_tensor(
            phi[0][0][:, :], t4[0][0][:, :], -TWO_PI, phi[0][0][:, :], ALU.mult, ALU.add),
            reads=(t4[0][1], phi[0][1]), writes=(phi[0][1],))
        S.add("vector", lambda eng: eng.tensor_scalar(
            phi[0][0][:, :], phi[0][0][:, :], PI_C, -PI_C, ALU.min, ALU.max),
            reads=(phi[0][1],), writes=(phi[0][1],))
        S.add("scalar", lambda eng: eng.activation(t4[0][0][:, :], phi[0][0][:, :], AF.Sin),
              reads=(phi[0][1], "fconsts"), writes=(t4[0][1],), tag="sinp")
        S.add("vector", lambda eng: eng.tensor_scalar_mul(t5[0][0][:, :], phi[0][0][:, :], -1.0),
              reads=(phi[0][1],), writes=(t5[0][1],))
        S.add("vector", lambda eng: eng.tensor_max(t5[0][0][:, :], t5[0][0][:, :], phi[0][0][:, :]),
              reads=(t5[0][1], phi[0][1]), writes=(t5[0][1],))
        S.add("scalar", lambda eng: eng.activation(t5[0][0][:, :], t5[0][0][:, :], AF.Sin,
                                                   bias=float(np.pi / 2), scale=-1.0),
              reads=(t5[0][1], "fconsts"), writes=(t5[0][1],), tag="cosp")
        S.add("vector", lambda eng, xf=xf: eng.tensor_mul(xf[0][0][:, :], xf[0][0][:, :], mg[0][0][:, :]),
              reads=(xf[0][1], mg[0][1]), writes=(xf[0][1],), tag="wgt")
        S.add("vector", lambda eng, xf=xf: eng.tensor_tensor(
            cb[0][0][:, :], xf[0][0][:, :], t5[0][0][:, :], ALU.mult),
            reads=(xf[0][1], t5[0][1]), writes=(cb[0][1],), tag="wc")
        S.add("vector", lambda eng, xf=xf: eng.tensor_tensor(
            cb[1][0][:, :], xf[0][0][:, :], t4[0][0][:, :], ALU.mult),
            reads=(xf[0][1], t4[0][1]), writes=(cb[1][1],), tag="ws")
        S.add("scalar", lambda eng: eng.copy(cb[2][0][:, :], mg[0][0][:, :]),
              reads=(mg[0][1],), writes=(cb[2][1],))
        for ci, nm in ((0, "l3_wc"), (1, "l3_ws"), (2, "l3_mg")):
            S.dma("sdma", p.dram[nm][bass.ts(i, 128), :], cb[ci][0][:, :],
                  reads=(cb[ci][1],), writes=(nm,), tag="st" + nm)
        for ci in range(3):
            for oc in range(2):
                def bs(eng, ci=ci, oc=oc, i=i):
                    return eng.matmul(p.bank(2 + 2 * ci + oc)[0:32, :],
                                      inds[:, i * 32:i * 32 + 32],
                                      cb[ci][0][:, bass.ts(oc, 512)],
                                      start=(i == 0), stop=(i == NT - 1))
                kk = bkey(2 + 2 * ci + oc)
                wkey = (kk + "@acc", kk) if i == 0 else ((kk,) if i == NT - 1 else (kk + "@acc",))
                S.add("tensor", bs, reads=("BIG1", cb[ci][1]), writes=wkey, tag="bs")
    trix = p.B1[0:32, 17408:17440]
    S.dma("sdma", trix, p.dram["triexc"][:, :], reads=("triexc",), writes=("BIG1",))
    bsb = [(p.B[0], "B0"), (p.B[1], "B1"), (p.B[2], "B2")]
    for ci in range(3):
        def ev(eng, ci=ci):
            return eng.tensor_copy(bsb[ci][0][0:32, :], p.bankpair(1 + ci)[0:32, :])
        S.add("vector", ev, reads=(bkey(2 + 2 * ci), bkey(3 + 2 * ci)),
              writes=(bsb[ci][1],), tag="bse")
    for ci in range(3):
        for oc in range(2):
            def pf(eng, ci=ci, oc=oc):
                return eng.matmul(p.bank(oc)[0:32, :], trix[:, :],
                                  bsb[ci][0][0:32, bass.ts(oc, 512)],
                                  start=True, stop=True)
            S.add("tensor", pf, reads=("BIG1", bsb[ci][1]), writes=(bkey(oc),), tag="pf")
        def ev2(eng, ci=ci):
            return eng.tensor_copy(p.B[3][0:32, :], p.bankpair(0)[0:32, :])
        S.add("vector", ev2, reads=(bkey(0), bkey(1)), writes=("B3",), tag="pfe")
        S.dma("sdma", p.dram["l3_carr"][:, ci * 1024:(ci + 1) * 1024], p.B[3][0:32, :],
              reads=("B3",), writes=("l3_carr",), tag="stcar")


def emit_l3b(p):
    S = p.sched
    def w1blk(f):
        if f < 16:
            return p.B1[:, f * 2048:(f + 1) * 2048], "BIG1"
        return p.B2[:, (f - 16) * 2048:(f - 16) * 2048 + 2048], "BIG2"
    for f in range(NT):
        dst, dk = w1blk(f)
        src = p.dram["l3_W1T"].rearrange("(f p) o -> f p o", p=128)[f]
        S.dma("gdma", dst, src, reads=("l3_W1T",), writes=(dk,), tag="lw1")
    gcol = p.S[1][:, 0:NT]
    S.dma("sdma", gcol, p.dram["l3_lng"][:, :], reads=("l3_lng",), writes=("S1",), tag="lgc")
    lbcol = p.S[2][:, 0:NT]
    S.dma("sdma", lbcol, p.dram["l3_lnb"][:, :], reads=("l3_lnb",), writes=("S2",), tag="lbc")
    ginv = p.S[3][:, 0:NT]
    S.add("vector", lambda eng: eng.reciprocal(ginv, gcol), reads=("S1",), writes=("S3",), tag="lbg0")
    S.add("vector", lambda eng: eng.tensor_tensor(lbcol, lbcol, ginv, ALU.mult),
          reads=("S2", "S3"), writes=("S2",), tag="lbg")
    lbb = p.S[2].ap().bitcast(BF16)[:, 64:96]
    S.add("vector", lambda eng: eng.tensor_copy(lbb, lbcol), reads=("S2",), writes=("S2",))
    for f in range(NT):
        dst, dk = w1blk(f)
        S.add("vector", lambda eng, dst=dst, f=f: eng.tensor_scalar(
            dst, dst, gcol[:, f:f + 1], None, ALU.mult, ALU.bypass),
            reads=(dk, "S1"), writes=(dk,), tag="gfold")
    for f in range(NT):
        wv, wk2 = w1blk(f)
        for oc in range(4):
            def mv(eng, f=f, oc=oc, wv=wv):
                return eng.matmul(p.bank(4 + oc)[0:1, :], lbb[:, f:f + 1],
                                  wv[:, oc * 512:(oc + 1) * 512],
                                  start=(f == 0), stop=(f == NT - 1))
            kk = bkey(4 + oc)
            wkey = (kk + "@acc", kk) if f == 0 else ((kk,) if f == NT - 1 else (kk + "@acc",))
            S.add("tensor", mv, reads=("S2", wk2), writes=wkey, tag="b1mv")
    b1r = [p.R[0].ap().bitcast(BF16)[0:1, :], p.R[1].ap().bitcast(BF16)[0:1, :]]
    b1l = p.R[2][0:1, 0:512]
    for oc in range(4):
        S.dma("sdma", b1l, p.dram["l3_b1"][0:1, oc * 512:(oc + 1) * 512],
              reads=("l3_b1",), writes=("R2",), tag="lb1")
        def bev(eng, oc=oc):
            return eng.tensor_tensor(b1l, b1l, p.bank(4 + oc)[0:1, :], ALU.add)
        S.add("vector", bev, reads=("R2", bkey(4 + oc)), writes=("R2",), tag="bev")
        def bcast(eng, oc=oc):
            return eng.tensor_copy(b1r[oc // 2][0:1, (oc % 2) * 512:(oc % 2) * 512 + 512], b1l)
        S.add("vector", bcast, reads=("R2",), writes=("R%d" % (oc // 2),), tag="bcv")

    cstg = p.R[2].ap().bitcast(BF16)[0:1, :]
    wcb = [(p.B[0], "B0"), (p.B[1], "B1"), (p.B[2], "B2")]
    trio = ("l3_wc", "l3_ws", "l3_mg")
    invs = p.F[2]
    cq, sq = p.F[5], p.F[6]
    sphi, cphi = p.F[3], p.F[4]
    phi = p.F[2]
    ta, tb = p.F[7], p.F[8]
    xf = p.F[0]
    qf2 = p.F[1]
    postc = p.S[1][:, 32:64]
    S.dma("sdma", postc, p.dram["poscols"][:, :], reads=("poscols",), writes=("S1",), tag="lpc")
    l3ph = p.sb("l3ph", [128, 1024], F32)
    phrv = l3ph[:, 0:512]
    phrv2 = l3ph[:, 512:1024]
    S.dma("sdma", l3ph[:, :], p.dram["l3_phases"][:, :], reads=("l3_phases",),
          writes=("l3ph",), tag="lph")
    p.fconst(float(np.pi / 2))
    p.fconst(1e-8)
    p.fconst(-0.5)

    for i in range(NT):
        for ci in range(3):
            S.dma("sdma", wcb[ci][0][:, :], p.dram[trio[ci]][bass.ts(i, 128), :],
                  reads=(trio[ci],), writes=(wcb[ci][1],), tag="lwc")
        S.dma("sdma", xf[:, :], p.dram["xn3"][bass.ts(i, 128), :],
              reads=("xn3",), writes=("F0",), tag="lx3b")
        S.dma("sdma", qf2[:, :], p.dram["l3_qoff"][bass.ts(i, 128), :],
              reads=("l3_qoff",), writes=("F1",), tag="lqo")
        for ci in range(3):
            S.dma("sdma", cstg, p.dram["l3_carr"][i:i + 1, ci * 1024:(ci + 1) * 1024],
                  reads=("l3_carr",), writes=("R2",), tag="lcar")
            for oc in range(2):
                def cs1(eng, ci=ci, oc=oc):
                    return eng.matmul(p.bank(2 * ci + oc), p.tri_sb[:, :],
                                      wcb[ci][0][:, bass.ts(oc, 512)],
                                      start=True, stop=False)
                S.add("tensor", cs1, reads=("tri", wcb[ci][1]),
                      writes=(bkey(2 * ci + oc) + "@acc", bkey(2 * ci + oc)), tag="cs1")
                def cs2(eng, ci=ci, oc=oc):
                    return eng.matmul(p.bank(2 * ci + oc), p.ones1_sb[0:1, :],
                                      cstg[0:1, oc * 512:oc * 512 + 512],
                                      start=False, stop=True)
                S.add("tensor", cs2, reads=("ones1", "R2"),
                      writes=(bkey(2 * ci + oc),), tag="cs2")
        S.add("scalar", lambda eng: eng.activation(invs[:, :], p.bankpair(2)[:, :],
                                                   AF.Ln, bias=1e-8),
              reads=(bkey(4), bkey(5), "fconsts"), writes=("F2",), tag="ln3")
        S.add("scalar", lambda eng: eng.activation(invs[:, :], invs[:, :],
                                                   AF.Exp, scale=-0.5),
              reads=("F2", "fconsts"), writes=("F2",), tag="ex3")
        def mkphi(dst, dstk, addq):
            def f1(eng, dst=dst, i=i):
                eng.tensor_scalar(dst[:, 0:512], phrv, postc[:, i:i + 1], 0.01,
                                  ALU.mult, ALU.mult)
                return eng.tensor_scalar(dst[:, 512:1024], phrv2, postc[:, i:i + 1],
                                         0.01, ALU.mult, ALU.mult)
            S.add("vector", f1, reads=("l3ph", "S1"), writes=(dstk,), tag="mph")
            if addq:
                S.add("vector", lambda eng, dst=dst: eng.tensor_add(dst[:, :], dst[:, :], qf2[:, :]),
                      reads=(dstk, "F1"), writes=(dstk,))
            S.add("vector", lambda eng, dst=dst: eng.tensor_scalar(
                ta[:, :], dst[:, :], 1.0 / TWO_PI, MAGIC, ALU.mult, ALU.add),
                reads=(dstk,), writes=("F7",))
            S.add("vector", lambda eng: eng.tensor_scalar(
                ta[:, :], ta[:, :], MAGIC, None, ALU.subtract, ALU.bypass),
                reads=("F7",), writes=("F7",))
            S.add("vector", lambda eng, dst=dst: eng.scalar_tensor_tensor(
                dst[:, :], ta[:, :], -TWO_PI, dst[:, :], ALU.mult, ALU.add),
                reads=("F7", dstk), writes=(dstk,))
            S.add("vector", lambda eng, dst=dst: eng.tensor_scalar(
                dst[:, :], dst[:, :], PI_C, -PI_C, ALU.min, ALU.max),
                reads=(dstk,), writes=(dstk,))

        mkphi(cq, "F5", True)
        S.add("scalar", lambda eng: eng.activation(sq[:, :], cq[:, :], AF.Sin),
              reads=("F5", "fconsts"), writes=("F6",), tag="sq3")
        S.add("vector", lambda eng: eng.tensor_scalar_mul(tb[:, :], cq[:, :], -1.0),
              reads=("F5",), writes=("F8",))
        S.add("vector", lambda eng: eng.tensor_max(tb[:, :], tb[:, :], cq[:, :]),
              reads=("F8", "F5"), writes=("F8",))
        S.add("scalar", lambda eng: eng.activation(cq[:, :], tb[:, :], AF.Sin,
                                                   bias=float(np.pi / 2), scale=-1.0),
              reads=("F8", "fconsts"), writes=("F5",), tag="cq3")
        mkphi(sphi, "F3", False)
        S.add("vector", lambda eng: eng.tensor_scalar_mul(tb[:, :], sphi[:, :], -1.0),
              reads=("F3",), writes=("F8",))
        S.add("vector", lambda eng: eng.tensor_max(tb[:, :], tb[:, :], sphi[:, :]),
              reads=("F8", "F3"), writes=("F8",))
        S.add("scalar", lambda eng: eng.activation(cphi[:, :], tb[:, :], AF.Sin,
                                                   bias=float(np.pi / 2), scale=-1.0),
              reads=("F8", "fconsts"), writes=("F4",), tag="cph3")
        S.add("scalar", lambda eng: eng.activation(sphi[:, :], sphi[:, :], AF.Sin),
              reads=("F3", "fconsts"), writes=("F3",), tag="sph3")
        S.add("vector", lambda eng: eng.tensor_tensor(ta[:, :], p.bankpair(0)[:, :],
                                                      cq[:, :], ALU.mult),
              reads=(bkey(0), bkey(1), "F5"), writes=("F7",), tag="rr1")
        S.add("vector", lambda eng: eng.tensor_tensor(tb[:, :], p.bankpair(1)[:, :],
                                                      sq[:, :], ALU.mult),
              reads=(bkey(2), bkey(3), "F6"), writes=("F8",), tag="rr2")
        S.add("vector", lambda eng: eng.tensor_add(ta[:, :], ta[:, :], tb[:, :]),
              reads=("F7", "F8"), writes=("F7",))
        S.add("vector", lambda eng: eng.tensor_mul(ta[:, :], ta[:, :], invs[:, :]),
              reads=("F7", "F2"), writes=("F7",))
        S.add("vector", lambda eng: eng.tensor_tensor(tb[:, :], p.bankpair(1)[:, :],
                                                      cq[:, :], ALU.mult),
              reads=(bkey(2), bkey(3), "F5"), writes=("F8",), tag="ri1")
        S.add("vector", lambda eng: eng.tensor_tensor(cq[:, :], p.bankpair(0)[:, :],
                                                      sq[:, :], ALU.mult),
              reads=(bkey(0), bkey(1), "F6"), writes=("F5",), tag="ri2")
        S.add("vector", lambda eng: eng.tensor_sub(tb[:, :], tb[:, :], cq[:, :]),
              reads=("F8", "F5"), writes=("F8",))
        S.add("vector", lambda eng: eng.tensor_mul(tb[:, :], tb[:, :], invs[:, :]),
              reads=("F8", "F2"), writes=("F8",))
        st4 = p.S[4]
        st5 = p.S[5]
        pieces = []
        c1 = cq
        S.add("vector", lambda eng: eng.tensor_tensor(c1[:, :], xf[:, :], cphi[:, :], ALU.mult),
              reads=("F0", "F4"), writes=("F5",), tag="c1")
        c2 = sq
        S.add("vector", lambda eng: eng.tensor_tensor(c2[:, :], xf[:, :], sphi[:, :], ALU.mult),
              reads=("F0", "F3"), writes=("F6",), tag="c2")
        pieces = [(c1, "F5"), (c2, "F6"), (ta, "F7"), (tb, "F8")]
        for pi, (pc, pk) in enumerate(pieces):
            S.add("vector", lambda eng, pc=pc, pi=pi: eng.tensor_reduce(
                st4[:, pi:pi + 1], pc[:, :], AX.X, ALU.add),
                reads=(pk,), writes=("S4",), tag="csum")
            S.add("scalar", lambda eng, pc=pc, pi=pi: eng.activation(
                p.B[3][:, :], pc[:, :], AF.Square, accum_out=st5[:, pi:pi + 1]),
                reads=(pk,), writes=("B3", "S5"), tag="csq")
        S.add("vector", lambda eng: eng.tensor_reduce(
            st4[:, 4:5], st4[:, 0:4], AX.X, ALU.add),
            reads=("S4",), writes=("S4",))
        S.add("vector", lambda eng: eng.tensor_reduce(
            st5[:, 4:5], st5[:, 0:4], AX.X, ALU.add),
            reads=("S5",), writes=("S5",))
        S.add("vector", lambda eng: eng.tensor_scalar_mul(st4[:, 5:6], st4[:, 4:5], 1.0 / (4 * D)),
              reads=("S4",), writes=("S4",))
        S.add("vector", lambda eng: eng.tensor_tensor(st4[:, 6:7], st4[:, 5:6], st4[:, 5:6], ALU.mult),
              reads=("S4",), writes=("S4",))
        S.add("vector", lambda eng: eng.scalar_tensor_tensor(
            st5[:, 5:6], st5[:, 4:5], 1.0 / (4 * D), st4[:, 6:7], ALU.mult, ALU.subtract),
            reads=("S5", "S4"), writes=("S5",))
        p.fconst(1e-5)
        S.add("scalar", lambda eng: eng.activation(st5[:, 6:7], st5[:, 5:6], AF.Ln, bias=1e-5),
              reads=("S5", "fconsts"), writes=("S5",))
        S.add("scalar", lambda eng: eng.activation(st5[:, 6:7], st5[:, 6:7], AF.Exp, scale=-0.5),
              reads=("S5", "fconsts"), writes=("S5",))
        S.add("vector", lambda eng: eng.tensor_tensor(st4[:, 7:8], st4[:, 5:6], st5[:, 6:7], ALU.mult),
              reads=("S4", "S5"), writes=("S4",))
        S.add("vector", lambda eng: eng.tensor_scalar_mul(st4[:, 7:8], st4[:, 7:8], -1.0),
              reads=("S4",), writes=("S4",))
        ctx = [(p.B[0], "B0"), (p.B[1], "B1"), (p.B[2], "B2"), (p.B[3], "B3")]
        for pi, (pc, pk) in enumerate(pieces):
            S.add("scalar", lambda eng, pc=pc, pi=pi: eng.activation(
                ctx[pi][0][:, :], pc[:, :], AF.Identity,
                bias=st4[:, 7:8], scale=st5[:, 6:7]),
                reads=(pk, "S4", "S5"), writes=(ctx[pi][1],), tag="lnap")
        for fb in range(NT):
            pc, pk = ctx[fb // 8]
            sl = fb % 8
            stgb = p.B[4].ap()[:, (fb % 8) * 128:(fb % 8) * 128 + 128]
            stgk = "B4s%d" % (fb % 8)
            bi = fb % 2
            def trf(eng, pc=pc, sl=sl, bi=bi):
                return eng.transpose(p.bank_bf(bi), pc[:, sl * 128:sl * 128 + 128],
                                     p.ident_sb[:, :])
            S.add("tensor", trf, reads=(pk, "ident"), writes=(bkey(bi),), tag="w1t")
            def tre(eng, stgb=stgb, bi=bi):
                return eng.tensor_copy(stgb, p.bank_bf(bi))
            S.add("vector", tre, reads=(bkey(bi),), writes=("B4", stgk), tag="w1e")
            wv, wk2 = w1blk(fb)
            for oc in range(4):
                def w1m(eng, stgb=stgb, wv=wv, oc=oc, fb=fb):
                    return eng.matmul(
                        p.bank(4 + oc), stgb,
                        wv[:, oc * 512: oc * 512 + 512],
                        start=(fb == 0), stop=False)
                kk = bkey(4 + oc)
                wkey = (kk + "@acc", kk) if fb == 0 else (kk + "@acc",)
                S.add("tensor", w1m, reads=(stgk, wk2), writes=wkey, tag="w1m")
        for oc in range(4):
            def b1m(eng, oc=oc):
                return eng.matmul(p.bank(4 + oc), p.ones1_sb[0:1, :],
                                  b1r[oc // 2][0:1, (oc % 2) * 512:(oc % 2) * 512 + 512],
                                  start=False, stop=True)
            S.add("tensor", b1m, reads=("ones1", "R%d" % (oc // 2)),
                  writes=(bkey(4 + oc),), tag="b1m")
        for och in range(2):
            def gel(eng, och=och):
                return eng.activation(p.B[5][:, :], p.bankpair(2 + och)[:, :], AF.Gelu)
            S.add("scalar", gel, reads=(bkey(4 + 2 * och), bkey(5 + 2 * och), "fconsts"),
                  writes=("B5",), tag="gelu")
            for f in range(NF):
                bi = f % 2
                def htf(eng, f=f, bi=bi):
                    return eng.transpose(p.bank_bf(bi), p.B[5][:, bass.ts(f, 128)],
                                         p.ident_sb[:, :])
                S.add("tensor", htf, reads=("B5", "ident"), writes=(bkey(bi),), tag="hT")
                def hte(eng, f=f, bi=bi):
                    return eng.tensor_copy(p.B[4][:, f * 128:f * 128 + 128], p.bank_bf(bi))
                S.add("vector", hte, reads=(bkey(bi),), writes=("B4", "B4s%d" % f), tag="hTe")
            dst = p.dram["l3_hT"].rearrange("(f p) t -> p f t", p=128)[
                :, och * 8:och * 8 + 8, bass.ts(i, 128)]
            S.dma("sdma", dst, p.B[4].ap().rearrange("p (f c) -> p f c", c=128),
                  reads=tuple("B4s%d" % f for f in range(NF)) + ("B4",),
                  writes=("l3_hT",), tag="sthT")


def emit_l3c(p):
    S = p.sched
    w2, w2k = big1(p, 0)
    nf2 = 16
    src = p.dram["l3_W2T"].rearrange("(f p) o -> p f o", p=128)
    S.dma("gdma", p.B1[:, 0:16384].rearrange("p (f o) -> p f o", f=nf2), src,
          reads=("l3_W2T",), writes=("BIG1",), tag="lw2")
    ws_, wsk = big1(p, 2)
    load_weight_bf16(p, S, "l3f_wT", ws_, wsk, D, D)
    for f in range(NF):
        blk = ws_[:, f * D:(f + 1) * D]
        S.add("scalar", lambda eng, blk=blk: eng.activation(blk, blk, AF.Sign),
              reads=(wsk,), writes=(wsk,), tag="sgn3")
    b2r = p.F[6]
    S.dma("sdma", b2r[:, :], p.dram["l3_b2"][:, :], reads=("l3_b2",), writes=("F6",))
    ht = [(p.B[2], "B2"), (p.B[3], "B3")]
    xt = [(p.B[4], "B4"), (p.B[5], "B5")]
    for i in range(NT):
        b = i % 2
        src2 = p.dram["l3_hT"].rearrange("(f p) t -> p f t", p=128)[:, :, bass.ts(i, 128)]
        S.dma("sdma", ht[b][0].ap().rearrange("p (f c) -> p f c", f=NF), src2[:, 0:8, :],
              reads=("l3_hT",), writes=(ht[b][1],), tag="lhT")
        S.dma("sdma", xt[b][0].ap().rearrange("p (f c) -> p f c", f=NF), src2[:, 8:16, :],
              reads=("l3_hT",), writes=(xt[b][1],), tag="lhT2")
        for oc in range(2):
            blocks = [(ht[b][0][:, bass.ts(f, 128)], ht[b][1]) for f in range(NF)] + \
                     [(xt[b][0][:, bass.ts(f, 128)], xt[b][1]) for f in range(NF)]
            mm_acc(p, S, "3w2%d" % oc, p.bank(oc), bkey(oc),
                   blocks,
                   [(p.B1[:, f * 1024 + oc * 512: f * 1024 + oc * 512 + 512], "BIG1")
                    for f in range(nf2)])
        sf = p.F[0]
        S.add("vector", lambda eng: eng.tensor_tensor(sf[:, :], p.bankpair(0)[:, :],
                                                      b2r[:, :], ALU.add),
              reads=(bkey(0), bkey(1), "F6"), writes=("F0",), tag="s3ev")
        xnf = p.F[1]
        S.dma("sdma", xnf[:, :], p.dram["xn3"][bass.ts(i, 128), :],
              reads=("xn3",), writes=("F1",), tag="lx3c")
        S.add("vector", lambda eng: eng.tensor_add(sf[:, :], sf[:, :], xnf[:, :]),
              reads=("F0", "F1"), writes=("F0",))
        xT = [(p.B[0], "B0"), (p.B[1], "B1")][b]
        src3 = p.dram["xn3T"].rearrange("(f p) t -> p f t", p=128)[:, :, bass.ts(i, 128)]
        S.dma("sdma", xT[0].ap().rearrange("p (f c) -> p f c", f=NF), src3,
              reads=("xn3T",), writes=(xT[1],), tag="lx3T")
        for oc in range(2):
            mm_acc(p, S, "3sg%d" % oc, p.bank(2 + oc), bkey(2 + oc),
                   [(xT[0][:, bass.ts(f, 128)], xT[1]) for f in range(NF)],
                   [(ws_[:, f * D + oc * 512: f * D + oc * 512 + 512], wsk)
                    for f in range(NF)])
        gf = p.F[2]
        S.add("scalar", lambda eng: eng.activation(gf[:, :], p.bankpair(1)[:, :], AF.Silu),
              reads=(bkey(2), bkey(3), "fconsts"), writes=("F2",), tag="g3sil")
        x3t = p.F[3]
        S.dma("sdma", x3t[:, :], p.dram["x3"][bass.ts(i, 128), :],
              reads=("x3",), writes=("F3",), tag="lx3e")
        S.add("vector", lambda eng: eng.tensor_mul(sf[:, :], sf[:, :], gf[:, :]),
              reads=("F0", "F2"), writes=("F0",))
        S.add("vector", lambda eng: eng.tensor_add(sf[:, :], sf[:, :], x3t[:, :]),
              reads=("F0", "F3"), writes=("F0",))
        S.dma("sdma", p.dram["y"][bass.ts(i, 128), :], sf[:, :],
              reads=("F0",), writes=("y",), tag="sty")


def emit_l2khat(p, dbg_nocc=False):
    S = p.sched
    p.fconst(float(np.pi / 2))
    ldt = p.S[0][:, 0:32]
    lar = p.S[1][:, 0:32]
    aim = p.S[2][:, 0:32]
    Cr = p.S[3][:, 0:32]
    Ci = p.S[4][:, 0:32]
    for nm, dst, dk in (("l2_ldt", ldt, "S0"), ("l2_lar", lar, "S1"),
                        ("l2_aim", aim, "S2"), ("l2_Cr", Cr, "S3"), ("l2_Ci", Ci, "S4")):
        S.dma("sdma", dst, p.dram[nm][:, :], reads=(nm,), writes=(dk,), tag="lmod")
    t1 = p.S[5][:, 0:32]
    t2 = p.S[5][:, 32:64]
    t3 = p.S[5][:, 64:96]
    lr = p.F[7][:, 0:32]
    li = p.F[7][:, 32:64]
    cfr = p.F[7][:, 64:96]
    cfi = p.F[7][:, 96:128]
    S.add("scalar", lambda eng: eng.activation(ldt, ldt, AF.Exp), reads=("S0", "fconsts"), writes=("S0",))
    S.add("scalar", lambda eng: eng.activation(lar, lar, AF.Exp), reads=("S1", "fconsts"), writes=("S1",))
    S.add("vector", lambda eng: eng.tensor_scalar_mul(lar, lar, -1.0), reads=("S1",), writes=("S1",))
    S.add("vector", lambda eng: eng.tensor_tensor(t1, lar, ldt, ALU.mult), reads=("S0", "S1"), writes=("S5",))
    S.add("vector", lambda eng: eng.tensor_tensor(t2, aim, ldt, ALU.mult), reads=("S0", "S2"), writes=("S5",))
    S.add("scalar", lambda eng: eng.activation(t1, t1, AF.Exp), reads=("S5", "fconsts"), writes=("S5",))
    S.add("vector", lambda eng: eng.tensor_scalar(t3, t2, 1.0 / TWO_PI, MAGIC, ALU.mult, ALU.add),
          reads=("S5",), writes=("S5",))
    S.add("vector", lambda eng: eng.tensor_scalar(t3, t3, MAGIC, None, ALU.subtract, ALU.bypass),
          reads=("S5",), writes=("S5",))
    S.add("vector", lambda eng: eng.scalar_tensor_tensor(t2, t3, -TWO_PI, t2, ALU.mult, ALU.add),
          reads=("S5",), writes=("S5",))
    S.add("vector", lambda eng: eng.tensor_scalar(t2, t2, PI_C, -PI_C, ALU.min, ALU.max),
          reads=("S5",), writes=("S5",))
    S.add("scalar", lambda eng: eng.activation(li, t2, AF.Sin), reads=("S5", "fconsts"), writes=("F7",))
    S.add("vector", lambda eng: eng.tensor_scalar_mul(t3, t2, -1.0), reads=("S5",), writes=("S5",))
    S.add("vector", lambda eng: eng.tensor_max(t3, t3, t2), reads=("S5",), writes=("S5",))
    S.add("scalar", lambda eng: eng.activation(lr, t3, AF.Sin, bias=float(np.pi / 2), scale=-1.0),
          reads=("S5", "fconsts"), writes=("F7",))
    S.add("vector", lambda eng: eng.tensor_tensor(lr, lr, t1, ALU.mult), reads=("F7", "S5"), writes=("F7",))
    S.add("vector", lambda eng: eng.tensor_tensor(li, li, t1, ALU.mult), reads=("F7", "S5"), writes=("F7",))
    aa = p.F[6][:, 0:32]
    nr = p.F[6][:, 32:64]
    ni = p.F[6][:, 64:96]
    S.add("vector", lambda eng: eng.tensor_tensor(aa, lar, lar, ALU.mult), reads=("S1",), writes=("F6",))
    S.add("vector", lambda eng: eng.scalar_tensor_tensor(t1, aim, 1.0, aim, ALU.mult, ALU.mult),
          reads=("S2",), writes=("S5",))
    S.add("vector", lambda eng: eng.tensor_add(aa, aa, t1), reads=("F6", "S5"), writes=("F6",))
    S.add("vector", lambda eng: eng.tensor_scalar(t2, lr, 1.0, None, ALU.subtract, ALU.bypass),
          reads=("F7",), writes=("S5",))
    S.add("vector", lambda eng: eng.tensor_tensor(nr, t2, lar, ALU.mult), reads=("S5", "S1"), writes=("F6",))
    S.add("vector", lambda eng: eng.scalar_tensor_tensor(t3, li, 1.0, aim, ALU.mult, ALU.mult),
          reads=("F7", "S2"), writes=("S5",))
    S.add("vector", lambda eng: eng.tensor_add(nr, nr, t3), reads=("F6", "S5"), writes=("F6",))
    S.add("vector", lambda eng: eng.tensor_tensor(ni, li, lar, ALU.mult), reads=("F7", "S1"), writes=("F6",))
    S.add("vector", lambda eng: eng.tensor_tensor(t3, t2, aim, ALU.mult), reads=("S5", "S2"), writes=("S5",))
    S.add("vector", lambda eng: eng.tensor_sub(ni, ni, t3), reads=("F6", "S5"), writes=("F6",))
    S.add("vector", lambda eng: eng.reciprocal(aa, aa), reads=("F6",), writes=("F6",))
    S.add("vector", lambda eng: eng.tensor_tensor(nr, nr, aa, ALU.mult), reads=("F6",), writes=("F6",))
    S.add("vector", lambda eng: eng.tensor_tensor(ni, ni, aa, ALU.mult), reads=("F6",), writes=("F6",))
    S.add("vector", lambda eng: eng.tensor_tensor(cfr, Cr, nr, ALU.mult), reads=("S3", "F6"), writes=("F7",))
    S.add("vector", lambda eng: eng.tensor_tensor(t3, Ci, ni, ALU.mult), reads=("S4", "F6"), writes=("S5",))
    S.add("vector", lambda eng: eng.tensor_sub(cfr, cfr, t3), reads=("F7", "S5"), writes=("F7",))
    S.add("vector", lambda eng: eng.tensor_scalar_mul(cfr, cfr, 2.0), reads=("F7",), writes=("F7",))
    S.add("vector", lambda eng: eng.tensor_tensor(cfi, Cr, ni, ALU.mult), reads=("S3", "F6"), writes=("F7",))
    S.add("vector", lambda eng: eng.tensor_tensor(t3, Ci, nr, ALU.mult), reads=("S4", "F6"), writes=("S5",))
    S.add("vector", lambda eng: eng.tensor_add(cfi, cfi, t3), reads=("F7", "S5"), writes=("F7",))
    S.add("vector", lambda eng: eng.tensor_scalar_mul(cfi, cfi, -2.0), reads=("F7",), writes=("F7",))
    Br = p.F[0]
    Bi = p.F[1]
    Ar = p.F[2]
    Ai = p.F[3]
    sr = p.S[5][:, 96:97]
    si = p.S[5][:, 97:98]
    tmpa = p.S[5][:, 98:99]
    tmpb = p.S[5][:, 99:100]
    kst = p.B[0].ap().bitcast(F32)[0:4, 0:512]
    for t in range(NT):
        def powtab(dstR, dstRk, dstI, dstIk, base_r, base_i, rdk):
            S.add("vector", lambda eng: eng.memset(dstR[:, t * 32:t * 32 + 1] if False else dstR[:, 0:1], 1.0),
                  writes=(dstRk,))
            S.add("vector", lambda eng: eng.memset(dstI[:, 0:1], 0.0), writes=(dstIk,))
            S.add("vector", lambda eng: eng.tensor_copy(dstR[:, 1:2], base_r), reads=rdk, writes=(dstRk,))
            S.add("vector", lambda eng: eng.tensor_copy(dstI[:, 1:2], base_i), reads=rdk, writes=(dstIk,))
            S.add("vector", lambda eng: eng.tensor_copy(sr, base_r), reads=rdk, writes=("S5",))
            S.add("vector", lambda eng: eng.tensor_copy(si, base_i), reads=rdk, writes=("S5",))
            n = 2
            while n < 64:
                S.add("vector", lambda eng: eng.tensor_tensor(tmpa, sr, sr, ALU.mult), reads=("S5",), writes=("S5",))
                S.add("vector", lambda eng: eng.tensor_tensor(tmpb, si, si, ALU.mult), reads=("S5",), writes=("S5",))
                S.add("vector", lambda eng: eng.tensor_tensor(si, sr, si, ALU.mult), reads=("S5",), writes=("S5",))
                S.add("vector", lambda eng: eng.tensor_scalar_mul(si, si, 2.0), reads=("S5",), writes=("S5",))
                S.add("vector", lambda eng: eng.tensor_sub(sr, tmpa, tmpb), reads=("S5",), writes=("S5",))
                nn = n
                S.add("vector", lambda eng, nn=nn: eng.tensor_scalar(
                    dstR[:, nn:2 * nn], dstR[:, 0:nn], sr, None, ALU.mult, ALU.bypass),
                    reads=(dstRk, "S5"), writes=(dstRk,))
                S.add("vector", lambda eng, nn=nn: eng.tensor_scalar(
                    tmpa if False else p.F[4][:, 0:nn], dstI[:, 0:nn], si, None, ALU.mult, ALU.bypass),
                    reads=(dstIk, "S5"), writes=("F4",))
                S.add("vector", lambda eng, nn=nn: eng.tensor_sub(
                    dstR[:, nn:2 * nn], dstR[:, nn:2 * nn], p.F[4][:, 0:nn]),
                    reads=(dstRk, "F4"), writes=(dstRk,))
                S.add("vector", lambda eng, nn=nn: eng.tensor_scalar(
                    dstI[:, nn:2 * nn], dstI[:, 0:nn], sr, None, ALU.mult, ALU.bypass),
                    reads=(dstIk, "S5"), writes=(dstIk,))
                S.add("vector", lambda eng, nn=nn: eng.tensor_scalar(
                    p.F[4][:, 0:nn], dstR[:, 0:nn], si, None, ALU.mult, ALU.bypass),
                    reads=(dstRk, "S5"), writes=("F4",))
                S.add("vector", lambda eng, nn=nn: eng.tensor_add(
                    dstI[:, nn:2 * nn], dstI[:, nn:2 * nn], p.F[4][:, 0:nn]),
                    reads=(dstIk, "F4"), writes=(dstIk,))
                n *= 2
        powtab(Br, "F0", Bi, "F1", lr[:, t:t + 1], li[:, t:t + 1], ("F7",))
        S.add("vector", lambda eng: eng.tensor_tensor(tmpa, sr, sr, ALU.mult), reads=("S5",), writes=("S5",))
        S.add("vector", lambda eng: eng.tensor_tensor(tmpb, si, si, ALU.mult), reads=("S5",), writes=("S5",))
        S.add("vector", lambda eng: eng.tensor_tensor(si, sr, si, ALU.mult), reads=("S5",), writes=("S5",))
        S.add("vector", lambda eng: eng.tensor_scalar_mul(si, si, 2.0), reads=("S5",), writes=("S5",))
        S.add("vector", lambda eng: eng.tensor_sub(sr, tmpa, tmpb), reads=("S5",), writes=("S5",))
        l64r = p.S[5][:, 100:101]
        l64i = p.S[5][:, 101:102]
        S.add("vector", lambda eng: eng.tensor_copy(l64r, sr), reads=("S5",), writes=("S5",))
        S.add("vector", lambda eng: eng.tensor_copy(l64i, si), reads=("S5",), writes=("S5",))
        powtab(Ar, "F2", Ai, "F3", l64r, l64i, ("S5",))
        for pl, col in ((0, cfr), (1, cfi)):
            dstap = p.dram["l2_bd"].rearrange("(q p) c -> p q c", p=32)
            dst2 = p.dram["l2_bd"][:, :].rearrange("(q r) c -> q r c", q=4)
            def sc(eng, pl=pl, col=col, t=t, dst2=dst2):
                return eng.dma_start(out=dst2[:, :, 4 * pl:4 * pl + 4].rearrange(
                    "q r c -> q r c")[:, :, 0:1] if False else
                    p.dram["l2_bd"].rearrange("(q r) c -> q r c", q=4)
                    .transpose([0, 2, 1])[:, 4 * pl:4 * pl + 1, :].squeeze(1)
                    .rearrange("q r -> (q r)").unsqueeze(0) if False else None, in_=None)
            for q in range(4):
                def sc2(eng, pl=pl, col=col, t=t, q=q):
                    return eng.dma_start(
                        out=p.dram["l2_bd"][32 * q:32 * q + 32, 4 * pl + q:4 * pl + q + 1],
                        in_=col[32 * q:32 * q + 32, t:t + 1],
                        allow_slow_non_contiguous=True)
                S.add("sdma", sc2, reads=("F7",), writes=("l2_bd",), tag="bdsc")
        S.dma("sdma", p.B[1].ap().bitcast(F32)[:, 0:8], p.dram["l2_bd"][:, :],
              reads=("l2_bd",), writes=("B1",), tag="bdld")
        bdb = p.B[2].ap()[:, 0:8]
        S.add("vector", lambda eng: eng.tensor_copy(bdb, p.B[1].ap().bitcast(F32)[:, 0:8]),
              reads=("B1",), writes=("B2",), tag="bdc")
        for lc in range(8):
            aof = lc * 8
            Ab = Ar.ap()[:, aof:aof + 8].unsqueeze(2).to_broadcast([128, 8, 64])
            Aib = Ai.ap()[:, aof:aof + 8].unsqueeze(2).to_broadcast([128, 8, 64])
            Bb = Br.ap()[:, 0:64].unsqueeze(1).to_broadcast([128, 8, 64])
            Bib = Bi.ap()[:, 0:64].unsqueeze(1).to_broadcast([128, 8, 64])
            L1v = p.F[4].ap().rearrange("p (a b) -> p a b", a=8)[:, :, 0:64]
            L2v = p.F[5].ap().rearrange("p (a b) -> p a b", a=8)[:, :, 0:64]
            lrb = p.B[3].ap()[:, 0:512].rearrange("p (a b) -> p a b", a=8)
            lib = p.B[4].ap()[:, 0:512].rearrange("p (a b) -> p a b", a=8)
            S.add("vector", lambda eng, Ab=Ab, Bb=Bb, L1v=L1v: eng.tensor_tensor(L1v, Ab, Bb, ALU.mult),
                  reads=("F2", "F0"), writes=("F4",), tag="lx1")
            S.add("vector", lambda eng, Aib=Aib, Bib=Bib, L2v=L2v: eng.tensor_tensor(L2v, Aib, Bib, ALU.mult),
                  reads=("F3", "F1"), writes=("F5",), tag="lx2")
            S.add("vector", lambda eng, L1v=L1v, L2v=L2v, lrb=lrb: eng.tensor_tensor(lrb, L1v, L2v, ALU.subtract),
                  reads=("F4", "F5"), writes=("B3",), tag="lx3")
            S.add("vector", lambda eng, Ab=Ab, Bib=Bib, L1v=L1v: eng.tensor_tensor(L1v, Ab, Bib, ALU.mult),
                  reads=("F2", "F1"), writes=("F4",), tag="lx4")
            S.add("vector", lambda eng, Aib=Aib, Bb=Bb, L2v=L2v: eng.tensor_tensor(L2v, Aib, Bb, ALU.mult),
                  reads=("F3", "F0"), writes=("F5",), tag="lx5")
            S.add("vector", lambda eng, L1v=L1v, L2v=L2v, lib=lib: eng.tensor_tensor(lib, L1v, L2v, ALU.add),
                  reads=("F4", "F5"), writes=("B4",), tag="lx6")
            def km1(eng, lc=lc):
                return eng.matmul(p.bank(0)[0:4, :], bdb[:, 0:4], p.B[3][:, 0:512],
                                  start=True, stop=False)
            S.add("tensor", km1, reads=("B2", "B3"), writes=(bkey(0) + "@acc", bkey(0)), tag="km1")
            def km2(eng, lc=lc):
                return eng.matmul(p.bank(0)[0:4, :], bdb[:, 4:8], p.B[4][:, 0:512],
                                  start=False, stop=True)
            S.add("tensor", km2, reads=("B2", "B4"), writes=(bkey(0),), tag="km2")
            def kev(eng, lc=lc):
                return eng.tensor_copy(p.B[5].ap().bitcast(F32)[0:4, lc * 512:(lc + 1) * 512]
                                       if False else p.B1.ap().bitcast(F32)[0:4, lc * 512:(lc + 1) * 512],
                                       p.bank(0)[0:4, :])
            S.add("vector", kev, reads=(bkey(0),), writes=("BIG1",), tag="kev")
        def kst2(eng, t=t):
            return eng.dma_start(out=p.dram["l2_kt"][4 * t:4 * t + 4, :],
                                 in_=p.B1.ap().bitcast(F32)[0:4, 0:4096])
        S.add("sdma", kst2, reads=("BIG1",), writes=("l2_kt",), tag="kstore")


def emit_l2inproj(p):
    S = p.sched
    wi, wik = big1(p, 0)
    load_weight_bf16(p, S, "l2_inWT", wi, wik, D, D)
    inb = p.S[0][:, 0:8]
    S.dma("sdma", inb, p.dram["l2_inb"][:, :], reads=("l2_inb",), writes=("S0",))
    xt = [(p.B[2], "B2"), (p.B[3], "B3")]
    ev = [(p.F[0], "F0"), (p.F[1], "F1")]
    for i in range(NT):
        b = i % 2
        src = p.dram["xn2T"].rearrange("(f p) t -> p f t", p=128)[:, :, bass.ts(i, 128)]
        S.dma("sdma", xt[b][0].ap().rearrange("p (f c) -> p f c", f=NF), src,
              reads=("xn2T",), writes=(xt[b][1],), tag="l2lx")
        for ot in range(NF):
            bk = ot // 4
            reg = p.bank(bk)[:, (ot % 4) * 128:(ot % 4) * 128 + 128]
            for f in range(NF):
                def ipm(eng, reg=reg, f=f, ot=ot, b=b):
                    return eng.matmul(reg,
                                      wi[:, f * D + ot * 128: f * D + ot * 128 + 128],
                                      xt[b][0][:, bass.ts(f, 128)],
                                      start=(ot % 4 == 0 and f == 0), stop=(ot % 4 == 3 and f == NF - 1))
                kk = bkey(bk)
                wkey = (kk + "@acc", kk) if (ot % 4 == 0 and f == 0) else                     ((kk,) if (ot % 4 == 3 and f == NF - 1) else (kk + "@acc",))
                S.add("tensor", ipm, reads=(wik, xt[b][1]), writes=wkey, tag="ipm")
        for bk in range(2):
            def uev(eng, bk=bk, b=b):
                srcv = p.bank(bk).rearrange("p (o c) -> p o c", o=4)
                dstv = ev[b][0].ap()[:, bk * 512:bk * 512 + 512].rearrange("p (o c) -> p o c", o=4)
                bb2 = inb[:, bk * 4:bk * 4 + 4].unsqueeze(2).to_broadcast([128, 4, 128])
                return eng.tensor_tensor(dstv, srcv, bb2, ALU.add)
            S.add("vector", uev, reads=(bkey(bk), "S0"), writes=(ev[b][1],), tag="uev")
        dst = p.dram["u32"].rearrange("(o p) t -> p o t", p=128)[:, :, bass.ts(i, 128)]
        S.dma("sdma", dst, ev[b][0].ap().rearrange("p (o c) -> p o c", o=NF),
              reads=(ev[b][1],), writes=("u32",), tag="ust")


def emit_l2fft(p, dbg_nocc=False, part="all"):
    S = p.sched
    nc = p.nc
    kdst = "l2_kout" if part in ("fused", "fusedk") else "l2_kin"
    S.add("vector", lambda eng: eng.engine_nop(), reads=("BIG1",), writes=("BIG1",),
          tag="fcbar")
    cF2r = p.B1[0:64, 0:128]
    cF2i = p.B1[0:64, 128:256]
    cF1r = p.B1[:, 256:322]
    cF1i = p.B1[:, 322:388]
    cF1n = p.B1[:, 388:454]
    cE1r = p.B1[0:66, 454:582]
    cE1i = p.B1[0:66, 582:710]
    cE1n2 = p.B1[0:66, 838:966]
    cE2r = p.B1[:, 710:774]
    cE2n = p.B1[:, 774:838]
    for nm, dst in (("fF2r", cF2r), ("fF2i", cF2i), ("fF1r", cF1r), ("fF1i", cF1i),
                    ("fF1n", cF1n), ("fE1r", cE1r), ("fE1i", cE1i), ("fE2r", cE2r),
                    ("fE2n", cE2n), ("fE1n2", cE1n2)):
        S.dma("sdma", dst, p.dram[nm][:, :], reads=(nm, "BIG1"), writes=("FC",), tag="lfc")
    tWr = p.B1.ap().bitcast(F32)[:, 512:1024]
    tWi = p.B1.ap().bitcast(F32)[:, 1024:1536]
    tVr = p.B1.ap().bitcast(F32)[:, 1536:2048]
    tVi = p.B1.ap().bitcast(F32)[:, 2048:2560]
    for nm, dst in (("ftWr", tWr), ("ftWi", tWi), ("ftVr", tVr), ("ftVi", tVi)):
        S.dma("sdma", dst, p.dram[nm][:, :], reads=(nm, "BIG1"), writes=("FC",), tag="lft")

    GP = 4
    Gp = [(p.B[2], "B2"), (p.B[3], "B3")]
    Hp = [(p.B[4], "B4")]
    Ap_ = [(p.B[5], "B5")]
    kh = p.B1.ap().bitcast(F32)[0:66, 4096:4608]
    ys = p.B1.ap().bitcast(F32)[0:64, 4608:5120]

    def fwd_batch(src_name, ch0, nch, out_khat):
        for gi in range(GP):
            c0 = ch0 + 2 * gi
            srcap = p.dram[src_name].rearrange("c (m2 m1) -> c m2 m1", m1=64)
            sl = srcap[c0:c0 + 2]
            sl2 = sl.transpose([1, 0, 2])
            def ld(eng, sl2=sl2, gi=gi):
                return eng.dma_start(out=p.B[0].ap()[0:64, gi * 128:(gi + 1) * 128]
                                     .rearrange("p (c m) -> p c m", c=2), in_=sl2)
            S.add("gdma", ld, reads=(src_name,), writes=("B0",), tag="fld")
        for gi in range(GP):
            bk = gi // 2
            first = (gi % 2 == 0)
            lastb = (gi % 2 == 1)
            def s1r(eng, gi=gi):
                return eng.matmul(p.pool[0][:, gi * 256:gi * 256 + 128],
                                  p.B[0][0:64, gi * 128:(gi + 1) * 128], cF2r,
                                  start=(gi % 2 == 0), stop=False)
            S.add("tensor", s1r, reads=("B0", "BIG1"),
                  writes=(bkey(bk) + "@acc", bkey(bk)) if first else (bkey(bk) + "@acc",),
                  tag="s1r")
            def s1i(eng, gi=gi):
                return eng.matmul(p.pool[0][:, gi * 256 + 128:gi * 256 + 256],
                                  p.B[0][0:64, gi * 128:(gi + 1) * 128], cF2i,
                                  start=False, stop=(gi % 2 == 1))
            S.add("tensor", s1i, reads=("B0", "BIG1"),
                  writes=(bkey(bk),) if lastb else (bkey(bk) + "@acc",), tag="s1i")
        ps4 = p.pool[0].ap().rearrange("p (g x) -> p g x", g=GP)
        Gr4 = ps4[:, :, 0:128]
        Gi4 = ps4[:, :, 128:256]
        W4r = tWr.rearrange("p (g x) -> p g x", g=GP)
        W4i = tWi.rearrange("p (g x) -> p g x", g=GP)
        g4r = Gp[0][0].ap().rearrange("p (g x) -> p g x", g=GP)[:, :, 0:128]
        g4i = Gp[0][0].ap().rearrange("p (g x) -> p g x", g=GP)[:, :, 128:256]
        scr1 = p.F[4].ap().rearrange("p (g x) -> p g x", g=GP)[:, :, 0:128]
        scr2 = p.F[5].ap().rearrange("p (g x) -> p g x", g=GP)[:, :, 0:128]
        S.add("vector", lambda eng: eng.tensor_tensor(scr1, Gr4, W4r, ALU.mult),
              reads=(bkey(0), bkey(1), "BIG1"), writes=("F4",), tag="tw1")
        S.add("vector", lambda eng: eng.tensor_tensor(scr2, Gi4, W4i, ALU.mult),
              reads=(bkey(0), bkey(1), "BIG1"), writes=("F5",), tag="tw2")
        S.add("vector", lambda eng: eng.tensor_tensor(g4r, scr1, scr2, ALU.subtract),
              reads=("F4", "F5"), writes=(Gp[0][1],), tag="tw3")
        S.add("vector", lambda eng: eng.tensor_tensor(scr1, Gr4, W4i, ALU.mult),
              reads=(bkey(0), bkey(1), "BIG1"), writes=("F4",), tag="tw4")
        S.add("vector", lambda eng: eng.tensor_tensor(scr2, Gi4, W4r, ALU.mult),
              reads=(bkey(0), bkey(1), "BIG1"), writes=("F5",), tag="tw5")
        S.add("vector", lambda eng: eng.tensor_tensor(g4i, scr1, scr2, ALU.add),
              reads=("F4", "F5"), writes=(Gp[0][1],), tag="tw6")
        for gi in range(GP):
            gr = Gp[0][0][:, gi * 256:gi * 256 + 128]
            gb = Gp[0][0][:, gi * 256 + 128:gi * 256 + 256]
            def x1(eng, gr=gr, gi2=gi):
                return eng.matmul(p.bank(2)[0:66, gi2 * 128:gi2 * 128 + 128], cF1r, gr,
                                  start=(gi2 == 0), stop=False)
            S.add("tensor", x1, reads=(Gp[0][1], "BIG1"),
                  writes=(bkey(2) + "@acc", bkey(2)) if gi == 0 else (bkey(2) + "@acc",), tag="x1")
            def x2(eng, gb=gb, gi2=gi):
                return eng.matmul(p.bank(2)[0:66, gi2 * 128:gi2 * 128 + 128], cF1n, gb,
                                  start=False, stop=(gi2 == GP - 1))
            S.add("tensor", x2, reads=(Gp[0][1], "BIG1"),
                  writes=(bkey(2),) if gi == GP - 1 else (bkey(2) + "@acc",), tag="x2")
            def x3(eng, gr=gr, gi2=gi):
                return eng.matmul(p.bank(3)[0:66, gi2 * 128:gi2 * 128 + 128], cF1i, gr,
                                  start=(gi2 == 0), stop=False)
            S.add("tensor", x3, reads=(Gp[0][1], "BIG1"),
                  writes=(bkey(3) + "@acc", bkey(3)) if gi == 0 else (bkey(3) + "@acc",), tag="x3")
            def x4(eng, gb=gb, gi2=gi):
                return eng.matmul(p.bank(3)[0:66, gi2 * 128:gi2 * 128 + 128], cF1r, gb,
                                  start=False, stop=(gi2 == GP - 1))
            S.add("tensor", x4, reads=(Gp[0][1], "BIG1"),
                  writes=(bkey(3),) if gi == GP - 1 else (bkey(3) + "@acc",), tag="x4")
        if out_khat:
            for gi in range(nch // 2):
                c0 = ch0 + 2 * gi
                def khe(eng, gi=gi):
                    eng.tensor_copy(kh[0:66, 0:128], p.bank(2)[0:66, gi * 128:gi * 128 + 128])
                    return eng.tensor_copy(kh[0:66, 128:256], p.bank(3)[0:66, gi * 128:gi * 128 + 128])
                S.add("vector", khe, reads=(bkey(2), bkey(3)), writes=("BIG1",), tag="khe")
                for ci in range(2):
                    dstc = p.dram[kdst][c0 + ci:c0 + ci + 1].rearrange(
                        "c (pl k1 k2) -> (c k1) pl k2", pl=2, k1=33)
                    def khs(eng, ci=ci, dstc=dstc):
                        srcv = kh[33 * ci:33 * ci + 33, 0:256].rearrange(
                            "p (pl k2) -> p pl k2", pl=2)
                        return eng.dma_start(out=dstc, in_=srcv)
                    S.add("sdma", khs, reads=("BIG1",), writes=(kdst,), tag="khs")
            return
        for gi in range(GP):
            c0 = ch0 + 2 * gi
            for ci in range(2):
                srckc = p.dram["l2_kout"][c0 + ci:c0 + ci + 1].rearrange(
                    "c (pl k1 k2) -> (c k1) pl k2", pl=2, k1=33)
                def kl(eng, ci=ci, srckc=srckc):
                    dstv = kh[33 * ci:33 * ci + 33, 0:256].rearrange(
                        "p (pl k2) -> p pl k2", pl=2)
                    return eng.dma_start(out=dstv, in_=srckc)
                S.add("adma", kl, reads=("l2_kout",), writes=("BIG1",), tag="kl")
            Xr = p.bank(2)[0:66, gi * 128:gi * 128 + 128]
            Xi = p.bank(3)[0:66, gi * 128:gi * 128 + 128]
            khr = kh[0:66, 0:128]
            khi = kh[0:66, 128:256]
            hr = Hp[0][0][0:66, gi * 256:gi * 256 + 128]
            hi = Hp[0][0][0:66, gi * 256 + 128:gi * 256 + 256]
            s1v = p.F[4][0:66, 0:128]
            s2v = p.F[5][0:66, 0:128]
            S.add("vector", lambda eng, Xr=Xr, khr=khr, s1v=s1v: eng.tensor_tensor(s1v, Xr, khr, ALU.mult),
                  reads=(bkey(2), "BIG1"), writes=("F4",), tag="pw1")
            S.add("vector", lambda eng, Xi=Xi, khi=khi, s2v=s2v: eng.tensor_tensor(s2v, Xi, khi, ALU.mult),
                  reads=(bkey(3), "BIG1"), writes=("F5",), tag="pw2")
            S.add("vector", lambda eng, hr=hr, s1v=s1v, s2v=s2v: eng.tensor_tensor(hr, s1v, s2v, ALU.subtract),
                  reads=("F4", "F5"), writes=(Hp[0][1],), tag="pw3")
            S.add("vector", lambda eng, Xr=Xr, khi=khi, s1v=s1v: eng.tensor_tensor(s1v, Xr, khi, ALU.mult),
                  reads=(bkey(2), "BIG1"), writes=("F4",), tag="pw4")
            S.add("vector", lambda eng, Xi=Xi, khr=khr, s2v=s2v: eng.tensor_tensor(s2v, Xi, khr, ALU.mult),
                  reads=(bkey(3), "BIG1"), writes=("F5",), tag="pw5")
            S.add("vector", lambda eng, hi=hi, s1v=s1v, s2v=s2v: eng.tensor_tensor(hi, s1v, s2v, ALU.add),
                  reads=("F4", "F5"), writes=(Hp[0][1],), tag="pw6")
        for gi in range(GP):
            hr = Hp[0][0][0:66, gi * 256:gi * 256 + 128]
            hi = Hp[0][0][0:66, gi * 256 + 128:gi * 256 + 256]
            def a1(eng, hr=hr, gi2=gi):
                return eng.matmul(p.bank(4)[:, gi2 * 128:gi2 * 128 + 128], hr, cE1r,
                                  start=(gi2 == 0), stop=False)
            S.add("tensor", a1, reads=(Hp[0][1], "BIG1"),
                  writes=(bkey(4) + "@acc", bkey(4)) if gi == 0 else (bkey(4) + "@acc",), tag="a1")
            def a2(eng, hi=hi, gi2=gi):
                return eng.matmul(p.bank(4)[:, gi2 * 128:gi2 * 128 + 128], hi, cE1n2,
                                  start=False, stop=(gi2 == GP - 1))
            S.add("tensor", a2, reads=(Hp[0][1], "BIG1"),
                  writes=(bkey(4),) if gi == GP - 1 else (bkey(4) + "@acc",), tag="a2")
            def a3(eng, hr=hr, gi2=gi):
                return eng.matmul(p.bank(5)[:, gi2 * 128:gi2 * 128 + 128], hr, cE1i,
                                  start=(gi2 == 0), stop=False)
            S.add("tensor", a3, reads=(Hp[0][1], "BIG1"),
                  writes=(bkey(5) + "@acc", bkey(5)) if gi == 0 else (bkey(5) + "@acc",), tag="a3")
            def a4(eng, hi=hi, gi2=gi):
                return eng.matmul(p.bank(5)[:, gi2 * 128:gi2 * 128 + 128], hi, cE1r,
                                  start=False, stop=(gi2 == GP - 1))
            S.add("tensor", a4, reads=(Hp[0][1], "BIG1"),
                  writes=(bkey(5),) if gi == GP - 1 else (bkey(5) + "@acc",), tag="a4")
        A4r = p.pool[2].ap().rearrange("p (g x) -> p g x", g=GP)[:, :, 0:128] if False else \
            p.bank(4).rearrange("p (g x) -> p g x", g=GP)
        A4i = p.bank(5).rearrange("p (g x) -> p g x", g=GP)
        V4r = tVr.rearrange("p (g x) -> p g x", g=GP)
        V4i = tVi.rearrange("p (g x) -> p g x", g=GP)
        ap4r = Ap_[0][0].ap().rearrange("p (g x) -> p g x", g=GP)[:, :, 0:128]
        ap4i = Ap_[0][0].ap().rearrange("p (g x) -> p g x", g=GP)[:, :, 128:256]
        scr1f = p.F[4].ap().rearrange("p (g x) -> p g x", g=GP)[:, :, 0:128]
        scr2f = p.F[5].ap().rearrange("p (g x) -> p g x", g=GP)[:, :, 0:128]
        S.add("vector", lambda eng: eng.tensor_tensor(scr1f, A4r, V4r, ALU.mult),
              reads=(bkey(4), "BIG1"), writes=("F4",), tag="iv1")
        S.add("vector", lambda eng: eng.tensor_tensor(scr2f, A4i, V4i, ALU.mult),
              reads=(bkey(5), "BIG1"), writes=("F5",), tag="iv2")
        S.add("vector", lambda eng: eng.tensor_tensor(ap4r, scr1f, scr2f, ALU.subtract),
              reads=("F4", "F5"), writes=(Ap_[0][1],), tag="iv3")
        S.add("vector", lambda eng: eng.tensor_tensor(scr1f, A4r, V4i, ALU.mult),
              reads=(bkey(4), "BIG1"), writes=("F4",), tag="iv4")
        S.add("vector", lambda eng: eng.tensor_tensor(scr2f, A4i, V4r, ALU.mult),
              reads=(bkey(5), "BIG1"), writes=("F5",), tag="iv5")
        S.add("vector", lambda eng: eng.tensor_tensor(ap4i, scr1f, scr2f, ALU.add),
              reads=("F4", "F5"), writes=(Ap_[0][1],), tag="iv6")
        for gi in range(GP):
            apr = Ap_[0][0][:, gi * 256:gi * 256 + 128]
            api = Ap_[0][0][:, gi * 256 + 128:gi * 256 + 256]
            def y1(eng, apr=apr, gi2=gi):
                return eng.matmul(p.bank(6)[0:64, gi2 * 128:gi2 * 128 + 128], cE2r, apr,
                                  start=(gi2 == 0), stop=False)
            S.add("tensor", y1, reads=(Ap_[0][1], "BIG1"),
                  writes=(bkey(6) + "@acc", bkey(6)) if gi == 0 else (bkey(6) + "@acc",), tag="y1")
            def y2(eng, api=api, gi2=gi):
                return eng.matmul(p.bank(6)[0:64, gi2 * 128:gi2 * 128 + 128], cE2n, api,
                                  start=False, stop=(gi2 == GP - 1))
            S.add("tensor", y2, reads=(Ap_[0][1], "BIG1"),
                  writes=(bkey(6),) if gi == GP - 1 else (bkey(6) + "@acc",), tag="y2")
        S.add("scalar", lambda eng: eng.copy(ys[0:64, :], p.bank(6)[0:64, :]),
              reads=(bkey(6),), writes=("BIG1",), tag="yev")
        for gi in range(GP):
            c0 = ch0 + 2 * gi
            dst = p.dram["yconv"][c0:c0 + 2].rearrange("c (m2 m1) -> m2 c m1", m1=64)
            def yst(eng, dst=dst, gi=gi):
                return eng.dma_start(out=dst, in_=ys[0:64, gi * 128:gi * 128 + 128]
                                     .rearrange("p (c m) -> p c m", c=2))
            S.add("sdma", yst, reads=("BIG1",), writes=("yconv",), tag="yst")

    if part == "u":
        S.add("vector", lambda eng: eng.engine_nop(), reads=(), writes=("l2_kout",))
        for bb in range(128):
            fwd_batch("u32", bb * 8, 8, False)
        return
    b1f = p.B1.ap().bitcast(F32)
    Bin = [(p.B[0], "B0"), (p.B[1], "B1")]
    Gpar = [(p.B[2].ap(), "B2"), (p.B[3].ap(), "B3")]
    Hpar = [(p.B[4].ap(), "B4"), (p.B2.ap()[:, 0:1024], "BIG2")]
    Apar = [(p.B[5].ap(), "B5"), (p.B2.ap()[:, 1024:2048], "BIG2")]
    kh4 = [b1f[0:66, 4096:5120], b1f[0:66, 5120:6144]]
    ys2 = [b1f[0:64, 6144:6656], b1f[0:64, 6656:7168]]

    def fld_st(i, src_name):
        par = i % 2
        ch0 = i * 8
        for gi in range(GP):
            c0 = ch0 + 2 * gi
            srcap = p.dram[src_name].rearrange("c (m2 m1) -> c m2 m1", m1=64)
            sl2 = srcap[c0:c0 + 2].transpose([1, 0, 2])
            def ld(eng, sl2=sl2, gi=gi, par=par):
                return eng.dma_start(out=Bin[par][0].ap()[0:64, gi * 128:(gi + 1) * 128]
                                     .rearrange("p (c m) -> p c m", c=2), in_=sl2)
            S.add("gdma", ld, reads=(src_name,), writes=(Bin[par][1],), tag="fld")

    def s1_st(i):
        par = i % 2
        for gi in range(GP):
            bk = gi // 2
            def s1r(eng, gi=gi, par=par):
                return eng.matmul(p.pool[0][:, gi * 256:gi * 256 + 128],
                                  Bin[par][0][0:64, gi * 128:(gi + 1) * 128], cF2r,
                                  start=(gi % 2 == 0), stop=False)
            S.add("tensor", s1r, reads=(Bin[par][1], "FC"),
                  writes=(bkey(bk) + "@acc", bkey(bk)) if gi % 2 == 0 else (bkey(bk) + "@acc",),
                  tag="s1r")
            def s1i(eng, gi=gi, par=par):
                return eng.matmul(p.pool[0][:, gi * 256 + 128:gi * 256 + 256],
                                  Bin[par][0][0:64, gi * 128:(gi + 1) * 128], cF2i,
                                  start=False, stop=(gi % 2 == 1))
            S.add("tensor", s1i, reads=(Bin[par][1], "FC"),
                  writes=(bkey(bk),) if gi % 2 == 1 else (bkey(bk) + "@acc",), tag="s1i")

    def tw_st(i):
        par = i % 2
        Ga, Gk = Gpar[par]
        ps4 = p.pool[0].ap().rearrange("p (g x) -> p g x", g=GP)
        Gr4 = ps4[:, :, 0:128]
        Gi4 = ps4[:, :, 128:256]
        W4r = tWr.rearrange("p (g x) -> p g x", g=GP)
        W4i = tWi.rearrange("p (g x) -> p g x", g=GP)
        g4r = Ga.rearrange("p (g x) -> p g x", g=GP)[:, :, 0:128]
        g4i = Ga.rearrange("p (g x) -> p g x", g=GP)[:, :, 128:256]
        scr1 = p.F[4].ap().rearrange("p (g x) -> p g x", g=GP)[:, :, 0:128]
        scr2 = p.F[5].ap().rearrange("p (g x) -> p g x", g=GP)[:, :, 0:128]
        S.add("vector", lambda eng: eng.tensor_tensor(scr1, Gr4, W4r, ALU.mult),
              reads=(bkey(0), bkey(1), "FC"), writes=("F4",), tag="tw1")
        S.add("vector", lambda eng: eng.tensor_tensor(scr2, Gi4, W4i, ALU.mult),
              reads=(bkey(0), bkey(1), "FC"), writes=("F5",), tag="tw2")
        S.add("vector", lambda eng, g4r=g4r: eng.tensor_tensor(g4r, scr1, scr2, ALU.subtract),
              reads=("F4", "F5"), writes=(Gk,), tag="tw3")
        S.add("vector", lambda eng: eng.tensor_tensor(scr1, Gr4, W4i, ALU.mult),
              reads=(bkey(0), bkey(1), "FC"), writes=("F4",), tag="tw4")
        S.add("vector", lambda eng: eng.tensor_tensor(scr2, Gi4, W4r, ALU.mult),
              reads=(bkey(0), bkey(1), "FC"), writes=("F5",), tag="tw5")
        S.add("vector", lambda eng, g4i=g4i: eng.tensor_tensor(g4i, scr1, scr2, ALU.add),
              reads=("F4", "F5"), writes=(Gk,), tag="tw6")

    def x_st(i):
        par = i % 2
        Ga, Gk = Gpar[par]
        for gi in range(GP):
            gr = Ga[:, gi * 256:gi * 256 + 128]
            gb = Ga[:, gi * 256 + 128:gi * 256 + 256]
            def x1(eng, gr=gr, gi2=gi):
                return eng.matmul(p.bank(2)[0:66, gi2 * 128:gi2 * 128 + 128], cF1r, gr,
                                  start=(gi2 == 0), stop=False)
            S.add("tensor", x1, reads=(Gk, "FC"),
                  writes=(bkey(2) + "@acc", bkey(2)) if gi == 0 else (bkey(2) + "@acc",), tag="x1")
            def x2(eng, gb=gb, gi2=gi):
                return eng.matmul(p.bank(2)[0:66, gi2 * 128:gi2 * 128 + 128], cF1n, gb,
                                  start=False, stop=(gi2 == GP - 1))
            S.add("tensor", x2, reads=(Gk, "FC"),
                  writes=(bkey(2),) if gi == GP - 1 else (bkey(2) + "@acc",), tag="x2")
            def x3(eng, gr=gr, gi2=gi):
                return eng.matmul(p.bank(3)[0:66, gi2 * 128:gi2 * 128 + 128], cF1i, gr,
                                  start=(gi2 == 0), stop=False)
            S.add("tensor", x3, reads=(Gk, "FC"),
                  writes=(bkey(3) + "@acc", bkey(3)) if gi == 0 else (bkey(3) + "@acc",), tag="x3")
            def x4(eng, gb=gb, gi2=gi):
                return eng.matmul(p.bank(3)[0:66, gi2 * 128:gi2 * 128 + 128], cF1r, gb,
                                  start=False, stop=(gi2 == GP - 1))
            S.add("tensor", x4, reads=(Gk, "FC"),
                  writes=(bkey(3),) if gi == GP - 1 else (bkey(3) + "@acc",), tag="x4")

    def khe_st(i):
        par = i % 2
        k4 = kh4[par]
        def khe(eng, k4=k4):
            eng.copy(k4.rearrange("p (g x) -> p g x", g=GP)[:, :, 0:128],
                     p.bank(2)[0:66, :].rearrange("p (g x) -> p g x", g=GP))
            return eng.copy(k4.rearrange("p (g x) -> p g x", g=GP)[:, :, 128:256],
                            p.bank(3)[0:66, :].rearrange("p (g x) -> p g x", g=GP))
        S.add("scalar", khe, reads=(bkey(2), bkey(3)), writes=("KH%d" % par,), tag="khe")
        ch0 = i * 8
        for gi in range(GP):
            c0 = ch0 + 2 * gi
            for ci in range(2):
                dstc = p.dram[kdst][c0 + ci:c0 + ci + 1].rearrange(
                    "c (pl k1 k2) -> (c k1) pl k2", pl=2, k1=33)
                def khs(eng, ci=ci, dstc=dstc, gi=gi, k4=k4):
                    srcv = k4[33 * ci:33 * ci + 33, 256 * gi:256 * gi + 256].rearrange(
                        "p (pl k2) -> p pl k2", pl=2)
                    return eng.dma_start(out=dstc, in_=srcv)
                S.add("sdma", khs, reads=("KH%d" % par,), writes=(kdst,), tag="khs")

    def kl_st(i):
        par = i % 2
        k4 = kh4[par]
        ch0 = i * 8
        for gi in range(GP):
            c0 = ch0 + 2 * gi
            for ci in range(2):
                srckc = p.dram["l2_kout"][c0 + ci:c0 + ci + 1].rearrange(
                    "c (pl k1 k2) -> (c k1) pl k2", pl=2, k1=33)
                def kl(eng, ci=ci, srckc=srckc, gi=gi, k4=k4):
                    dstv = k4[33 * ci:33 * ci + 33, 256 * gi:256 * gi + 256].rearrange(
                        "p (pl k2) -> p pl k2", pl=2)
                    return eng.dma_start(out=dstv, in_=srckc)
                S.add("adma", kl, reads=("l2_kout",), writes=("KH%d" % par,), tag="kl")

    def pw_st(i):
        par = i % 2
        k4 = kh4[par]
        Ha, Hk = Hpar[par]
        X4r = p.bank(2)[0:66, :].rearrange("p (g x) -> p g x", g=GP)
        X4i = p.bank(3)[0:66, :].rearrange("p (g x) -> p g x", g=GP)
        kv = k4.rearrange("p (g x) -> p g x", g=GP)
        khr4 = kv[:, :, 0:128]
        khi4 = kv[:, :, 128:256]
        Hv = Ha[0:66, 0:1024].rearrange("p (g x) -> p g x", g=GP)
        hr4 = Hv[:, :, 0:128]
        hi4 = Hv[:, :, 128:256]
        s1v = p.F[4].ap()[0:66, 0:512].rearrange("p (g x) -> p g x", g=GP)
        s2v = p.F[5].ap()[0:66, 0:512].rearrange("p (g x) -> p g x", g=GP)
        S.add("vector", lambda eng: eng.tensor_tensor(s1v, X4r, khr4, ALU.mult),
              reads=(bkey(2), "KH%d" % par), writes=("F4",), tag="pw1")
        S.add("vector", lambda eng: eng.tensor_tensor(s2v, X4i, khi4, ALU.mult),
              reads=(bkey(3), "KH%d" % par), writes=("F5",), tag="pw2")
        S.add("vector", lambda eng: eng.tensor_tensor(hr4, s1v, s2v, ALU.subtract),
              reads=("F4", "F5"), writes=(Hk,), tag="pw3")
        S.add("vector", lambda eng: eng.tensor_tensor(s1v, X4r, khi4, ALU.mult),
              reads=(bkey(2), "KH%d" % par), writes=("F4",), tag="pw4")
        S.add("vector", lambda eng: eng.tensor_tensor(s2v, X4i, khr4, ALU.mult),
              reads=(bkey(3), "KH%d" % par), writes=("F5",), tag="pw5")
        S.add("vector", lambda eng: eng.tensor_tensor(hi4, s1v, s2v, ALU.add),
              reads=("F4", "F5"), writes=(Hk,), tag="pw6")

    def a_st(i):
        par = i % 2
        Ha, Hk = Hpar[par]
        for gi in range(GP):
            hr = Ha[0:66, gi * 256:gi * 256 + 128]
            hi = Ha[0:66, gi * 256 + 128:gi * 256 + 256]
            def a1(eng, hr=hr, gi2=gi):
                return eng.matmul(p.bank(4)[:, gi2 * 128:gi2 * 128 + 128], hr, cE1r,
                                  start=(gi2 == 0), stop=False)
            S.add("tensor", a1, reads=(Hk, "FC"),
                  writes=(bkey(4) + "@acc", bkey(4)) if gi == 0 else (bkey(4) + "@acc",), tag="a1")
            def a2(eng, hi=hi, gi2=gi):
                return eng.matmul(p.bank(4)[:, gi2 * 128:gi2 * 128 + 128], hi, cE1n2,
                                  start=False, stop=(gi2 == GP - 1))
            S.add("tensor", a2, reads=(Hk, "FC"),
                  writes=(bkey(4),) if gi == GP - 1 else (bkey(4) + "@acc",), tag="a2")
            def a3(eng, hr=hr, gi2=gi):
                return eng.matmul(p.bank(5)[:, gi2 * 128:gi2 * 128 + 128], hr, cE1i,
                                  start=(gi2 == 0), stop=False)
            S.add("tensor", a3, reads=(Hk, "FC"),
                  writes=(bkey(5) + "@acc", bkey(5)) if gi == 0 else (bkey(5) + "@acc",), tag="a3")
            def a4(eng, hi=hi, gi2=gi):
                return eng.matmul(p.bank(5)[:, gi2 * 128:gi2 * 128 + 128], hi, cE1r,
                                  start=False, stop=(gi2 == GP - 1))
            S.add("tensor", a4, reads=(Hk, "FC"),
                  writes=(bkey(5),) if gi == GP - 1 else (bkey(5) + "@acc",), tag="a4")

    def iv_st(i):
        par = i % 2
        Aa, Ak = Apar[par]
        A4r = p.bank(4).rearrange("p (g x) -> p g x", g=GP)
        A4i = p.bank(5).rearrange("p (g x) -> p g x", g=GP)
        V4r = tVr.rearrange("p (g x) -> p g x", g=GP)
        V4i = tVi.rearrange("p (g x) -> p g x", g=GP)
        ap4r = Aa.rearrange("p (g x) -> p g x", g=GP)[:, :, 0:128]
        ap4i = Aa.rearrange("p (g x) -> p g x", g=GP)[:, :, 128:256]
        scr1f = p.F[4].ap().rearrange("p (g x) -> p g x", g=GP)[:, :, 0:128]
        scr2f = p.F[5].ap().rearrange("p (g x) -> p g x", g=GP)[:, :, 0:128]
        S.add("vector", lambda eng: eng.tensor_tensor(scr1f, A4r, V4r, ALU.mult),
              reads=(bkey(4), "FC"), writes=("F4",), tag="iv1")
        S.add("vector", lambda eng: eng.tensor_tensor(scr2f, A4i, V4i, ALU.mult),
              reads=(bkey(5), "FC"), writes=("F5",), tag="iv2")
        S.add("vector", lambda eng, ap4r=ap4r: eng.tensor_tensor(ap4r, scr1f, scr2f, ALU.subtract),
              reads=("F4", "F5"), writes=(Ak,), tag="iv3")
        S.add("vector", lambda eng: eng.tensor_tensor(scr1f, A4r, V4i, ALU.mult),
              reads=(bkey(4), "FC"), writes=("F4",), tag="iv4")
        S.add("vector", lambda eng: eng.tensor_tensor(scr2f, A4i, V4r, ALU.mult),
              reads=(bkey(5), "FC"), writes=("F5",), tag="iv5")
        S.add("vector", lambda eng, ap4i=ap4i: eng.tensor_tensor(ap4i, scr1f, scr2f, ALU.add),
              reads=("F4", "F5"), writes=(Ak,), tag="iv6")

    def y_st(i):
        par = i % 2
        Aa, Ak = Apar[par]
        for gi in range(GP):
            apr = Aa[:, gi * 256:gi * 256 + 128]
            api = Aa[:, gi * 256 + 128:gi * 256 + 256]
            def y1(eng, apr=apr, gi2=gi):
                return eng.matmul(p.bank(6)[0:64, gi2 * 128:gi2 * 128 + 128], cE2r, apr,
                                  start=(gi2 == 0), stop=False)
            S.add("tensor", y1, reads=(Ak, "FC"),
                  writes=(bkey(6) + "@acc", bkey(6)) if gi == 0 else (bkey(6) + "@acc",), tag="y1")
            def y2(eng, api=api, gi2=gi):
                return eng.matmul(p.bank(6)[0:64, gi2 * 128:gi2 * 128 + 128], cE2n, api,
                                  start=False, stop=(gi2 == GP - 1))
            S.add("tensor", y2, reads=(Ak, "FC"),
                  writes=(bkey(6),) if gi == GP - 1 else (bkey(6) + "@acc",), tag="y2")

    def yev_st(i):
        par = i % 2
        ysb = ys2[par]
        S.add("scalar", lambda eng, ysb=ysb: eng.copy(ysb[0:64, :], p.bank(6)[0:64, :]),
              reads=(bkey(6),), writes=("YS%d" % par,), tag="yev")
        ch0 = i * 8
        for gi in range(GP):
            c0 = ch0 + 2 * gi
            dst = p.dram["yconv"][c0:c0 + 2].rearrange("c (m2 m1) -> m2 c m1", m1=64)
            def yst(eng, dst=dst, gi=gi, ysb=ysb):
                return eng.dma_start(out=dst, in_=ysb[0:64, gi * 128:gi * 128 + 128]
                                     .rearrange("p (c m) -> p c m", c=2))
            S.add("sdma", yst, reads=("YS%d" % par,), writes=("yconv",), tag="yst")

    if part in ("fused", "fusedk"):
        nbk = 128
        for j in range(nbk + 6):
            if 0 <= j - 5 < nbk:
                khe_st(j - 5)
            if 0 <= j - 4 < nbk:
                x_st(j - 4)
            if 0 <= j - 3 < nbk:
                tw_st(j - 3)
            if 0 <= j - 2 < nbk:
                s1_st(j - 2)
            if 0 <= j < nbk:
                fld_st(j, "l2_kt")
        m32f = p.B1.ap().bitcast(F32)[:, 7168:7424]
        S.dma("sdma", m32f, p.dram["l2_k32m"][:, :], reads=("l2_k32m", "BIG1"), writes=("M32",))
        stg32 = p.F[4][:, 0:256]
        stg0 = p.F[5][:, 0:2]
        for tt in range(8):
            src32 = p.dram["l2_kout"].rearrange(
                "c (pl k1 k2) -> c pl k1 k2", pl=2, k1=33)[128 * tt:128 * tt + 128, :, 32, :]
            S.dma("sdma", stg32.rearrange("p (pl k2) -> p pl k2", pl=2), src32,
                  reads=("l2_kout",), writes=("F4",), tag="l32")
            S.add("vector", lambda eng: eng.tensor_tensor(stg32, stg32, m32f, ALU.mult),
                  reads=("F4", "M32"), writes=("F4",), tag="m32")
            S.dma("sdma", src32, stg32.rearrange("p (pl k2) -> p pl k2", pl=2),
                  reads=("F4",), writes=("l2_kout",), tag="s32")
            src0 = p.dram["l2_kout"].rearrange(
                "c (pl x) -> c pl x", pl=2)[128 * tt:128 * tt + 128, :, 0]
            S.dma("sdma", stg0, src0, reads=("l2_kout",), writes=("F5",), tag="l0b",
                  allow_slow_non_contiguous=True)
            S.add("vector", lambda eng: eng.tensor_scalar_mul(stg0, stg0, 0.5),
                  reads=("F5",), writes=("F5",), tag="h0b")
            S.dma("sdma", src0, stg0, reads=("F5",), writes=("l2_kout",), tag="s0b",
                  allow_slow_non_contiguous=True)
        if part == "fusedk":
            return
        nbu = 128
        for j in range(nbu + 10):
            if 0 <= j - 9 < nbu:
                yev_st(j - 9)
            if 0 <= j - 8 < nbu:
                y_st(j - 8)
            if 0 <= j - 7 < nbu:
                iv_st(j - 7)
            if 0 <= j - 6 < nbu:
                a_st(j - 6)
            if 0 <= j - 5 < nbu:
                pw_st(j - 5)
            if 0 <= j - 4 < nbu:
                x_st(j - 4)
            if 0 <= j - 3 < nbu:
                tw_st(j - 3)
                kl_st(j - 3)
            if 0 <= j - 2 < nbu:
                s1_st(j - 2)
            if 0 <= j < nbu:
                fld_st(j, "u32")
        S.add("vector", lambda eng: eng.engine_nop(),
              reads=("FC", "KH0", "KH1", "YS0", "YS1", "M32"), writes=("BIG1",),
              tag="fcbar2")
        return
    for bb in (range(16) if dbg_nocc != "input" else ()):
        fwd_batch("l2_kt", bb * 8, 8, True)
    if dbg_nocc == "input":
        S.add("vector", lambda eng: eng.engine_nop(), reads=(), writes=("l2_kout",))
        for bb in range(128):
            fwd_batch("u32", bb * 8, 8, False)
        return
    m32 = p.B1.ap().bitcast(F32)[:, 5120:5376]
    S.dma("sdma", m32, p.dram["l2_k32m"][:, :], reads=("l2_k32m",), writes=("BIG1",))
    stg32 = p.F[4][:, 0:256]
    src32 = p.dram["l2_kin"].rearrange("c (pl k1 k2) -> c pl k1 k2", pl=2, k1=33)[:, :, 32, :]
    S.dma("sdma", stg32.rearrange("p (pl k2) -> p pl k2", pl=2), src32,
          reads=("l2_kin",), writes=("F4",), tag="l32")
    S.add("vector", lambda eng: eng.tensor_tensor(stg32, stg32, m32, ALU.mult),
          reads=("F4", "BIG1"), writes=("F4",), tag="m32")
    S.dma("sdma", src32, stg32.rearrange("p (pl k2) -> p pl k2", pl=2),
          reads=("F4",), writes=("l2_kin",), tag="s32")
    stg0 = p.F[4][:, 256:258]
    src0 = p.dram["l2_kin"].rearrange("c (pl x) -> c pl x", pl=2)[:, :, 0]
    S.dma("sdma", stg0, src0, reads=("l2_kin",), writes=("F4",), tag="l0b",
          allow_slow_non_contiguous=True)
    S.add("vector", lambda eng: eng.tensor_scalar_mul(stg0, stg0, 0.5),
          reads=("F4",), writes=("F4",))
    S.dma("sdma", src0, stg0, reads=("F4",), writes=("l2_kin",), tag="s0b",
          allow_slow_non_contiguous=True)
    if part == "k":
        return
    if not dbg_nocc:
        def cc(eng):
            return eng.collective_compute(
                "AllGather", ALU.bypass,
                replica_groups=[list(range(N_CORES))],
                ins=[p.dram["l2_kin"][:, :]],
                outs=[p.dram["l2_kout"][:, :]])
        S.add("gdma", cc, reads=("l2_kin",), writes=("l2_kout",), tag="ccag")
    elif dbg_nocc == "copy":
        S.dma("sdma", p.dram["l2_kout"][0:128, :], p.dram["l2_kin"][:, :],
              reads=("l2_kin",), writes=("l2_kout",), tag="ccdbg")
    for bb in range(128):
        fwd_batch("u32", bb * 8, 8, False)





def emit_l2glu(p):
    S = p.sched
    src = p.dram["l2_gluWT"].rearrange("(f p) o -> p f o", p=128)
    S.dma("gdma", p.B1[:, 0:16384].rearrange("p (f o) -> p f o", f=NF), src,
          reads=("l2_gluWT",), writes=("BIG1",), tag="lglu")
    dsk = p.S[0][:, 0:8]
    S.dma("sdma", dsk, p.dram["l2_dsk"][:, :], reads=("l2_dsk",), writes=("S0",))
    glb = p.S[1][:, 0:16]
    S.dma("sdma", glb, p.dram["l2_glub"][:, :], reads=("l2_glub",), writes=("S1",))
    yb = [(p.F[0], "F0"), (p.F[1], "F1")]
    ub = [(p.F[2], "F2")]
    for ot in range(NF):
        for tc in range(8):
            b = tc % 2
            S.dma("sdma", yb[b][0][:, 0:512], p.dram["yconv"][bass.ts(ot, 128), bass.ts(tc, 512)],
                  reads=("yconv",), writes=(yb[b][1],), tag="ly")
            S.dma("sdma", ub[0][0][:, 0:512], p.dram["u32"][bass.ts(ot, 128), bass.ts(tc, 512)],
                  reads=("u32",), writes=(ub[0][1],), tag="lu")
            def yds(eng, b=b, ot=ot):
                return eng.tensor_scalar(ub[0][0][:, 0:512], ub[0][0][:, 0:512],
                                         dsk[:, ot:ot + 1], None, ALU.mult, ALU.bypass)
            S.add("vector", yds, reads=(ub[0][1], "S0"), writes=(ub[0][1],), tag="uds")
            S.add("vector", lambda eng, b=b: eng.tensor_add(yb[b][0][:, 0:512],
                                                            yb[b][0][:, 0:512],
                                                            ub[0][0][:, 0:512]),
                  reads=(yb[b][1], ub[0][1]), writes=(yb[b][1],))
            def gl(eng, b=b, ot=ot, tc=tc):
                return eng.activation(p.B2[:, ot * 4096 + tc * 512: ot * 4096 + tc * 512 + 512],
                                      yb[b][0][:, 0:512], AF.Gelu)
            S.add("scalar", gl, reads=(yb[b][1], "fconsts"), writes=("BIG2",), tag="gel2")
    gsb = [(p.F[3], "F3")]
    for tc in range(8):
        for ot in range(NF):
            mm_acc(p, S, "gla%d_%d" % (tc, ot), p.bank(0), bkey(0),
                   [(p.B1[:, f * 2048 + ot * 128: f * 2048 + ot * 128 + 128], "BIG1")
                    for f in range(NF)],
                   [(p.B2.ap()[:, f * 4096 + tc * 512: f * 4096 + tc * 512 + 512], "BIG2")
                    for f in range(NF)])
            mm_acc(p, S, "glg%d_%d" % (tc, ot), p.bank(1), bkey(1),
                   [(p.B1[:, f * 2048 + (ot + 8) * 128: f * 2048 + (ot + 8) * 128 + 128],
                     "BIG1") for f in range(NF)],
                   [(p.B2.ap()[:, f * 4096 + tc * 512: f * 4096 + tc * 512 + 512], "BIG2")
                    for f in range(NF)])
            def sg(eng, ot=ot):
                return eng.activation(gsb[0][0][:, 0:512], p.bank(1), AF.Sigmoid,
                                      bias=glb[:, 8 + ot:9 + ot])
            S.add("scalar", sg, reads=(bkey(1), "S1", "fconsts"), writes=("F3",), tag="sg2")
            def y3a(eng, ot=ot):
                return eng.scalar_tensor_tensor(p.F[4][:, 0:512], p.bank(0),
                                                glb[:, ot:ot + 1], gsb[0][0][:, 0:512],
                                                ALU.add, ALU.mult)
            S.add("vector", y3a, reads=(bkey(0), "S1", "F3"), writes=("F4",), tag="y3a")
            def y3c(eng, ot=ot):
                return eng.copy(p.B[0][:, 0:512], p.F[4][:, 0:512])
            S.add("scalar", y3c, reads=("F4",), writes=("B0",), tag="y3c")
            S.dma("sdma", p.dram["l2_y3"][bass.ts(ot, 128), bass.ts(tc, 512)],
                  p.B[0][:, 0:512], reads=("B0",), writes=("l2_y3",), tag="y3s")
    load_weight_bf16(p, S, "l2_outWT", p.Wsl[0][:, :], "W0", D, D)
    obr = p.F[6]
    S.dma("sdma", obr[:, :], p.dram["l2_ob"][:, :], reads=("l2_ob",), writes=("F6",))
    y3t = [(p.B[2], "B2"), (p.B[3], "B3")]
    xnt = [(p.F[7], "F7")]
    sev = [(p.F[5], "F5")]
    for i in range(NT):
        b = i % 2
        src2 = p.dram["l2_y3"].rearrange("(f p) t -> p f t", p=128)[:, :, bass.ts(i, 128)]
        S.dma("sdma", y3t[b][0].ap().rearrange("p (f c) -> p f c", f=NF), src2,
              reads=("l2_y3",), writes=(y3t[b][1],), tag="ly3")
        for oc in range(2):
            mm_acc(p, S, "op%d_%d" % (i, oc), p.bank(4 + oc), bkey(4 + oc),
                   [(y3t[b][0][:, bass.ts(f, 128)], y3t[b][1]) for f in range(NF)],
                   [(p.Wsl[0][:, f * D + oc * 512: f * D + oc * 512 + 512], "W0")
                    for f in range(NF)])
        S.add("vector", lambda eng, b=b: eng.tensor_tensor(
            sev[0][0][:, :], p.bankpair(2)[:, :], obr[:, :], ALU.add),
            reads=(bkey(4), bkey(5), "F6"), writes=(sev[0][1],), tag="s2ev")
        S.dma("sdma", xnt[0][0][:, :], p.dram["xn2"][bass.ts(i, 128), :],
              reads=("xn2",), writes=(xnt[0][1],), tag="lxn2g")
        S.add("vector", lambda eng: eng.tensor_add(sev[0][0][:, :], sev[0][0][:, :],
                                                   xnt[0][0][:, :]),
              reads=(sev[0][1], xnt[0][1]), writes=(sev[0][1],))
        S.dma("sdma", p.dram["s2"][bass.ts(i, 128), :], sev[0][0][:, :],
              reads=(sev[0][1],), writes=("s2",), tag="sts2")


def full_phase_list():
    return [
        emit_l2khat,
        lambda p: emit_ln(p, "x0", "xn0", "xn0T", "ln0"),
        emit_l0attn,
        emit_l0ffn,
        lambda p: emit_ln(p, "x1", "xn1", "xn1T", "ln1"),
        emit_l1attn,
        emit_l1ffn,
        lambda p: emit_ln(p, "x2", "xn2", "xn2T", "ln2"),
        emit_l2inproj,
        emit_l2fft,
        emit_l2glu,
        emit_l2ffn_a,
        emit_l2ffn_b,
        lambda p: emit_ln(p, "x3", "xn3", "xn3T", "ln3"),
        emit_l3a,
        emit_l3b,
        emit_l3c,
    ]


def full_dram_specs():
    EI = "ExternalInput"
    IN = "Internal"
    sp = [
        ("ident", [128, 128], BF16, EI), ("fconsts", [128, 64], F32, EI),
        ("ones1", [1, 128], BF16, EI), ("tri128", [128, 128], BF16, EI),
        ("triexc", [32, 32], BF16, EI), ("indcols", [128, 1024], BF16, EI),
        ("jrow16", [1, 16], F32, EI), ("poscols", [128, NT], F32, EI),
        ("posrows2", [1, 2 * T], BF16, EI), ("cs32", [T, 64], F32, EI),
        ("l1_masks", [128, 5 * 512], BF16, EI),
        ("fF2r", [64, 128], BF16, EI), ("fF2i", [64, 128], BF16, EI),
        ("fF1r", [128, 66], BF16, EI), ("fF1i", [128, 66], BF16, EI),
        ("fF1n", [128, 66], BF16, EI),
        ("fE1r", [66, 128], BF16, EI), ("fE1i", [66, 128], BF16, EI),
        ("fE1n2", [66, 128], BF16, EI),
        ("fE2r", [128, 64], BF16, EI), ("fE2n", [128, 64], BF16, EI),
        ("ftWr", [128, 512], F32, EI), ("ftWi", [128, 512], F32, EI),
        ("ftVr", [128, 512], F32, EI), ("ftVi", [128, 512], F32, EI),
        ("l2_k32m", [128, 256], F32, EI),
        ("x0", [T, D], F32, EI),
        ("l0_Wq", [D, D], F32, EI), ("l0_Wk", [D, D], F32, EI),
        ("l0_Wv", [D, D], F32, EI), ("l0_Wo", [D, D], F32, EI),
        ("l0_bq", [128, D], F32, EI), ("l0_bk", [128, D], F32, EI),
        ("l0_bv", [128, D], F32, EI), ("l0_bo", [128, D], F32, EI),
        ("l0_nrope", [1, 1], F32, EI),
        ("l0f_WrT", [D, D], F32, EI), ("l0f_WkT", [D, D], F32, EI),
        ("l0f_WvT", [D, D], F32, EI),
        ("l0f_mur", [128, D], F32, EI), ("l0f_muk", [128, D], F32, EI),
        ("l1_wattnT", [D, 3 * D], F32, EI), ("l1_wprojT", [D, D], F32, EI),
        ("l1f_wT", [D, D], F32, EI),
        ("l1f_scale", [128, D], F32, EI), ("l1f_bias", [128, D], F32, EI),
        ("l2_inWT", [D, D], F32, EI), ("l2_inb", [128, 8], F32, EI),
        ("l2_ldt", [128, 32], F32, EI), ("l2_lar", [128, 32], F32, EI),
        ("l2_aim", [128, 32], F32, EI), ("l2_Cr", [128, 32], F32, EI),
        ("l2_Ci", [128, 32], F32, EI), ("l2_bd", [128, 8], F32, EI),
        ("l2_gluWT", [D, 2048], F32, EI), ("l2_glub", [128, 16], F32, EI),
        ("l2_dsk", [128, 8], F32, EI),
        ("l2_outWT", [D, D], F32, EI), ("l2_ob", [128, D], F32, EI),
        ("l2f_Wq", [D, D], F32, EI), ("l2f_Wo", [D, D], F32, EI),
        ("l2f_Wdown", [D, 64], F32, EI), ("l2f_Wup", [64, 2048], F32, EI),
        ("l2f_Wg", [D, 8], F32, EI), ("l2f_We", [8, D, D], F32, EI),
        ("l2f_g1", [128, D], F32, EI), ("l2f_g2", [128, D], F32, EI),
        ("l2f_bq", [128, D], F32, EI), ("l2f_bo", [128, D], F32, EI),
        ("l2f_bup", [128, 2048], F32, EI), ("l2f_bdn", [128, 64], F32, EI),
        ("l2f_bg", [128, 8], F32, EI),
        ("l3_WmT", [D, D], F32, EI), ("l3_WqT", [D, D], F32, EI),
        ("l3_bm", [128, D], F32, EI), ("l3_bq", [128, D], F32, EI),
        ("l3_phases", [128, D], F32, EI),
        ("l3_W1T", [4 * D, 2048], F32, EI), ("l3_b1", [1, 2048], F32, EI),
        ("l3_lng", [128, 32], F32, EI), ("l3_lnb", [128, 32], F32, EI),
        ("l3_W2T", [2048, D], F32, EI), ("l3_b2", [128, D], F32, EI),
        ("l3f_wT", [D, D], F32, EI),
        ("xn0", [T, D], F32, IN), ("xn0T", [D, T], BF16, IN),
        ("l0_qT", [D, T], BF16, IN), ("s0", [T, D], F32, IN),
        ("x1", [T, D], F32, IN), ("xn1", [T, D], F32, IN), ("xn1T", [D, T], BF16, IN),
        ("l1_qT", [D, T], BF16, IN), ("l1_kT", [D, T], BF16, IN),
        ("s1", [T, D], F32, IN),
        ("x2", [T, D], F32, IN), ("xn2", [T, D], F32, IN), ("xn2T", [D, T], BF16, IN),
        ("u32", [D, T], F32, IN), ("l2_kt", [128, T], F32, IN),
        ("l2_kin", [128, 8448], F32, IN), ("l2_kout", [1024, 8448], F32, IN, "Shared"),
        ("yconv", [D, T], F32, IN), ("l2_y3", [D, T], BF16, IN),
        ("s2", [T, D], F32, IN),
        ("l2_xmid", [T, D], F32, IN), ("l2_h2T", [D, T], BF16, IN),
        ("l2_gate", [T, 8], F32, IN),
        ("x3", [T, D], F32, IN), ("xn3", [T, D], F32, IN), ("xn3T", [D, T], BF16, IN),
        ("l3_qoff", [T, D], F32, IN),
        ("l3_wc", [T, D], BF16, IN), ("l3_ws", [T, D], BF16, IN),
        ("l3_mg", [T, D], BF16, IN),
        ("l3_carr", [32, 3072], BF16, IN), ("l3_hT", [2048, T], BF16, IN),
        ("y", [T, D], F32, "ExternalOutput"),
    ]
    return sp


def build_inmaps(inputs, fconst_vals, fused=False):
    z = inputs
    c = build_consts()
    fc = np.zeros((128, 64), np.float32)
    fc[:, :len(fconst_vals)] = np.asarray(fconst_vals, np.float32)[None, :]
    shared = dict(c)
    shared["fconsts"] = fc
    shared["l2_bd"] = c["l2_bdz"]
    del shared["l2_bdz"]
    shared.update({
        "l0_Wq": z["L0_attn_Wq"], "l0_Wk": z["L0_attn_Wk"],
        "l0_Wv": z["L0_attn_Wv"], "l0_Wo": z["L0_attn_Wo"],
        "l0_bq": _tile128(z["L0_attn_bq"]), "l0_bk": _tile128(z["L0_attn_bk"]),
        "l0_bv": _tile128(z["L0_attn_bv"]), "l0_bo": _tile128(z["L0_attn_bo"]),
        "l0_nrope": np.asarray(z["L0_attn_nrope_freq"], np.float32).reshape(1, 1),
        "l0f_WrT": np.ascontiguousarray(np.asarray(z["L0_ffn_W_r"]).T),
        "l0f_WkT": np.ascontiguousarray(np.asarray(z["L0_ffn_W_k"]).T),
        "l0f_WvT": np.ascontiguousarray(np.asarray(z["L0_ffn_W_v"]).T),
        "l0f_mur": _tile128(z["L0_ffn_mu_r"]), "l0f_muk": _tile128(z["L0_ffn_mu_k"]),
        "l1_wattnT": np.ascontiguousarray(np.asarray(z["L1_attn_w_attn"]).T),
        "l1_wprojT": np.ascontiguousarray(np.asarray(z["L1_attn_w_proj"]).T),
        "l1f_wT": np.ascontiguousarray(np.asarray(z["L1_ffn_weights"]).T),
        "l1f_scale": _tile128(z["L1_ffn_scale"]), "l1f_bias": _tile128(z["L1_ffn_bias"]),
        "l2_inWT": np.ascontiguousarray(np.asarray(z["L2_attn_in_W"]).T),
        "l2_inb": np.ascontiguousarray(np.asarray(z["L2_attn_in_b"], np.float32).reshape(8, 128).T),
        "l2_gluWT": np.ascontiguousarray(np.asarray(z["L2_attn_glu_W"]).T),
        "l2_glub": np.ascontiguousarray(np.asarray(z["L2_attn_glu_b"], np.float32).reshape(16, 128).T),
        "l2_dsk": np.ascontiguousarray(np.asarray(z["L2_attn_D_skip"], np.float32).reshape(8, 128).T),
        "l2_outWT": np.ascontiguousarray(np.asarray(z["L2_attn_out_W"]).T),
        "l2_ob": _tile128(z["L2_attn_out_b"]),
        "l2f_Wq": np.asarray(z["L2_ffn_Wq"]), "l2f_Wo": np.asarray(z["L2_ffn_Wo"]),
        "l2f_Wdown": np.asarray(z["L2_ffn_Wdown"]), "l2f_Wup": np.asarray(z["L2_ffn_Wup"]),
        "l2f_Wg": np.asarray(z["L2_ffn_Wg"]), "l2f_We": np.asarray(z["L2_ffn_We"]),
        "l2f_g1": _tile128(z["L2_ffn_gamma1"]), "l2f_g2": _tile128(z["L2_ffn_gamma2"]),
        "l2f_bq": _tile128(z["L2_ffn_bq"]), "l2f_bo": _tile128(z["L2_ffn_bo"]),
        "l2f_bup": _tile128(z["L2_ffn_bup"]),
        "l2f_bdn": np.ascontiguousarray(np.broadcast_to(
            np.asarray(z["L2_ffn_bdown"], np.float32)[None, :], (128, 64))),
        "l2f_bg": np.ascontiguousarray(np.broadcast_to(
            np.asarray(z["L2_ffn_bg"], np.float32)[None, :], (128, 8))),
        "l3_WmT": np.ascontiguousarray(np.asarray(z["L3_attn_W_mag"]).T),
        "l3_WqT": np.ascontiguousarray(np.asarray(z["L3_attn_W_q"]).T),
        "l3_bm": _tile128(z["L3_attn_b_mag"]), "l3_bq": _tile128(z["L3_attn_b_q"]),
        "l3_phases": _tile128(z["L3_attn_random_phases"]),
        "l3_W1T": np.ascontiguousarray(np.asarray(z["L3_attn_W1"]).T),
        "l3_b1": np.asarray(z["L3_attn_b1"], np.float32).reshape(1, 2048),
        "l3_lng": np.ascontiguousarray(np.asarray(z["L3_attn_ln_g"], np.float32).reshape(32, 128).T),
        "l3_lnb": np.ascontiguousarray(np.asarray(z["L3_attn_ln_b"], np.float32).reshape(32, 128).T),
        "l3_W2T": np.ascontiguousarray(np.asarray(z["L3_attn_W2"]).T),
        "l3_b2": _tile128(z["L3_attn_b2"]),
        "l3f_wT": np.ascontiguousarray(np.asarray(z["L3_ffn_weight"]).T),
    })

    def shard_arrange(v2):
        v2 = np.asarray(v2, np.float32).reshape(128, 32)
        out = v2.reshape(32, 4, 32).transpose(1, 2, 0).reshape(128, 32)
        return np.ascontiguousarray(out)

    X = np.asarray(z["L0_attn_x"], np.float32)
    ldt_full = np.broadcast_to(np.asarray(z["L2_attn_log_dt"], np.float32)[:, None], (1024, 32))
    if fused:
        def chrow(v2):
            v2 = np.asarray(v2, np.float32).reshape(1024, 32)
            return np.ascontiguousarray(
                v2.reshape(8, 128, 32).transpose(1, 0, 2).reshape(128, 256))
        shared["l2_ldt"] = chrow(ldt_full)
        shared["l2_lar"] = chrow(np.asarray(z["L2_attn_log_A_real"]))
        shared["l2_aim"] = chrow(np.asarray(z["L2_attn_A_imag"]))
        shared["l2_Cr"] = chrow(np.asarray(z["L2_attn_C"])[:, :, 0])
        shared["l2_Ci"] = chrow(np.asarray(z["L2_attn_C"])[:, :, 1])
        return [dict(shared, x0=np.ascontiguousarray(X[core]))
                for core in range(N_CORES)]
    maps = []
    for core in range(N_CORES):
        m = dict(shared)
        m["x0"] = np.ascontiguousarray(X[core])
        c0 = 128 * core
        m["l2_ldt"] = shard_arrange(ldt_full[c0:c0 + 128])
        m["l2_lar"] = shard_arrange(np.asarray(z["L2_attn_log_A_real"])[c0:c0 + 128])
        m["l2_aim"] = shard_arrange(np.asarray(z["L2_attn_A_imag"])[c0:c0 + 128])
        m["l2_Cr"] = shard_arrange(np.asarray(z["L2_attn_C"])[c0:c0 + 128, :, 0])
        m["l2_Ci"] = shard_arrange(np.asarray(z["L2_attn_C"])[c0:c0 + 128, :, 1])
        maps.append(m)
    return maps


_NC_CACHE = {}


def phases_launch1():
    return [
        emit_l2khat,
        lambda p: emit_ln(p, "x0", "xn0", "xn0T", "ln0"),
        emit_l0attn,
        emit_l0ffn,
        lambda p: emit_ln(p, "x1", "xn1", "xn1T", "ln1"),
        emit_l1attn,
        emit_l1ffn,
        lambda p: emit_ln(p, "x2", "xn2", "xn2T", "ln2"),
        emit_l2inproj,
        lambda p: emit_l2fft(p, part="k"),
    ]


def phases_launch2():
    return [
        lambda p: emit_l2fft(p, part="u"),
        emit_l2glu,
        emit_l2ffn_a,
        emit_l2ffn_b,
        lambda p: emit_ln(p, "x3", "xn3", "xn3T", "ln3"),
        emit_l3a,
        emit_l3b,
        emit_l3c,
    ]


_L1_OUT = ("l2_kin", "x2", "xn2", "xn2T", "u32")
_L2_EXTRA_IN = ("l2_kout", "x2", "xn2", "xn2T", "u32")


def specs_launch1():
    used = ("ident fconsts ones1 tri128 jrow16 poscols posrows2 cs32 l1_masks "
            "fF2r fF2i fF1r fF1i fF1n fE1r fE1i fE1n2 fE2r fE2n ftWr ftWi ftVr ftVi "
            "l2_k32m x0 l0_Wq l0_Wk l0_Wv l0_Wo l0_bq l0_bk l0_bv l0_bo l0_nrope "
            "l0f_WrT l0f_WkT l0f_WvT l0f_mur l0f_muk l1_wattnT l1_wprojT l1f_wT "
            "l1f_scale l1f_bias l2_inWT l2_inb l2_ldt l2_lar l2_aim l2_Cr l2_Ci "
            "l2_bd").split()
    scr = ("xn0 xn0T l0_qT s0 x1 xn1 xn1T l1_qT l1_kT s1 l2_kt").split()
    sp = [s for s in full_dram_specs() if s[0] in used]
    sp += [(n, sh, dt, "Internal") for (n, sh, dt, k, *r) in full_dram_specs() if n in scr]
    for (n, sh, dt, k, *r) in full_dram_specs():
        if n in _L1_OUT:
            sp.append((n, sh, dt, "ExternalOutput"))
    return sp


def specs_launch2():
    used = ("ident fconsts ones1 tri128 triexc indcols poscols "
            "fF2r fF2i fF1r fF1i fF1n fE1r fE1i fE1n2 fE2r fE2n ftWr ftWi ftVr ftVi "
            "l2_gluWT l2_glub l2_dsk l2_outWT l2_ob "
            "l2f_Wq l2f_Wo l2f_Wdown l2f_Wup l2f_Wg l2f_We l2f_g1 l2f_g2 l2f_bq "
            "l2f_bo l2f_bup l2f_bdn l2f_bg l3_WmT l3_WqT l3_bm l3_bq l3_phases "
            "l3_W1T l3_b1 l3_lng l3_lnb l3_W2T l3_b2 l3f_wT").split()
    scr = ("yconv l2_y3 s2 l2_xmid l2_h2T l2_gate x3 xn3 xn3T l3_qoff l3_wc "
           "l3_ws l3_mg l3_carr l3_hT").split()
    sp = [s for s in full_dram_specs() if s[0] in used]
    sp += [(n, sh, dt, "Internal") for (n, sh, dt, k, *r) in full_dram_specs() if n in scr]
    for (n, sh, dt, k, *r) in full_dram_specs():
        if n in _L2_EXTRA_IN:
            sp.append((n, sh, dt, "ExternalInput"))
    sp.append(("y", [T, D], F32, "ExternalOutput"))
    return sp


def kernel_2launch(**inputs):
    if "nc1" not in _NC_CACHE:
        _NC_CACHE["nc1"] = build_nc_phase(phases_launch1(), specs_launch1())
        _NC_CACHE["nc2"] = build_nc_phase(phases_launch2(), specs_launch2())
    nc1, nc2 = _NC_CACHE["nc1"], _NC_CACHE["nc2"]
    fvals = sorted(set(nc1._prog.fconst_vals) | set(nc2._prog.fconst_vals))
    m1_all = build_inmaps(inputs, nc1._prog.fconst_vals)
    names1 = set(s[0] for s in specs_launch1() if s[3] == "ExternalInput")
    in1 = [{k2: v for k2, v in m.items() if k2 in names1} for m in m1_all]
    r1 = run_bass_kernel_spmd(nc1, in1, core_ids=list(range(N_CORES)))
    kout = np.concatenate([np.asarray(r1.results[c]["l2_kin"]) for c in range(N_CORES)], 0)
    m2_all = build_inmaps(inputs, nc2._prog.fconst_vals)
    names2 = set(s[0] for s in specs_launch2() if s[3] == "ExternalInput")
    in2 = []
    for c in range(N_CORES):
        m = {k2: v for k2, v in m2_all[c].items() if k2 in names2}
        m["l2_kout"] = kout
        for nm in ("x2", "xn2", "xn2T", "u32"):
            m[nm] = np.asarray(r1.results[c][nm])
        in2.append(m)
    r2 = run_bass_kernel_spmd(nc2, in2, core_ids=list(range(N_CORES)))
    return np.stack([np.asarray(r2.results[c]["y"]) for c in range(N_CORES)])



def emit_l2khat_full(p):
    S = p.sched
    p.fconst(float(np.pi / 2))
    W = 256
    ldt = p.F[4][:, 0:W]
    lar = p.F[4][:, 256:512]
    aim = p.F[4][:, 512:768]
    Cr = p.F[5][:, 0:W]
    Ci = p.F[5][:, 256:512]
    t1 = p.F[6][:, 0:W]
    t2 = p.F[6][:, 256:512]
    t3 = p.F[6][:, 512:768]
    lr = p.F[8][:, 0:256]
    li = p.F[8][:, 256:512]
    cfr = p.F[8][:, 512:768]
    cfi = p.F[8][:, 768:1024]
    for nm, dst, dk in (("l2_ldt", ldt, "F4"), ("l2_lar", lar, "F4"),
                        ("l2_aim", aim, "F4"), ("l2_Cr", Cr, "F5"),
                        ("l2_Ci", Ci, "F5")):
        S.dma("sdma", dst, p.dram[nm][:, :], reads=(nm,), writes=(dk,), tag="lmod")
    A_ = lambda u, fn, r, w, tg="kh": S.add(u, fn, reads=r, writes=w, tag=tg)
    A_("scalar", lambda eng: eng.activation(ldt, ldt, AF.Exp), ("F4", "fconsts"), ("F4",))
    A_("scalar", lambda eng: eng.activation(lar, lar, AF.Exp), ("F4", "fconsts"), ("F4",))
    A_("vector", lambda eng: eng.tensor_scalar_mul(lar, lar, -1.0), ("F4",), ("F4",))
    A_("vector", lambda eng: eng.tensor_tensor(t1, lar, ldt, ALU.mult), ("F4",), ("F6",))
    A_("vector", lambda eng: eng.tensor_tensor(t2, aim, ldt, ALU.mult), ("F4",), ("F6",))
    A_("scalar", lambda eng: eng.activation(t1, t1, AF.Exp), ("F6", "fconsts"), ("F6",))
    A_("vector", lambda eng: eng.tensor_scalar(t3, t2, 1.0 / TWO_PI, MAGIC, ALU.mult, ALU.add),
       ("F6",), ("F6",))
    A_("vector", lambda eng: eng.tensor_scalar(t3, t3, MAGIC, None, ALU.subtract, ALU.bypass),
       ("F6",), ("F6",))
    A_("vector", lambda eng: eng.scalar_tensor_tensor(t2, t3, -TWO_PI, t2, ALU.mult, ALU.add),
       ("F6",), ("F6",))
    A_("vector", lambda eng: eng.tensor_scalar(t2, t2, PI_C, -PI_C, ALU.min, ALU.max),
       ("F6",), ("F6",))
    A_("scalar", lambda eng: eng.activation(li, t2, AF.Sin), ("F6", "fconsts"), ("F8",))
    A_("vector", lambda eng: eng.tensor_scalar_mul(t3, t2, -1.0), ("F6",), ("F6",))
    A_("vector", lambda eng: eng.tensor_max(t3, t3, t2), ("F6",), ("F6",))
    A_("scalar", lambda eng: eng.activation(lr, t3, AF.Sin, bias=float(np.pi / 2), scale=-1.0),
       ("F6", "fconsts"), ("F8",))
    A_("vector", lambda eng: eng.tensor_tensor(lr, lr, t1, ALU.mult), ("F8", "F6"), ("F8",))
    A_("vector", lambda eng: eng.tensor_tensor(li, li, t1, ALU.mult), ("F8", "F6"), ("F8",))
    aa = p.F[7][:, 0:W]
    nr = p.F[7][:, 256:512]
    ni = p.F[7][:, 512:768]
    A_("vector", lambda eng: eng.tensor_tensor(aa, lar, lar, ALU.mult), ("F4",), ("F7",))
    A_("vector", lambda eng: eng.tensor_tensor(t1, aim, aim, ALU.mult), ("F4",), ("F6",))
    A_("vector", lambda eng: eng.tensor_add(aa, aa, t1), ("F7", "F6"), ("F7",))
    A_("vector", lambda eng: eng.reciprocal(aa, aa), ("F7",), ("F7",))
    A_("vector", lambda eng: eng.tensor_scalar(t2, lr, 1.0, None, ALU.subtract, ALU.bypass),
       ("F8",), ("F6",))
    A_("vector", lambda eng: eng.tensor_tensor(nr, t2, lar, ALU.mult), ("F6", "F4"), ("F7",))
    A_("vector", lambda eng: eng.tensor_tensor(t3, li, aim, ALU.mult), ("F8", "F4"), ("F6",))
    A_("vector", lambda eng: eng.tensor_add(nr, nr, t3), ("F7", "F6"), ("F7",))
    A_("vector", lambda eng: eng.tensor_tensor(ni, li, lar, ALU.mult), ("F8", "F4"), ("F7",))
    A_("vector", lambda eng: eng.tensor_tensor(t3, t2, aim, ALU.mult), ("F6", "F4"), ("F6",))
    A_("vector", lambda eng: eng.tensor_sub(ni, ni, t3), ("F7", "F6"), ("F7",))
    A_("vector", lambda eng: eng.tensor_tensor(nr, nr, aa, ALU.mult), ("F7",), ("F7",))
    A_("vector", lambda eng: eng.tensor_tensor(ni, ni, aa, ALU.mult), ("F7",), ("F7",))
    A_("vector", lambda eng: eng.tensor_tensor(cfr, Cr, nr, ALU.mult), ("F5", "F7"), ("F8",))
    A_("vector", lambda eng: eng.tensor_tensor(t3, Ci, ni, ALU.mult), ("F5", "F7"), ("F6",))
    A_("vector", lambda eng: eng.tensor_sub(cfr, cfr, t3), ("F8", "F6"), ("F8",))
    A_("vector", lambda eng: eng.tensor_scalar_mul(cfr, cfr, 2.0), ("F8",), ("F8",))
    A_("vector", lambda eng: eng.tensor_tensor(cfi, Cr, ni, ALU.mult), ("F5", "F7"), ("F8",))
    A_("vector", lambda eng: eng.tensor_tensor(t3, Ci, nr, ALU.mult), ("F5", "F7"), ("F6",))
    A_("vector", lambda eng: eng.tensor_add(cfi, cfi, t3), ("F8", "F6"), ("F8",))
    A_("vector", lambda eng: eng.tensor_scalar_mul(cfi, cfi, -2.0), ("F8",), ("F8",))

    b1f = p.B2.ap().bitcast(F32)
    vBr = b1f[:, 0:2048].rearrange("p (n b) -> p n b", n=32)
    vBi = b1f[:, 2048:4096].rearrange("p (n b) -> p n b", n=32)
    vAr = b1f[:, 4096:6144].rearrange("p (n b) -> p n b", n=32)
    vAi = b1f[:, 6144:8192].rearrange("p (n b) -> p n b", n=32)
    vCr = b1f[:, 10240:12288]
    vCn = b1f[:, 12288:14336]
    bstg_r = p.B2[:, 28672:30720]
    bstg_i = p.B2[:, 30720:32768]
    pr = p.S[4][:, 0:32]
    pi_ = p.S[4][:, 32:64]
    sa = p.S[4][:, 64:96]
    sb_ = p.S[4][:, 96:128]
    l64r = p.S[5][:, 0:32]
    l64i = p.S[5][:, 32:64]

    def sq_pow():
        A_("vector", lambda eng: eng.tensor_tensor(sa, pr, pr, ALU.mult), ("S4",), ("S4",))
        A_("vector", lambda eng: eng.tensor_tensor(sb_, pi_, pi_, ALU.mult), ("S4",), ("S4",))
        A_("vector", lambda eng: eng.tensor_tensor(pi_, pr, pi_, ALU.mult), ("S4",), ("S4",))
        A_("vector", lambda eng: eng.tensor_scalar_mul(pi_, pi_, 2.0), ("S4",), ("S4",))
        A_("vector", lambda eng: eng.tensor_sub(pr, sa, sb_), ("S4",), ("S4",))

    def powtab(vR, vI, base_r, base_i, rdk):
        A_("vector", lambda eng, vR=vR: eng.memset(vR[:, :, 0:1], 1.0), rdk, ("BIG2",))
        A_("vector", lambda eng, vI=vI: eng.memset(vI[:, :, 0:1], 0.0), rdk, ("BIG2",))
        A_("vector", lambda eng, vR=vR, base_r=base_r: eng.tensor_copy(
            vR[:, :, 1:2], base_r.unsqueeze(2)), rdk, ("BIG2",))
        A_("vector", lambda eng, vI=vI, base_i=base_i: eng.tensor_copy(
            vI[:, :, 1:2], base_i.unsqueeze(2)), rdk, ("BIG2",))
        A_("vector", lambda eng, base_r=base_r: eng.tensor_copy(pr, base_r), rdk, ("S4",))
        A_("vector", lambda eng, base_i=base_i: eng.tensor_copy(pi_, base_i), rdk, ("S4",))
        m = 2
        while m < 64:
            sq_pow()
            mm = m
            prb = pr.unsqueeze(2).to_broadcast([128, 32, mm])
            pib = pi_.unsqueeze(2).to_broadcast([128, 32, mm])
            s1v = b1f[:, 8192:8192 + 32 * mm].rearrange("p (n b) -> p n b", n=32)
            s2v = b1f[:, 9216:9216 + 32 * mm].rearrange("p (n b) -> p n b", n=32)
            A_("vector", lambda eng, vR=vR, prb=prb, s1v=s1v, mm=mm: eng.tensor_tensor(
                s1v, vR[:, :, 0:mm], prb, ALU.mult), ("BIG2", "S4"), ("BIG2",))
            A_("vector", lambda eng, vI=vI, pib=pib, s2v=s2v, mm=mm: eng.tensor_tensor(
                s2v, vI[:, :, 0:mm], pib, ALU.mult), ("BIG2", "S4"), ("BIG2",))
            A_("vector", lambda eng, vR=vR, s1v=s1v, s2v=s2v, mm=mm: eng.tensor_tensor(
                vR[:, :, mm:2 * mm], s1v, s2v, ALU.subtract), ("BIG2",), ("BIG2",))
            A_("vector", lambda eng, vR=vR, pib=pib, s1v=s1v, mm=mm: eng.tensor_tensor(
                s1v, vR[:, :, 0:mm], pib, ALU.mult), ("BIG2", "S4"), ("BIG2",))
            A_("vector", lambda eng, vI=vI, prb=prb, s2v=s2v, mm=mm: eng.tensor_tensor(
                s2v, vI[:, :, 0:mm], prb, ALU.mult), ("BIG2", "S4"), ("BIG2",))
            A_("vector", lambda eng, vI=vI, s1v=s1v, s2v=s2v, mm=mm: eng.tensor_tensor(
                vI[:, :, mm:2 * mm], s1v, s2v, ALU.add), ("BIG2",), ("BIG2",))
            m *= 2

    for t in range(NF):
        powtab(vBr, vBi, lr[:, t * 32:(t + 1) * 32], li[:, t * 32:(t + 1) * 32], ("F8",))
        A_("vector", lambda eng: eng.tensor_copy(bstg_r, b1f[:, 0:2048]), ("BIG2",), ("BIG2",))
        A_("vector", lambda eng: eng.tensor_copy(bstg_i, b1f[:, 2048:4096]), ("BIG2",), ("BIG2",))
        for nm, stg in (("l2_Btr", bstg_r), ("l2_Bti", bstg_i)):
            def bst(eng, nm=nm, stg=stg, t=t):
                return eng.dma_start(
                    out=p.dram[nm][128 * t:128 * t + 128, :], in_=stg)
            S.add("sdma", bst, reads=("BIG2",), writes=(nm,), tag="bst")
        sq_pow()
        A_("vector", lambda eng: eng.tensor_copy(l64r, pr), ("S4",), ("S5",))
        A_("vector", lambda eng: eng.tensor_copy(l64i, pi_), ("S4",), ("S5",))
        powtab(vAr, vAi, l64r, l64i, ("S5",))
        cfrb = cfr[:, t * 32:(t + 1) * 32].unsqueeze(2).to_broadcast([128, 32, 64])
        cfib = cfi[:, t * 32:(t + 1) * 32].unsqueeze(2).to_broadcast([128, 32, 64])
        tX = b1f[:, 8192:10240].rearrange("p (n b) -> p n b", n=32)
        tY = b1f[:, 0:2048].rearrange("p (n b) -> p n b", n=32)
        v3Cr = vCr.rearrange("p (n b) -> p n b", n=32)
        v3Cn = vCn.rearrange("p (n b) -> p n b", n=32)
        A_("vector", lambda eng, cfrb=cfrb: eng.tensor_tensor(tX, vAr, cfrb, ALU.mult),
           ("BIG2", "F8"), ("BIG2",))
        A_("vector", lambda eng, cfib=cfib: eng.tensor_tensor(tY, vAi, cfib, ALU.mult),
           ("BIG2", "F8"), ("BIG2",))
        A_("vector", lambda eng, v3Cr=v3Cr: eng.tensor_tensor(v3Cr, tX, tY, ALU.add),
           ("BIG2",), ("BIG2",))
        A_("vector", lambda eng, cfib=cfib: eng.tensor_tensor(tX, vAr, cfib, ALU.mult),
           ("BIG2", "F8"), ("BIG2",))
        A_("vector", lambda eng, cfrb=cfrb: eng.tensor_tensor(tY, vAi, cfrb, ALU.mult),
           ("BIG2", "F8"), ("BIG2",))
        A_("vector", lambda eng, v3Cn=v3Cn: eng.tensor_tensor(v3Cn, tX, tY, ALU.subtract),
           ("BIG2",), ("BIG2",))
        A_("vector", lambda eng: eng.tensor_copy(bstg_r, vCr), ("BIG2",), ("BIG2",))
        A_("vector", lambda eng: eng.tensor_copy(bstg_i, vCn), ("BIG2",), ("BIG2",))
        for cof, stg in ((0, bstg_r), (2048, bstg_i)):
            def cst(eng, cof=cof, stg=stg, t=t):
                return eng.dma_start(
                    out=p.dram["l2_CAc"][128 * t:128 * t + 128, cof:cof + 2048], in_=stg)
            S.add("sdma", cst, reads=("BIG2",), writes=("l2_CAc",), tag="cst")

    cabd = p.B2.ap()[:, 0:512]
    A_("vector", lambda eng: eng.memset(cabd, 0.0), (), ("BIG2",))
    for tq in range(256):
        ch0 = 4 * tq
        brs = p.B2.ap()[:, 512:576]
        bis = p.B2.ap()[:, 576:640]
        for nm, dst in (("l2_Btr", brs), ("l2_Bti", bis)):
            def lb(eng, nm=nm, dst=dst, ch0=ch0):
                return eng.dma_start(
                    out=dst,
                    in_=p.dram[nm][ch0:ch0 + 4, :].rearrange("c (n b) -> (c n) b", n=32))
            S.add("adma", lb, reads=(nm,), writes=("BIG2",), tag="lb")
        for q in range(4):
            def sc(eng, q=q, ch0=ch0):
                dstv = p.B2.ap()[32 * q:32 * q + 32, 0:512].rearrange(
                    "p (pl x) -> p pl x", pl=2)[:, :, 64 * q:64 * q + 64]
                return eng.dma_start(
                    out=dstv,
                    in_=p.dram["l2_CAc"][ch0 + q:ch0 + q + 1, :].rearrange(
                        "c (pl n a) -> n (c pl) a", pl=2, n=32))
            S.add("sdma", sc, reads=("l2_CAc",), writes=("BIG2",), tag="casc")
        kb = tq % 2
        ps = p.bank(kb)
        kk = bkey(kb)
        mms = [(cabd[:, 0:128], brs, 0), (cabd[:, 256:384], bis, 0),
               (cabd[:, 128:256], brs, 64), (cabd[:, 384:512], bis, 64)]
        for j, (lh, rh, of) in enumerate(mms):
            def km(eng, lh=lh, rh=rh, of=of, j=j, ps=ps):
                return eng.matmul(ps[:, of:of + 64], lh, rh,
                                  start=(j == 0), stop=(j == 3))
            wk = (kk + "@acc", kk) if j == 0 else ((kk,) if j == 3 else (kk + "@acc",))
            S.add("tensor", km, reads=("BIG2",), writes=wk, tag="km")
        ev = p.F[8][:, 0:128] if tq % 2 == 0 else p.F[8][:, 128:256]
        evk = "F8"
        A_("vector", lambda eng, ev=ev, ps=ps, kk=kk: eng.tensor_copy(ev, ps[:, 0:128]),
           (kk,), (evk,))
        for hp in range(2):
            def kst(eng, hp=hp, ev=ev, ch0=ch0):
                dstv = p.dram["l2_kt"][ch0:ch0 + 4, :].rearrange(
                    "(c u) m -> u c m", u=2)[hp].rearrange("c (a b) -> a c b", b=64)
                return eng.dma_start(
                    out=dstv,
                    in_=ev[64 * hp:64 * hp + 64, :].rearrange("p (h b) -> p h b", h=2))
            S.add("gdma", kst, reads=(evk,), writes=("l2_kt",), tag="kst")


def phases_fused():
    return [
        lambda p: emit_ln(p, "x0", "xn0", "xn0T", "ln0"),
        emit_l0attn,
        emit_l0ffn,
        emit_l2khat_full,
        lambda p: emit_ln(p, "x1", "xn1", "xn1T", "ln1"),
        emit_l1attn,
        emit_l1ffn,
        lambda p: emit_ln(p, "x2", "xn2", "xn2T", "ln2"),
        emit_l2inproj,
        lambda p: emit_l2fft(p, part="fused"),
        emit_l2glu,
        emit_l2ffn_a,
        emit_l2ffn_b,
        lambda p: emit_ln(p, "x3", "xn3", "xn3T", "ln3"),
        emit_l3a,
        emit_l3b,
        emit_l3c,
    ]


def specs_fused():
    drop = {"l2_kin", "l2_bd"}
    out = []
    for s in full_dram_specs():
        n = s[0]
        if n in drop:
            continue
        if n in ("l2_ldt", "l2_lar", "l2_aim", "l2_Cr", "l2_Ci"):
            out.append((n, [128, 256], F32, "ExternalInput"))
        elif n == "l2_kt":
            out.append((n, [1024, T], F32, "Internal"))
        elif n == "l2_kout":
            out.append((n, [1024, 8448], F32, "Internal"))
        else:
            out.append(s)
    out += [("l2_Btr", [1024, 2048], BF16, "Internal"),
            ("l2_Bti", [1024, 2048], BF16, "Internal"),
            ("l2_CAc", [1024, 4096], BF16, "Internal")]
    return out


def kernel(**inputs):
    if "ncf" not in _NC_CACHE:
        _NC_CACHE["ncf"] = build_nc_phase(phases_fused(), specs_fused())
    ncf = _NC_CACHE["ncf"]
    maps = build_inmaps(inputs, ncf._prog.fconst_vals, fused=True)
    names = set(s[0] for s in specs_fused() if s[3] == "ExternalInput")
    ins = [{k2: v for k2, v in m.items() if k2 in names} for m in maps]
    r = run_bass_kernel_spmd(ncf, ins, core_ids=list(range(N_CORES)))
    return np.stack([np.asarray(r.results[c]["y"]) for c in range(N_CORES)])
